# version 01
# baseline (speedup 1.0000x reference)
import contextlib
import numpy as np
import ml_dtypes
import concourse.bass as bass
import concourse.mybir as mybir
from concourse.bass_utils import run_bass_kernel_spmd

F32 = mybir.dt.float32
BF16 = mybir.dt.bfloat16
I32 = mybir.dt.int32
I8 = mybir.dt.int8
AF = mybir.ActivationFunctionType
ALU = mybir.AluOpType
AX = mybir.AxisListType

S, D, NSEQ = 2048, 1024, 2
T = NSEQ * S
NT = T // 128
TPS = S // 128
NH, HD, NPAIR = 16, 64, 8
LNEPS = 1e-5

TWO_PI = 2.0 * np.pi
C1 = float(np.float32(TWO_PI))
C2 = float(np.float32(TWO_PI - C1))
INV2PI = float(1.0 / TWO_PI)
LN1E4 = float(np.log(10000.0))
E_HI = float(np.float32(LN1E4 / 512.0))
E_LO = float(np.float32(LN1E4 / 512.0 - E_HI))

ENGINES = ("sync", "tensor", "vector", "scalar", "gpsimd")
DEBUG_STOP = None


class Sem:
    __slots__ = ("h", "n")

    def __init__(self, nc, name):
        self.h = nc.alloc_semaphore(name=name)
        self.n = 0


class PL:
    def __init__(self, nc):
        self.nc = nc
        self.q = {e: [] for e in ENGINES}
        self._sems = {}

    def sem(self, name):
        if name not in self._sems:
            self._sems[name] = Sem(self.nc, name)
        return self._sems[name]

    def em(self, eng, f):
        self.q[eng].append(f)

    def wait(self, eng, sem, val=None):
        v = sem.n if val is None else val
        if v > 0:
            self.em(eng, lambda e, s=sem.h, v=v: e.wait_ge(s, v))

    def dma(self, eng, out, in_, sem):
        sem.n += 16
        self.em(eng, lambda e, o=out, i=in_, s=sem.h: e.dma_start(out=o, in_=i).then_inc(s, 16))

    def dma_nc(self, eng, out, in_, sem):
        sem.n += 16
        self.em(eng, lambda e, o=out, i=in_, s=sem.h: e.dma_start(
            out=o, in_=i, allow_slow_non_contiguous=True).then_inc(s, 16))

    def op(self, eng, f, sem=None):
        if sem is None:
            self.em(eng, f)
        else:
            sem.n += 1
            self.em(eng, lambda e, f=f, s=sem.h: f(e).then_inc(s, 1))

    def barrier(self):
        bs = self.sem("bar")
        bs.n += len(ENGINES)
        tgt = bs.n
        for e in ENGINES:
            self.em(e, lambda eng, s=bs.h: eng.drain().then_inc(s, 1))
        for e in ENGINES:
            self.em(e, lambda eng, s=bs.h, v=tgt: eng.wait_ge(s, v))


def roundneg(pl, eng, sem, X, R1I, DD, OUTKN, n=None):
    pl.op(eng, lambda e: e.tensor_copy(R1I, X))
    pl.op(eng, lambda e: e.scalar_tensor_tensor(DD, R1I, 1.0, X, ALU.mult, ALU.subtract))
    pl.op(eng, lambda e: e.scalar_tensor_tensor(OUTKN, DD, -0.5, R1I, ALU.is_lt, ALU.add))
    pl.op(eng, lambda e: e.scalar_tensor_tensor(OUTKN, DD, 0.5, OUTKN, ALU.is_gt, ALU.subtract), sem)


def sin_reduce(pl, sem_v, sem_s, X, SCR_I, SCR_D, SCR_K, OUTSIN, pl_eng="vector"):
    pl.op(pl_eng, lambda e: e.tensor_scalar(SCR_K, X, INV2PI, None, ALU.mult))
    roundneg(pl, pl_eng, None, SCR_K, SCR_I, SCR_D, SCR_K)
    pl.op(pl_eng, lambda e: e.scalar_tensor_tensor(SCR_D, SCR_K, -C1, X, ALU.mult, ALU.subtract))
    pl.op(pl_eng, lambda e: e.scalar_tensor_tensor(SCR_D, SCR_K, C2, SCR_D, ALU.mult, ALU.subtract), sem_v)
    pl.wait("scalar", sem_v)
    pl.op("scalar", lambda e: e.activation(OUTSIN, SCR_D, AF.Sin), sem_s)


def build(debug_stop=None):
    nc = bass.Bass("TRN2", target_bir_lowering=False)

    def din(name, shape, dt=F32):
        return nc.dram_tensor(name, shape, dt, kind="ExternalInput").ap()

    X0 = din("L0_attn_x", [T, D])
    WQ = din("L0_attn_Wq", [D, D]); BQ = din("L0_attn_bq", [D])
    WK = din("L0_attn_Wk", [D, D]); BK = din("L0_attn_bk", [D])
    WV = din("L0_attn_Wv", [D, D]); BV = din("L0_attn_bv", [D])
    W0 = din("L0_attn_W0", [D, D]); B0 = din("L0_attn_b0", [D])
    W8 = din("L0_ffn_weight_int8", [D, D], I8)
    WS8 = din("L0_ffn_weight_scale", [D, 1])
    BF8 = din("L0_ffn_bias", [D])
    VW = din("L1_attn_v_w", [D, D]); VB = din("L1_attn_v_b", [D])
    OW = din("L1_attn_out_w", [D, D]); OB = din("L1_attn_out_b", [D])
    QMW = din("L1_attn_qmod_w", [D, NH]); QMB = din("L1_attn_qmod_b", [NH])
    KMW = din("L1_attn_kmod_w", [D, NH]); KMB = din("L1_attn_kmod_b", [NH])
    GW = din("L1_attn_gate_w", [D, 256]); GB = din("L1_attn_gate_b", [256])
    MW = din("L1_attn_mod_w", [D, 128]); MB = din("L1_attn_mod_b", [128])
    MBAS = din("L1_attn_mod_basis", [8, NH, 16])
    FRQ = din("L1_attn_freqs", [NH, 16, 8, 2])
    AMP = din("L1_attn_amps", [NH, 16, 8])
    PHS = din("L1_attn_phases", [NH, 16, 8])
    CQW = din("L1_attn_cq_w", [D, 128])
    CKW = din("L1_attn_ck_w", [D, 128])
    FW1 = din("L1_attn_ffn_w1", [D, 4 * D]); FB1 = din("L1_attn_ffn_b1", [4 * D])
    FW2 = din("L1_attn_ffn_w2", [4 * D, D]); FB2 = din("L1_attn_ffn_b2", [D])
    LN1G = din("L1_attn_ln1_g", [D]); LN1B = din("L1_attn_ln1_b", [D])
    LN2G = din("L1_attn_ln2_g", [D]); LN2B = din("L1_attn_ln2_b", [D])
    QW1 = din("L1_ffn_W1", [4 * D, D]); QB1 = din("L1_ffn_b1", [4 * D])
    QW2 = din("L1_ffn_W2", [D, 4 * D]); QB2 = din("L1_ffn_b2", [D])

    OUT = nc.dram_tensor("out", [T, D], F32, kind="ExternalOutput").ap()

    RESH = nc.dram_tensor("resh", [T, D], F32, kind="Internal").ap()
    R1H = nc.dram_tensor("r1h", [T, D], F32, kind="Internal").ap()
    TRIGC = nc.dram_tensor("trigc", [S, 512], F32, kind="Internal").ap()
    TRIGS = nc.dram_tensor("trigs", [S, 512], F32, kind="Internal").ap()
    BOUNCE = nc.dram_tensor("bounce", [8192], F32, kind="Internal").ap()
    CCI = nc.dram_tensor("cci", [128, 128], F32, kind="Internal")
    CCO = nc.dram_tensor("cco", [128, 128], F32, kind="Internal")
    STSTRIDE = 128 * 4095
    STAIR = nc.dram_tensor("stair", [32 * STSTRIDE + 8], F32, kind="Internal").ap()
    CKH = nc.dram_tensor("ckh", [32 * 4095 + 8], F32, kind="Internal").ap()

    es = contextlib.ExitStack()

    def sbuf(name, shape, dt):
        return es.enter_context(nc.sbuf_tensor(name, shape, dt))

    IDN = sbuf("idn", [128, 128], BF16)
    EPSC = sbuf("epsc", [128, 1], F32)
    EPS6 = sbuf("eps6", [128, 1], F32)
    IDNI = sbuf("idni", [128, 128], F32)
    ONESB = sbuf("onesb", [128, 128], BF16)
    PS = es.enter_context(nc.psum_tensor("ps", [128, 4096], F32))

    def bank(i):
        return PS[:, 512 * i:512 * (i + 1)]

    def bankslab(i, nb):
        return PS[:, 512 * i:512 * (i + nb)]

    def bankb(i):
        return PS[:, 512 * i:512 * (i + 1)].bitcast(BF16)

    pl = PL(nc)
    qs = pl.sem("qsync")
    qg = pl.sem("qgp")
    ste = pl.sem("ste")
    sve = pl.sem("sve")
    sse = pl.sem("sse")
    sgx = pl.sem("sgx")
    sout = pl.sem("sout")

    pl.dma("sync", RESH, X0, qs)
    pl.op("gpsimd", lambda e: e.iota(IDNI[:], [[1, 128]], channel_multiplier=-1,
                                     allow_small_or_imprecise_dtypes=True))
    pl.op("gpsimd", lambda e: e.tensor_scalar(IDN[:], IDNI[:], 0.0, None, ALU.is_equal))
    pl.op("gpsimd", lambda e: e.memset(ONESB[:], 1.0))
    pl.op("gpsimd", lambda e: e.memset(EPSC[:], LNEPS))
    pl.op("gpsimd", lambda e: e.memset(EPS6[:], 1e-6), sgx)

    def p0_trig():
        tes = contextlib.ExitStack()

        def tsb(name, shape, dt):
            return tes.enter_context(nc.sbuf_tensor(name, shape, dt))
        IVF = tsb("t_ivf", [1, 512], F32)
        IOT = tsb("t_iot", [1, 512], F32)
        IVR = tsb("t_ivr", [128, 2048], F32)
        PCOL = tsb("t_pc", [128, 4], F32)
        ANG = tsb("t_ang", [128, 2048], F32)
        R1I = tsb("t_r1i", [128, 2048], I32)
        DD = tsb("t_dd", [128, 2048], F32)
        KN = tsb("t_kn", [128, 2048], F32)
        SN = tsb("t_sn", [128, 2048], F32)
        pl.op("gpsimd", lambda e: e.iota(IOT[:], [[1, 512]], channel_multiplier=0,
                                         allow_small_or_imprecise_dtypes=True))
        pl.op("gpsimd", lambda e: e.tensor_scalar(IVF[:], IOT[:], -E_HI, None, ALU.mult))
        pl.op("gpsimd", lambda e: e.tensor_scalar(IOT[:], IOT[:], -E_LO, None, ALU.mult))
        pl.op("gpsimd", lambda e: e.drain())
        pl.op("gpsimd", lambda e: e.tensor_tensor(IVF[:], IVF[:], IOT[:], ALU.add), sgx)
        pl.wait("scalar", sgx)
        pl.op("scalar", lambda e: e.activation(IVF[:], IVF[:], AF.Exp), sse)
        pl.wait("sync", sse)
        pl.dma("sync", bass.AP(BOUNCE.tensor, 0, [[512, 1], [1, 512]]), IVF[:], qs)
        pl.wait("sync", qs)
        pl.dma("sync", IVR[:], bass.AP(BOUNCE.tensor, 0, [[0, 128], [0, 4], [1, 512]]), qs)
        ivr_ready = qs.n
        pl.wait("vector", qs, ivr_ready)
        pl.wait("gpsimd", qs, ivr_ready)
        for g in range(4):
            pl.wait("gpsimd", sve)
            pl.op("gpsimd", lambda e, g=g: e.iota(PCOL[:], [[128, 4]], channel_multiplier=1,
                                                  base=512 * g,
                                                  allow_small_or_imprecise_dtypes=True), sgx)
            pl.wait("vector", sgx)
            for c in range(4):
                pl.op("vector", lambda e, c=c: e.tensor_scalar(
                    ANG[:, 512 * c:512 * (c + 1)], IVR[:, 512 * c:512 * (c + 1)],
                    PCOL[:, c:c + 1], None, ALU.mult))
            sin_reduce(pl, sve, sse, ANG[:], R1I[:], DD[:], KN[:], SN[:])
            pl.wait("sync", sse)
            pl.dma("sync", bass.AP(TRIGS.tensor, 512 * 512 * g,
                                   [[512, 128], [512 * 128, 4], [1, 512]]), SN[:], qs)
            sin_done = qs.n
            pl.wait("vector", qs, sin_done)
            pl.op("vector", lambda e: e.tensor_scalar(ANG[:], ANG[:], float(np.pi / 2), None, ALU.add))
            sin_reduce(pl, sve, sse, ANG[:], R1I[:], DD[:], KN[:], SN[:])
            pl.wait("sync", sse)
            pl.dma("sync", bass.AP(TRIGC.tensor, 512 * 512 * g,
                                   [[512, 128], [512 * 128, 4], [1, 512]]), SN[:], qs)
            pl.wait("vector", qs)

        return tes

    _tes = p0_trig()
    pl.barrier()
    _tes.close()

    RT = sbuf("ln_rt", [128, 2048], F32)
    ZB = sbuf("ln_z", [128, 2048], BF16)
    TINY = sbuf("ln_tiny", [128, 8], F32)
    LNSQ = sbuf("ln_sq", [128, 1024], F32)
    RSTD = sbuf("rstd", [128, NT], F32)
    NEGB = sbuf("negb", [128, NT], F32)
    ZT = sbuf("zt", [128, 8 * 2048], BF16)

    def resh_tile(src, tt):
        return bass.AP(src.tensor, tt * 128 * D, [[D, 128], [1, D]])

    def stats_tile(buf, rs_col, nb_col):
        c0, c1v, c2v, c3v = TINY[:, 0:1], TINY[:, 1:2], TINY[:, 2:3], TINY[:, 3:4]
        pl.op("scalar", lambda e, b=buf: e.activation(LNSQ[:], b, AF.Square), sse)
        pl.wait("vector", sse)
        pl.op("vector", lambda e, b=buf, c=c0: e.tensor_reduce(c, b, AX.X, ALU.add))
        pl.op("vector", lambda e, c=c1v: e.tensor_reduce(c, LNSQ[:], AX.X, ALU.add))
        pl.op("vector", lambda e, c=c0: e.tensor_scalar(c, c, 1.0 / D, None, ALU.mult))
        pl.op("vector", lambda e: e.drain())
        pl.op("vector", lambda e, c=c0, o=c2v: e.tensor_tensor(o, c, c, ALU.mult))
        pl.op("vector", lambda e: e.drain())
        pl.op("vector", lambda e, c=c1v, m=c2v, o=c3v: e.scalar_tensor_tensor(
            o, c, 1.0 / D, m, ALU.mult, ALU.subtract), sve)
        pl.wait("scalar", sve)
        pl.op("scalar", lambda e, v=c3v: e.activation(v, v, AF.Sqrt, bias=EPSC[:, 0:1]), sse)
        pl.wait("vector", sse)
        pl.op("vector", lambda e, v=c3v, o=rs_col: e.reciprocal(o, v))
        pl.op("vector", lambda e: e.drain())
        pl.op("vector", lambda e, m=c0, r=rs_col, o=c2v: e.tensor_tensor(o, m, r, ALU.mult))
        pl.op("vector", lambda e: e.drain())
        pl.op("vector", lambda e, i=c2v, o=nb_col: e.tensor_scalar(o, i, -1.0, None, ALU.mult), sve)
        pl.op("vector", lambda e: e.drain())

    def ln_pass(tiles, src=RESH, neg=False, make_zt=True, absmax_col=None, zt_dst=None,
                pooled=False):
        apply_done = {}
        tr_done = {}
        for li, tt in enumerate(tiles):
            buf = RT[:, 1024 * (li % 2):1024 * (li % 2) + 1024]
            zb = ZB[:, 1024 * (li % 2):1024 * (li % 2) + 1024]
            if li >= 2:
                pl.wait("sync", sve, apply_done[li - 2])
            pl.dma("sync", buf, resh_tile(src, tt), qs)
            pl.wait("scalar", qs)
            c0, c1v, c2v, c3v = TINY[:, 0:1], TINY[:, 1:2], TINY[:, 2:3], TINY[:, 3:4]
            pl.wait("scalar", sve)
            if li >= 2 and make_zt:
                pl.wait("scalar", ste, tr_done[li - 2])
            pl.op("scalar", lambda e, b=buf: e.activation(LNSQ[:], b, AF.Square), sse)
            pl.wait("vector", sse)
            pl.op("vector", lambda e, b=buf, c=c0: e.tensor_reduce(c, b, AX.X, ALU.add))
            pl.op("vector", lambda e, c=c1v: e.tensor_reduce(c, LNSQ[:], AX.X, ALU.add))
            pl.op("vector", lambda e, c=c0: e.tensor_scalar(c, c, 1.0 / D, None, ALU.mult))
            pl.op("vector", lambda e: e.drain())
            pl.op("vector", lambda e, c=c0, o=c2v: e.tensor_tensor(o, c, c, ALU.mult))
            pl.op("vector", lambda e: e.drain())
            pl.op("vector", lambda e, c=c1v, m=c2v, o=c3v: e.scalar_tensor_tensor(
                o, c, 1.0 / D, m, ALU.mult, ALU.subtract), sve)
            pl.wait("scalar", sve)
            pl.op("scalar", lambda e, v=c3v: e.activation(v, v, AF.Sqrt, bias=EPSC[:, 0:1]), sse)
            pl.wait("vector", sse)
            rs_col = RSTD[:, tt:tt + 1]
            nb_col = NEGB[:, tt:tt + 1]
            pl.op("vector", lambda e, v=c3v, o=rs_col: e.reciprocal(o, v))
            pl.op("vector", lambda e: e.drain())
            pl.op("vector", lambda e, m=c0, r=rs_col, o=c2v: e.tensor_tensor(o, m, r, ALU.mult))
            pl.op("vector", lambda e: e.drain())
            pl.op("vector", lambda e, i=c2v, o=nb_col: e.tensor_scalar(o, i, -1.0, None, ALU.mult), sve)
            pl.op("vector", lambda e: e.drain())
            if make_zt:
                if not neg:
                    pl.op("vector", lambda e, b=buf, z=zb, r=rs_col, nb=nb_col:
                          e.tensor_scalar(z, b, r, nb, ALU.mult, ALU.add), sve)
                else:
                    pl.op("vector", lambda e, b=buf, z=zb, r=rs_col, nb=nb_col:
                          e.tensor_scalar(z, b, r, nb, ALU.mult, ALU.add))
                    pl.op("vector", lambda e, z=zb: e.tensor_scalar(z, z, -1.0, None, ALU.mult), sve)
            if absmax_col is not None:
                pl.op("vector", lambda e, b=buf, r=rs_col, nb=nb_col:
                      e.tensor_scalar(LNSQ[:], b, r, nb, ALU.mult, ALU.add))
                pl.op("vector", lambda e: e.drain())
                pl.op("vector", lambda e, c=c2v: e.tensor_reduce(
                    c, LNSQ[:], AX.X, ALU.max, apply_absolute_value=True))
                pl.op("vector", lambda e: e.drain())
                pl.op("vector", lambda e, c=c2v, a=absmax_col: e.tensor_tensor(a, a, c, ALU.max), sve)
                pl.op("vector", lambda e: e.drain())
            apply_done[li] = sve.n
            if make_zt:
                ZTD = zt_dst
                pl.wait("tensor", sve)
                for fc in range(8):
                    pl.op("tensor", lambda e, fc=fc, z=zb: e.transpose(
                        bankb(7)[:, 128 * fc:128 * (fc + 1)], z[:, 128 * fc:128 * (fc + 1)], IDN[:]),
                        ste if (fc == 7 and not pooled) else None)
                if pooled:
                    for half in range(2):
                        pl.op("tensor", lambda e, z=zb, half=half, li=li: e.matmul(
                            bank(3 + half)[0:1, :], ONESB[:, 0:1], z[:, 512 * half:512 * (half + 1)],
                            start=(li == 0), stop=(li == len(tiles) - 1)),
                            ste if half == 1 else None)
                tr_done[li] = ste.n
                pl.wait("vector", ste)
                pbv = bankb(7)
                ps_in = pbv.rearrange("p (a b) -> p a b", a=8)
                zt_out = bass.AP(ZTD[:].tensor, 128 * li, [[8 * 2048, 128], [2048, 8], [1, 128]])
                pl.op("vector", lambda e, o=zt_out, i=ps_in: e.tensor_copy(o, i), sve)
                pl.wait("tensor", sve)
        return apply_done

    def ln_seq(s, src=RESH, neg=False, pooled=False):
        return ln_pass(list(range(s * TPS, (s + 1) * TPS)), src=src, neg=neg, zt_dst=ZT,
                       pooled=pooled)

    PLDR = sbuf("pldr", [1, 1024], BF16)
    PLDC = sbuf("pldc", [128, 16], BF16)
    ORT = sbuf("ort", [128, 2048], F32)
    ORT2 = sbuf("ort2", [128, 2048], F32)

    l0_es = contextlib.ExitStack()

    def sbuf0(name, shape, dt):
        return l0_es.enter_context(nc.sbuf_tensor(name, shape, dt))

    WQKV = sbuf0("wqkv", [128, 8 * 384], BF16)
    BQKVR = sbuf0("bqkv", [1, 384], BF16)
    QKP = sbuf0("qkp", [128, 16 * 256], BF16)
    VT = sbuf0("vt", [128, 16 * 130], BF16)
    COSA = sbuf0("cosa", [128, 16 * 64], F32)
    SINA = sbuf0("sina", [128, 16 * 64], F32)
    TMP1 = sbuf0("tmp1", [128, 1024], F32)
    TMP2 = sbuf0("tmp2", [128, 1024], F32)
    QR = sbuf0("qr", [128, 2048], BF16)
    KR = sbuf0("kr", [128, 2048], BF16)
    SQB = sbuf0("sqb", [128, 2048], BF16)
    NEGBC = sbuf0("negbc", [128, 2], F32)
    ATTN = sbuf0("attn", [128, 2048], BF16)
    CTXA = sbuf0("ctxa", [128, 8 * 2048], BF16)
    SROW = sbuf0("srow", [128, 4096], F32)
    CTXTMP = sbuf0("ctxtmp", [128, 512], BF16)
    RCPR = sbuf0("rcpr", [128, 2048], F32)
    W0S = sbuf0("w0s", [128, 8 * 1024], BF16)
    B0R = sbuf0("b0r", [1, 1024], BF16)

    def l0_attention(s, dbg_pair=None):
        pairs = range(NPAIR) if dbg_pair is None else [dbg_pair]
        for hp in pairs:
            hA, hB = 2 * hp, 2 * hp + 1
            last_mm_prev = ste.n
            pl.wait("gpsimd", ste, last_mm_prev)
            for wi, W in enumerate((WQ, WK, WV)):
                src = bass.AP(W.tensor, 128 * hp, [[D, 128], [D * 128, 8], [1, 128]])
                dst = bass.AP(WQKV[:].tensor, 128 * wi, [[8 * 384, 128], [384, 8], [1, 128]])
                pl.dma("gpsimd", dst, src, qg)
            for bi, B in enumerate((BQ, BK, BV)):
                src = bass.AP(B.tensor, 128 * hp, [[128, 1], [1, 128]])
                pl.dma("gpsimd", BQKVR[0:1, 128 * bi:128 * (bi + 1)], src, qg)
            pl.wait("sync", sve)
            pl.dma("sync", COSA[:], bass.AP(TRIGC.tensor, 64 * hp,
                                            [[512, 128], [512 * 128, 16], [1, 64]]), qs)
            pl.dma("sync", SINA[:], bass.AP(TRIGS.tensor, 64 * hp,
                                            [[512, 128], [512 * 128, 16], [1, 64]]), qs)
            trig_ready = qs.n
            pl.wait("vector", ste, last_mm_prev)
            vt_ones = bass.AP(VT[:].tensor, 64, [[16 * 130, 128], [130, 16], [65, 2], [1, 1]])
            pl.op("vector", lambda e, a=vt_ones: e.memset(a, 1.0), sve)
            vt_ones_done = sve.n

            w_ready = qg.n
            pl.wait("tensor", qg, w_ready)
            for g in range(4):
                for i in range(4):
                    lt = 4 * g + i
                    for fc in range(8):
                        pl.op("tensor", lambda e, fc=fc, lt=lt, i=i: e.matmul(
                            PS[:, 512 * i:512 * i + 384],
                            ZT[:, 2048 * fc + 128 * lt:2048 * fc + 128 * (lt + 1)],
                            WQKV[:, 384 * fc:384 * (fc + 1)],
                            start=(fc == 0), stop=False))
                    pl.op("tensor", lambda e, i=i: e.matmul(
                        PS[:, 512 * i:512 * i + 384],
                        ONESB[0:1, 0:128], BQKVR[0:1, :],
                        start=False, stop=True), ste if i == 3 else None)
                mm_done = ste.n
                pl.wait("vector", ste, mm_done)
                if g == 0:
                    pl.wait("vector", sve, vt_ones_done)
                qk_in = bass.AP(PS[:].tensor, 0, [[4096, 128], [512, 4], [1, 256]])
                qk_out = bass.AP(QKP[:].tensor, 1024 * g, [[16 * 256, 128], [256, 4], [1, 256]])
                pl.op("vector", lambda e, o=qk_out, i=qk_in: e.tensor_copy(o, i))
                v_in = bass.AP(PS[:].tensor, 256, [[4096, 128], [512, 4], [64, 2], [1, 64]])
                v_out = bass.AP(VT[:].tensor, 130 * 4 * g, [[16 * 130, 128], [130, 4], [65, 2], [1, 64]])
                pl.op("vector", lambda e, o=v_out, i=v_in: e.tensor_copy(o, i), sve)
                pl.wait("tensor", sve)

            pl.wait("vector", qs, trig_ready)

            def qkp_ap(base):
                return bass.AP(QKP[:].tensor, base, [[16 * 256, 128], [256, 16], [64, 2], [2, 32]])

            def trig_ap(TTT):
                return bass.AP(TTT[:].tensor, 0, [[16 * 64, 128], [64, 16], [32, 2], [1, 32]])

            def rot_out(R, base):
                return bass.AP(R[:].tensor, base, [[2048, 128], [128, 16], [64, 2], [2, 32]])

            for base_qk, R in ((0, QR), (128, KR)):
                X1 = qkp_ap(base_qk)
                X2 = qkp_ap(base_qk + 1)
                CO = trig_ap(COSA)
                SI = trig_ap(SINA)
                pl.op("vector", lambda e, a=X1, b=CO: e.tensor_tensor(TMP1[:], a, b, ALU.mult))
                pl.op("vector", lambda e, a=X2, b=SI: e.tensor_tensor(TMP2[:], a, b, ALU.mult))
                pl.op("vector", lambda e, o=rot_out(R, 0): e.tensor_tensor(o, TMP1[:], TMP2[:], ALU.subtract))
                pl.op("vector", lambda e, a=X1, b=SI: e.tensor_tensor(TMP1[:], a, b, ALU.mult))
                pl.op("vector", lambda e, a=X2, b=CO: e.tensor_tensor(TMP2[:], a, b, ALU.mult))
                pl.op("vector", lambda e, o=rot_out(R, 1): e.tensor_tensor(o, TMP1[:], TMP2[:], ALU.add), sve)
            rot_done = sve.n

            pl.wait("tensor", sve, rot_done)
            for R in (QR, KR):
                for half in range(2):
                    for i in range(8):
                        lt = 8 * half + i
                        pl.op("tensor", lambda e, R=R, lt=lt, i=i: e.transpose(
                            bankb(7)[:, 128 * i:128 * (i + 1)],
                            R[:, 128 * lt:128 * (lt + 1)], IDN[:]), ste)
                    tr_done = ste.n
                    pl.wait("vector", ste, tr_done)
                    pbv = bankb(7).rearrange("p (a b) -> p a b", a=8)
                    out = bass.AP(R[:].tensor, 1024 * half, [[2048, 128], [128, 8], [1, 128]])
                    pl.op("vector", lambda e, o=out, i=pbv: e.tensor_copy(o, i), sve)
                    pl.wait("tensor", sve)
            trans_done = sve.n

            for which, R in (("k", KR), ("q", QR)):
                pl.op("vector", lambda e, R=R: e.tensor_tensor(SQB[:], R[:], R[:], ALU.mult), sve)
                sq_done = sve.n
                pl.wait("tensor", sve, sq_done)
                for h in range(2):
                    for qcn in range(4):
                        pl.op("tensor", lambda e, h=h, qcn=qcn: e.matmul(
                            bank(qcn)[0:1, :],
                            ONESB[64 * h:64 * h + 64, 0:1],
                            SQB[64 * h:64 * h + 64, 512 * qcn:512 * (qcn + 1)],
                            start=True, stop=True), ste if qcn == 3 else None)
                    nm_done = ste.n
                    pl.wait("vector", ste, nm_done)
                    ci = 4 + 2 * h + (0 if which == "k" else 1)
                    col = TINY[0:1, ci:ci + 1]
                    pl.op("vector", lambda e, c=col: e.tensor_reduce(
                        c, PS[0:1, 0:2048], AX.X, ALU.max), sve)
                    pl.wait("tensor", sve)
            pl.wait("scalar", sve)
            for h in range(2):
                mk2 = TINY[0:1, 4 + 2 * h:4 + 2 * h + 1]
                mq2 = TINY[0:1, 5 + 2 * h:5 + 2 * h + 1]
                pl.op("scalar", lambda e, a=mk2: e.activation(a, a, AF.Sqrt))
                pl.op("scalar", lambda e, a=mq2: e.activation(a, a, AF.Sqrt), sse)
            pl.wait("vector", sse)
            for h in range(2):
                mk = TINY[0:1, 4 + 2 * h:4 + 2 * h + 1]
                mq = TINY[0:1, 5 + 2 * h:5 + 2 * h + 1]
                pl.op("vector", lambda e, a=mq, b=mk: e.tensor_tensor(a, a, b, ALU.mult))
                pl.op("vector", lambda e: e.drain())
                pl.op("vector", lambda e, a=mq: e.tensor_scalar(a, a, -0.125, None, ALU.mult), sve)
            pl.op("vector", lambda e: e.drain(), sve)
            pl.wait("sync", sve)
            bounce_negb = bass.AP(BOUNCE.tensor, 4096, [[2, 1], [1, 2]])
            tiny_negb = bass.AP(TINY[:].tensor, 5, [[8, 1], [2, 2]])
            pl.em("sync", lambda e, o=bounce_negb, i=tiny_negb, s=qs.h: e.dma_start(
                out=o, in_=i, allow_slow_non_contiguous=True).then_inc(s, 16))
            qs.n += 16
            pl.wait("sync", qs)
            pl.dma("sync", NEGBC[:], bass.AP(BOUNCE.tensor, 4096, [[0, 128], [1, 2]]), qs)
            negbc_ready = qs.n

            pl.wait("scalar", qs, negbc_ready)
            for h in range(2):
                hbase = 64 * h
                for qc in range(4):
                    pl.wait("tensor", sse)
                    for grp in range(8):
                        jt0 = 2 * grp
                        gb = 2 * (grp % 2)
                        pl.wait("tensor", sse)
                        for j in range(2):
                            jt = jt0 + j
                            pl.op("tensor", lambda e, jt=jt, gb=gb, j=j, hbase=hbase, qc=qc: e.matmul(
                                bank(gb + j),
                                KR[hbase:hbase + 64, 128 * jt:128 * (jt + 1)],
                                QR[hbase:hbase + 64, 512 * qc:512 * (qc + 1)],
                                start=True, stop=True), ste)
                        sc_done = ste.n
                        pl.wait("scalar", ste, sc_done)
                        abuf = ATTN[:, 1024 * (grp % 2):1024 * (grp % 2) + 1024]
                        pl.op("scalar", lambda e, gb=gb, a=abuf, h=h: e.activation(
                            a, bankslab(gb, 2), AF.Exp, bias=NEGBC[:, h:h + 1], scale=0.125), sse)
                        exp_done = sse.n
                        pl.wait("tensor", sse, exp_done)
                        for j in range(2):
                            jt = jt0 + j
                            pl.op("tensor", lambda e, jt=jt, j=j, h=h, a=abuf: e.matmul(
                                bank(4)[0:65, :],
                                VT[:, 130 * jt + 65 * h:130 * jt + 65 * h + 65],
                                a[:, 512 * j:512 * (j + 1)],
                                start=(jt == 0), stop=(jt == 15)), ste if j == 1 else None)
                    ctx_done = ste.n
                    pl.wait("scalar", ste, ctx_done)
                    if h == 0:
                        pl.op("scalar", lambda e, hp=hp, qc=qc: e.copy(
                            CTXA[0:64, 2048 * hp + 512 * qc:2048 * hp + 512 * (qc + 1)],
                            bank(4)[0:64, :]))
                    else:
                        pl.op("scalar", lambda e: e.copy(CTXTMP[0:64, :], bank(4)[0:64, :]), sse)
                        pl.wait("gpsimd", sse)
                        pl.dma("gpsimd", CTXA[64:128, 2048 * hp + 512 * qc:2048 * hp + 512 * (qc + 1)],
                               CTXTMP[0:64, :], qg)
                    pl.op("scalar", lambda e, h=h, qc=qc: e.copy(
                        SROW[64:65, 2048 * h + 512 * qc:2048 * h + 512 * (qc + 1)],
                        bank(4)[64:65, :]), sse)
            pl.wait("gpsimd", sse)
            pl.wait("scalar", qg)
            pl.op("scalar", lambda e: e.activation(SROW[64:65, :], SROW[64:65, :], AF.Ln))
            pl.op("scalar", lambda e: e.activation(SROW[64:65, :], SROW[64:65, :], AF.Exp, scale=-1.0), sse)
            pl.wait("sync", sse)
            pl.dma("sync", bass.AP(BOUNCE.tensor, 0, [[4096, 1], [1, 4096]]), SROW[64:65, :], qs)
            pl.wait("sync", qs)
            pl.dma("sync", RCPR[0:64, :], bass.AP(BOUNCE.tensor, 0, [[0, 64], [1, 2048]]), qs)
            pl.dma("sync", RCPR[64:128, :], bass.AP(BOUNCE.tensor, 2048, [[0, 64], [1, 2048]]), qs)
            pl.wait("vector", qs)
            pl.wait("vector", qg)
            pl.op("vector", lambda e, hp=hp: e.tensor_tensor(
                CTXA[:, 2048 * hp:2048 * (hp + 1)], CTXA[:, 2048 * hp:2048 * (hp + 1)],
                RCPR[:], ALU.mult), sve)

        if dbg_pair is not None:
            return
        pl.dma("gpsimd", W0S[:], bass.AP(W0.tensor, 0, [[D, 128], [D * 128, 8], [1, 1024]]), qg)
        pl.dma("gpsimd", B0R[:], bass.AP(B0.tensor, 0, [[1024, 1], [1, 1024]]), qg)
        w0_ready = qg.n
        pl.wait("tensor", qg, w0_ready)
        pl.wait("tensor", sve)
        store_sem = pl.sem("l0store")
        store_base = store_sem.n
        for lt in range(TPS):
            tt = s * TPS + lt
            rbuf = ORT[:, 1024 * (lt % 2):1024 * (lt % 2) + 1024]
            obuf = ORT2[:, 1024 * (lt % 2):1024 * (lt % 2) + 1024]
            if lt >= 2:
                pl.wait("sync", store_sem, store_base + 16 * (lt - 1))
            pl.dma("sync", rbuf, resh_tile(RESH, tt), qs)
            for oc in range(2):
                bk = 5 + oc
                pl.wait("tensor", sve)
                for hp2 in range(8):
                    pl.op("tensor", lambda e, hp2=hp2, lt=lt, oc=oc, bk=bk: e.matmul(
                        bank(bk),
                        CTXA[:, 2048 * hp2 + 128 * lt:2048 * hp2 + 128 * (lt + 1)],
                        W0S[:, 1024 * hp2 + 512 * oc:1024 * hp2 + 512 * (oc + 1)],
                        start=(hp2 == 0), stop=False))
                pl.op("tensor", lambda e, oc=oc, bk=bk: e.matmul(
                    bank(bk), ONESB[0:1, 0:128], B0R[0:1, 512 * oc:512 * (oc + 1)],
                    start=False, stop=True), ste)
                mm_done = ste.n
                pl.wait("vector", ste, mm_done)
                pl.wait("vector", qs)
                pl.op("vector", lambda e, bk=bk, oc=oc, r=rbuf, o=obuf: e.scalar_tensor_tensor(
                    o[:, 512 * oc:512 * (oc + 1)], bank(bk), 1.0,
                    r[:, 512 * oc:512 * (oc + 1)], ALU.mult, ALU.add), sve)
            pl.wait("sync", sve)
            pl.dma("sync", resh_tile(RESH, tt), obuf, store_sem)
        pl.wait("sync", store_sem)

    def emit_l0_ffn():
        fes = contextlib.ExitStack()

        def sbf(n, s, d):
            return fes.enter_context(nc.sbuf_tensor(n, s, d))
        W8T = sbf("w8t", [128, 8 * 1024], BF16)
        W8N = sbf("w8n", [128, 8 * 1024], BF16)
        WSC = sbf("wsc", [128, 8], F32)
        AMX = sbf("amx", [128, 1], F32)
        GXS = sbf("gxs", [1, 4], F32)
        RXN = sbf("rxn", [128, 2], F32)
        BN8 = sbf("bn8", [1, 1024], BF16)
        B8F = sbf("b8f", [1, 1024], F32)
        CCS = sbf("ccs", [128, 128], F32)
        XQN = sbf("xqn", [128, 1024], BF16)
        QI = sbf("qi", [128, 1024], I32)
        QD = sbf("qd", [128, 1024], F32)
        QG = sbf("qg2", [128, 1024], F32)
        XQT = sbf("xqt", [128, 1024], BF16)

        pl.dma("gpsimd", W8N[:], bass.AP(W8.tensor, 0, [[1024, 128], [128 * 1024, 8], [1, 1024]]), qg)
        wsc_src = bass.AP(WS8.tensor, 0, [[1, 128], [128, 8]])
        pl.em("gpsimd", lambda e, o=WSC[:], i=wsc_src, s=qg.h: e.dma_start(
            out=o, in_=i, allow_slow_non_contiguous=True).then_inc(s, 16))
        qg.n += 16
        pl.dma("sync", B8F[:], bass.AP(BF8.tensor, 0, [[1024, 1], [1, 1024]]), qs)
        pl.op("gpsimd", lambda e: e.memset(CCS[:], 0.0), sgx)
        w8_ready = qg.n
        pl.wait("vector", qg, w8_ready)
        for oc in range(8):
            pl.op("vector", lambda e, oc=oc: e.tensor_scalar(
                W8N[:, 1024 * oc:1024 * (oc + 1)], W8N[:, 1024 * oc:1024 * (oc + 1)],
                WSC[:, oc:oc + 1], None, ALU.mult), sve)
        w8s_done = sve.n
        pl.wait("tensor", sve, w8s_done)
        for oc in range(8):
            for fc in range(8):
                pl.op("tensor", lambda e, oc=oc, fc=fc: e.transpose(
                    bankb(7)[:, 128 * fc:128 * (fc + 1)],
                    W8N[:, 1024 * oc + 128 * fc:1024 * oc + 128 * (fc + 1)], IDN[:]),
                    ste if fc == 7 else None)
            pl.wait("vector", ste)
            pbv = bankb(7).rearrange("p (a b) -> p a b", a=8)
            w8t_out = bass.AP(W8T[:].tensor, 128 * oc, [[8 * 1024, 128], [1024, 8], [1, 128]])
            pl.op("vector", lambda e, o=w8t_out, i=pbv: e.tensor_copy(o, i), sve)
            pl.wait("tensor", sve)

        pl.op("vector", lambda e: e.memset(AMX[:], 0.0), sve)
        ln_pass(list(range(NT)), make_zt=False, absmax_col=AMX[:, 0:1])
        pl.wait("sync", sve)
        pl.dma("sync", bass.AP(BOUNCE.tensor, 0, [[1, 128], [1, 1]]), AMX[:], qs)
        pl.wait("sync", qs)
        pl.dma("sync", CCS[0:1, 0:128], bass.AP(BOUNCE.tensor, 0, [[128, 1], [1, 128]]), qs)
        pl.wait("vector", qs)
        pl.wait("vector", sgx)
        pl.op("vector", lambda e: e.tensor_reduce(GXS[0:1, 0:1], CCS[0:1, 0:128], AX.X, ALU.max))
        pl.op("vector", lambda e: e.drain())
        pl.op("vector", lambda e: e.tensor_scalar(
            GXS[0:1, 0:1], GXS[0:1, 0:1], 1.0 / 127.0, 1e-5, ALU.mult, ALU.max))
        pl.op("vector", lambda e: e.drain())
        pl.op("vector", lambda e: e.tensor_copy(CCS[0:1, 0:1], GXS[0:1, 0:1]), sve)
        pl.wait("gpsimd", sve)
        pl.dma("gpsimd", CCI.ap(), CCS[:], qg)
        pl.wait("gpsimd", qg)
        ccs_done = pl.sem("ccdone")
        ccs_done.n += 1
        pl.em("gpsimd", lambda e, s=ccs_done.h: e.collective_compute(
            "AllReduce", ALU.max, replica_groups=[list(range(8))],
            ins=[CCI.ap().opt()], outs=[CCO.ap().opt()]).then_inc(s, 1))
        pl.wait("gpsimd", ccs_done)
        pl.dma("gpsimd", GXS[0:1, 1:2], CCO.ap()[0:1, 0:1], qg)
        gxs_ready = qg.n
        pl.wait("vector", qg, gxs_ready)
        pl.op("vector", lambda e: e.reciprocal(GXS[0:1, 2:3], GXS[0:1, 1:2]))
        pl.op("vector", lambda e: e.drain())
        pl.op("vector", lambda e: e.tensor_scalar(GXS[0:1, 3:4], GXS[0:1, 1:2], -1.0, None, ALU.mult), sve)
        pl.wait("sync", sve)
        pl.dma("sync", bass.AP(BOUNCE.tensor, 0, [[2, 1], [1, 2]]),
               GXS[0:1, 2:4], qs)
        pl.wait("sync", qs)
        pl.dma("sync", RXN[:], bass.AP(BOUNCE.tensor, 0, [[0, 128], [1, 2]]), qs)
        rxn_ready = qs.n
        pl.wait("vector", qs, rxn_ready)
        pl.op("vector", lambda e: e.tensor_scalar(BN8[:], B8F[:], GXS[0:1, 2:3], -1.0, ALU.mult, ALU.mult), sve)
        bn8_ready = sve.n

        store_sem = pl.sem("ffnst")
        st_base = store_sem.n
        for tt in range(NT):
            rbuf = ORT[:, 1024 * (tt % 2):1024 * (tt % 2) + 1024]
            obuf = ORT2[:, 1024 * (tt % 2):1024 * (tt % 2) + 1024]
            if tt >= 2:
                pl.wait("sync", store_sem, st_base + 16 * (tt - 1))
            pl.dma("sync", rbuf, resh_tile(RESH, tt), qs)
            rq = qs.n
            pl.wait("vector", qs, rq)
            rs_col = RSTD[:, tt:tt + 1]
            nb_col = NEGB[:, tt:tt + 1]
            pl.wait("vector", ste)
            pl.op("vector", lambda e, b=rbuf, r=rs_col, nb=nb_col: e.tensor_scalar(
                QG[:], b, r, nb, ALU.mult, ALU.add))
            pl.op("vector", lambda e: e.tensor_scalar(QD[:], QG[:], RXN[:, 0:1], None, ALU.mult))
            pl.op("vector", lambda e: e.tensor_copy(QI[:], QD[:]))
            pl.op("vector", lambda e: e.scalar_tensor_tensor(QG[:], QI[:], 1.0, QD[:], ALU.mult, ALU.subtract))
            pl.op("vector", lambda e: e.scalar_tensor_tensor(QD[:], QG[:], -0.5, QI[:], ALU.is_lt, ALU.add))
            pl.op("vector", lambda e: e.scalar_tensor_tensor(XQN[:], QG[:], 0.5, QD[:], ALU.is_gt, ALU.subtract), sve)
            q_done = sve.n
            pl.wait("tensor", sve, q_done)
            for fc in range(8):
                pl.op("tensor", lambda e, fc=fc: e.transpose(
                    bankb(7)[:, 128 * fc:128 * (fc + 1)], XQN[:, 128 * fc:128 * (fc + 1)], IDN[:]),
                    ste if fc == 7 else None)
            pl.wait("vector", ste)
            pbv = bankb(7).rearrange("p (a b) -> p a b", a=8)
            pl.op("vector", lambda e, i=pbv: e.tensor_copy(XQT[:], i), sve)
            xqt_done = sve.n
            pl.wait("tensor", sve, xqt_done)
            for oc in range(2):
                for fc in range(8):
                    pl.op("tensor", lambda e, oc=oc, fc=fc: e.matmul(
                        bank(oc), XQT[:, 128 * fc:128 * (fc + 1)],
                        W8T[:, 1024 * fc + 512 * oc:1024 * fc + 512 * (oc + 1)],
                        start=(fc == 0), stop=False))
                pl.op("tensor", lambda e, oc=oc: e.matmul(
                    bank(oc), ONESB[0:1, 0:128], BN8[0:1, 512 * oc:512 * (oc + 1)],
                    start=False, stop=True), ste)
                mm_done = ste.n
                pl.wait("vector", ste, mm_done)
                pl.op("vector", lambda e, oc=oc, o=obuf, rb=rbuf: e.scalar_tensor_tensor(
                    o[:, 512 * oc:512 * (oc + 1)], bank(oc), RXN[:, 1:2],
                    rb[:, 512 * oc:512 * (oc + 1)], ALU.mult, ALU.add), sve)
            pl.wait("sync", sve)
            pl.dma("sync", resh_tile(RESH, tt), obuf, store_sem)
        pl.wait("sync", store_sem)
        pl.barrier()
        fes.close()

    ALPHA1, MIX1, STEPA1 = 3.0, 0.15, 6.0
    LRSCALE = float(MIX1 / np.sqrt(8.0))

    def emit_pooled_finish(s):
        pl.wait("vector", ste)
        for half in range(2):
            pl.op("vector", lambda e, half=half: e.tensor_scalar(
                PLDR[0:1, 512 * half:512 * (half + 1)], bank(3 + half)[0:1, :],
                1.0 / S, None, ALU.mult), sve if half == 1 else None)
        pr_done = sve.n
        pl.wait("tensor", sve, pr_done)
        for fc in range(8):
            pl.op("tensor", lambda e, fc=fc: e.transpose(
                bankb(2)[:, 2 * fc:2 * fc + 1], PLDR[0:1, 128 * fc:128 * (fc + 1)],
                IDN[0:1, 0:1]), ste if fc == 7 else None)
        pl.wait("vector", ste)
        pin = bass.AP(bankb(2).tensor, bankb(2).offset, [[8192, 128], [2, 8]])
        pl.op("vector", lambda e, s=s, i=pin: e.tensor_copy(PLDC[:, 8 * s:8 * s + 8], i), sve)
        pl.wait("tensor", sve)

    def l1_alloc(stk):
        def sb1(n, s, d):
            return stk.enter_context(nc.sbuf_tensor(n, s, d))
        p = {}
        p["PFM"] = sb1("pfm", [128, 8], F32)
        p["PFMB"] = sb1("pfmb", [128, 8], BF16)
        p["GWS"] = sb1("gws", [128, 8 * 256], BF16)
        p["GBR"] = sb1("gbr", [1, 256], BF16)
        p["MWS"] = sb1("mws", [128, 8 * 128], BF16)
        p["MBC"] = sb1("mbc", [1, 128], BF16)
        p["BASB"] = sb1("basb", [128, 256], BF16)
        p["MODR"] = sb1("modr", [1, 128], BF16)
        p["MODCC"] = sb1("modcc", [128, 1], BF16)
        p["GLROW"] = sb1("glrow", [1, 512], F32)
        p["WTSROW"] = sb1("wtsrow", [1, 256], BF16)
        p["WTC"] = sb1("wtc", [128, 4], F32)
        p["HSEL"] = sb1("hsel", [128, 64], F32)
        p["W2HS"] = sb1("w2hs", [128, 32], BF16)
        p["P1R"] = sb1("p1r", [128, 4095], F32)
        p["P2R"] = sb1("p2r", [128, 4095], F32)
        p["F1C"] = sb1("f1c", [128, 1], F32)
        p["F2C"] = sb1("f2c", [128, 1], F32)
        p["PHC"] = sb1("phc", [128, 1], F32)
        p["AMPC"] = sb1("ampc", [128, 1], F32)
        p["SELF"] = sb1("self", [128, 32], F32)
        p["AT2"] = sb1("at2", [128, 2048], F32)
        p["SELAB"] = sb1("selab", [128, 128], BF16)
        p["SINM"] = sb1("sinm", [128, 4095], BF16)
        p["ARG"] = sb1("arg", [128, 2048], F32)
        p["KI"] = sb1("l1ki", [128, 2048], I32)
        p["KD"] = sb1("l1kd", [128, 2048], F32)
        p["KK"] = sb1("l1kk", [128, 2048], F32)
        p["KMH"] = sb1("kmh", [128, 8192], BF16)
        p["CK"] = sb1("l1ck", [16, 4096], F32)
        p["PKT"] = sb1("pkt", [128, 5 * 32], F32)
        p["WREP"] = sb1("wrep", [16, 4095], F32)
        return p

    def emit_l1_weights(p):
        pl.dma("gpsimd", p["GWS"][:], bass.AP(GW.tensor, 0, [[256, 128], [256 * 128, 8], [1, 256]]), qg)
        pl.dma("gpsimd", p["GBR"][:], bass.AP(GB.tensor, 0, [[256, 1], [1, 256]]), qg)
        pl.dma("gpsimd", p["MWS"][:], bass.AP(MW.tensor, 0, [[128, 128], [128 * 128, 8], [1, 128]]), qg)
        pl.dma("gpsimd", p["MBC"][:], bass.AP(MB.tensor, 0, [[128, 1], [1, 128]]), qg)
        pl.op("gpsimd", lambda e, t=p["BASB"]: e.memset(t[:], 0.0), sgx)
        pl.wait("gpsimd", sgx)
        for h in range(NH):
            pl.dma("gpsimd", p["BASB"][8 * h:8 * h + 8, 16 * h:16 * (h + 1)],
                   bass.AP(MBAS.tensor, 16 * h, [[256, 8], [1, 16]]), qg)

    def emit_l1_pooled(s, p):
        pl.wait("tensor", qg)
        for fc in range(8):
            pl.op("tensor", lambda e, fc=fc, s=s, w=p["GWS"]: e.matmul(
                bank(0)[0:1, 0:256], PLDC[:, 8 * s + fc:8 * s + fc + 1],
                w[:, 256 * fc:256 * (fc + 1)],
                start=(fc == 0), stop=False))
        pl.op("tensor", lambda e, r=p["GBR"]: e.matmul(
            bank(0)[0:1, 0:256], ONESB[0:1, 0:1], r[0:1, :], start=False, stop=False))
        for fc in range(8):
            pl.op("tensor", lambda e, fc=fc, s=s, w=p["MWS"]: e.matmul(
                bank(1)[0:1, 0:128], PLDC[:, 8 * s + fc:8 * s + fc + 1],
                w[:, 128 * fc:128 * (fc + 1)],
                start=(fc == 0), stop=False))
        pl.op("tensor", lambda e, r=p["MBC"]: e.matmul(
            bank(1)[0:1, 0:128], ONESB[0:1, 0:1], r[0:1, :], start=False, stop=True), ste)
        mod_done = ste.n
        pl.wait("scalar", ste, mod_done)
        pl.op("scalar", lambda e, m=p["MODR"]: e.copy(m[0:1, :], bank(1)[0:1, 0:128]), sse)
        pl.wait("tensor", sse)
        pl.op("tensor", lambda e, m=p["MODR"]: e.transpose(bankb(2)[:, 0:1], m[0:1, :], IDN[0:1, 0:1]), ste)
        pl.wait("vector", ste)
        pl.op("vector", lambda e, m=p["MODCC"]: e.tensor_copy(m[:, 0:1], bankb(2)[:, 0:1]), sve)
        modc_done = sve.n
        pl.wait("tensor", sve, modc_done)
        pl.op("tensor", lambda e, m=p["MODCC"], bb=p["BASB"]: e.matmul(
            bank(0)[0:1, 0:256], m[:, 0:1], bb[:], start=False, stop=True), ste)
        gl_done = ste.n
        G = p["GLROW"]
        pl.wait("scalar", ste, gl_done)
        pl.op("scalar", lambda e, G=G: e.copy(G[0:1, 0:256], bank(0)[0:1, 0:256]), sse)
        pl.wait("vector", sse)
        g3 = bass.AP(G[:].tensor, 0, [[512, 1], [16, 16], [1, 16]])
        mx = bass.AP(G[:].tensor, 256, [[512, 1], [1, 16], [1, 1]])
        mxb = bass.AP(G[:].tensor, 256, [[512, 1], [1, 16], [0, 16]])
        sm = bass.AP(G[:].tensor, 272, [[512, 1], [1, 16], [1, 1]])
        smb = bass.AP(G[:].tensor, 288, [[512, 1], [1, 16], [0, 16]])
        pl.op("vector", lambda e: e.tensor_reduce(mx, g3, AX.X, ALU.max))
        pl.op("vector", lambda e: e.drain())
        pl.op("vector", lambda e: e.tensor_tensor(g3, g3, mxb, ALU.subtract), sve)
        pl.wait("scalar", sve)
        pl.op("scalar", lambda e, G=G: e.activation(G[0:1, 0:256], G[0:1, 0:256], AF.Exp), sse)
        pl.wait("vector", sse)
        pl.op("vector", lambda e: e.tensor_reduce(sm, g3, AX.X, ALU.add))
        pl.op("vector", lambda e: e.drain())
        pl.op("vector", lambda e: e.reciprocal(
            bass.AP(G[:].tensor, 288, [[512, 1], [1, 16]]),
            bass.AP(G[:].tensor, 272, [[512, 1], [1, 16]])))
        pl.op("vector", lambda e: e.drain())
        pl.op("vector", lambda e: e.tensor_tensor(g3, g3, smb, ALU.mult))
        pl.op("vector", lambda e: e.drain())
        pl.op("vector", lambda e, w=p["WTSROW"], G=G: e.tensor_copy(w[0:1, :], G[0:1, 0:256]), sve)
        wts_done = sve.n
        pl.wait("tensor", sve, wts_done)
        for half in range(2):
            pl.op("tensor", lambda e, half=half, w=p["WTSROW"]: e.transpose(
                bankb(2)[:, 2 * half:2 * half + 1],
                w[0:1, 128 * half:128 * (half + 1)], IDN[0:1, 0:1]), ste if half == 1 else None)
        pl.wait("vector", ste)
        for half in range(2):
            pl.op("vector", lambda e, p=p, s=s, half=half: e.tensor_copy(
                p["WTC"][:, 2 * s + half:2 * s + half + 1], bankb(2)[:, 2 * half:2 * half + 1]),
                sve if half == 1 else None)
        pl.wait("tensor", sve)

    def emit_l1_masks(p):
        PKT = p["PKT"]
        REL, P1, P2, WIN, SCR = (PKT[:, 0:32], PKT[:, 32:64], PKT[:, 64:96],
                                 PKT[:, 96:128], PKT[:, 128:160])
        pl.op("gpsimd", lambda e: e.iota(REL, [[1, 32]], channel_multiplier=32, base=-2047,
                                         allow_small_or_imprecise_dtypes=True), sgx)
        pl.wait("vector", sgx)
        pl.op("vector", lambda e: e.tensor_scalar(P1, REL, 1.0 / S, None, ALU.mult), sve)
        pl.wait("scalar", sgx)
        pl.op("scalar", lambda e: e.activation(SCR, REL, AF.Abs))
        pl.op("scalar", lambda e: e.activation(SCR, SCR, AF.Sqrt, bias=EPS6[:, 0:1]))
        pl.op("scalar", lambda e: e.activation(P2, REL, AF.Sign), sse)
        pl.wait("vector", sse)
        pl.op("vector", lambda e: e.scalar_tensor_tensor(
            P2, SCR, float(1.0 / np.sqrt(S)), P2, ALU.mult, ALU.mult), sve)
        p2_done = sve.n
        pl.wait("scalar", sve, p2_done)
        pl.op("scalar", lambda e: e.activation(SCR, REL, AF.Square))
        pl.op("scalar", lambda e: e.activation(WIN, SCR, AF.Exp,
                                               scale=float(-1.0 / (2.0 * (S / 4.0) ** 2))), sse)
        pl.wait("vector", sse)
        pl.op("vector", lambda e: e.tensor_scalar(WIN, WIN, ALPHA1, None, ALU.mult), sve)
        win_done = sve.n
        pl.wait("sync", sve, win_done)
        pl.dma("sync", bass.AP(BOUNCE.tensor, 0, [[32, 128], [1, 32]]), P1, qs)
        pl.wait("sync", qs)
        pl.dma("sync", p["P1R"][:], bass.AP(BOUNCE.tensor, 0, [[0, 128], [1, 4095]]), qs)
        pl.wait("sync", qs)
        pl.dma("sync", bass.AP(BOUNCE.tensor, 0, [[32, 128], [1, 32]]), P2, qs)
        pl.wait("sync", qs)
        pl.dma("sync", p["P2R"][:], bass.AP(BOUNCE.tensor, 0, [[0, 128], [1, 4095]]), qs)
        pl.wait("sync", qs)
        pl.dma("sync", bass.AP(BOUNCE.tensor, 0, [[32, 128], [1, 32]]), WIN, qs)
        pl.wait("sync", qs)
        pl.dma("sync", p["WREP"][:], bass.AP(BOUNCE.tensor, 0, [[0, 16], [1, 4095]]), qs)
        prep_done = qs.n
        for half in range(2):
            hs = p["HSEL"][:, 16 * half:16 * (half + 1)]
            pl.op("gpsimd", lambda e, hs=hs, half=half: e.iota(
                hs, [[-16, 16]], channel_multiplier=1, base=128 * half,
                allow_small_or_imprecise_dtypes=True), sgx)
        pl.wait("vector", sgx)
        for half in range(2):
            hs = p["HSEL"][:, 16 * half:16 * (half + 1)]
            m1 = p["HSEL"][:, 32 + 16 * half:48 + 16 * half]
            pl.op("vector", lambda e, hs=hs, m1=m1: e.tensor_scalar(m1, hs, 0.0, None, ALU.is_ge))
            pl.op("vector", lambda e, hs=hs: e.tensor_scalar(hs, hs, 16.0, None, ALU.is_lt))
            pl.op("vector", lambda e: e.drain())
            pl.op("vector", lambda e, hs=hs, m1=m1: e.tensor_tensor(hs, hs, m1, ALU.mult), sve)
        pl.wait("gpsimd", qs, prep_done)
        for ci in range(16):
            pl.wait("gpsimd", ste)
            pl.dma_nc("gpsimd", p["F1C"][:], bass.AP(FRQ.tensor, 256 * ci, [[2, 128], [1, 1]]), qg)
            pl.dma_nc("gpsimd", p["F2C"][:], bass.AP(FRQ.tensor, 256 * ci + 1, [[2, 128], [1, 1]]), qg)
            pl.dma("gpsimd", p["PHC"][:], bass.AP(PHS.tensor, 128 * ci, [[1, 128], [1, 1]]), qg)
            pl.dma("gpsimd", p["AMPC"][:], bass.AP(AMP.tensor, 128 * ci, [[1, 128], [1, 1]]), qg)
            cready = qg.n
            pl.wait("gpsimd", qg, cready)
            for hw in range(2):
                w = 2048 if hw == 0 else 2047
                c0_ = 2048 * hw
                A = p["ARG"][:, 0:w]
                KK_, KI_, KD_ = p["KK"][:, 0:w], p["KI"][:, 0:w], p["KD"][:, 0:w]
                pl.wait("gpsimd", sve)
                pl.op("gpsimd", lambda e, A=A, p=p, c0_=c0_, w=w: e.tensor_scalar(
                    A, p["P1R"][:, c0_:c0_ + w], p["F1C"][:, 0:1], None, ALU.mult))
                pl.op("gpsimd", lambda e, p=p, c0_=c0_, w=w: e.tensor_scalar(
                    p["AT2"][:, 0:w], p["P2R"][:, c0_:c0_ + w], p["F2C"][:, 0:1], None, ALU.mult))
                pl.op("gpsimd", lambda e: e.drain())
                pl.op("gpsimd", lambda e, A=A, p=p, w=w: e.tensor_tensor(
                    A, A, p["AT2"][:, 0:w], ALU.add))
                pl.op("gpsimd", lambda e: e.drain())
                pl.op("gpsimd", lambda e, A=A, p=p: e.tensor_scalar(
                    A, A, p["PHC"][:, 0:1], None, ALU.add), sgx)
                arg_done = sgx.n
                pl.wait("vector", sgx, arg_done)
                pl.op("vector", lambda e, A=A, KK_=KK_: e.tensor_scalar(KK_, A, INV2PI, None, ALU.mult))
                pl.op("vector", lambda e, KK_=KK_, KI_=KI_: e.tensor_copy(KI_, KK_))
                pl.op("vector", lambda e, KI_=KI_, KK_=KK_, KD_=KD_: e.scalar_tensor_tensor(
                    KD_, KI_, 1.0, KK_, ALU.mult, ALU.subtract))
                pl.op("vector", lambda e, KD_=KD_, KI_=KI_, KK_=KK_: e.scalar_tensor_tensor(
                    KK_, KD_, -0.5, KI_, ALU.is_lt, ALU.add))
                pl.op("vector", lambda e: e.drain())
                pl.op("vector", lambda e, KD_=KD_, KK_=KK_: e.scalar_tensor_tensor(
                    KK_, KD_, 0.5, KK_, ALU.is_gt, ALU.subtract))
                pl.op("vector", lambda e: e.drain())
                pl.op("vector", lambda e, KK_=KK_, A=A, KD_=KD_: e.scalar_tensor_tensor(
                    KD_, KK_, -C1, A, ALU.mult, ALU.subtract))
                pl.op("vector", lambda e: e.drain())
                pl.op("vector", lambda e, KK_=KK_, KD_=KD_: e.scalar_tensor_tensor(
                    KD_, KK_, C2, KD_, ALU.mult, ALU.subtract), sve)
                rr_done = sve.n
                pl.wait("scalar", sve, rr_done)
                if hw == 0 and ci > 0:
                    pl.wait("scalar", ste)
                pl.op("scalar", lambda e, p=p, c0_=c0_, w=w, KD_=KD_: e.activation(
                    p["SINM"][:, c0_:c0_ + w], KD_, AF.Sin), sse)
            sin_done = sse.n
            pl.op("gpsimd", lambda e, p=p: e.memset(p["SELAB"][:], 0.0))
            sel = p["SELF"][:, 0:16]
            m1 = p["SELF"][:, 16:32]
            pl.op("gpsimd", lambda e, sel=sel: e.iota(sel, [[-8, 16]], channel_multiplier=1, base=0,
                                                      allow_small_or_imprecise_dtypes=True))
            pl.op("gpsimd", lambda e: e.drain())
            pl.op("gpsimd", lambda e, sel=sel, m1=m1: e.tensor_scalar(m1, sel, 0.0, None, ALU.is_ge))
            pl.op("gpsimd", lambda e, sel=sel: e.tensor_scalar(sel, sel, 8.0, None, ALU.is_lt))
            pl.op("gpsimd", lambda e: e.drain())
            pl.op("gpsimd", lambda e, sel=sel, m1=m1: e.tensor_tensor(sel, sel, m1, ALU.mult))
            pl.op("gpsimd", lambda e: e.drain())
            pl.op("gpsimd", lambda e, sel=sel, p=p, ci=ci: e.tensor_scalar(
                p["SELAB"][:, 16 * (ci % 8):16 * (ci % 8) + 16],
                sel, p["AMPC"][:, 0:1], None, ALU.mult), sgx)
            sela_done = sgx.n
            pl.wait("tensor", sse, sin_done)
            pl.wait("tensor", sgx, sela_done)
            for pc in range(8):
                n = 512 if pc < 7 else 511
                pl.op("tensor", lambda e, pc=pc, ci=ci, n=n, p=p: e.matmul(
                    bank(pc)[:, 0:n], p["SELAB"][:],
                    p["SINM"][:, 512 * pc:512 * pc + n],
                    start=(ci % 8 == 0), stop=(ci % 8 == 7)), ste if pc == 7 else None)
            if ci % 8 == 7:
                half = ci // 8
                mmk_done = ste.n
                pl.wait("scalar", ste, mmk_done)
                for pc in range(8):
                    n = 512 if pc < 7 else 511
                    pl.op("scalar", lambda e, pc=pc, half=half, n=n, p=p: e.copy(
                        p["KMH"][:, 4096 * half + 512 * pc:4096 * half + 512 * pc + n],
                        bank(pc)[:, 0:n]), sse if pc == 7 else None)
                pl.wait("tensor", sse)

    def emit_l1_kern_ck(s, p):
        pl.wait("vector", ste)
        for half in range(2):
            pl.op("vector", lambda e, half=half, p=p, s=s: e.tensor_scalar(
                p["W2HS"][:, 16 * half:16 * (half + 1)],
                p["HSEL"][:, 16 * half:16 * (half + 1)],
                p["WTC"][:, 2 * s + half:2 * s + half + 1], None, ALU.mult), sve)
        w2_done = sve.n
        pl.wait("tensor", sve, w2_done)
        for pc in range(8):
            n = 512 if pc < 7 else 511
            for half in range(2):
                pl.op("tensor", lambda e, pc=pc, half=half, n=n, p=p: e.matmul(
                    bank(pc)[0:16, 0:n],
                    p["W2HS"][:, 16 * half:16 * (half + 1)],
                    p["KMH"][:, 4096 * half + 512 * pc:4096 * half + 512 * pc + n],
                    start=(half == 0), stop=(half == 1)), ste if half == 1 else None)
        kern_done = ste.n
        pl.wait("scalar", ste, kern_done)
        for pc in range(8):
            n = 512 if pc < 7 else 511
            pl.op("scalar", lambda e, pc=pc, n=n, p=p: e.activation(
                p["CK"][0:16, 512 * pc:512 * pc + n], bank(pc)[0:16, 0:n],
                AF.Sigmoid, scale=STEPA1), sse if pc == 7 else None)
        sig_done = sse.n
        pl.wait("vector", sse, sig_done)
        pl.op("vector", lambda e, p=p: e.tensor_tensor(
            p["CK"][0:16, 0:4095], p["CK"][0:16, 0:4095], p["WREP"][:], ALU.mult))
        pl.op("vector", lambda e, p=p: e.memset(p["CK"][0:16, 0:2047], -4e9), sve)
        ck_done = sve.n
        pl.wait("sync", sve, ck_done)
        pl.dma("sync", bass.AP(CKH.tensor, 4095 * 16 * s, [[4095, 16], [1, 4095]]),
               p["CK"][0:16, 0:4095], qs)
        pl.wait("sync", qs)
        stair_sem = pl.sem("stair")
        pl.wait("gpsimd", qs)
        for h in range(16):
            bh = 16 * s + h
            pl.dma("gpsimd",
                   bass.AP(STAIR.tensor, bh * STSTRIDE, [[4095, 128], [1, 4095]]),
                   bass.AP(CKH.tensor, 4095 * bh, [[0, 128], [1, 4095]]), stair_sem)
        pl.wait("gpsimd", stair_sem)

    _l1a_ctr = [0]

    def l1a_alloc(stk):
        _l1a_ctr[0] += 1
        sfx = str(_l1a_ctr[0])

        def sb2(n, s, d):
            return stk.enter_context(nc.sbuf_tensor(n + sfx, s, d))
        q = {}
        q["QSW"] = sb2("qsw", [128, 8 * 512], BF16)
        q["KSW"] = sb2("ksw", [128, 8 * 512], BF16)
        q["BRK"] = sb2("brk", [1, 512], BF16)
        q["AUGQ"] = sb2("augq", [128, 4 * 2048], BF16)
        q["AUGK"] = sb2("augk", [128, 4 * 2048], BF16)
        q["KMT"] = sb2("kmt", [128, 256], F32)
        q["KMB"] = sb2("kmb", [128, 256], F32)
        q["QTMP"] = sb2("qtmp", [128, 512], BF16)
        q["KTMP"] = sb2("ktmp", [128, 512], BF16)
        q["VT1"] = sb2("vt1", [128, 16 * 130], BF16)
        q["VWS"] = sb2("vws", [128, 8 * 128], BF16)
        q["VBR"] = sb2("vbr", [1, 128], BF16)
        q["ATT1"] = sb2("att1", [128, 2048], BF16)
        q["TOEP"] = sb2("toep", [128, 1024], F32)
        q["SRO"] = sb2("sro", [128, 2048], F32)
        q["RCP1"] = sb2("rcp1", [128, 2048], F32)
        q["CTX1"] = sb2("ctx1", [128, 8 * 2048], BF16)
        q["CTXT"] = sb2("ctxt1", [128, 512], BF16)
        q["SQ2"] = sb2("sq2", [128, 512], F32)
        q["BND"] = sb2("bnd", [128, 2], F32)
        q["OWS"] = sb2("ows", [128, 8 * 1024], BF16)
        q["OBR"] = sb2("obr", [1, 1024], BF16)
        q["R1Z"] = sb2("r1z", [128, 1024], F32)
        q["CTX1"] = q["CTX1"]
        return q

    def emit_l1_proj(s, q):
        pl.op("gpsimd", lambda e, q=q: e.memset(q["QSW"][:], 0.0))
        pl.op("gpsimd", lambda e, q=q: e.memset(q["KSW"][:], 0.0))
        pl.op("gpsimd", lambda e, q=q: e.memset(q["BRK"][:], 0.0), sgx)
        pl.wait("gpsimd", sgx)
        for fc in range(8):
            pl.dma("gpsimd",
                   bass.AP(q["QSW"][:].tensor, 512 * fc, [[8 * 512, 128], [32, 16], [1, 8]]),
                   bass.AP(CQW.tensor, 128 * 128 * fc, [[128, 128], [8, 16], [1, 8]]), qg)
            pl.dma("gpsimd",
                   bass.AP(q["KSW"][:].tensor, 512 * fc, [[8 * 512, 128], [32, 16], [1, 8]]),
                   bass.AP(CKW.tensor, 128 * 128 * fc, [[128, 128], [8, 16], [1, 8]]), qg)
            pl.dma_nc("gpsimd",
                      bass.AP(q["KSW"][:].tensor, 512 * fc + 8, [[8 * 512, 128], [32, 16], [1, 1]]),
                      bass.AP(KMW.tensor, 16 * 128 * fc, [[16, 128], [1, 16]]), qg)
        pl.dma_nc("gpsimd", bass.AP(q["BRK"][:].tensor, 8, [[512, 1], [32, 16], [1, 1]]),
                  bass.AP(KMB.tensor, 0, [[16, 1], [1, 16]]), qg)
        w_ready = qg.n
        pl.wait("tensor", qg, w_ready)
        for lt in range(TPS):
            pl.wait("tensor", sve)
            for fc in range(8):
                pl.op("tensor", lambda e, fc=fc, lt=lt, q=q: e.matmul(
                    bank(0), ZT[:, 2048 * fc + 128 * lt:2048 * fc + 128 * (lt + 1)],
                    q["QSW"][:, 512 * fc:512 * (fc + 1)], start=(fc == 0), stop=(fc == 7)))
            for fc in range(8):
                pl.op("tensor", lambda e, fc=fc, lt=lt, q=q: e.matmul(
                    bank(1), ZT[:, 2048 * fc + 128 * lt:2048 * fc + 128 * (lt + 1)],
                    q["KSW"][:, 512 * fc:512 * (fc + 1)], start=(fc == 0), stop=False))
            pl.op("tensor", lambda e, q=q: e.matmul(
                bank(1), ONESB[0:1, 0:128], q["BRK"][0:1, :], start=False, stop=True), ste)
            mm_done = ste.n
            pl.wait("vector", ste, mm_done)
            pl.op("vector", lambda e, q=q: e.tensor_scalar(q["QTMP"][:], bank(0), LRSCALE, None, ALU.mult))
            pl.op("vector", lambda e, q=q: e.tensor_copy(q["KTMP"][:], bank(1)))
            kmt_in = bass.AP(PS[:].tensor, 512 * 1 + 8, [[4096, 128], [32, 16]])
            pl.op("vector", lambda e, q=q, lt=lt, i=kmt_in: e.tensor_copy(
                q["KMT"][:, 16 * lt:16 * (lt + 1)], i), sve)
            ev_done = sve.n
            pl.wait("tensor", sve, ev_done)
            for ti, SRC in ((0, q["QTMP"]), (1, q["KTMP"])):
                for c in range(4):
                    pl.op("tensor", lambda e, SRC=SRC, c=c, ti=ti: e.transpose(
                        bankb(7)[:, 512 * ti + 128 * c:512 * ti + 128 * (c + 1)],
                        SRC[:, 128 * c:128 * (c + 1)], IDN[:]),
                        ste if (ti, c) == (1, 3) else None)
            tr_done = ste.n
            pl.wait("vector", ste, tr_done)
            pbv = bankb(7).rearrange("p (a b) -> p a b", a=8)
            for ti, DST in ((0, q["AUGQ"]), (1, q["AUGK"])):
                out = bass.AP(DST[:].tensor, 128 * lt, [[4 * 2048, 128], [2048, 4], [1, 128]])
                inp = bass.AP(pbv.tensor, pbv.offset + 512 * ti, [[8192, 128], [128, 4], [1, 128]])
                pl.op("vector", lambda e, o=out, i=inp: e.tensor_copy(o, i),
                      sve if ti == 1 else None)
            pl.wait("tensor", sve)
        pl.op("vector", lambda e, q=q: e.tensor_reduce(
            q["BND"][:, 0:1], q["KMT"][:], AX.X, ALU.max, apply_absolute_value=True))
        pl.op("vector", lambda e: e.drain())
        pl.op("vector", lambda e, q=q: e.tensor_reduce(
            q["SQ2"][:, 0:1], q["AUGQ"][:], AX.X, ALU.max, apply_absolute_value=True))
        pl.op("vector", lambda e: e.drain())
        pl.op("vector", lambda e, q=q: e.tensor_reduce(
            q["SQ2"][:, 1:2], q["AUGK"][:], AX.X, ALU.max, apply_absolute_value=True))
        pl.op("vector", lambda e: e.drain())
        pl.op("vector", lambda e, q=q: e.tensor_copy(q["SQ2"][:, 2:3], q["BND"][:, 0:1]), sve)
        pl.wait("sync", sve)
        pl.dma("sync", bass.AP(BOUNCE.tensor, 0, [[3, 128], [1, 3]]),
               q["SQ2"][:, 0:3], qs)
        pl.wait("sync", qs)
        pl.dma_nc("sync", q["SQ2"][0:1, 4:388],
                  bass.AP(BOUNCE.tensor, 0, [[384, 1], [1, 384]]), qs)
        pl.wait("vector", qs)
        qm = q["SQ2"][0:1, 388:389]
        km_ = q["SQ2"][0:1, 389:390]
        mm_ = q["SQ2"][0:1, 390:391]
        pl.op("vector", lambda e, q=q, o=qm: e.tensor_reduce(
            o, bass.AP(q["SQ2"][:].tensor, 4, [[512, 1], [3, 128]]), AX.X, ALU.max))
        pl.op("vector", lambda e, q=q, o=km_: e.tensor_reduce(
            o, bass.AP(q["SQ2"][:].tensor, 5, [[512, 1], [3, 128]]), AX.X, ALU.max))
        pl.op("vector", lambda e, q=q, o=mm_: e.tensor_reduce(
            o, bass.AP(q["SQ2"][:].tensor, 6, [[512, 1], [3, 128]]), AX.X, ALU.max))
        pl.op("vector", lambda e: e.drain())
        pl.op("vector", lambda e, qm=qm, km_=km_: e.tensor_tensor(qm, qm, km_, ALU.mult))
        pl.op("vector", lambda e: e.drain())
        pl.op("vector", lambda e, qm=qm: e.tensor_scalar(qm, qm, float(8.0 * LRSCALE), 3.0, ALU.mult, ALU.add))
        pl.op("vector", lambda e: e.drain())
        pl.op("vector", lambda e, qm=qm, mm_=mm_: e.tensor_tensor(qm, qm, mm_, ALU.add), sve)
        pl.wait("sync", sve)
        pl.dma("sync", bass.AP(BOUNCE.tensor, 0, [[1, 1], [1, 1]]), qm, qs)
        pl.wait("sync", qs)
        pl.dma("sync", q["BND"][:, 1:2], bass.AP(BOUNCE.tensor, 0, [[0, 128], [1, 1]]), qs)
        b_ready = qs.n
        pl.wait("vector", qs, b_ready)
        pl.op("vector", lambda e, q=q: e.tensor_scalar(
            q["KMB"][:], q["KMT"][:], q["BND"][:, 1:2], None, ALU.subtract), sve)

    def emit_l1_attn(s, q):
        for hp in range(NPAIR):
            pl.wait("gpsimd", ste)
            pl.dma("gpsimd", q["VWS"][:],
                   bass.AP(VW.tensor, 128 * hp, [[D, 128], [D * 128, 8], [1, 128]]), qg)
            pl.dma("gpsimd", q["VBR"][:], bass.AP(VB.tensor, 128 * hp, [[128, 1], [1, 128]]), qg)
            vw_ready = qg.n
            pl.wait("vector", ste)
            vt_ones = bass.AP(q["VT1"][:].tensor, 64, [[16 * 130, 128], [130, 16], [65, 2], [1, 1]])
            pl.op("vector", lambda e, a=vt_ones: e.memset(a, 1.0), sve)
            pl.wait("tensor", qg, vw_ready)
            pl.wait("tensor", sve)
            for lt in range(TPS):
                for fc in range(8):
                    pl.op("tensor", lambda e, fc=fc, lt=lt, q=q: e.matmul(
                        bank(5)[:, 0:128], ZT[:, 2048 * fc + 128 * lt:2048 * fc + 128 * (lt + 1)],
                        q["VWS"][:, 128 * fc:128 * (fc + 1)], start=(fc == 0), stop=False))
                pl.op("tensor", lambda e, q=q: e.matmul(
                    bank(5)[:, 0:128], ONESB[0:1, 0:128], q["VBR"][0:1, :],
                    start=False, stop=True), ste)
                vmm = ste.n
                pl.wait("vector", ste, vmm)
                vo = bass.AP(q["VT1"][:].tensor, 130 * lt, [[16 * 130, 128], [65, 2], [1, 64]])
                vi = bass.AP(PS[:].tensor, 512 * 5, [[4096, 128], [64, 2], [1, 64]])
                pl.op("vector", lambda e, o=vo, i=vi: e.tensor_copy(o, i), sve)
                pl.wait("tensor", sve)
            for hh in range(2):
                h = 2 * hp + hh
                bh = 16 * s + h
                c = h // 4
                rowblk = 32 * (h % 4)
                toep_sem = pl.sem("toepq")
                pl.wait("scalar", qs)
                for qc in range(4):
                    njt = 4 * (qc + 1)
                    pl.wait("tensor", sse)
                    for jt in range(njt):
                        gb = jt % 4
                        tb = jt % 2
                        pl.wait("sync", sse)
                        pl.dma("sync", q["TOEP"][:, 512 * tb:512 * (tb + 1)],
                               bass.AP(STAIR.tensor,
                                       bh * STSTRIDE + 2047 + 512 * qc - 128 * jt,
                                       [[4094, 128], [1, 512]]), toep_sem)
                        t_ready = toep_sem.n
                        pl.wait("tensor", sve)
                        pl.op("tensor", lambda e, gb=gb, jt=jt, qc=qc, q=q, c=c, rowblk=rowblk: e.matmul(
                            bank(gb),
                            q["AUGK"][rowblk:rowblk + 8, 2048 * c + 128 * jt:2048 * c + 128 * (jt + 1)],
                            q["AUGQ"][rowblk:rowblk + 8, 2048 * c + 512 * qc:2048 * c + 512 * (qc + 1)],
                            start=True, stop=True, tile_position=(rowblk, 0)), ste)
                        sc_done = ste.n
                        pl.wait("vector", ste, sc_done)
                        pl.wait("vector", toep_sem, t_ready)
                        pl.op("vector", lambda e, gb=gb, tb=tb, q=q: e.scalar_tensor_tensor(
                            q["TOEP"][:, 512 * tb:512 * (tb + 1)], bank(gb), 1.0,
                            q["TOEP"][:, 512 * tb:512 * (tb + 1)], ALU.mult, ALU.add), sve)
                        add_done = sve.n
                        pl.wait("scalar", sve, add_done)
                        ab = q["ATT1"][:, 512 * (jt % 4):512 * (jt % 4 + 1)]
                        pl.op("scalar", lambda e, tb=tb, ab=ab, q=q, jt=jt, h=h: e.activation(
                            ab, q["TOEP"][:, 512 * tb:512 * (tb + 1)], AF.Exp,
                            bias=q["KMB"][:, 16 * jt + h:16 * jt + h + 1]), sse)
                        exp_done = sse.n
                        pl.wait("tensor", sse, exp_done)
                        pl.op("tensor", lambda e, jt=jt, hh=hh, ab=ab, q=q, njt=njt: e.matmul(
                            bank(4)[0:65, :],
                            q["VT1"][:, 130 * jt + 65 * hh:130 * jt + 65 * hh + 65],
                            ab, start=(jt == 0), stop=(jt == njt - 1)),
                            ste if jt == njt - 1 else None)
                    ctx_done = ste.n
                    pl.wait("scalar", ste, ctx_done)
                    if hh == 0:
                        pl.op("scalar", lambda e, hp=hp, qc=qc, q=q: e.copy(
                            q["CTX1"][0:64, 2048 * hp + 512 * qc:2048 * hp + 512 * (qc + 1)],
                            bank(4)[0:64, :]))
                    else:
                        pl.op("scalar", lambda e, q=q: e.copy(q["CTXT"][0:64, :], bank(4)[0:64, :]), sse)
                        pl.wait("gpsimd", sse)
                        pl.dma("gpsimd", q["CTX1"][64:128, 2048 * hp + 512 * qc:2048 * hp + 512 * (qc + 1)],
                               q["CTXT"][0:64, :], qg)
                    pl.op("scalar", lambda e, qc=qc, q=q: e.copy(
                        q["SRO"][64:65, 512 * qc:512 * (qc + 1)], bank(4)[64:65, :]), sse)
                pl.wait("scalar", sse)
                pl.op("scalar", lambda e, q=q: e.activation(q["SRO"][64:65, :], q["SRO"][64:65, :], AF.Ln))
                pl.op("scalar", lambda e, q=q: e.activation(q["SRO"][64:65, :], q["SRO"][64:65, :],
                                                            AF.Exp, scale=-1.0), sse)
                pl.wait("sync", sse)
                pl.dma("sync", bass.AP(BOUNCE.tensor, 0, [[2048, 1], [1, 2048]]), q["SRO"][64:65, :], qs)
                pl.wait("sync", qs)
                pl.dma("sync", q["RCP1"][64 * hh:64 * hh + 64, :],
                       bass.AP(BOUNCE.tensor, 0, [[0, 64], [1, 2048]]), qs)
            pl.wait("vector", qs)
            pl.wait("vector", qg)
            pl.op("vector", lambda e, hp=hp, q=q: e.tensor_tensor(
                q["CTX1"][:, 2048 * hp:2048 * (hp + 1)], q["CTX1"][:, 2048 * hp:2048 * (hp + 1)],
                q["RCP1"][:], ALU.mult), sve)

    def emit_l1_outproj(s, q2):
        OWS, OBR, R1Z = q2["OWS"], q2["OBR"], q2["R1Z"]
        CTX1 = q2["CTX1"]
        pl.dma("gpsimd", OWS[:], bass.AP(OW.tensor, 0, [[D, 128], [D * 128, 8], [1, 1024]]), qg)
        pl.dma("gpsimd", OBR[:], bass.AP(OB.tensor, 0, [[1024, 1], [1, 1024]]), qg)
        ow_ready = qg.n
        pl.wait("tensor", qg, ow_ready)
        pl.wait("tensor", sve)
        store_sem = pl.sem("l1store")
        st_base = store_sem.n
        for lt in range(TPS):
            tt = s * TPS + lt
            rbuf = ORT[:, 1024 * (lt % 2):1024 * (lt % 2) + 1024]
            obuf = ORT2[:, 1024 * (lt % 2):1024 * (lt % 2) + 1024]
            if lt >= 2:
                pl.wait("sync", store_sem, st_base + 16 * (lt - 1))
            pl.dma("sync", rbuf, resh_tile(RESH, tt), qs)
            rq = qs.n
            pl.wait("vector", qs, rq)
            pl.wait("vector", ste)
            pl.op("vector", lambda e, b=rbuf, r=RSTD[:, tt:tt + 1], nb=NEGB[:, tt:tt + 1], R1Z=R1Z:
                  e.tensor_scalar(R1Z[:], b, r, nb, ALU.mult, ALU.add), sve)
            z3_done = sve.n
            for oc in range(2):
                bk = 5 + oc
                pl.wait("tensor", sve)
                for hp2 in range(8):
                    pl.op("tensor", lambda e, hp2=hp2, lt=lt, oc=oc, bk=bk, CTX1=CTX1: e.matmul(
                        bank(bk),
                        CTX1[:, 2048 * hp2 + 128 * lt:2048 * hp2 + 128 * (lt + 1)],
                        OWS[:, 1024 * hp2 + 512 * oc:1024 * hp2 + 512 * (oc + 1)],
                        start=(hp2 == 0), stop=False))
                pl.op("tensor", lambda e, oc=oc, bk=bk, OBR=OBR: e.matmul(
                    bank(bk), ONESB[0:1, 0:128], OBR[0:1, 512 * oc:512 * (oc + 1)],
                    start=False, stop=True), ste)
                mm_done = ste.n
                pl.wait("vector", ste, mm_done)
                pl.op("vector", lambda e, bk=bk, oc=oc, o=obuf, R1Z=R1Z: e.scalar_tensor_tensor(
                    o[:, 512 * oc:512 * (oc + 1)], bank(bk), 1.0,
                    R1Z[:, 512 * oc:512 * (oc + 1)], ALU.mult, ALU.add), sve)
            pl.wait("sync", sve)
            pl.dma("sync", resh_tile(R1H, tt), obuf, store_sem)
        pl.wait("sync", store_sem)

    def emit_l1_inner():
        ies = contextlib.ExitStack()

        def sbi(n, s, d):
            return ies.enter_context(nc.sbuf_tensor(n, s, d))
        W1S = sbi("iw1", [128, 8 * 2048], BF16)
        W2S = sbi("iw2", [128, 16 * 1024], BF16)
        G = sbi("ig", [128, 32 * 512], BF16)
        HBF = sbi("ihbf", [128, 4 * 1024], BF16)
        HT = sbi("iht", [128, 8 * 512], BF16)
        R2T = sbi("ir2", [128, 1024], F32)
        T2F = sbi("it2", [128, 1024], F32)
        G1R = sbi("ig1r", [128, 1024], BF16)
        B1R = sbi("ib1r", [128, 1024], BF16)
        G2R = sbi("ig2r", [128, 1024], BF16)
        B2R = sbi("ib2r", [128, 1024], BF16)
        FB1C = sbi("ifb1c", [128, 32], F32)
        FB2R = sbi("ifb2r", [1, 1024], BF16)
        for T_, SRCW in ((G1R, LN1G), (B1R, LN1B), (G2R, LN2G), (B2R, LN2B)):
            pl.dma("gpsimd", T_[:], bass.AP(SRCW.tensor, 0, [[0, 128], [1, 1024]]), qg)
        pl.dma_nc("gpsimd", FB1C[:], bass.AP(FB1.tensor, 0, [[1, 128], [128, 32]]), qg)
        pl.dma("gpsimd", FB2R[:], bass.AP(FB2.tensor, 0, [[1024, 1], [1, 1024]]), qg)
        const_ready = qg.n
        pl.wait("vector", qg, const_ready)
        pl.wait("tensor", qg, const_ready)
        w1_sem = pl.sem("iw1s")
        w2_sem = pl.sem("iw2s")
        store_sem = pl.sem("istore")
        for g in range(8):
            for ti in range(4):
                tt = 4 * g + ti
                rbuf = ORT[:, 1024 * (ti % 2):1024 * (ti % 2) + 1024]
                pl.wait("sync", sve)
                pl.wait("sync", ste)
                pl.dma("sync", rbuf, resh_tile(R1H, tt), qs)
                pl.wait("scalar", qs)
                pl.wait("scalar", sve)
                stats_tile(rbuf, TINY[:, 4:5], TINY[:, 5:6])
                pl.wait("vector", sse)
                hb = HBF[:, 1024 * ti:1024 * (ti + 1)]
                pl.op("vector", lambda e, b=rbuf: e.tensor_scalar(
                    T2F[:], b, TINY[:, 4:5], TINY[:, 5:6], ALU.mult, ALU.add))
                pl.op("vector", lambda e: e.drain())
                pl.op("vector", lambda e, hb=hb: e.tensor_tensor(hb, T2F[:], G1R[:], ALU.mult))
                pl.op("vector", lambda e: e.drain())
                pl.op("vector", lambda e, hb=hb: e.tensor_tensor(hb, hb, B1R[:], ALU.add), sve)
                h_done = sve.n
                pl.wait("tensor", sve, h_done)
                for fc in range(8):
                    pl.op("tensor", lambda e, fc=fc, ti=ti, hb=hb: e.transpose(
                        bankb(7)[:, 128 * fc:128 * (fc + 1)], hb[:, 128 * fc:128 * (fc + 1)],
                        IDN[:]), ste if fc == 7 else None)
                pl.wait("vector", ste)
                pbv = bankb(7).rearrange("p (a b) -> p a b", a=8)
                ht_out = bass.AP(HT[:].tensor, 128 * ti, [[8 * 512, 128], [512, 8], [1, 128]])
                pl.op("vector", lambda e, o=ht_out, i=pbv: e.tensor_copy(o, i), sve)
                pl.wait("tensor", sve)
            pl.wait("tensor", sve)
            for half in range(2):
                pl.wait("gpsimd", ste)
                pl.dma("gpsimd", W1S[:],
                       bass.AP(FW1.tensor, 2048 * half, [[4096, 128], [4096 * 128, 8], [1, 2048]]),
                       w1_sem)
                pl.wait("tensor", w1_sem)
                for ohh in range(16):
                    oc = 16 * half + ohh
                    bk = ohh % 2
                    pl.wait("tensor", sse)
                    for fc in range(8):
                        pl.op("tensor", lambda e, fc=fc, ohh=ohh, bk=bk: e.matmul(
                            bank(bk), W1S[:, 2048 * fc + 128 * ohh:2048 * fc + 128 * (ohh + 1)],
                            HT[:, 512 * fc:512 * (fc + 1)],
                            start=(fc == 0), stop=(fc == 7)), ste if fc == 7 else None)
                    mm_done = ste.n
                    pl.wait("scalar", ste, mm_done)
                    pl.op("scalar", lambda e, oc=oc, bk=bk: e.activation(
                        G[:, 512 * oc:512 * (oc + 1)], bank(bk), AF.Gelu,
                        bias=FB1C[:, oc:oc + 1]), sse)
            pl.wait("tensor", sse)
            for h2 in range(2):
                pl.wait("gpsimd", ste)
                pl.dma("gpsimd", W2S[:],
                       bass.AP(FW2.tensor, 16 * 128 * 1024 * h2,
                               [[1024, 128], [1024 * 128, 16], [1, 1024]]), w2_sem)
                pl.wait("tensor", w2_sem)
                for ti in range(4):
                    for dc in range(2):
                        bk = 2 * ti + dc
                        for occ in range(16):
                            last = (ti, dc, occ) == (3, 1, 15)
                            pl.op("tensor", lambda e, occ=occ, ti=ti, dc=dc, bk=bk, h2=h2: e.matmul(
                                bank(bk),
                                G[:, 512 * (16 * h2 + occ) + 128 * ti:512 * (16 * h2 + occ) + 128 * (ti + 1)],
                                W2S[:, 1024 * occ + 512 * dc:1024 * occ + 512 * (dc + 1)],
                                start=(h2 == 0 and occ == 0), stop=False),
                                ste if last else None)
                if h2 == 1:
                    for ti in range(4):
                        for dc in range(2):
                            bk = 2 * ti + dc
                            pl.op("tensor", lambda e, bk=bk, dc=dc: e.matmul(
                                bank(bk), ONESB[0:1, 0:128], FB2R[0:1, 512 * dc:512 * (dc + 1)],
                                start=False, stop=True), ste if bk == 7 else None)
            f2_done = ste.n
            dbg_g0 = (debug_stop == "l1h" and g == 0)
            pl.wait("vector", ste, f2_done)
            pl.wait("vector", store_sem)
            for ti in range(4):
                tt = 4 * g + ti
                rbuf = ORT[:, 1024 * (ti % 2):1024 * (ti % 2) + 1024]
                obuf = ORT2[:, 1024 * (ti % 2):1024 * (ti % 2) + 1024]
                if ti >= 2:
                    pl.wait("sync", store_sem, store_sem.n - 16)
                pl.dma("sync", rbuf, resh_tile(RESH, tt), qs)
                rq = qs.n
                for dc in range(2):
                    pl.op("vector", lambda e, ti=ti, dc=dc: e.scalar_tensor_tensor(
                        R2T[:, 512 * dc:512 * (dc + 1)], bank(2 * ti + dc), 1.0,
                        HBF[:, 1024 * ti + 512 * dc:1024 * ti + 512 * (dc + 1)],
                        ALU.mult, ALU.add), sve if dc == 1 else None)
                pl.wait("scalar", sve)
                stats_tile(R2T[:], TINY[:, 6:7], TINY[:, 7:8])
                pl.wait("vector", sse)
                pl.op("vector", lambda e: e.tensor_scalar(
                    T2F[:], R2T[:], TINY[:, 6:7], TINY[:, 7:8], ALU.mult, ALU.add))
                pl.op("vector", lambda e: e.drain())
                pl.op("vector", lambda e: e.tensor_tensor(T2F[:], T2F[:], G2R[:], ALU.mult))
                pl.op("vector", lambda e: e.drain())
                pl.op("vector", lambda e: e.tensor_tensor(T2F[:], T2F[:], B2R[:], ALU.add))
                pl.wait("vector", qs, rq)
                pl.op("vector", lambda e, o=obuf, rb=rbuf: e.tensor_tensor(o, T2F[:], rb, ALU.add), sve)
                pl.wait("sync", sve)
                pl.dma("sync", resh_tile(RESH, tt), obuf, store_sem)
                if dbg_g0 and ti == 3:
                    pl.wait("sync", store_sem)
                    pl.barrier()
                    for tx in range(4):
                        dbg(bass.AP(RESH.tensor, tx * 128 * 1024, [[1024, 128], [1, 1024]]), 128 * 1024)
                    pl.wait("sync", sout)
                    pl.wait("gpsimd", sout)
                    ies.close()
                    return "dbg"
        pl.wait("sync", store_sem)
        pl.barrier()
        ies.close()

    W1QH = nc.dram_tensor("w1qh", [4 * D * D + 8], BF16, kind="Internal").ap()
    W2QH = nc.dram_tensor("w2qh", [4 * D * D + 8], BF16, kind="Internal").ap()

    def emit_l1_ffn():
        fes = contextlib.ExitStack()

        def sbq(n, s, d):
            return fes.enter_context(nc.sbuf_tensor(n, s, d))
        WB = sbq("qwb", [128, 2048], F32)
        QI2 = sbq("qqi", [128, 1024], I32)
        QD2 = sbq("qqd", [128, 1024], F32)
        QG2 = sbq("qqg", [128, 1024], F32)
        XQ2 = sbq("qxq", [128, 1024], BF16)
        GC = sbq("qgc", [128, 2], F32)
        RG = sbq("qrg", [128, 2], F32)
        GSR = sbq("qgsr", [1, 272], F32)
        QB1C = sbq("qb1c", [128, 32], F32)
        NB2R = sbq("qnb2", [1, 1024], BF16)
        B2F = sbq("qb2f", [1, 1024], F32)
        pl.op("vector", lambda e: e.memset(GC[:], 0.0), sve)
        gsem = pl.sem("qgsem")
        for wi, W_ in enumerate((QW1, QW2)):
            for c in range(16):
                if c > 0 or wi > 0:
                    pl.wait("sync", sve)
                pl.dma("sync", WB[:],
                       bass.AP(W_.tensor, 128 * 2048 * c, [[2048, 128], [1, 2048]]), gsem)
                pl.wait("vector", gsem)
                pl.op("vector", lambda e: e.tensor_reduce(
                    TINY[:, 0:1], WB[:], AX.X, ALU.add, apply_absolute_value=True))
                pl.op("vector", lambda e: e.drain())
                pl.op("vector", lambda e, wi=wi: e.tensor_tensor(
                    GC[:, wi:wi + 1], GC[:, wi:wi + 1], TINY[:, 0:1], ALU.add), sve)
                pl.op("vector", lambda e: e.drain())
        pl.wait("sync", sve)
        pl.dma("sync", bass.AP(BOUNCE.tensor, 0, [[2, 128], [1, 2]]), GC[:], qs)
        pl.wait("sync", qs)
        pl.dma("sync", GSR[0:1, 0:256], bass.AP(BOUNCE.tensor, 0, [[256, 1], [1, 256]]), qs)
        pl.wait("vector", qs)
        for wi in range(2):
            pl.op("vector", lambda e, wi=wi: e.tensor_reduce(
                GSR[0:1, 256 + wi:257 + wi],
                bass.AP(GSR[:].tensor, wi, [[272, 1], [2, 128]]), AX.X, ALU.add))
            pl.op("vector", lambda e: e.drain())
        NW = float(4 * D * D)
        pl.op("vector", lambda e: e.tensor_scalar(
            GSR[0:1, 260:262], GSR[0:1, 256:258], 1.0 / NW, 1e-5, ALU.mult, ALU.add))
        pl.op("vector", lambda e: e.drain())
        pl.op("vector", lambda e: e.reciprocal(GSR[0:1, 264:266], GSR[0:1, 260:262]), sve)
        pl.wait("sync", sve)
        pl.dma("sync", bass.AP(BOUNCE.tensor, 0, [[2, 1], [1, 2]]), GSR[0:1, 264:266], qs)
        pl.wait("sync", qs)
        pl.dma("sync", RG[:], bass.AP(BOUNCE.tensor, 0, [[0, 128], [1, 2]]), qs)
        rg_ready = qs.n
        pl.wait("vector", qs, rg_ready)
        qs2 = pl.sem("qffw")
        for wi, W_ in enumerate((QW1, QW2)):
            for c in range(32):
                if wi == 0:
                    src_ap = bass.AP(W_.tensor, 128 * 1024 * c, [[1024, 128], [1, 1024]])
                else:
                    dr, ob = c // 4, c % 4
                    src_ap = bass.AP(W_.tensor, 128 * 4096 * dr + 1024 * ob, [[4096, 128], [1, 1024]])
                pl.wait("sync", sve)
                pl.dma("sync", WB[:, 0:1024], src_ap, gsem)
                pl.wait("vector", gsem)
                pl.op("vector", lambda e, wi=wi: e.tensor_scalar(
                    QD2[:], WB[:, 0:1024], RG[:, wi:wi + 1], None, ALU.mult))
                pl.op("vector", lambda e: e.tensor_copy(QI2[:], QD2[:]))
                pl.op("vector", lambda e: e.scalar_tensor_tensor(
                    QG2[:], QI2[:], 1.0, QD2[:], ALU.mult, ALU.subtract))
                pl.op("vector", lambda e: e.scalar_tensor_tensor(
                    QD2[:], QG2[:], -0.5, QI2[:], ALU.is_lt, ALU.add))
                pl.op("vector", lambda e: e.drain())
                pl.op("vector", lambda e: e.scalar_tensor_tensor(
                    QG2[:], QG2[:], 0.5, QD2[:], ALU.is_gt, ALU.subtract))
                pl.op("vector", lambda e: e.drain())
                pl.op("vector", lambda e: e.tensor_scalar(
                    XQ2[:], QG2[:], -1.0, 1.0, ALU.max, ALU.min), sve)
                xq_done = sve.n
                pl.wait("tensor", sve, xq_done)
                for fs in range(8):
                    pl.op("tensor", lambda e, fs=fs: e.transpose(
                        bankb(7)[:, 128 * fs:128 * (fs + 1)], XQ2[:, 128 * fs:128 * (fs + 1)],
                        IDN[:]), ste if fs == 7 else None)
                pl.wait("vector", ste)
                pbv = bankb(7)
                pl.op("vector", lambda e, i=pbv: e.tensor_copy(XQ2[:], i), sve)
                tq_done = sve.n
                pl.wait("tensor", sve)
                pl.wait("sync", sve, tq_done)
                if wi == 0:
                    out_ap = bass.AP(W1QH.tensor, 128 * c,
                                     [[4096, 128], [4096 * 128, 8], [1, 128]])
                else:
                    dr, ob = c // 4, c % 4
                    out_ap = bass.AP(W2QH.tensor, 1024 * 1024 * ob + 128 * dr,
                                     [[1024, 128], [1024 * 128, 8], [1, 128]])
                pl.dma("sync", out_ap, XQ2[:], qs2)
        pl.wait("sync", qs2)
        pl.barrier()
        fes.close()
        mes = contextlib.ExitStack()

        def sbm(n, s, d):
            return mes.enter_context(nc.sbuf_tensor(n, s, d))
        W1QS = sbm("fw1q", [128, 8 * 2048], BF16)
        W2QS = sbm("fw2q", [128, 16 * 1024], BF16)
        HFM = sbm("fhfm", [128, 32 * 512], BF16)
        QB1Cm = sbm("fqb1", [128, 32], F32)
        NB2Rm = sbm("fnb2", [1, 1024], BF16)
        B2Fm = sbm("fb2f", [1, 1024], F32)
        pl.dma_nc("gpsimd", QB1Cm[:], bass.AP(QB1.tensor, 0, [[1, 128], [128, 32]]), qg)
        pl.dma("gpsimd", B2Fm[:], bass.AP(QB2.tensor, 0, [[1024, 1], [1, 1024]]), qg)
        cready = qg.n
        pl.wait("vector", qg, cready)
        pl.op("vector", lambda e: e.tensor_scalar(NB2Rm[:], B2Fm[:], -1.0, None, ALU.mult), sve)
        w1_sem = pl.sem("fw1s")
        w2_sem = pl.sem("fw2s")
        for s_ in range(NSEQ):
            ln_seq(s_, neg=True)
            for gg in range(4):
                tt0 = 16 * s_ + 4 * gg
                pl.wait("tensor", sve)
                for half in range(2):
                    pl.wait("gpsimd", ste)
                    pl.dma("gpsimd", W1QS[:],
                           bass.AP(W1QH.tensor, 2048 * half, [[4096, 128], [4096 * 128, 8], [1, 2048]]),
                           w1_sem)
                    pl.wait("tensor", w1_sem)
                    for ohh in range(16):
                        oc = 16 * half + ohh
                        bk = ohh % 2
                        pl.wait("tensor", sse)
                        for fc in range(8):
                            pl.op("tensor", lambda e, fc=fc, ohh=ohh, bk=bk, gg=gg: e.matmul(
                                bank(bk), W1QS[:, 2048 * fc + 128 * ohh:2048 * fc + 128 * (ohh + 1)],
                                ZT[:, 2048 * fc + 512 * gg:2048 * fc + 512 * (gg + 1)],
                                start=(fc == 0), stop=(fc == 7)), ste if fc == 7 else None)
                        mm_done = ste.n
                        pl.wait("scalar", ste, mm_done)
                        pl.op("scalar", lambda e, oc=oc, bk=bk: e.activation(
                            HFM[:, 512 * oc:512 * (oc + 1)], bank(bk), AF.Relu,
                            bias=QB1Cm[:, oc:oc + 1]), sse)
                pl.wait("tensor", sse)
                for h2 in range(2):
                    pl.wait("gpsimd", ste)
                    pl.dma("gpsimd", W2QS[:],
                           bass.AP(W2QH.tensor, 1024 * 1024 * 2 * h2,
                                   [[1024, 128], [1024 * 128, 16], [1, 1024]]), w2_sem)
                    pl.wait("tensor", w2_sem)
                    for ti in range(4):
                        for dc in range(2):
                            bk = 2 * ti + dc
                            for occ in range(16):
                                last = (ti, dc, occ) == (3, 1, 15)
                                pl.op("tensor", lambda e, occ=occ, ti=ti, dc=dc, bk=bk, h2=h2: e.matmul(
                                    bank(bk),
                                    HFM[:, 512 * (16 * h2 + occ) + 128 * ti:512 * (16 * h2 + occ) + 128 * (ti + 1)],
                                    W2QS[:, 1024 * occ + 512 * dc:1024 * occ + 512 * (dc + 1)],
                                    start=(h2 == 0 and occ == 0), stop=False),
                                    ste if last else None)
                if True:
                    for ti in range(4):
                        for dc in range(2):
                            bk = 2 * ti + dc
                            pl.op("tensor", lambda e, bk=bk, dc=dc: e.matmul(
                                bank(bk), ONESB[0:1, 0:128], NB2Rm[0:1, 512 * dc:512 * (dc + 1)],
                                start=False, stop=True), ste if bk == 7 else None)
                f2_done = ste.n
                pl.wait("vector", ste, f2_done)
                pl.wait("vector", sout)
                for ti in range(4):
                    tt = tt0 + ti
                    rbuf = ORT[:, 1024 * (ti % 2):1024 * (ti % 2) + 1024]
                    obuf = ORT2[:, 1024 * (ti % 2):1024 * (ti % 2) + 1024]
                    pl.dma("sync", rbuf, resh_tile(RESH, tt), qs)
                    rq = qs.n
                    pl.wait("vector", qs, rq)
                    for dc in range(2):
                        pl.op("vector", lambda e, ti=ti, dc=dc, o=obuf, rb=rbuf: e.scalar_tensor_tensor(
                            o[:, 512 * dc:512 * (dc + 1)], bank(2 * ti + dc), -1.0,
                            rb[:, 512 * dc:512 * (dc + 1)], ALU.mult, ALU.add),
                            sve if dc == 1 else None)
                    pl.wait("sync", sve)
                    pl.dma("sync", resh_tile(OUT, tt), obuf, sout)
        pl.wait("sync", sout)
        pl.wait("gpsimd", sout)
        pl.barrier()
        mes.close()

    dbg_off = [0]

    def dbg(ap_in, nelem, cast=False, parts=128):
        off = dbg_off[0]
        per = nelem // parts
        o = bass.AP(OUT.tensor, off, [[per, parts], [1, per]])
        pl.dma("gpsimd" if cast else "sync", o, ap_in, sout)
        dbg_off[0] += nelem

    def finish():
        pl.wait("sync", sout)
        pl.wait("gpsimd", sout)

    if debug_stop == "resh":
        pl.barrier()
        dbg(bass.AP(RESH.tensor, 0, [[8192, 128], [1, 8192]]), 128 * 8192)
        finish()
    elif debug_stop == "lnraw":
        ln_seq(0)
        pl.barrier()
        dbg(RT[:], 128 * 2048)
        dbg(TINY[:], 128 * 8)
        dbg(ZB[:], 128 * 2048, cast=True)
        finish()
    elif debug_stop == "trig":
        dbg(TRIGC, S * 512)
        dbg(TRIGS, S * 512)
        finish()
    elif debug_stop == "ln0":
        ln_seq(0)
        pl.barrier()
        dbg(ZT[:], 128 * 16384, cast=True)
        dbg(RSTD[:, 0:16], 128 * 16, cast=False)
        dbg(NEGB[:, 0:16], 128 * 16, cast=False)
        finish()
    elif debug_stop == "qkv0":
        ln_seq(0)
        pl.barrier()
        l0_attention(0, dbg_pair=0)
        pl.barrier()
        dbg(QKP[:], 128 * 4096, cast=True)
        dbg(QR[:], 128 * 2048, cast=True)
        dbg(KR[:], 128 * 2048, cast=True)
        dbg(VT[:], 128 * 2080, cast=True)
        dbg(NEGBC[:], 128 * 2)
        finish()
    elif debug_stop == "ctx0":
        ln_seq(0)
        pl.barrier()
        l0_attention(0, dbg_pair=0)
        pl.barrier()
        dbg(CTXA[:, 0:2048], 128 * 2048, cast=True)
        dbg(RCPR[:], 128 * 2048)
        finish()
    else:
        ln_seq(0)
        pl.barrier()
        l0_attention(0)
        pl.barrier()
        ln_seq(1)
        pl.barrier()
        l0_attention(1)
        pl.barrier()
        if debug_stop != "l0att":
            l0_es.close()
            emit_l0_ffn()
        if debug_stop in ("r1h", "l1att", "l1proj", "l1ctx", "l1h") or debug_stop is None:
            l1_es = contextlib.ExitStack()
            p1 = l1_alloc(l1_es)
            emit_l1_weights(p1)
            ln_seq(0, pooled=True)
            emit_pooled_finish(0)
            pl.barrier()
            ln_seq(1, pooled=True)
            emit_pooled_finish(1)
            pl.barrier()
            emit_l1_masks(p1)
            emit_l1_pooled(0, p1)
            emit_l1_kern_ck(0, p1)
            emit_l1_pooled(1, p1)
            emit_l1_kern_ck(1, p1)
            pl.barrier()
            l1_es.close()
            for s_ in range(NSEQ):
                ln_seq(s_)
                pl.barrier()
                l1a_es = contextlib.ExitStack()
                qq = l1a_alloc(l1a_es)
                emit_l1_proj(s_, qq)
                pl.barrier()
                if debug_stop == "l1proj" and s_ == 0:
                    dbg(qq["QSW"][:], 128 * 4096, cast=True)
                    dbg(qq["AUGQ"][:], 128 * 8192, cast=True)
                    dbg(ZT[:], 128 * 16384, cast=True)
                    finish()
                    l1a_es.close()
                    return _mk_finish(nc, es, pl)
                emit_l1_attn(s_, qq)
                pl.barrier()
                if debug_stop == "l1ctx" and s_ == 0:
                    dbg(qq["CTX1"][:], 128 * 16384, cast=True)
                    dbg(qq["KMT"][:], 128 * 256)
                    dbg(qq["BND"][:, 1:2], 128 * 1)
                    finish()
                    l1a_es.close()
                    return _mk_finish(nc, es, pl)
                emit_l1_outproj(s_, qq)
                pl.barrier()
                l1a_es.close()
            if debug_stop == "l1ctx":
                pass
            if debug_stop == "r1h":
                pl.dma("sync", OUT, R1H, sout)
                finish()
                return _mk_finish(nc, es, pl)
            r_ = emit_l1_inner()
            if debug_stop == "l1h":
                return _mk_finish(nc, es, pl)
            if debug_stop == "l1att":
                pl.dma("sync", OUT, RESH, sout)
                finish()
                return _mk_finish(nc, es, pl)
            emit_l1_ffn()
            finish()
            return _mk_finish(nc, es, pl)
        if debug_stop in ("ck0",):
            l1_es = contextlib.ExitStack()
            p1 = l1_alloc(l1_es)
            emit_l1_weights(p1)
            emit_l1_masks(p1)
            pl.barrier()
            ln_seq(0)
            pl.barrier()
            emit_l1_pooled(0, p1)
            pl.barrier()
            emit_l1_kern_ck(0, p1)
            pl.barrier()
            dbg(p1["CK"][0:16, 0:4096], 16 * 4096, parts=16)
            dbg(p1["WTC"][:], 128 * 4, cast=True)
            dbg(p1["KMH"][:], 128 * 8192, cast=True)
            finish()
            l1_es.close()
            return _mk_finish(nc, es, pl)
        pl.dma("sync", OUT, RESH, sout)
        finish()

    return _mk_finish(nc, es, pl)


def _mk_finish(nc, es, pl):
    with es:
        block = es.enter_context(nc.Block())
        for name in ENGINES:
            def mk(name):
                def body(e):
                    for f in pl.q[name]:
                        f(e)
                return body
            getattr(block, name)(mk(name))
    return nc


def kernel(**inputs):
    nc = build(debug_stop=DEBUG_STOP)
    x = np.ascontiguousarray(np.asarray(inputs["L0_attn_x"], np.float32).reshape(16, S, D))
    shards = np.split(x, 8, axis=0)
    in_maps = []
    for c in range(8):
        m = {k: np.ascontiguousarray(np.asarray(v)) for k, v in inputs.items() if k != "L0_attn_x"}
        m["L0_attn_x"] = np.ascontiguousarray(shards[c].reshape(T, D))
        in_maps.append(m)
    res = run_bass_kernel_spmd(nc, in_maps, core_ids=list(range(8)))
    return np.concatenate([r["out"].reshape(2, S, D) for r in res.results], axis=0)


# version 02
# speedup vs baseline: 1.1091x; 1.1091x over previous
import contextlib
import numpy as np
import ml_dtypes
import concourse.bass as bass
import concourse.mybir as mybir
from concourse.bass_utils import run_bass_kernel_spmd

F32 = mybir.dt.float32
BF16 = mybir.dt.bfloat16
I32 = mybir.dt.int32
I8 = mybir.dt.int8
AF = mybir.ActivationFunctionType
ALU = mybir.AluOpType
AX = mybir.AxisListType

S, D, NSEQ = 2048, 1024, 2
T = NSEQ * S
NT = T // 128
TPS = S // 128
NH, HD, NPAIR = 16, 64, 8
LNEPS = 1e-5

TWO_PI = 2.0 * np.pi
C1 = float(np.float32(TWO_PI))
C2 = float(np.float32(TWO_PI - C1))
INV2PI = float(1.0 / TWO_PI)
LN1E4 = float(np.log(10000.0))
E_HI = float(np.float32(LN1E4 / 512.0))
E_LO = float(np.float32(LN1E4 / 512.0 - E_HI))

ENGINES = ("sync", "tensor", "vector", "scalar", "gpsimd")
DEBUG_STOP = None


class Sem:
    __slots__ = ("h", "n")

    def __init__(self, nc, name):
        self.h = nc.alloc_semaphore(name=name)
        self.n = 0


class PL:
    def __init__(self, nc):
        self.nc = nc
        self.q = {e: [] for e in ENGINES}
        self._sems = {}

    def sem(self, name):
        if name not in self._sems:
            self._sems[name] = Sem(self.nc, name)
        return self._sems[name]

    def em(self, eng, f):
        self.q[eng].append(f)

    def wait(self, eng, sem, val=None):
        v = sem.n if val is None else val
        if v > 0:
            self.em(eng, lambda e, s=sem.h, v=v: e.wait_ge(s, v))

    def dma(self, eng, out, in_, sem):
        sem.n += 16
        self.em(eng, lambda e, o=out, i=in_, s=sem.h: e.dma_start(out=o, in_=i).then_inc(s, 16))

    def dma_nc(self, eng, out, in_, sem):
        sem.n += 16
        self.em(eng, lambda e, o=out, i=in_, s=sem.h: e.dma_start(
            out=o, in_=i, allow_slow_non_contiguous=True).then_inc(s, 16))

    def op(self, eng, f, sem=None):
        if sem is None:
            self.em(eng, f)
        else:
            sem.n += 1
            self.em(eng, lambda e, f=f, s=sem.h: f(e).then_inc(s, 1))

    def barrier(self):
        bs = self.sem("bar")
        bs.n += len(ENGINES)
        tgt = bs.n
        for e in ENGINES:
            self.em(e, lambda eng, s=bs.h: eng.drain().then_inc(s, 1))
        for e in ENGINES:
            self.em(e, lambda eng, s=bs.h, v=tgt: eng.wait_ge(s, v))


def roundneg(pl, eng, sem, X, R1I, DD, OUTKN, n=None):
    pl.op(eng, lambda e: e.tensor_copy(R1I, X))
    pl.op(eng, lambda e: e.scalar_tensor_tensor(DD, R1I, 1.0, X, ALU.mult, ALU.subtract))
    pl.op(eng, lambda e: e.scalar_tensor_tensor(OUTKN, DD, -0.5, R1I, ALU.is_lt, ALU.add))
    pl.op(eng, lambda e: e.scalar_tensor_tensor(OUTKN, DD, 0.5, OUTKN, ALU.is_gt, ALU.subtract), sem)


def sin_reduce(pl, sem_v, sem_s, X, SCR_I, SCR_D, SCR_K, OUTSIN, pl_eng="vector"):
    pl.op(pl_eng, lambda e: e.tensor_scalar(SCR_K, X, INV2PI, None, ALU.mult))
    roundneg(pl, pl_eng, None, SCR_K, SCR_I, SCR_D, SCR_K)
    pl.op(pl_eng, lambda e: e.scalar_tensor_tensor(SCR_D, SCR_K, -C1, X, ALU.mult, ALU.subtract))
    pl.op(pl_eng, lambda e: e.scalar_tensor_tensor(SCR_D, SCR_K, C2, SCR_D, ALU.mult, ALU.subtract), sem_v)
    pl.wait("scalar", sem_v)
    pl.op("scalar", lambda e: e.activation(OUTSIN, SCR_D, AF.Sin), sem_s)


def build(debug_stop=None):
    nc = bass.Bass("TRN2", target_bir_lowering=False)

    def din(name, shape, dt=F32):
        return nc.dram_tensor(name, shape, dt, kind="ExternalInput").ap()

    X0 = din("L0_attn_x", [T, D])
    WQ = din("L0_attn_Wq", [D, D]); BQ = din("L0_attn_bq", [D])
    WK = din("L0_attn_Wk", [D, D]); BK = din("L0_attn_bk", [D])
    WV = din("L0_attn_Wv", [D, D]); BV = din("L0_attn_bv", [D])
    W0 = din("L0_attn_W0", [D, D]); B0 = din("L0_attn_b0", [D])
    W8 = din("L0_ffn_weight_int8", [D, D], I8)
    WS8 = din("L0_ffn_weight_scale", [D, 1])
    BF8 = din("L0_ffn_bias", [D])
    VW = din("L1_attn_v_w", [D, D]); VB = din("L1_attn_v_b", [D])
    OW = din("L1_attn_out_w", [D, D]); OB = din("L1_attn_out_b", [D])
    QMW = din("L1_attn_qmod_w", [D, NH]); QMB = din("L1_attn_qmod_b", [NH])
    KMW = din("L1_attn_kmod_w", [D, NH]); KMB = din("L1_attn_kmod_b", [NH])
    GW = din("L1_attn_gate_w", [D, 256]); GB = din("L1_attn_gate_b", [256])
    MW = din("L1_attn_mod_w", [D, 128]); MB = din("L1_attn_mod_b", [128])
    MBAS = din("L1_attn_mod_basis", [8, NH, 16])
    FRQ = din("L1_attn_freqs", [NH, 16, 8, 2])
    AMP = din("L1_attn_amps", [NH, 16, 8])
    PHS = din("L1_attn_phases", [NH, 16, 8])
    CQW = din("L1_attn_cq_w", [D, 128])
    CKW = din("L1_attn_ck_w", [D, 128])
    FW1 = din("L1_attn_ffn_w1", [D, 4 * D]); FB1 = din("L1_attn_ffn_b1", [4 * D])
    FW2 = din("L1_attn_ffn_w2", [4 * D, D]); FB2 = din("L1_attn_ffn_b2", [D])
    LN1G = din("L1_attn_ln1_g", [D]); LN1B = din("L1_attn_ln1_b", [D])
    LN2G = din("L1_attn_ln2_g", [D]); LN2B = din("L1_attn_ln2_b", [D])
    QW1 = din("L1_ffn_W1", [4 * D, D]); QB1 = din("L1_ffn_b1", [4 * D])
    QW2 = din("L1_ffn_W2", [D, 4 * D]); QB2 = din("L1_ffn_b2", [D])

    OUT = nc.dram_tensor("out", [T, D], F32, kind="ExternalOutput").ap()

    RESH = nc.dram_tensor("resh", [T, D], F32, kind="Internal").ap()
    R1H = nc.dram_tensor("r1h", [T, D], F32, kind="Internal").ap()
    TRIGC = nc.dram_tensor("trigc", [S, 512], F32, kind="Internal").ap()
    TRIGS = nc.dram_tensor("trigs", [S, 512], F32, kind="Internal").ap()
    BOUNCE = nc.dram_tensor("bounce", [8192], F32, kind="Internal").ap()
    CCI = nc.dram_tensor("cci", [128, 128], F32, kind="Internal")
    CCO = nc.dram_tensor("cco", [128, 128], F32, kind="Internal")
    STSTRIDE = 128 * 4095
    STAIR = nc.dram_tensor("stair", [32 * STSTRIDE + 8], BF16, kind="Internal").ap()
    CKH = nc.dram_tensor("ckh", [32 * 4095 + 8], BF16, kind="Internal").ap()

    es = contextlib.ExitStack()

    def sbuf(name, shape, dt):
        return es.enter_context(nc.sbuf_tensor(name, shape, dt))

    IDN = sbuf("idn", [128, 128], BF16)
    EPSC = sbuf("epsc", [128, 1], F32)
    EPS6 = sbuf("eps6", [128, 1], F32)
    IDNI = sbuf("idni", [128, 128], F32)
    ONESB = sbuf("onesb", [128, 128], BF16)
    PS = es.enter_context(nc.psum_tensor("ps", [128, 4096], F32))

    def bank(i):
        return PS[:, 512 * i:512 * (i + 1)]

    def bankslab(i, nb):
        return PS[:, 512 * i:512 * (i + nb)]

    def bankb(i):
        return PS[:, 512 * i:512 * (i + 1)].bitcast(BF16)

    pl = PL(nc)
    qs = pl.sem("qsync")
    qg = pl.sem("qgp")
    ste = pl.sem("ste")
    sve = pl.sem("sve")
    sse = pl.sem("sse")
    sgx = pl.sem("sgx")
    sout = pl.sem("sout")

    pl.dma("sync", RESH, X0, qs)
    pl.op("gpsimd", lambda e: e.iota(IDNI[:], [[1, 128]], channel_multiplier=-1,
                                     allow_small_or_imprecise_dtypes=True))
    pl.op("gpsimd", lambda e: e.tensor_scalar(IDN[:], IDNI[:], 0.0, None, ALU.is_equal))
    pl.op("gpsimd", lambda e: e.memset(ONESB[:], 1.0))
    pl.op("gpsimd", lambda e: e.memset(EPSC[:], LNEPS))
    pl.op("gpsimd", lambda e: e.memset(EPS6[:], 1e-6), sgx)

    def p0_trig():
        tes = contextlib.ExitStack()

        def tsb(name, shape, dt):
            return tes.enter_context(nc.sbuf_tensor(name, shape, dt))
        IVF = tsb("t_ivf", [1, 512], F32)
        IOT = tsb("t_iot", [1, 512], F32)
        IVR = tsb("t_ivr", [128, 2048], F32)
        PCOL = tsb("t_pc", [128, 4], F32)
        ANG = tsb("t_ang", [128, 2048], F32)
        R1I = tsb("t_r1i", [128, 2048], I32)
        DD = tsb("t_dd", [128, 2048], F32)
        KN = tsb("t_kn", [128, 2048], F32)
        SN = tsb("t_sn", [128, 2048], F32)
        pl.op("gpsimd", lambda e: e.iota(IOT[:], [[1, 512]], channel_multiplier=0,
                                         allow_small_or_imprecise_dtypes=True))
        pl.op("gpsimd", lambda e: e.tensor_scalar(IVF[:], IOT[:], -E_HI, None, ALU.mult))
        pl.op("gpsimd", lambda e: e.tensor_scalar(IOT[:], IOT[:], -E_LO, None, ALU.mult))
        pl.op("gpsimd", lambda e: e.drain())
        pl.op("gpsimd", lambda e: e.tensor_tensor(IVF[:], IVF[:], IOT[:], ALU.add), sgx)
        pl.wait("scalar", sgx)
        pl.op("scalar", lambda e: e.activation(IVF[:], IVF[:], AF.Exp), sse)
        pl.wait("sync", sse)
        pl.dma("sync", bass.AP(BOUNCE.tensor, 0, [[512, 1], [1, 512]]), IVF[:], qs)
        pl.wait("sync", qs)
        pl.dma("sync", IVR[:], bass.AP(BOUNCE.tensor, 0, [[0, 128], [0, 4], [1, 512]]), qs)
        ivr_ready = qs.n
        pl.wait("vector", qs, ivr_ready)
        pl.wait("gpsimd", qs, ivr_ready)
        for g in range(4):
            pl.wait("gpsimd", sve)
            pl.op("gpsimd", lambda e, g=g: e.iota(PCOL[:], [[128, 4]], channel_multiplier=1,
                                                  base=512 * g,
                                                  allow_small_or_imprecise_dtypes=True), sgx)
            pl.wait("vector", sgx)
            for c in range(4):
                pl.op("vector", lambda e, c=c: e.tensor_scalar(
                    ANG[:, 512 * c:512 * (c + 1)], IVR[:, 512 * c:512 * (c + 1)],
                    PCOL[:, c:c + 1], None, ALU.mult))
            sin_reduce(pl, sve, sse, ANG[:], R1I[:], DD[:], KN[:], SN[:])
            pl.wait("sync", sse)
            pl.dma("sync", bass.AP(TRIGS.tensor, 512 * 512 * g,
                                   [[512, 128], [512 * 128, 4], [1, 512]]), SN[:], qs)
            sin_done = qs.n
            pl.wait("vector", qs, sin_done)
            pl.op("vector", lambda e: e.tensor_scalar(ANG[:], ANG[:], float(np.pi / 2), None, ALU.add))
            sin_reduce(pl, sve, sse, ANG[:], R1I[:], DD[:], KN[:], SN[:])
            pl.wait("sync", sse)
            pl.dma("sync", bass.AP(TRIGC.tensor, 512 * 512 * g,
                                   [[512, 128], [512 * 128, 4], [1, 512]]), SN[:], qs)
            pl.wait("vector", qs)

        return tes

    _tes = p0_trig()
    pl.barrier()
    _tes.close()

    RT = sbuf("ln_rt", [128, 2048], F32)
    ZB = sbuf("ln_z", [128, 2048], BF16)
    TINY = sbuf("ln_tiny", [128, 8], F32)
    LNSQ = sbuf("ln_sq", [128, 1024], F32)
    RSTD = sbuf("rstd", [128, NT], F32)
    NEGB = sbuf("negb", [128, NT], F32)
    ZT = sbuf("zt", [128, 8 * 2048], BF16)

    def resh_tile(src, tt):
        return bass.AP(src.tensor, tt * 128 * D, [[D, 128], [1, D]])

    def stats_tile(buf, rs_col, nb_col):
        c0, c1v, c2v, c3v = TINY[:, 0:1], TINY[:, 1:2], TINY[:, 2:3], TINY[:, 3:4]
        pl.op("scalar", lambda e, b=buf: e.activation(LNSQ[:], b, AF.Square), sse)
        pl.wait("vector", sse)
        pl.op("vector", lambda e, b=buf, c=c0: e.tensor_reduce(c, b, AX.X, ALU.add))
        pl.op("vector", lambda e, c=c1v: e.tensor_reduce(c, LNSQ[:], AX.X, ALU.add))
        pl.op("vector", lambda e, c=c0: e.tensor_scalar(c, c, 1.0 / D, None, ALU.mult))
        pl.op("vector", lambda e: e.drain())
        pl.op("vector", lambda e, c=c0, o=c2v: e.tensor_tensor(o, c, c, ALU.mult))
        pl.op("vector", lambda e: e.drain())
        pl.op("vector", lambda e, c=c1v, m=c2v, o=c3v: e.scalar_tensor_tensor(
            o, c, 1.0 / D, m, ALU.mult, ALU.subtract), sve)
        pl.wait("scalar", sve)
        pl.op("scalar", lambda e, v=c3v: e.activation(v, v, AF.Sqrt, bias=EPSC[:, 0:1]), sse)
        pl.wait("vector", sse)
        pl.op("vector", lambda e, v=c3v, o=rs_col: e.reciprocal(o, v))
        pl.op("vector", lambda e: e.drain())
        pl.op("vector", lambda e, m=c0, r=rs_col, o=c2v: e.tensor_tensor(o, m, r, ALU.mult))
        pl.op("vector", lambda e: e.drain())
        pl.op("vector", lambda e, i=c2v, o=nb_col: e.tensor_scalar(o, i, -1.0, None, ALU.mult), sve)
        pl.op("vector", lambda e: e.drain())

    def ln_pass(tiles, src=RESH, neg=False, make_zt=True, absmax_col=None, zt_dst=None,
                pooled=False):
        apply_done = {}
        tr_done = {}
        for li, tt in enumerate(tiles):
            buf = RT[:, 1024 * (li % 2):1024 * (li % 2) + 1024]
            zb = ZB[:, 1024 * (li % 2):1024 * (li % 2) + 1024]
            if li >= 2:
                pl.wait("sync", sve, apply_done[li - 2])
            pl.dma("sync", buf, resh_tile(src, tt), qs)
            pl.wait("scalar", qs)
            c0, c1v, c2v, c3v = TINY[:, 0:1], TINY[:, 1:2], TINY[:, 2:3], TINY[:, 3:4]
            pl.wait("scalar", sve)
            if li >= 2 and make_zt:
                pl.wait("scalar", ste, tr_done[li - 2])
            pl.op("scalar", lambda e, b=buf: e.activation(LNSQ[:], b, AF.Square), sse)
            pl.wait("vector", sse)
            pl.op("vector", lambda e, b=buf, c=c0: e.tensor_reduce(c, b, AX.X, ALU.add))
            pl.op("vector", lambda e, c=c1v: e.tensor_reduce(c, LNSQ[:], AX.X, ALU.add))
            pl.op("vector", lambda e, c=c0: e.tensor_scalar(c, c, 1.0 / D, None, ALU.mult))
            pl.op("vector", lambda e: e.drain())
            pl.op("vector", lambda e, c=c0, o=c2v: e.tensor_tensor(o, c, c, ALU.mult))
            pl.op("vector", lambda e: e.drain())
            pl.op("vector", lambda e, c=c1v, m=c2v, o=c3v: e.scalar_tensor_tensor(
                o, c, 1.0 / D, m, ALU.mult, ALU.subtract), sve)
            pl.wait("scalar", sve)
            pl.op("scalar", lambda e, v=c3v: e.activation(v, v, AF.Sqrt, bias=EPSC[:, 0:1]), sse)
            pl.wait("vector", sse)
            rs_col = RSTD[:, tt:tt + 1]
            nb_col = NEGB[:, tt:tt + 1]
            pl.op("vector", lambda e, v=c3v, o=rs_col: e.reciprocal(o, v))
            pl.op("vector", lambda e: e.drain())
            pl.op("vector", lambda e, m=c0, r=rs_col, o=c2v: e.tensor_tensor(o, m, r, ALU.mult))
            pl.op("vector", lambda e: e.drain())
            pl.op("vector", lambda e, i=c2v, o=nb_col: e.tensor_scalar(o, i, -1.0, None, ALU.mult), sve)
            pl.op("vector", lambda e: e.drain())
            if make_zt:
                if not neg:
                    pl.op("vector", lambda e, b=buf, z=zb, r=rs_col, nb=nb_col:
                          e.tensor_scalar(z, b, r, nb, ALU.mult, ALU.add), sve)
                else:
                    pl.op("vector", lambda e, b=buf, z=zb, r=rs_col, nb=nb_col:
                          e.tensor_scalar(z, b, r, nb, ALU.mult, ALU.add))
                    pl.op("vector", lambda e, z=zb: e.tensor_scalar(z, z, -1.0, None, ALU.mult), sve)
            if absmax_col is not None:
                pl.op("vector", lambda e, b=buf, r=rs_col, nb=nb_col:
                      e.tensor_scalar(LNSQ[:], b, r, nb, ALU.mult, ALU.add))
                pl.op("vector", lambda e: e.drain())
                pl.op("vector", lambda e, c=c2v: e.tensor_reduce(
                    c, LNSQ[:], AX.X, ALU.max, apply_absolute_value=True))
                pl.op("vector", lambda e: e.drain())
                pl.op("vector", lambda e, c=c2v, a=absmax_col: e.tensor_tensor(a, a, c, ALU.max), sve)
                pl.op("vector", lambda e: e.drain())
            apply_done[li] = sve.n
            if make_zt:
                ZTD = zt_dst
                pl.wait("tensor", sve)
                for fc in range(8):
                    pl.op("tensor", lambda e, fc=fc, z=zb: e.transpose(
                        bankb(7)[:, 128 * fc:128 * (fc + 1)], z[:, 128 * fc:128 * (fc + 1)], IDN[:]),
                        ste if (fc == 7 and not pooled) else None)
                if pooled:
                    for half in range(2):
                        pl.op("tensor", lambda e, z=zb, half=half, li=li: e.matmul(
                            bank(3 + half)[0:1, :], ONESB[:, 0:1], z[:, 512 * half:512 * (half + 1)],
                            start=(li == 0), stop=(li == len(tiles) - 1)),
                            ste if half == 1 else None)
                tr_done[li] = ste.n
                pl.wait("vector", ste)
                pbv = bankb(7)
                ps_in = pbv.rearrange("p (a b) -> p a b", a=8)
                zt_out = bass.AP(ZTD[:].tensor, 128 * li, [[8 * 2048, 128], [2048, 8], [1, 128]])
                pl.op("vector", lambda e, o=zt_out, i=ps_in: e.tensor_copy(o, i), sve)
                pl.wait("tensor", sve)
        return apply_done

    def ln_seq(s, src=RESH, neg=False, pooled=False):
        return ln_pass(list(range(s * TPS, (s + 1) * TPS)), src=src, neg=neg, zt_dst=ZT,
                       pooled=pooled)

    PLDR = sbuf("pldr", [1, 1024], BF16)
    PLDC = sbuf("pldc", [128, 16], BF16)
    ORT = sbuf("ort", [128, 2048], F32)
    ORT2 = sbuf("ort2", [128, 2048], F32)

    l0_es = contextlib.ExitStack()

    def sbuf0(name, shape, dt):
        return l0_es.enter_context(nc.sbuf_tensor(name, shape, dt))

    WQKV = sbuf0("wqkv", [128, 8 * 384], BF16)
    BQKVR = sbuf0("bqkv", [1, 384], BF16)
    QKP = sbuf0("qkp", [128, 16 * 256], BF16)
    VT = sbuf0("vt", [128, 16 * 130], BF16)
    COSA = sbuf0("cosa", [128, 16 * 64], F32)
    SINA = sbuf0("sina", [128, 16 * 64], F32)
    TMP1 = sbuf0("tmp1", [128, 1024], F32)
    TMP2 = sbuf0("tmp2", [128, 1024], F32)
    QR = sbuf0("qr", [128, 2048], BF16)
    KR = sbuf0("kr", [128, 2048], BF16)
    SQB = sbuf0("sqb", [128, 2048], BF16)
    NEGBC = sbuf0("negbc", [128, 2], F32)
    ATTN = sbuf0("attn", [128, 2048], BF16)
    CTXA = sbuf0("ctxa", [128, 8 * 2048], BF16)
    SROW = sbuf0("srow", [128, 4096], F32)
    CTXTMP = sbuf0("ctxtmp", [128, 512], BF16)
    RCPR = sbuf0("rcpr", [128, 2048], F32)
    W0S = sbuf0("w0s", [128, 8 * 1024], BF16)
    B0R = sbuf0("b0r", [1, 1024], BF16)

    def l0_attention(s, dbg_pair=None):
        pairs = range(NPAIR) if dbg_pair is None else [dbg_pair]
        for hp in pairs:
            hA, hB = 2 * hp, 2 * hp + 1
            last_mm_prev = ste.n
            pl.wait("gpsimd", ste, last_mm_prev)
            for wi, W in enumerate((WQ, WK, WV)):
                src = bass.AP(W.tensor, 128 * hp, [[D, 128], [D * 128, 8], [1, 128]])
                dst = bass.AP(WQKV[:].tensor, 128 * wi, [[8 * 384, 128], [384, 8], [1, 128]])
                pl.dma("gpsimd", dst, src, qg)
            for bi, B in enumerate((BQ, BK, BV)):
                src = bass.AP(B.tensor, 128 * hp, [[128, 1], [1, 128]])
                pl.dma("gpsimd", BQKVR[0:1, 128 * bi:128 * (bi + 1)], src, qg)
            pl.wait("sync", sve)
            pl.dma("sync", COSA[:], bass.AP(TRIGC.tensor, 64 * hp,
                                            [[512, 128], [512 * 128, 16], [1, 64]]), qs)
            pl.dma("sync", SINA[:], bass.AP(TRIGS.tensor, 64 * hp,
                                            [[512, 128], [512 * 128, 16], [1, 64]]), qs)
            trig_ready = qs.n
            pl.wait("vector", ste, last_mm_prev)
            vt_ones = bass.AP(VT[:].tensor, 64, [[16 * 130, 128], [130, 16], [65, 2], [1, 1]])
            pl.op("vector", lambda e, a=vt_ones: e.memset(a, 1.0), sve)
            vt_ones_done = sve.n

            w_ready = qg.n
            pl.wait("tensor", qg, w_ready)
            for g in range(4):
                for i in range(4):
                    lt = 4 * g + i
                    for fc in range(8):
                        pl.op("tensor", lambda e, fc=fc, lt=lt, i=i: e.matmul(
                            PS[:, 512 * i:512 * i + 384],
                            ZT[:, 2048 * fc + 128 * lt:2048 * fc + 128 * (lt + 1)],
                            WQKV[:, 384 * fc:384 * (fc + 1)],
                            start=(fc == 0), stop=False))
                    pl.op("tensor", lambda e, i=i: e.matmul(
                        PS[:, 512 * i:512 * i + 384],
                        ONESB[0:1, 0:128], BQKVR[0:1, :],
                        start=False, stop=True), ste if i == 3 else None)
                mm_done = ste.n
                pl.wait("vector", ste, mm_done)
                if g == 0:
                    pl.wait("vector", sve, vt_ones_done)
                qk_in = bass.AP(PS[:].tensor, 0, [[4096, 128], [512, 4], [1, 256]])
                qk_out = bass.AP(QKP[:].tensor, 1024 * g, [[16 * 256, 128], [256, 4], [1, 256]])
                pl.op("vector", lambda e, o=qk_out, i=qk_in: e.tensor_copy(o, i))
                v_in = bass.AP(PS[:].tensor, 256, [[4096, 128], [512, 4], [64, 2], [1, 64]])
                v_out = bass.AP(VT[:].tensor, 130 * 4 * g, [[16 * 130, 128], [130, 4], [65, 2], [1, 64]])
                pl.op("vector", lambda e, o=v_out, i=v_in: e.tensor_copy(o, i), sve)
                pl.wait("tensor", sve)

            pl.wait("vector", qs, trig_ready)

            def qkp_ap(base):
                return bass.AP(QKP[:].tensor, base, [[16 * 256, 128], [256, 16], [64, 2], [2, 32]])

            def trig_ap(TTT):
                return bass.AP(TTT[:].tensor, 0, [[16 * 64, 128], [64, 16], [32, 2], [1, 32]])

            def rot_out(R, base):
                return bass.AP(R[:].tensor, base, [[2048, 128], [128, 16], [64, 2], [2, 32]])

            for base_qk, R in ((0, QR), (128, KR)):
                X1 = qkp_ap(base_qk)
                X2 = qkp_ap(base_qk + 1)
                CO = trig_ap(COSA)
                SI = trig_ap(SINA)
                pl.op("vector", lambda e, a=X1, b=CO: e.tensor_tensor(TMP1[:], a, b, ALU.mult))
                pl.op("vector", lambda e, a=X2, b=SI: e.tensor_tensor(TMP2[:], a, b, ALU.mult))
                pl.op("vector", lambda e, o=rot_out(R, 0): e.tensor_tensor(o, TMP1[:], TMP2[:], ALU.subtract))
                pl.op("vector", lambda e, a=X1, b=SI: e.tensor_tensor(TMP1[:], a, b, ALU.mult))
                pl.op("vector", lambda e, a=X2, b=CO: e.tensor_tensor(TMP2[:], a, b, ALU.mult))
                pl.op("vector", lambda e, o=rot_out(R, 1): e.tensor_tensor(o, TMP1[:], TMP2[:], ALU.add), sve)
            rot_done = sve.n

            pl.wait("tensor", sve, rot_done)
            for R in (QR, KR):
                for half in range(2):
                    for i in range(8):
                        lt = 8 * half + i
                        pl.op("tensor", lambda e, R=R, lt=lt, i=i: e.transpose(
                            bankb(7)[:, 128 * i:128 * (i + 1)],
                            R[:, 128 * lt:128 * (lt + 1)], IDN[:]), ste)
                    tr_done = ste.n
                    pl.wait("vector", ste, tr_done)
                    pbv = bankb(7).rearrange("p (a b) -> p a b", a=8)
                    out = bass.AP(R[:].tensor, 1024 * half, [[2048, 128], [128, 8], [1, 128]])
                    pl.op("vector", lambda e, o=out, i=pbv: e.tensor_copy(o, i), sve)
                    pl.wait("tensor", sve)
            trans_done = sve.n

            for which, R in (("k", KR), ("q", QR)):
                pl.op("vector", lambda e, R=R: e.tensor_tensor(SQB[:], R[:], R[:], ALU.mult), sve)
                sq_done = sve.n
                pl.wait("tensor", sve, sq_done)
                for h in range(2):
                    for qcn in range(4):
                        pl.op("tensor", lambda e, h=h, qcn=qcn: e.matmul(
                            bank(qcn)[0:1, :],
                            ONESB[64 * h:64 * h + 64, 0:1],
                            SQB[64 * h:64 * h + 64, 512 * qcn:512 * (qcn + 1)],
                            start=True, stop=True), ste if qcn == 3 else None)
                    nm_done = ste.n
                    pl.wait("vector", ste, nm_done)
                    ci = 4 + 2 * h + (0 if which == "k" else 1)
                    col = TINY[0:1, ci:ci + 1]
                    pl.op("vector", lambda e, c=col: e.tensor_reduce(
                        c, PS[0:1, 0:2048], AX.X, ALU.max), sve)
                    pl.wait("tensor", sve)
            pl.wait("scalar", sve)
            for h in range(2):
                mk2 = TINY[0:1, 4 + 2 * h:4 + 2 * h + 1]
                mq2 = TINY[0:1, 5 + 2 * h:5 + 2 * h + 1]
                pl.op("scalar", lambda e, a=mk2: e.activation(a, a, AF.Sqrt))
                pl.op("scalar", lambda e, a=mq2: e.activation(a, a, AF.Sqrt), sse)
            pl.wait("vector", sse)
            for h in range(2):
                mk = TINY[0:1, 4 + 2 * h:4 + 2 * h + 1]
                mq = TINY[0:1, 5 + 2 * h:5 + 2 * h + 1]
                pl.op("vector", lambda e, a=mq, b=mk: e.tensor_tensor(a, a, b, ALU.mult))
                pl.op("vector", lambda e: e.drain())
                pl.op("vector", lambda e, a=mq: e.tensor_scalar(a, a, -0.125, None, ALU.mult), sve)
            pl.op("vector", lambda e: e.drain(), sve)
            pl.wait("sync", sve)
            bounce_negb = bass.AP(BOUNCE.tensor, 4096, [[2, 1], [1, 2]])
            tiny_negb = bass.AP(TINY[:].tensor, 5, [[8, 1], [2, 2]])
            pl.em("sync", lambda e, o=bounce_negb, i=tiny_negb, s=qs.h: e.dma_start(
                out=o, in_=i, allow_slow_non_contiguous=True).then_inc(s, 16))
            qs.n += 16
            pl.wait("sync", qs)
            pl.dma("sync", NEGBC[:], bass.AP(BOUNCE.tensor, 4096, [[0, 128], [1, 2]]), qs)
            negbc_ready = qs.n

            pl.wait("scalar", qs, negbc_ready)
            for h in range(2):
                hbase = 64 * h
                for qc in range(4):
                    pl.wait("tensor", sse)
                    for grp in range(8):
                        jt0 = 2 * grp
                        gb = 2 * (grp % 2)
                        pl.wait("tensor", sse)
                        for j in range(2):
                            jt = jt0 + j
                            pl.op("tensor", lambda e, jt=jt, gb=gb, j=j, hbase=hbase, qc=qc: e.matmul(
                                bank(gb + j),
                                KR[hbase:hbase + 64, 128 * jt:128 * (jt + 1)],
                                QR[hbase:hbase + 64, 512 * qc:512 * (qc + 1)],
                                start=True, stop=True), ste)
                        sc_done = ste.n
                        pl.wait("scalar", ste, sc_done)
                        abuf = ATTN[:, 1024 * (grp % 2):1024 * (grp % 2) + 1024]
                        pl.op("scalar", lambda e, gb=gb, a=abuf, h=h: e.activation(
                            a, bankslab(gb, 2), AF.Exp, bias=NEGBC[:, h:h + 1], scale=0.125), sse)
                        exp_done = sse.n
                        pl.wait("tensor", sse, exp_done)
                        for j in range(2):
                            jt = jt0 + j
                            pl.op("tensor", lambda e, jt=jt, j=j, h=h, a=abuf: e.matmul(
                                bank(4)[0:65, :],
                                VT[:, 130 * jt + 65 * h:130 * jt + 65 * h + 65],
                                a[:, 512 * j:512 * (j + 1)],
                                start=(jt == 0), stop=(jt == 15)), ste if j == 1 else None)
                    ctx_done = ste.n
                    pl.wait("scalar", ste, ctx_done)
                    if h == 0:
                        pl.op("scalar", lambda e, hp=hp, qc=qc: e.copy(
                            CTXA[0:64, 2048 * hp + 512 * qc:2048 * hp + 512 * (qc + 1)],
                            bank(4)[0:64, :]))
                    else:
                        pl.op("scalar", lambda e: e.copy(CTXTMP[0:64, :], bank(4)[0:64, :]), sse)
                        pl.wait("gpsimd", sse)
                        pl.dma("gpsimd", CTXA[64:128, 2048 * hp + 512 * qc:2048 * hp + 512 * (qc + 1)],
                               CTXTMP[0:64, :], qg)
                    pl.op("scalar", lambda e, h=h, qc=qc: e.copy(
                        SROW[64:65, 2048 * h + 512 * qc:2048 * h + 512 * (qc + 1)],
                        bank(4)[64:65, :]), sse)
            pl.wait("gpsimd", sse)
            pl.wait("scalar", qg)
            pl.op("scalar", lambda e: e.activation(SROW[64:65, :], SROW[64:65, :], AF.Ln))
            pl.op("scalar", lambda e: e.activation(SROW[64:65, :], SROW[64:65, :], AF.Exp, scale=-1.0), sse)
            pl.wait("sync", sse)
            pl.dma("sync", bass.AP(BOUNCE.tensor, 0, [[4096, 1], [1, 4096]]), SROW[64:65, :], qs)
            pl.wait("sync", qs)
            pl.dma("sync", RCPR[0:64, :], bass.AP(BOUNCE.tensor, 0, [[0, 64], [1, 2048]]), qs)
            pl.dma("sync", RCPR[64:128, :], bass.AP(BOUNCE.tensor, 2048, [[0, 64], [1, 2048]]), qs)
            pl.wait("vector", qs)
            pl.wait("vector", qg)
            pl.op("vector", lambda e, hp=hp: e.tensor_tensor(
                CTXA[:, 2048 * hp:2048 * (hp + 1)], CTXA[:, 2048 * hp:2048 * (hp + 1)],
                RCPR[:], ALU.mult), sve)

        if dbg_pair is not None:
            return
        pl.dma("gpsimd", W0S[:], bass.AP(W0.tensor, 0, [[D, 128], [D * 128, 8], [1, 1024]]), qg)
        pl.dma("gpsimd", B0R[:], bass.AP(B0.tensor, 0, [[1024, 1], [1, 1024]]), qg)
        w0_ready = qg.n
        pl.wait("tensor", qg, w0_ready)
        pl.wait("tensor", sve)
        store_sem = pl.sem("l0store")
        store_base = store_sem.n
        for lt in range(TPS):
            tt = s * TPS + lt
            rbuf = ORT[:, 1024 * (lt % 2):1024 * (lt % 2) + 1024]
            obuf = ORT2[:, 1024 * (lt % 2):1024 * (lt % 2) + 1024]
            if lt >= 2:
                pl.wait("sync", store_sem, store_base + 16 * (lt - 1))
            pl.dma("sync", rbuf, resh_tile(RESH, tt), qs)
            for oc in range(2):
                bk = 5 + oc
                pl.wait("tensor", sve)
                for hp2 in range(8):
                    pl.op("tensor", lambda e, hp2=hp2, lt=lt, oc=oc, bk=bk: e.matmul(
                        bank(bk),
                        CTXA[:, 2048 * hp2 + 128 * lt:2048 * hp2 + 128 * (lt + 1)],
                        W0S[:, 1024 * hp2 + 512 * oc:1024 * hp2 + 512 * (oc + 1)],
                        start=(hp2 == 0), stop=False))
                pl.op("tensor", lambda e, oc=oc, bk=bk: e.matmul(
                    bank(bk), ONESB[0:1, 0:128], B0R[0:1, 512 * oc:512 * (oc + 1)],
                    start=False, stop=True), ste)
                mm_done = ste.n
                pl.wait("vector", ste, mm_done)
                pl.wait("vector", qs)
                pl.op("vector", lambda e, bk=bk, oc=oc, r=rbuf, o=obuf: e.scalar_tensor_tensor(
                    o[:, 512 * oc:512 * (oc + 1)], bank(bk), 1.0,
                    r[:, 512 * oc:512 * (oc + 1)], ALU.mult, ALU.add), sve)
            pl.wait("sync", sve)
            pl.dma("sync", resh_tile(RESH, tt), obuf, store_sem)
        pl.wait("sync", store_sem)

    def emit_l0_ffn():
        fes = contextlib.ExitStack()

        def sbf(n, s, d):
            return fes.enter_context(nc.sbuf_tensor(n, s, d))
        W8T = sbf("w8t", [128, 8 * 1024], BF16)
        W8N = sbf("w8n", [128, 8 * 1024], BF16)
        WSC = sbf("wsc", [128, 8], F32)
        AMX = sbf("amx", [128, 1], F32)
        GXS = sbf("gxs", [1, 4], F32)
        RXN = sbf("rxn", [128, 2], F32)
        BN8 = sbf("bn8", [1, 1024], BF16)
        B8F = sbf("b8f", [1, 1024], F32)
        CCS = sbf("ccs", [128, 128], F32)
        XQN = sbf("xqn", [128, 1024], BF16)
        QI = sbf("qi", [128, 1024], I32)
        QD = sbf("qd", [128, 1024], F32)
        QG = sbf("qg2", [128, 1024], F32)
        XQT = sbf("xqt", [128, 1024], BF16)

        pl.dma("gpsimd", W8N[:], bass.AP(W8.tensor, 0, [[1024, 128], [128 * 1024, 8], [1, 1024]]), qg)
        wsc_src = bass.AP(WS8.tensor, 0, [[1, 128], [128, 8]])
        pl.em("gpsimd", lambda e, o=WSC[:], i=wsc_src, s=qg.h: e.dma_start(
            out=o, in_=i, allow_slow_non_contiguous=True).then_inc(s, 16))
        qg.n += 16
        pl.dma("sync", B8F[:], bass.AP(BF8.tensor, 0, [[1024, 1], [1, 1024]]), qs)
        pl.op("gpsimd", lambda e: e.memset(CCS[:], 0.0), sgx)
        w8_ready = qg.n
        pl.wait("vector", qg, w8_ready)
        for oc in range(8):
            pl.op("vector", lambda e, oc=oc: e.tensor_scalar(
                W8N[:, 1024 * oc:1024 * (oc + 1)], W8N[:, 1024 * oc:1024 * (oc + 1)],
                WSC[:, oc:oc + 1], None, ALU.mult), sve)
        w8s_done = sve.n
        pl.wait("tensor", sve, w8s_done)
        for oc in range(8):
            for fc in range(8):
                pl.op("tensor", lambda e, oc=oc, fc=fc: e.transpose(
                    bankb(7)[:, 128 * fc:128 * (fc + 1)],
                    W8N[:, 1024 * oc + 128 * fc:1024 * oc + 128 * (fc + 1)], IDN[:]),
                    ste if fc == 7 else None)
            pl.wait("vector", ste)
            pbv = bankb(7).rearrange("p (a b) -> p a b", a=8)
            w8t_out = bass.AP(W8T[:].tensor, 128 * oc, [[8 * 1024, 128], [1024, 8], [1, 128]])
            pl.op("vector", lambda e, o=w8t_out, i=pbv: e.tensor_copy(o, i), sve)
            pl.wait("tensor", sve)

        pl.op("vector", lambda e: e.memset(AMX[:], 0.0), sve)
        ln_pass(list(range(NT)), make_zt=False, absmax_col=AMX[:, 0:1])
        pl.wait("sync", sve)
        pl.dma("sync", bass.AP(BOUNCE.tensor, 0, [[1, 128], [1, 1]]), AMX[:], qs)
        pl.wait("sync", qs)
        pl.dma("sync", CCS[0:1, 0:128], bass.AP(BOUNCE.tensor, 0, [[128, 1], [1, 128]]), qs)
        pl.wait("vector", qs)
        pl.wait("vector", sgx)
        pl.op("vector", lambda e: e.tensor_reduce(GXS[0:1, 0:1], CCS[0:1, 0:128], AX.X, ALU.max))
        pl.op("vector", lambda e: e.drain())
        pl.op("vector", lambda e: e.tensor_scalar(
            GXS[0:1, 0:1], GXS[0:1, 0:1], 1.0 / 127.0, 1e-5, ALU.mult, ALU.max))
        pl.op("vector", lambda e: e.drain())
        pl.op("vector", lambda e: e.tensor_copy(CCS[0:1, 0:1], GXS[0:1, 0:1]), sve)
        pl.wait("gpsimd", sve)
        pl.dma("gpsimd", CCI.ap(), CCS[:], qg)
        pl.wait("gpsimd", qg)
        ccs_done = pl.sem("ccdone")
        ccs_done.n += 1
        pl.em("gpsimd", lambda e, s=ccs_done.h: e.collective_compute(
            "AllReduce", ALU.max, replica_groups=[list(range(8))],
            ins=[CCI.ap().opt()], outs=[CCO.ap().opt()]).then_inc(s, 1))
        pl.wait("gpsimd", ccs_done)
        pl.dma("gpsimd", GXS[0:1, 1:2], CCO.ap()[0:1, 0:1], qg)
        gxs_ready = qg.n
        pl.wait("vector", qg, gxs_ready)
        pl.op("vector", lambda e: e.reciprocal(GXS[0:1, 2:3], GXS[0:1, 1:2]))
        pl.op("vector", lambda e: e.drain())
        pl.op("vector", lambda e: e.tensor_scalar(GXS[0:1, 3:4], GXS[0:1, 1:2], -1.0, None, ALU.mult), sve)
        pl.wait("sync", sve)
        pl.dma("sync", bass.AP(BOUNCE.tensor, 0, [[2, 1], [1, 2]]),
               GXS[0:1, 2:4], qs)
        pl.wait("sync", qs)
        pl.dma("sync", RXN[:], bass.AP(BOUNCE.tensor, 0, [[0, 128], [1, 2]]), qs)
        rxn_ready = qs.n
        pl.wait("vector", qs, rxn_ready)
        pl.op("vector", lambda e: e.tensor_scalar(BN8[:], B8F[:], GXS[0:1, 2:3], -1.0, ALU.mult, ALU.mult), sve)
        bn8_ready = sve.n

        store_sem = pl.sem("ffnst")
        st_base = store_sem.n
        for tt in range(NT):
            rbuf = ORT[:, 1024 * (tt % 2):1024 * (tt % 2) + 1024]
            obuf = ORT2[:, 1024 * (tt % 2):1024 * (tt % 2) + 1024]
            if tt >= 2:
                pl.wait("sync", store_sem, st_base + 16 * (tt - 1))
            pl.dma("sync", rbuf, resh_tile(RESH, tt), qs)
            rq = qs.n
            pl.wait("vector", qs, rq)
            rs_col = RSTD[:, tt:tt + 1]
            nb_col = NEGB[:, tt:tt + 1]
            pl.wait("vector", ste)
            pl.op("vector", lambda e, b=rbuf, r=rs_col, nb=nb_col: e.tensor_scalar(
                QG[:], b, r, nb, ALU.mult, ALU.add))
            pl.op("vector", lambda e: e.tensor_scalar(QD[:], QG[:], RXN[:, 0:1], None, ALU.mult))
            pl.op("vector", lambda e: e.tensor_copy(QI[:], QD[:]))
            pl.op("vector", lambda e: e.scalar_tensor_tensor(QG[:], QI[:], 1.0, QD[:], ALU.mult, ALU.subtract))
            pl.op("vector", lambda e: e.scalar_tensor_tensor(QD[:], QG[:], -0.5, QI[:], ALU.is_lt, ALU.add))
            pl.op("vector", lambda e: e.scalar_tensor_tensor(XQN[:], QG[:], 0.5, QD[:], ALU.is_gt, ALU.subtract), sve)
            q_done = sve.n
            pl.wait("tensor", sve, q_done)
            for fc in range(8):
                pl.op("tensor", lambda e, fc=fc: e.transpose(
                    bankb(7)[:, 128 * fc:128 * (fc + 1)], XQN[:, 128 * fc:128 * (fc + 1)], IDN[:]),
                    ste if fc == 7 else None)
            pl.wait("vector", ste)
            pbv = bankb(7).rearrange("p (a b) -> p a b", a=8)
            pl.op("vector", lambda e, i=pbv: e.tensor_copy(XQT[:], i), sve)
            xqt_done = sve.n
            pl.wait("tensor", sve, xqt_done)
            for oc in range(2):
                for fc in range(8):
                    pl.op("tensor", lambda e, oc=oc, fc=fc: e.matmul(
                        bank(oc), XQT[:, 128 * fc:128 * (fc + 1)],
                        W8T[:, 1024 * fc + 512 * oc:1024 * fc + 512 * (oc + 1)],
                        start=(fc == 0), stop=False))
                pl.op("tensor", lambda e, oc=oc: e.matmul(
                    bank(oc), ONESB[0:1, 0:128], BN8[0:1, 512 * oc:512 * (oc + 1)],
                    start=False, stop=True), ste)
                mm_done = ste.n
                pl.wait("vector", ste, mm_done)
                pl.op("vector", lambda e, oc=oc, o=obuf, rb=rbuf: e.scalar_tensor_tensor(
                    o[:, 512 * oc:512 * (oc + 1)], bank(oc), RXN[:, 1:2],
                    rb[:, 512 * oc:512 * (oc + 1)], ALU.mult, ALU.add), sve)
            pl.wait("sync", sve)
            pl.dma("sync", resh_tile(RESH, tt), obuf, store_sem)
        pl.wait("sync", store_sem)
        pl.barrier()
        fes.close()

    ALPHA1, MIX1, STEPA1 = 3.0, 0.15, 6.0
    LRSCALE = float(MIX1 / np.sqrt(8.0))

    def emit_pooled_finish(s):
        pl.wait("vector", ste)
        for half in range(2):
            pl.op("vector", lambda e, half=half: e.tensor_scalar(
                PLDR[0:1, 512 * half:512 * (half + 1)], bank(3 + half)[0:1, :],
                1.0 / S, None, ALU.mult), sve if half == 1 else None)
        pr_done = sve.n
        pl.wait("tensor", sve, pr_done)
        for fc in range(8):
            pl.op("tensor", lambda e, fc=fc: e.transpose(
                bankb(2)[:, 2 * fc:2 * fc + 1], PLDR[0:1, 128 * fc:128 * (fc + 1)],
                IDN[0:1, 0:1]), ste if fc == 7 else None)
        pl.wait("vector", ste)
        pin = bass.AP(bankb(2).tensor, bankb(2).offset, [[8192, 128], [2, 8]])
        pl.op("vector", lambda e, s=s, i=pin: e.tensor_copy(PLDC[:, 8 * s:8 * s + 8], i), sve)
        pl.wait("tensor", sve)

    def l1_alloc(stk):
        def sb1(n, s, d):
            return stk.enter_context(nc.sbuf_tensor(n, s, d))
        p = {}
        p["PFM"] = sb1("pfm", [128, 8], F32)
        p["PFMB"] = sb1("pfmb", [128, 8], BF16)
        p["GWS"] = sb1("gws", [128, 8 * 256], BF16)
        p["GBR"] = sb1("gbr", [1, 256], BF16)
        p["MWS"] = sb1("mws", [128, 8 * 128], BF16)
        p["MBC"] = sb1("mbc", [1, 128], BF16)
        p["BASB"] = sb1("basb", [128, 256], BF16)
        p["MODR"] = sb1("modr", [1, 128], BF16)
        p["MODCC"] = sb1("modcc", [128, 1], BF16)
        p["GLROW"] = sb1("glrow", [1, 512], F32)
        p["WTSROW"] = sb1("wtsrow", [1, 256], BF16)
        p["WTC"] = sb1("wtc", [128, 4], F32)
        p["HSEL"] = sb1("hsel", [128, 64], F32)
        p["W2HS"] = sb1("w2hs", [128, 32], BF16)
        p["P1R"] = sb1("p1r", [128, 4095], F32)
        p["P2R"] = sb1("p2r", [128, 4095], F32)
        p["F1C"] = sb1("f1c", [128, 1], F32)
        p["F2C"] = sb1("f2c", [128, 1], F32)
        p["PHC"] = sb1("phc", [128, 1], F32)
        p["AMPC"] = sb1("ampc", [128, 1], F32)
        p["SELF"] = sb1("self", [128, 32], F32)
        p["AT2"] = sb1("at2", [128, 2048], F32)
        p["SELAB"] = sb1("selab", [128, 128], BF16)
        p["SINM"] = sb1("sinm", [128, 4095], BF16)
        p["ARG"] = sb1("arg", [128, 2048], F32)
        p["KI"] = sb1("l1ki", [128, 2048], I32)
        p["KD"] = sb1("l1kd", [128, 2048], F32)
        p["KK"] = sb1("l1kk", [128, 2048], F32)
        p["KMH"] = sb1("kmh", [128, 8192], BF16)
        p["CK"] = sb1("l1ck", [16, 4096], F32)
        p["PKT"] = sb1("pkt", [128, 5 * 32], F32)
        p["WREP"] = sb1("wrep", [16, 4095], F32)
        p["CKB"] = p["KD"][:].bitcast(BF16)
        return p

    def emit_l1_weights(p):
        pl.dma("gpsimd", p["GWS"][:], bass.AP(GW.tensor, 0, [[256, 128], [256 * 128, 8], [1, 256]]), qg)
        pl.dma("gpsimd", p["GBR"][:], bass.AP(GB.tensor, 0, [[256, 1], [1, 256]]), qg)
        pl.dma("gpsimd", p["MWS"][:], bass.AP(MW.tensor, 0, [[128, 128], [128 * 128, 8], [1, 128]]), qg)
        pl.dma("gpsimd", p["MBC"][:], bass.AP(MB.tensor, 0, [[128, 1], [1, 128]]), qg)
        pl.op("gpsimd", lambda e, t=p["BASB"]: e.memset(t[:], 0.0), sgx)
        pl.wait("gpsimd", sgx)
        for h in range(NH):
            pl.dma("gpsimd", p["BASB"][8 * h:8 * h + 8, 16 * h:16 * (h + 1)],
                   bass.AP(MBAS.tensor, 16 * h, [[256, 8], [1, 16]]), qg)

    def emit_l1_pooled(s, p):
        pl.wait("tensor", qg)
        for fc in range(8):
            pl.op("tensor", lambda e, fc=fc, s=s, w=p["GWS"]: e.matmul(
                bank(0)[0:1, 0:256], PLDC[:, 8 * s + fc:8 * s + fc + 1],
                w[:, 256 * fc:256 * (fc + 1)],
                start=(fc == 0), stop=False))
        pl.op("tensor", lambda e, r=p["GBR"]: e.matmul(
            bank(0)[0:1, 0:256], ONESB[0:1, 0:1], r[0:1, :], start=False, stop=False))
        for fc in range(8):
            pl.op("tensor", lambda e, fc=fc, s=s, w=p["MWS"]: e.matmul(
                bank(1)[0:1, 0:128], PLDC[:, 8 * s + fc:8 * s + fc + 1],
                w[:, 128 * fc:128 * (fc + 1)],
                start=(fc == 0), stop=False))
        pl.op("tensor", lambda e, r=p["MBC"]: e.matmul(
            bank(1)[0:1, 0:128], ONESB[0:1, 0:1], r[0:1, :], start=False, stop=True), ste)
        mod_done = ste.n
        pl.wait("scalar", ste, mod_done)
        pl.op("scalar", lambda e, m=p["MODR"]: e.copy(m[0:1, :], bank(1)[0:1, 0:128]), sse)
        pl.wait("tensor", sse)
        pl.op("tensor", lambda e, m=p["MODR"]: e.transpose(bankb(2)[:, 0:1], m[0:1, :], IDN[0:1, 0:1]), ste)
        pl.wait("vector", ste)
        pl.op("vector", lambda e, m=p["MODCC"]: e.tensor_copy(m[:, 0:1], bankb(2)[:, 0:1]), sve)
        modc_done = sve.n
        pl.wait("tensor", sve, modc_done)
        pl.op("tensor", lambda e, m=p["MODCC"], bb=p["BASB"]: e.matmul(
            bank(0)[0:1, 0:256], m[:, 0:1], bb[:], start=False, stop=True), ste)
        gl_done = ste.n
        G = p["GLROW"]
        pl.wait("scalar", ste, gl_done)
        pl.op("scalar", lambda e, G=G: e.copy(G[0:1, 0:256], bank(0)[0:1, 0:256]), sse)
        pl.wait("vector", sse)
        g3 = bass.AP(G[:].tensor, 0, [[512, 1], [16, 16], [1, 16]])
        mx = bass.AP(G[:].tensor, 256, [[512, 1], [1, 16], [1, 1]])
        mxb = bass.AP(G[:].tensor, 256, [[512, 1], [1, 16], [0, 16]])
        sm = bass.AP(G[:].tensor, 272, [[512, 1], [1, 16], [1, 1]])
        smb = bass.AP(G[:].tensor, 288, [[512, 1], [1, 16], [0, 16]])
        pl.op("vector", lambda e: e.tensor_reduce(mx, g3, AX.X, ALU.max))
        pl.op("vector", lambda e: e.drain())
        pl.op("vector", lambda e: e.tensor_tensor(g3, g3, mxb, ALU.subtract), sve)
        pl.wait("scalar", sve)
        pl.op("scalar", lambda e, G=G: e.activation(G[0:1, 0:256], G[0:1, 0:256], AF.Exp), sse)
        pl.wait("vector", sse)
        pl.op("vector", lambda e: e.tensor_reduce(sm, g3, AX.X, ALU.add))
        pl.op("vector", lambda e: e.drain())
        pl.op("vector", lambda e: e.reciprocal(
            bass.AP(G[:].tensor, 288, [[512, 1], [1, 16]]),
            bass.AP(G[:].tensor, 272, [[512, 1], [1, 16]])))
        pl.op("vector", lambda e: e.drain())
        pl.op("vector", lambda e: e.tensor_tensor(g3, g3, smb, ALU.mult))
        pl.op("vector", lambda e: e.drain())
        pl.op("vector", lambda e, w=p["WTSROW"], G=G: e.tensor_copy(w[0:1, :], G[0:1, 0:256]), sve)
        wts_done = sve.n
        pl.wait("tensor", sve, wts_done)
        for half in range(2):
            pl.op("tensor", lambda e, half=half, w=p["WTSROW"]: e.transpose(
                bankb(2)[:, 2 * half:2 * half + 1],
                w[0:1, 128 * half:128 * (half + 1)], IDN[0:1, 0:1]), ste if half == 1 else None)
        pl.wait("vector", ste)
        for half in range(2):
            pl.op("vector", lambda e, p=p, s=s, half=half: e.tensor_copy(
                p["WTC"][:, 2 * s + half:2 * s + half + 1], bankb(2)[:, 2 * half:2 * half + 1]),
                sve if half == 1 else None)
        pl.wait("tensor", sve)

    def emit_l1_masks(p):
        PKT = p["PKT"]
        REL, P1, P2, WIN, SCR = (PKT[:, 0:32], PKT[:, 32:64], PKT[:, 64:96],
                                 PKT[:, 96:128], PKT[:, 128:160])
        pl.op("gpsimd", lambda e: e.iota(REL, [[1, 32]], channel_multiplier=32, base=-2047,
                                         allow_small_or_imprecise_dtypes=True), sgx)
        pl.wait("vector", sgx)
        pl.op("vector", lambda e: e.tensor_scalar(P1, REL, 1.0 / S, None, ALU.mult), sve)
        pl.wait("scalar", sgx)
        pl.op("scalar", lambda e: e.activation(SCR, REL, AF.Abs))
        pl.op("scalar", lambda e: e.activation(SCR, SCR, AF.Sqrt, bias=EPS6[:, 0:1]))
        pl.op("scalar", lambda e: e.activation(P2, REL, AF.Sign), sse)
        pl.wait("vector", sse)
        pl.op("vector", lambda e: e.scalar_tensor_tensor(
            P2, SCR, float(1.0 / np.sqrt(S)), P2, ALU.mult, ALU.mult), sve)
        p2_done = sve.n
        pl.wait("scalar", sve, p2_done)
        pl.op("scalar", lambda e: e.activation(SCR, REL, AF.Square))
        pl.op("scalar", lambda e: e.activation(WIN, SCR, AF.Exp,
                                               scale=float(-1.0 / (2.0 * (S / 4.0) ** 2))), sse)
        pl.wait("vector", sse)
        pl.op("vector", lambda e: e.tensor_scalar(WIN, WIN, ALPHA1, None, ALU.mult), sve)
        win_done = sve.n
        pl.wait("sync", sve, win_done)
        pl.dma("sync", bass.AP(BOUNCE.tensor, 0, [[32, 128], [1, 32]]), P1, qs)
        pl.wait("sync", qs)
        pl.dma("sync", p["P1R"][:], bass.AP(BOUNCE.tensor, 0, [[0, 128], [1, 4095]]), qs)
        pl.wait("sync", qs)
        pl.dma("sync", bass.AP(BOUNCE.tensor, 0, [[32, 128], [1, 32]]), P2, qs)
        pl.wait("sync", qs)
        pl.dma("sync", p["P2R"][:], bass.AP(BOUNCE.tensor, 0, [[0, 128], [1, 4095]]), qs)
        pl.wait("sync", qs)
        pl.dma("sync", bass.AP(BOUNCE.tensor, 0, [[32, 128], [1, 32]]), WIN, qs)
        pl.wait("sync", qs)
        pl.dma("sync", p["WREP"][:], bass.AP(BOUNCE.tensor, 0, [[0, 16], [1, 4095]]), qs)
        prep_done = qs.n
        for half in range(2):
            hs = p["HSEL"][:, 16 * half:16 * (half + 1)]
            pl.op("gpsimd", lambda e, hs=hs, half=half: e.iota(
                hs, [[-16, 16]], channel_multiplier=1, base=128 * half,
                allow_small_or_imprecise_dtypes=True), sgx)
        pl.wait("vector", sgx)
        for half in range(2):
            hs = p["HSEL"][:, 16 * half:16 * (half + 1)]
            m1 = p["HSEL"][:, 32 + 16 * half:48 + 16 * half]
            pl.op("vector", lambda e, hs=hs, m1=m1: e.tensor_scalar(m1, hs, 0.0, None, ALU.is_ge))
            pl.op("vector", lambda e, hs=hs: e.tensor_scalar(hs, hs, 16.0, None, ALU.is_lt))
            pl.op("vector", lambda e: e.drain())
            pl.op("vector", lambda e, hs=hs, m1=m1: e.tensor_tensor(hs, hs, m1, ALU.mult), sve)
        pl.wait("gpsimd", qs, prep_done)
        for ci in range(16):
            pl.wait("gpsimd", ste)
            pl.dma_nc("gpsimd", p["F1C"][:], bass.AP(FRQ.tensor, 256 * ci, [[2, 128], [1, 1]]), qg)
            pl.dma_nc("gpsimd", p["F2C"][:], bass.AP(FRQ.tensor, 256 * ci + 1, [[2, 128], [1, 1]]), qg)
            pl.dma("gpsimd", p["PHC"][:], bass.AP(PHS.tensor, 128 * ci, [[1, 128], [1, 1]]), qg)
            pl.dma("gpsimd", p["AMPC"][:], bass.AP(AMP.tensor, 128 * ci, [[1, 128], [1, 1]]), qg)
            cready = qg.n
            pl.wait("gpsimd", qg, cready)
            for hw in range(2):
                w = 2048 if hw == 0 else 2047
                c0_ = 2048 * hw
                A = p["ARG"][:, 0:w]
                KK_, KI_, KD_ = p["KK"][:, 0:w], p["KI"][:, 0:w], p["KD"][:, 0:w]
                pl.wait("gpsimd", sve)
                pl.op("gpsimd", lambda e, A=A, p=p, c0_=c0_, w=w: e.tensor_scalar(
                    A, p["P1R"][:, c0_:c0_ + w], p["F1C"][:, 0:1], None, ALU.mult))
                pl.op("gpsimd", lambda e, p=p, c0_=c0_, w=w: e.tensor_scalar(
                    p["AT2"][:, 0:w], p["P2R"][:, c0_:c0_ + w], p["F2C"][:, 0:1], None, ALU.mult))
                pl.op("gpsimd", lambda e: e.drain())
                pl.op("gpsimd", lambda e, A=A, p=p, w=w: e.tensor_tensor(
                    A, A, p["AT2"][:, 0:w], ALU.add))
                pl.op("gpsimd", lambda e: e.drain())
                pl.op("gpsimd", lambda e, A=A, p=p: e.tensor_scalar(
                    A, A, p["PHC"][:, 0:1], None, ALU.add), sgx)
                arg_done = sgx.n
                pl.wait("vector", sgx, arg_done)
                pl.op("vector", lambda e, A=A, KK_=KK_: e.tensor_scalar(KK_, A, INV2PI, None, ALU.mult))
                pl.op("vector", lambda e, KK_=KK_, KI_=KI_: e.tensor_copy(KI_, KK_))
                pl.op("vector", lambda e, KI_=KI_, KK_=KK_, KD_=KD_: e.scalar_tensor_tensor(
                    KD_, KI_, 1.0, KK_, ALU.mult, ALU.subtract))
                pl.op("vector", lambda e, KD_=KD_, KI_=KI_, KK_=KK_: e.scalar_tensor_tensor(
                    KK_, KD_, -0.5, KI_, ALU.is_lt, ALU.add))
                pl.op("vector", lambda e: e.drain())
                pl.op("vector", lambda e, KD_=KD_, KK_=KK_: e.scalar_tensor_tensor(
                    KK_, KD_, 0.5, KK_, ALU.is_gt, ALU.subtract))
                pl.op("vector", lambda e: e.drain())
                pl.op("vector", lambda e, KK_=KK_, A=A, KD_=KD_: e.scalar_tensor_tensor(
                    KD_, KK_, -C1, A, ALU.mult, ALU.subtract))
                pl.op("vector", lambda e: e.drain())
                pl.op("vector", lambda e, KK_=KK_, KD_=KD_: e.scalar_tensor_tensor(
                    KD_, KK_, C2, KD_, ALU.mult, ALU.subtract), sve)
                rr_done = sve.n
                pl.wait("scalar", sve, rr_done)
                if hw == 0 and ci > 0:
                    pl.wait("scalar", ste)
                pl.op("scalar", lambda e, p=p, c0_=c0_, w=w, KD_=KD_: e.activation(
                    p["SINM"][:, c0_:c0_ + w], KD_, AF.Sin), sse)
            sin_done = sse.n
            pl.op("gpsimd", lambda e, p=p: e.memset(p["SELAB"][:], 0.0))
            sel = p["SELF"][:, 0:16]
            m1 = p["SELF"][:, 16:32]
            pl.op("gpsimd", lambda e, sel=sel: e.iota(sel, [[-8, 16]], channel_multiplier=1, base=0,
                                                      allow_small_or_imprecise_dtypes=True))
            pl.op("gpsimd", lambda e: e.drain())
            pl.op("gpsimd", lambda e, sel=sel, m1=m1: e.tensor_scalar(m1, sel, 0.0, None, ALU.is_ge))
            pl.op("gpsimd", lambda e, sel=sel: e.tensor_scalar(sel, sel, 8.0, None, ALU.is_lt))
            pl.op("gpsimd", lambda e: e.drain())
            pl.op("gpsimd", lambda e, sel=sel, m1=m1: e.tensor_tensor(sel, sel, m1, ALU.mult))
            pl.op("gpsimd", lambda e: e.drain())
            pl.op("gpsimd", lambda e, sel=sel, p=p, ci=ci: e.tensor_scalar(
                p["SELAB"][:, 16 * (ci % 8):16 * (ci % 8) + 16],
                sel, p["AMPC"][:, 0:1], None, ALU.mult), sgx)
            sela_done = sgx.n
            pl.wait("tensor", sse, sin_done)
            pl.wait("tensor", sgx, sela_done)
            for pc in range(8):
                n = 512 if pc < 7 else 511
                pl.op("tensor", lambda e, pc=pc, ci=ci, n=n, p=p: e.matmul(
                    bank(pc)[:, 0:n], p["SELAB"][:],
                    p["SINM"][:, 512 * pc:512 * pc + n],
                    start=(ci % 8 == 0), stop=(ci % 8 == 7)), ste if pc == 7 else None)
            if ci % 8 == 7:
                half = ci // 8
                mmk_done = ste.n
                pl.wait("scalar", ste, mmk_done)
                for pc in range(8):
                    n = 512 if pc < 7 else 511
                    pl.op("scalar", lambda e, pc=pc, half=half, n=n, p=p: e.copy(
                        p["KMH"][:, 4096 * half + 512 * pc:4096 * half + 512 * pc + n],
                        bank(pc)[:, 0:n]), sse if pc == 7 else None)
                pl.wait("tensor", sse)

    def emit_l1_kern_ck(s, p):
        pl.wait("vector", ste)
        for half in range(2):
            pl.op("vector", lambda e, half=half, p=p, s=s: e.tensor_scalar(
                p["W2HS"][:, 16 * half:16 * (half + 1)],
                p["HSEL"][:, 16 * half:16 * (half + 1)],
                p["WTC"][:, 2 * s + half:2 * s + half + 1], None, ALU.mult), sve)
        w2_done = sve.n
        pl.wait("tensor", sve, w2_done)
        for pc in range(8):
            n = 512 if pc < 7 else 511
            for half in range(2):
                pl.op("tensor", lambda e, pc=pc, half=half, n=n, p=p: e.matmul(
                    bank(pc)[0:16, 0:n],
                    p["W2HS"][:, 16 * half:16 * (half + 1)],
                    p["KMH"][:, 4096 * half + 512 * pc:4096 * half + 512 * pc + n],
                    start=(half == 0), stop=(half == 1)), ste if half == 1 else None)
        kern_done = ste.n
        pl.wait("scalar", ste, kern_done)
        for pc in range(8):
            n = 512 if pc < 7 else 511
            pl.op("scalar", lambda e, pc=pc, n=n, p=p: e.activation(
                p["CK"][0:16, 512 * pc:512 * pc + n], bank(pc)[0:16, 0:n],
                AF.Sigmoid, scale=STEPA1), sse if pc == 7 else None)
        sig_done = sse.n
        pl.wait("vector", sse, sig_done)
        pl.op("vector", lambda e, p=p: e.tensor_tensor(
            p["CK"][0:16, 0:4095], p["CK"][0:16, 0:4095], p["WREP"][:], ALU.mult))
        pl.op("vector", lambda e, p=p: e.memset(p["CK"][0:16, 0:2047], -4e9))
        pl.op("vector", lambda e: e.drain())
        pl.op("vector", lambda e, p=p: e.tensor_copy(
            p["CKB"][0:16, 0:4095], p["CK"][0:16, 0:4095]), sve)
        ck_done = sve.n
        pl.wait("sync", sve, ck_done)
        pl.dma("sync", bass.AP(CKH.tensor, 4095 * 16 * s, [[4095, 16], [1, 4095]]),
               p["CKB"][0:16, 0:4095], qs)
        pl.wait("sync", qs)
        stair_sem = pl.sem("stair")
        pl.wait("gpsimd", qs)
        for h in range(16):
            bh = 16 * s + h
            pl.dma("gpsimd",
                   bass.AP(STAIR.tensor, bh * STSTRIDE, [[4095, 128], [1, 4095]]),
                   bass.AP(CKH.tensor, 4095 * bh, [[0, 128], [1, 4095]]), stair_sem)
        pl.wait("gpsimd", stair_sem)

    _l1a_ctr = [0]

    def l1a_alloc(stk):
        _l1a_ctr[0] += 1
        sfx = str(_l1a_ctr[0])

        def sb2(n, s, d):
            return stk.enter_context(nc.sbuf_tensor(n + sfx, s, d))
        q = {}
        q["QSW"] = sb2("qsw", [128, 8 * 512], BF16)
        q["KSW"] = sb2("ksw", [128, 8 * 512], BF16)
        q["BRK"] = sb2("brk", [1, 512], BF16)
        q["AUGQ"] = sb2("augq", [128, 4 * 2048], BF16)
        q["AUGK"] = sb2("augk", [128, 4 * 2048], BF16)
        q["KMT"] = sb2("kmt", [128, 256], F32)
        q["KMB"] = sb2("kmb", [128, 256], F32)
        q["QTMP"] = sb2("qtmp", [128, 512], BF16)
        q["KTMP"] = sb2("ktmp", [128, 512], BF16)
        q["VT1"] = sb2("vt1", [128, 16 * 130], BF16)
        q["VWS"] = sb2("vws", [128, 8 * 128], BF16)
        q["VBR"] = sb2("vbr", [1, 128], BF16)
        q["ATT1"] = sb2("att1", [128, 2048], BF16)
        q["TOEPM"] = sb2("toepm", [128, 16 * 512], BF16)
        q["EXPIN"] = sb2("expin", [128, 1024], F32)
        q["SRO"] = sb2("sro", [128, 2048], F32)
        q["RCP1"] = sb2("rcp1", [128, 2048], F32)
        q["CTX1"] = sb2("ctx1", [128, 8 * 2048], BF16)
        q["CTXT"] = sb2("ctxt1", [128, 512], BF16)
        q["SQ2"] = sb2("sq2", [128, 512], F32)
        q["BND"] = sb2("bnd", [128, 2], F32)

        q["OBR"] = sb2("obr", [1, 1024], BF16)
        q["R1Z"] = sb2("r1z", [128, 1024], F32)
        q["OWS"] = q["AUGQ"]
        q["_toep_free"] = [0]
        return q

    def emit_l1_proj(s, q):
        pl.op("gpsimd", lambda e, q=q: e.memset(q["QSW"][:], 0.0))
        pl.op("gpsimd", lambda e, q=q: e.memset(q["KSW"][:], 0.0))
        pl.op("gpsimd", lambda e, q=q: e.memset(q["BRK"][:], 0.0), sgx)
        pl.wait("gpsimd", sgx)
        for fc in range(8):
            pl.dma("gpsimd",
                   bass.AP(q["QSW"][:].tensor, 512 * fc, [[8 * 512, 128], [32, 16], [1, 8]]),
                   bass.AP(CQW.tensor, 128 * 128 * fc, [[128, 128], [8, 16], [1, 8]]), qg)
            pl.dma("gpsimd",
                   bass.AP(q["KSW"][:].tensor, 512 * fc, [[8 * 512, 128], [32, 16], [1, 8]]),
                   bass.AP(CKW.tensor, 128 * 128 * fc, [[128, 128], [8, 16], [1, 8]]), qg)
            pl.dma_nc("gpsimd",
                      bass.AP(q["KSW"][:].tensor, 512 * fc + 8, [[8 * 512, 128], [32, 16], [1, 1]]),
                      bass.AP(KMW.tensor, 16 * 128 * fc, [[16, 128], [1, 16]]), qg)
        pl.dma_nc("gpsimd", bass.AP(q["BRK"][:].tensor, 8, [[512, 1], [32, 16], [1, 1]]),
                  bass.AP(KMB.tensor, 0, [[16, 1], [1, 16]]), qg)
        w_ready = qg.n
        pl.wait("tensor", qg, w_ready)
        for lt in range(TPS):
            pl.wait("tensor", sve)
            for fc in range(8):
                pl.op("tensor", lambda e, fc=fc, lt=lt, q=q: e.matmul(
                    bank(0), ZT[:, 2048 * fc + 128 * lt:2048 * fc + 128 * (lt + 1)],
                    q["QSW"][:, 512 * fc:512 * (fc + 1)], start=(fc == 0), stop=(fc == 7)))
            for fc in range(8):
                pl.op("tensor", lambda e, fc=fc, lt=lt, q=q: e.matmul(
                    bank(1), ZT[:, 2048 * fc + 128 * lt:2048 * fc + 128 * (lt + 1)],
                    q["KSW"][:, 512 * fc:512 * (fc + 1)], start=(fc == 0), stop=False))
            pl.op("tensor", lambda e, q=q: e.matmul(
                bank(1), ONESB[0:1, 0:128], q["BRK"][0:1, :], start=False, stop=True), ste)
            mm_done = ste.n
            pl.wait("vector", ste, mm_done)
            pl.op("vector", lambda e, q=q: e.tensor_scalar(q["QTMP"][:], bank(0), LRSCALE, None, ALU.mult))
            pl.op("vector", lambda e, q=q: e.tensor_copy(q["KTMP"][:], bank(1)))
            kmt_in = bass.AP(PS[:].tensor, 512 * 1 + 8, [[4096, 128], [32, 16]])
            pl.op("vector", lambda e, q=q, lt=lt, i=kmt_in: e.tensor_copy(
                q["KMT"][:, 16 * lt:16 * (lt + 1)], i), sve)
            ev_done = sve.n
            pl.wait("tensor", sve, ev_done)
            for ti, SRC in ((0, q["QTMP"]), (1, q["KTMP"])):
                for c in range(4):
                    pl.op("tensor", lambda e, SRC=SRC, c=c, ti=ti: e.transpose(
                        bankb(7)[:, 512 * ti + 128 * c:512 * ti + 128 * (c + 1)],
                        SRC[:, 128 * c:128 * (c + 1)], IDN[:]),
                        ste if (ti, c) == (1, 3) else None)
            tr_done = ste.n
            pl.wait("vector", ste, tr_done)
            pbv = bankb(7).rearrange("p (a b) -> p a b", a=8)
            for ti, DST in ((0, q["AUGQ"]), (1, q["AUGK"])):
                out = bass.AP(DST[:].tensor, 128 * lt, [[4 * 2048, 128], [2048, 4], [1, 128]])
                inp = bass.AP(pbv.tensor, pbv.offset + 512 * ti, [[8192, 128], [128, 4], [1, 128]])
                pl.op("vector", lambda e, o=out, i=inp: e.tensor_copy(o, i),
                      sve if ti == 1 else None)
            pl.wait("tensor", sve)
        pl.op("vector", lambda e, q=q: e.tensor_reduce(
            q["BND"][:, 0:1], q["KMT"][:], AX.X, ALU.max, apply_absolute_value=True))
        pl.op("vector", lambda e: e.drain())
        pl.op("vector", lambda e, q=q: e.tensor_reduce(
            q["SQ2"][:, 0:1], q["AUGQ"][:], AX.X, ALU.max, apply_absolute_value=True))
        pl.op("vector", lambda e: e.drain())
        pl.op("vector", lambda e, q=q: e.tensor_reduce(
            q["SQ2"][:, 1:2], q["AUGK"][:], AX.X, ALU.max, apply_absolute_value=True))
        pl.op("vector", lambda e: e.drain())
        pl.op("vector", lambda e, q=q: e.tensor_copy(q["SQ2"][:, 2:3], q["BND"][:, 0:1]), sve)
        pl.wait("sync", sve)
        pl.dma("sync", bass.AP(BOUNCE.tensor, 0, [[3, 128], [1, 3]]),
               q["SQ2"][:, 0:3], qs)
        pl.wait("sync", qs)
        pl.dma_nc("sync", q["SQ2"][0:1, 4:388],
                  bass.AP(BOUNCE.tensor, 0, [[384, 1], [1, 384]]), qs)
        pl.wait("vector", qs)
        qm = q["SQ2"][0:1, 388:389]
        km_ = q["SQ2"][0:1, 389:390]
        mm_ = q["SQ2"][0:1, 390:391]
        pl.op("vector", lambda e, q=q, o=qm: e.tensor_reduce(
            o, bass.AP(q["SQ2"][:].tensor, 4, [[512, 1], [3, 128]]), AX.X, ALU.max))
        pl.op("vector", lambda e, q=q, o=km_: e.tensor_reduce(
            o, bass.AP(q["SQ2"][:].tensor, 5, [[512, 1], [3, 128]]), AX.X, ALU.max))
        pl.op("vector", lambda e, q=q, o=mm_: e.tensor_reduce(
            o, bass.AP(q["SQ2"][:].tensor, 6, [[512, 1], [3, 128]]), AX.X, ALU.max))
        pl.op("vector", lambda e: e.drain())
        pl.op("vector", lambda e, qm=qm, km_=km_: e.tensor_tensor(qm, qm, km_, ALU.mult))
        pl.op("vector", lambda e: e.drain())
        pl.op("vector", lambda e, qm=qm: e.tensor_scalar(qm, qm, float(8.0 * LRSCALE), 3.0, ALU.mult, ALU.add))
        pl.op("vector", lambda e: e.drain())
        pl.op("vector", lambda e, qm=qm, mm_=mm_: e.tensor_tensor(qm, qm, mm_, ALU.add), sve)
        pl.wait("sync", sve)
        pl.dma("sync", bass.AP(BOUNCE.tensor, 0, [[1, 1], [1, 1]]), qm, qs)
        pl.wait("sync", qs)
        pl.dma("sync", q["BND"][:, 1:2], bass.AP(BOUNCE.tensor, 0, [[0, 128], [1, 1]]), qs)
        b_ready = qs.n
        pl.wait("vector", qs, b_ready)
        pl.op("vector", lambda e, q=q: e.tensor_scalar(
            q["KMB"][:], q["KMT"][:], q["BND"][:, 1:2], None, ALU.subtract), sve)

    def emit_l1_attn(s, q):
        for hp in range(NPAIR):
            pl.wait("gpsimd", ste)
            pl.dma("gpsimd", q["VWS"][:],
                   bass.AP(VW.tensor, 128 * hp, [[D, 128], [D * 128, 8], [1, 128]]), qg)
            pl.dma("gpsimd", q["VBR"][:], bass.AP(VB.tensor, 128 * hp, [[128, 1], [1, 128]]), qg)
            vw_ready = qg.n
            pl.wait("vector", ste)
            vt_ones = bass.AP(q["VT1"][:].tensor, 64, [[16 * 130, 128], [130, 16], [65, 2], [1, 1]])
            pl.op("vector", lambda e, a=vt_ones: e.memset(a, 1.0), sve)
            pl.wait("tensor", qg, vw_ready)
            pl.wait("tensor", sve)
            for lt in range(TPS):
                for fc in range(8):
                    pl.op("tensor", lambda e, fc=fc, lt=lt, q=q: e.matmul(
                        bank(5)[:, 0:128], ZT[:, 2048 * fc + 128 * lt:2048 * fc + 128 * (lt + 1)],
                        q["VWS"][:, 128 * fc:128 * (fc + 1)], start=(fc == 0), stop=False))
                pl.op("tensor", lambda e, q=q: e.matmul(
                    bank(5)[:, 0:128], ONESB[0:1, 0:128], q["VBR"][0:1, :],
                    start=False, stop=True), ste)
                vmm = ste.n
                pl.wait("vector", ste, vmm)
                vo = bass.AP(q["VT1"][:].tensor, 130 * lt, [[16 * 130, 128], [65, 2], [1, 64]])
                vi = bass.AP(PS[:].tensor, 512 * 5, [[4096, 128], [64, 2], [1, 64]])
                pl.op("vector", lambda e, o=vo, i=vi: e.tensor_copy(o, i), sve)
                pl.wait("tensor", sve)
            for hh in range(2):
                h = 2 * hp + hh
                bh = 16 * s + h
                c = h // 4
                rowblk = 32 * (h % 4)
                toep_sem = pl.sem("toepq")
                if hh == 0:
                    pl.wait("scalar", qs)
                pl.wait("sync", sve, q["_toep_free"][0])
                half_cnt = []
                for half in range(2):
                    pl.dma("sync",
                           bass.AP(q["TOEPM"][:].tensor, 4096 * half,
                                   [[8192, 128], [512, 8], [1, 512]]),
                           bass.AP(STAIR.tensor, bh * STSTRIDE + 1663 + 1024 * half,
                                   [[4094, 128], [128, 8], [1, 512]]), toep_sem)
                    half_cnt.append(toep_sem.n)
                exp_cnt = []
                fj = 0
                pl.wait("vector", sse)
                for qc in range(4):
                    njt = 4 * (qc + 1)
                    pl.wait("tensor", sse)
                    if qc == 0:
                        pl.wait("vector", toep_sem, half_cnt[0])
                    elif qc == 2:
                        pl.wait("vector", toep_sem, half_cnt[1])
                    for jt in range(njt):
                        gb = jt % 4
                        tb = fj % 2
                        mi = 4 * qc - jt + 3
                        pl.wait("tensor", sve)
                        pl.op("tensor", lambda e, gb=gb, jt=jt, qc=qc, q=q, c=c, rowblk=rowblk: e.matmul(
                            bank(gb),
                            q["AUGK"][rowblk:rowblk + 8, 2048 * c + 128 * jt:2048 * c + 128 * (jt + 1)],
                            q["AUGQ"][rowblk:rowblk + 8, 2048 * c + 512 * qc:2048 * c + 512 * (qc + 1)],
                            start=True, stop=True, tile_position=(rowblk, 0)), ste)
                        sc_done = ste.n
                        pl.wait("vector", ste, sc_done)
                        if fj >= 2:
                            pl.wait("vector", sse, exp_cnt[fj - 2])
                        pl.op("vector", lambda e, gb=gb, tb=tb, mi=mi, q=q: e.scalar_tensor_tensor(
                            q["EXPIN"][:, 512 * tb:512 * (tb + 1)], bank(gb), 1.0,
                            q["TOEPM"][:, 512 * mi:512 * (mi + 1)], ALU.mult, ALU.add), sve)
                        add_done = sve.n
                        pl.wait("scalar", sve, add_done)
                        ab = q["ATT1"][:, 512 * (jt % 4):512 * (jt % 4 + 1)]
                        pl.op("scalar", lambda e, tb=tb, ab=ab, q=q, jt=jt, h=h: e.activation(
                            ab, q["EXPIN"][:, 512 * tb:512 * (tb + 1)], AF.Exp,
                            bias=q["KMB"][:, 16 * jt + h:16 * jt + h + 1]), sse)
                        exp_cnt.append(sse.n)
                        pl.wait("tensor", sse, exp_cnt[-1])
                        pl.op("tensor", lambda e, jt=jt, hh=hh, ab=ab, q=q, njt=njt: e.matmul(
                            bank(4)[0:65, :],
                            q["VT1"][:, 130 * jt + 65 * hh:130 * jt + 65 * hh + 65],
                            ab, start=(jt == 0), stop=(jt == njt - 1)),
                            ste if jt == njt - 1 else None)
                        fj += 1
                    ctx_done = ste.n
                    pl.wait("scalar", ste, ctx_done)
                    if hh == 0:
                        pl.op("scalar", lambda e, hp=hp, qc=qc, q=q: e.copy(
                            q["CTX1"][0:64, 2048 * hp + 512 * qc:2048 * hp + 512 * (qc + 1)],
                            bank(4)[0:64, :]))
                    else:
                        pl.op("scalar", lambda e, q=q: e.copy(q["CTXT"][0:64, :], bank(4)[0:64, :]), sse)
                        pl.wait("gpsimd", sse)
                        pl.dma("gpsimd", q["CTX1"][64:128, 2048 * hp + 512 * qc:2048 * hp + 512 * (qc + 1)],
                               q["CTXT"][0:64, :], qg)
                    pl.op("scalar", lambda e, qc=qc, hh=hh, q=q: e.copy(
                        q["SRO"][64 + 32 * hh:65 + 32 * hh, 512 * qc:512 * (qc + 1)],
                        bank(4)[64:65, :]), sse)
                q["_toep_free"][0] = sve.n
            pl.wait("scalar", sse)
            for hh in range(2):
                r0 = 64 + 32 * hh
                pl.op("scalar", lambda e, q=q, r0=r0: e.activation(
                    q["SRO"][r0:r0 + 1, :], q["SRO"][r0:r0 + 1, :], AF.Ln))
                pl.op("scalar", lambda e, q=q, r0=r0: e.activation(
                    q["SRO"][r0:r0 + 1, :], q["SRO"][r0:r0 + 1, :], AF.Exp, scale=-1.0),
                    sse if hh == 1 else None)
            pl.wait("sync", sse)
            for hh in range(2):
                pl.dma("sync", bass.AP(BOUNCE.tensor, 2048 * hh, [[2048, 1], [1, 2048]]),
                       q["SRO"][64 + 32 * hh:65 + 32 * hh, :], qs)
            pl.wait("sync", qs)
            for hh in range(2):
                pl.dma("sync", q["RCP1"][64 * hh:64 * hh + 64, :],
                       bass.AP(BOUNCE.tensor, 2048 * hh, [[0, 64], [1, 2048]]), qs)
            pl.wait("vector", qs)
            pl.wait("vector", qg)
            pl.op("vector", lambda e, hp=hp, q=q: e.tensor_tensor(
                q["CTX1"][:, 2048 * hp:2048 * (hp + 1)], q["CTX1"][:, 2048 * hp:2048 * (hp + 1)],
                q["RCP1"][:], ALU.mult), sve)

    def emit_l1_outproj(s, q2):
        OWS, OBR, R1Z = q2["OWS"], q2["OBR"], q2["R1Z"]
        CTX1 = q2["CTX1"]
        pl.dma("gpsimd", OWS[:], bass.AP(OW.tensor, 0, [[D, 128], [D * 128, 8], [1, 1024]]), qg)
        pl.dma("gpsimd", OBR[:], bass.AP(OB.tensor, 0, [[1024, 1], [1, 1024]]), qg)
        ow_ready = qg.n
        pl.wait("tensor", qg, ow_ready)
        pl.wait("tensor", sve)
        store_sem = pl.sem("l1store")
        st_base = store_sem.n
        for lt in range(TPS):
            tt = s * TPS + lt
            rbuf = ORT[:, 1024 * (lt % 2):1024 * (lt % 2) + 1024]
            obuf = ORT2[:, 1024 * (lt % 2):1024 * (lt % 2) + 1024]
            if lt >= 2:
                pl.wait("sync", store_sem, st_base + 16 * (lt - 1))
            pl.dma("sync", rbuf, resh_tile(RESH, tt), qs)
            rq = qs.n
            pl.wait("vector", qs, rq)
            pl.wait("vector", ste)
            pl.op("vector", lambda e, b=rbuf, r=RSTD[:, tt:tt + 1], nb=NEGB[:, tt:tt + 1], R1Z=R1Z:
                  e.tensor_scalar(R1Z[:], b, r, nb, ALU.mult, ALU.add), sve)
            z3_done = sve.n
            for oc in range(2):
                bk = 5 + oc
                pl.wait("tensor", sve)
                for hp2 in range(8):
                    pl.op("tensor", lambda e, hp2=hp2, lt=lt, oc=oc, bk=bk, CTX1=CTX1: e.matmul(
                        bank(bk),
                        CTX1[:, 2048 * hp2 + 128 * lt:2048 * hp2 + 128 * (lt + 1)],
                        OWS[:, 1024 * hp2 + 512 * oc:1024 * hp2 + 512 * (oc + 1)],
                        start=(hp2 == 0), stop=False))
                pl.op("tensor", lambda e, oc=oc, bk=bk, OBR=OBR: e.matmul(
                    bank(bk), ONESB[0:1, 0:128], OBR[0:1, 512 * oc:512 * (oc + 1)],
                    start=False, stop=True), ste)
                mm_done = ste.n
                pl.wait("vector", ste, mm_done)
                pl.op("vector", lambda e, bk=bk, oc=oc, o=obuf, R1Z=R1Z: e.scalar_tensor_tensor(
                    o[:, 512 * oc:512 * (oc + 1)], bank(bk), 1.0,
                    R1Z[:, 512 * oc:512 * (oc + 1)], ALU.mult, ALU.add), sve)
            pl.wait("sync", sve)
            pl.dma("sync", resh_tile(R1H, tt), obuf, store_sem)
        pl.wait("sync", store_sem)

    def emit_l1_inner():
        ies = contextlib.ExitStack()

        def sbi(n, s, d):
            return ies.enter_context(nc.sbuf_tensor(n, s, d))
        W1S = sbi("iw1", [128, 8 * 2048], BF16)
        W2S = sbi("iw2", [128, 16 * 1024], BF16)
        G = sbi("ig", [128, 32 * 512], BF16)
        HBF = sbi("ihbf", [128, 4 * 1024], BF16)
        HT = sbi("iht", [128, 8 * 512], BF16)
        R2T = sbi("ir2", [128, 1024], F32)
        T2F = sbi("it2", [128, 1024], F32)
        G1R = sbi("ig1r", [128, 1024], BF16)
        B1R = sbi("ib1r", [128, 1024], BF16)
        G2R = sbi("ig2r", [128, 1024], BF16)
        B2R = sbi("ib2r", [128, 1024], BF16)
        FB1C = sbi("ifb1c", [128, 32], F32)
        FB2R = sbi("ifb2r", [1, 1024], BF16)
        for T_, SRCW in ((G1R, LN1G), (B1R, LN1B), (G2R, LN2G), (B2R, LN2B)):
            pl.dma("gpsimd", T_[:], bass.AP(SRCW.tensor, 0, [[0, 128], [1, 1024]]), qg)
        pl.dma_nc("gpsimd", FB1C[:], bass.AP(FB1.tensor, 0, [[1, 128], [128, 32]]), qg)
        pl.dma("gpsimd", FB2R[:], bass.AP(FB2.tensor, 0, [[1024, 1], [1, 1024]]), qg)
        const_ready = qg.n
        pl.wait("vector", qg, const_ready)
        pl.wait("tensor", qg, const_ready)
        w1_sem = pl.sem("iw1s")
        w2_sem = pl.sem("iw2s")
        store_sem = pl.sem("istore")
        for g in range(8):
            for ti in range(4):
                tt = 4 * g + ti
                rbuf = ORT[:, 1024 * (ti % 2):1024 * (ti % 2) + 1024]
                pl.wait("sync", sve)
                pl.wait("sync", ste)
                pl.dma("sync", rbuf, resh_tile(R1H, tt), qs)
                pl.wait("scalar", qs)
                pl.wait("scalar", sve)
                stats_tile(rbuf, TINY[:, 4:5], TINY[:, 5:6])
                pl.wait("vector", sse)
                hb = HBF[:, 1024 * ti:1024 * (ti + 1)]
                pl.op("vector", lambda e, b=rbuf: e.tensor_scalar(
                    T2F[:], b, TINY[:, 4:5], TINY[:, 5:6], ALU.mult, ALU.add))
                pl.op("vector", lambda e: e.drain())
                pl.op("vector", lambda e, hb=hb: e.tensor_tensor(hb, T2F[:], G1R[:], ALU.mult))
                pl.op("vector", lambda e: e.drain())
                pl.op("vector", lambda e, hb=hb: e.tensor_tensor(hb, hb, B1R[:], ALU.add), sve)
                h_done = sve.n
                pl.wait("tensor", sve, h_done)
                for fc in range(8):
                    pl.op("tensor", lambda e, fc=fc, ti=ti, hb=hb: e.transpose(
                        bankb(7)[:, 128 * fc:128 * (fc + 1)], hb[:, 128 * fc:128 * (fc + 1)],
                        IDN[:]), ste if fc == 7 else None)
                pl.wait("vector", ste)
                pbv = bankb(7).rearrange("p (a b) -> p a b", a=8)
                ht_out = bass.AP(HT[:].tensor, 128 * ti, [[8 * 512, 128], [512, 8], [1, 128]])
                pl.op("vector", lambda e, o=ht_out, i=pbv: e.tensor_copy(o, i), sve)
                pl.wait("tensor", sve)
            pl.wait("tensor", sve)
            for half in range(2):
                pl.wait("gpsimd", ste)
                pl.dma("gpsimd", W1S[:],
                       bass.AP(FW1.tensor, 2048 * half, [[4096, 128], [4096 * 128, 8], [1, 2048]]),
                       w1_sem)
                pl.wait("tensor", w1_sem)
                for ohh in range(16):
                    oc = 16 * half + ohh
                    bk = ohh % 2
                    pl.wait("tensor", sse)
                    for fc in range(8):
                        pl.op("tensor", lambda e, fc=fc, ohh=ohh, bk=bk: e.matmul(
                            bank(bk), W1S[:, 2048 * fc + 128 * ohh:2048 * fc + 128 * (ohh + 1)],
                            HT[:, 512 * fc:512 * (fc + 1)],
                            start=(fc == 0), stop=(fc == 7)), ste if fc == 7 else None)
                    mm_done = ste.n
                    pl.wait("scalar", ste, mm_done)
                    pl.op("scalar", lambda e, oc=oc, bk=bk: e.activation(
                        G[:, 512 * oc:512 * (oc + 1)], bank(bk), AF.Gelu,
                        bias=FB1C[:, oc:oc + 1]), sse)
            pl.wait("tensor", sse)
            for h2 in range(2):
                pl.wait("gpsimd", ste)
                pl.dma("gpsimd", W2S[:],
                       bass.AP(FW2.tensor, 16 * 128 * 1024 * h2,
                               [[1024, 128], [1024 * 128, 16], [1, 1024]]), w2_sem)
                pl.wait("tensor", w2_sem)
                for ti in range(4):
                    for dc in range(2):
                        bk = 2 * ti + dc
                        for occ in range(16):
                            last = (ti, dc, occ) == (3, 1, 15)
                            pl.op("tensor", lambda e, occ=occ, ti=ti, dc=dc, bk=bk, h2=h2: e.matmul(
                                bank(bk),
                                G[:, 512 * (16 * h2 + occ) + 128 * ti:512 * (16 * h2 + occ) + 128 * (ti + 1)],
                                W2S[:, 1024 * occ + 512 * dc:1024 * occ + 512 * (dc + 1)],
                                start=(h2 == 0 and occ == 0), stop=False),
                                ste if last else None)
                if h2 == 1:
                    for ti in range(4):
                        for dc in range(2):
                            bk = 2 * ti + dc
                            pl.op("tensor", lambda e, bk=bk, dc=dc: e.matmul(
                                bank(bk), ONESB[0:1, 0:128], FB2R[0:1, 512 * dc:512 * (dc + 1)],
                                start=False, stop=True), ste if bk == 7 else None)
            f2_done = ste.n
            dbg_g0 = (debug_stop == "l1h" and g == 0)
            pl.wait("vector", ste, f2_done)
            pl.wait("vector", store_sem)
            for ti in range(4):
                tt = 4 * g + ti
                rbuf = ORT[:, 1024 * (ti % 2):1024 * (ti % 2) + 1024]
                obuf = ORT2[:, 1024 * (ti % 2):1024 * (ti % 2) + 1024]
                if ti >= 2:
                    pl.wait("sync", store_sem, store_sem.n - 16)
                pl.dma("sync", rbuf, resh_tile(RESH, tt), qs)
                rq = qs.n
                for dc in range(2):
                    pl.op("vector", lambda e, ti=ti, dc=dc: e.scalar_tensor_tensor(
                        R2T[:, 512 * dc:512 * (dc + 1)], bank(2 * ti + dc), 1.0,
                        HBF[:, 1024 * ti + 512 * dc:1024 * ti + 512 * (dc + 1)],
                        ALU.mult, ALU.add), sve if dc == 1 else None)
                pl.wait("scalar", sve)
                stats_tile(R2T[:], TINY[:, 6:7], TINY[:, 7:8])
                pl.wait("vector", sse)
                pl.op("vector", lambda e: e.tensor_scalar(
                    T2F[:], R2T[:], TINY[:, 6:7], TINY[:, 7:8], ALU.mult, ALU.add))
                pl.op("vector", lambda e: e.drain())
                pl.op("vector", lambda e: e.tensor_tensor(T2F[:], T2F[:], G2R[:], ALU.mult))
                pl.op("vector", lambda e: e.drain())
                pl.op("vector", lambda e: e.tensor_tensor(T2F[:], T2F[:], B2R[:], ALU.add))
                pl.wait("vector", qs, rq)
                pl.op("vector", lambda e, o=obuf, rb=rbuf: e.tensor_tensor(o, T2F[:], rb, ALU.add), sve)
                pl.wait("sync", sve)
                pl.dma("sync", resh_tile(RESH, tt), obuf, store_sem)
                if dbg_g0 and ti == 3:
                    pl.wait("sync", store_sem)
                    pl.barrier()
                    for tx in range(4):
                        dbg(bass.AP(RESH.tensor, tx * 128 * 1024, [[1024, 128], [1, 1024]]), 128 * 1024)
                    pl.wait("sync", sout)
                    pl.wait("gpsimd", sout)
                    ies.close()
                    return "dbg"
        pl.wait("sync", store_sem)
        pl.barrier()
        ies.close()

    W1QH = nc.dram_tensor("w1qh", [4 * D * D + 8], BF16, kind="Internal").ap()
    W2QH = nc.dram_tensor("w2qh", [4 * D * D + 8], BF16, kind="Internal").ap()

    def emit_l1_ffn():
        fes = contextlib.ExitStack()

        def sbq(n, s, d):
            return fes.enter_context(nc.sbuf_tensor(n, s, d))
        WB = sbq("qwb", [128, 2048], F32)
        QI2 = sbq("qqi", [128, 1024], I32)
        QD2 = sbq("qqd", [128, 1024], F32)
        QG2 = sbq("qqg", [128, 1024], F32)
        XQ2 = sbq("qxq", [128, 1024], BF16)
        GC = sbq("qgc", [128, 2], F32)
        RG = sbq("qrg", [128, 2], F32)
        GSR = sbq("qgsr", [1, 272], F32)
        QB1C = sbq("qb1c", [128, 32], F32)
        NB2R = sbq("qnb2", [1, 1024], BF16)
        B2F = sbq("qb2f", [1, 1024], F32)
        pl.op("vector", lambda e: e.memset(GC[:], 0.0), sve)
        gsem = pl.sem("qgsem")
        for wi, W_ in enumerate((QW1, QW2)):
            for c in range(16):
                if c > 0 or wi > 0:
                    pl.wait("sync", sve)
                pl.dma("sync", WB[:],
                       bass.AP(W_.tensor, 128 * 2048 * c, [[2048, 128], [1, 2048]]), gsem)
                pl.wait("vector", gsem)
                pl.op("vector", lambda e: e.tensor_reduce(
                    TINY[:, 0:1], WB[:], AX.X, ALU.add, apply_absolute_value=True))
                pl.op("vector", lambda e: e.drain())
                pl.op("vector", lambda e, wi=wi: e.tensor_tensor(
                    GC[:, wi:wi + 1], GC[:, wi:wi + 1], TINY[:, 0:1], ALU.add), sve)
                pl.op("vector", lambda e: e.drain())
        pl.wait("sync", sve)
        pl.dma("sync", bass.AP(BOUNCE.tensor, 0, [[2, 128], [1, 2]]), GC[:], qs)
        pl.wait("sync", qs)
        pl.dma("sync", GSR[0:1, 0:256], bass.AP(BOUNCE.tensor, 0, [[256, 1], [1, 256]]), qs)
        pl.wait("vector", qs)
        for wi in range(2):
            pl.op("vector", lambda e, wi=wi: e.tensor_reduce(
                GSR[0:1, 256 + wi:257 + wi],
                bass.AP(GSR[:].tensor, wi, [[272, 1], [2, 128]]), AX.X, ALU.add))
            pl.op("vector", lambda e: e.drain())
        NW = float(4 * D * D)
        pl.op("vector", lambda e: e.tensor_scalar(
            GSR[0:1, 260:262], GSR[0:1, 256:258], 1.0 / NW, 1e-5, ALU.mult, ALU.add))
        pl.op("vector", lambda e: e.drain())
        pl.op("vector", lambda e: e.reciprocal(GSR[0:1, 264:266], GSR[0:1, 260:262]), sve)
        pl.wait("sync", sve)
        pl.dma("sync", bass.AP(BOUNCE.tensor, 0, [[2, 1], [1, 2]]), GSR[0:1, 264:266], qs)
        pl.wait("sync", qs)
        pl.dma("sync", RG[:], bass.AP(BOUNCE.tensor, 0, [[0, 128], [1, 2]]), qs)
        rg_ready = qs.n
        pl.wait("vector", qs, rg_ready)
        qs2 = pl.sem("qffw")
        for wi, W_ in enumerate((QW1, QW2)):
            for c in range(32):
                if wi == 0:
                    src_ap = bass.AP(W_.tensor, 128 * 1024 * c, [[1024, 128], [1, 1024]])
                else:
                    dr, ob = c // 4, c % 4
                    src_ap = bass.AP(W_.tensor, 128 * 4096 * dr + 1024 * ob, [[4096, 128], [1, 1024]])
                pl.wait("sync", sve)
                pl.dma("sync", WB[:, 0:1024], src_ap, gsem)
                pl.wait("vector", gsem)
                pl.op("vector", lambda e, wi=wi: e.tensor_scalar(
                    QD2[:], WB[:, 0:1024], RG[:, wi:wi + 1], None, ALU.mult))
                pl.op("vector", lambda e: e.tensor_copy(QI2[:], QD2[:]))
                pl.op("vector", lambda e: e.scalar_tensor_tensor(
                    QG2[:], QI2[:], 1.0, QD2[:], ALU.mult, ALU.subtract))
                pl.op("vector", lambda e: e.scalar_tensor_tensor(
                    QD2[:], QG2[:], -0.5, QI2[:], ALU.is_lt, ALU.add))
                pl.op("vector", lambda e: e.drain())
                pl.op("vector", lambda e: e.scalar_tensor_tensor(
                    QG2[:], QG2[:], 0.5, QD2[:], ALU.is_gt, ALU.subtract))
                pl.op("vector", lambda e: e.drain())
                pl.op("vector", lambda e: e.tensor_scalar(
                    XQ2[:], QG2[:], -1.0, 1.0, ALU.max, ALU.min), sve)
                xq_done = sve.n
                pl.wait("tensor", sve, xq_done)
                for fs in range(8):
                    pl.op("tensor", lambda e, fs=fs: e.transpose(
                        bankb(7)[:, 128 * fs:128 * (fs + 1)], XQ2[:, 128 * fs:128 * (fs + 1)],
                        IDN[:]), ste if fs == 7 else None)
                pl.wait("vector", ste)
                pbv = bankb(7)
                pl.op("vector", lambda e, i=pbv: e.tensor_copy(XQ2[:], i), sve)
                tq_done = sve.n
                pl.wait("tensor", sve)
                pl.wait("sync", sve, tq_done)
                if wi == 0:
                    out_ap = bass.AP(W1QH.tensor, 128 * c,
                                     [[4096, 128], [4096 * 128, 8], [1, 128]])
                else:
                    dr, ob = c // 4, c % 4
                    out_ap = bass.AP(W2QH.tensor, 1024 * 1024 * ob + 128 * dr,
                                     [[1024, 128], [1024 * 128, 8], [1, 128]])
                pl.dma("sync", out_ap, XQ2[:], qs2)
        pl.wait("sync", qs2)
        pl.barrier()
        fes.close()
        mes = contextlib.ExitStack()

        def sbm(n, s, d):
            return mes.enter_context(nc.sbuf_tensor(n, s, d))
        W1QS = sbm("fw1q", [128, 8 * 2048], BF16)
        W2QS = sbm("fw2q", [128, 16 * 1024], BF16)
        HFM = sbm("fhfm", [128, 32 * 512], BF16)
        QB1Cm = sbm("fqb1", [128, 32], F32)
        NB2Rm = sbm("fnb2", [1, 1024], BF16)
        B2Fm = sbm("fb2f", [1, 1024], F32)
        pl.dma_nc("gpsimd", QB1Cm[:], bass.AP(QB1.tensor, 0, [[1, 128], [128, 32]]), qg)
        pl.dma("gpsimd", B2Fm[:], bass.AP(QB2.tensor, 0, [[1024, 1], [1, 1024]]), qg)
        cready = qg.n
        pl.wait("vector", qg, cready)
        pl.op("vector", lambda e: e.tensor_scalar(NB2Rm[:], B2Fm[:], -1.0, None, ALU.mult), sve)
        w1_sem = pl.sem("fw1s")
        w2_sem = pl.sem("fw2s")
        for s_ in range(NSEQ):
            ln_seq(s_, neg=True)
            for gg in range(4):
                tt0 = 16 * s_ + 4 * gg
                pl.wait("tensor", sve)
                for half in range(2):
                    pl.wait("gpsimd", ste)
                    pl.dma("gpsimd", W1QS[:],
                           bass.AP(W1QH.tensor, 2048 * half, [[4096, 128], [4096 * 128, 8], [1, 2048]]),
                           w1_sem)
                    pl.wait("tensor", w1_sem)
                    for ohh in range(16):
                        oc = 16 * half + ohh
                        bk = ohh % 2
                        pl.wait("tensor", sse)
                        for fc in range(8):
                            pl.op("tensor", lambda e, fc=fc, ohh=ohh, bk=bk, gg=gg: e.matmul(
                                bank(bk), W1QS[:, 2048 * fc + 128 * ohh:2048 * fc + 128 * (ohh + 1)],
                                ZT[:, 2048 * fc + 512 * gg:2048 * fc + 512 * (gg + 1)],
                                start=(fc == 0), stop=(fc == 7)), ste if fc == 7 else None)
                        mm_done = ste.n
                        pl.wait("scalar", ste, mm_done)
                        pl.op("scalar", lambda e, oc=oc, bk=bk: e.activation(
                            HFM[:, 512 * oc:512 * (oc + 1)], bank(bk), AF.Relu,
                            bias=QB1Cm[:, oc:oc + 1]), sse)
                pl.wait("tensor", sse)
                for h2 in range(2):
                    pl.wait("gpsimd", ste)
                    pl.dma("gpsimd", W2QS[:],
                           bass.AP(W2QH.tensor, 1024 * 1024 * 2 * h2,
                                   [[1024, 128], [1024 * 128, 16], [1, 1024]]), w2_sem)
                    pl.wait("tensor", w2_sem)
                    for ti in range(4):
                        for dc in range(2):
                            bk = 2 * ti + dc
                            for occ in range(16):
                                last = (ti, dc, occ) == (3, 1, 15)
                                pl.op("tensor", lambda e, occ=occ, ti=ti, dc=dc, bk=bk, h2=h2: e.matmul(
                                    bank(bk),
                                    HFM[:, 512 * (16 * h2 + occ) + 128 * ti:512 * (16 * h2 + occ) + 128 * (ti + 1)],
                                    W2QS[:, 1024 * occ + 512 * dc:1024 * occ + 512 * (dc + 1)],
                                    start=(h2 == 0 and occ == 0), stop=False),
                                    ste if last else None)
                if True:
                    for ti in range(4):
                        for dc in range(2):
                            bk = 2 * ti + dc
                            pl.op("tensor", lambda e, bk=bk, dc=dc: e.matmul(
                                bank(bk), ONESB[0:1, 0:128], NB2Rm[0:1, 512 * dc:512 * (dc + 1)],
                                start=False, stop=True), ste if bk == 7 else None)
                f2_done = ste.n
                pl.wait("vector", ste, f2_done)
                pl.wait("vector", sout)
                for ti in range(4):
                    tt = tt0 + ti
                    rbuf = ORT[:, 1024 * (ti % 2):1024 * (ti % 2) + 1024]
                    obuf = ORT2[:, 1024 * (ti % 2):1024 * (ti % 2) + 1024]
                    pl.dma("sync", rbuf, resh_tile(RESH, tt), qs)
                    rq = qs.n
                    pl.wait("vector", qs, rq)
                    for dc in range(2):
                        pl.op("vector", lambda e, ti=ti, dc=dc, o=obuf, rb=rbuf: e.scalar_tensor_tensor(
                            o[:, 512 * dc:512 * (dc + 1)], bank(2 * ti + dc), -1.0,
                            rb[:, 512 * dc:512 * (dc + 1)], ALU.mult, ALU.add),
                            sve if dc == 1 else None)
                    pl.wait("sync", sve)
                    pl.dma("sync", resh_tile(OUT, tt), obuf, sout)
        pl.wait("sync", sout)
        pl.wait("gpsimd", sout)
        pl.barrier()
        mes.close()

    dbg_off = [0]

    def dbg(ap_in, nelem, cast=False, parts=128):
        off = dbg_off[0]
        per = nelem // parts
        o = bass.AP(OUT.tensor, off, [[per, parts], [1, per]])
        pl.dma("gpsimd" if cast else "sync", o, ap_in, sout)
        dbg_off[0] += nelem

    def finish():
        pl.wait("sync", sout)
        pl.wait("gpsimd", sout)

    if debug_stop == "resh":
        pl.barrier()
        dbg(bass.AP(RESH.tensor, 0, [[8192, 128], [1, 8192]]), 128 * 8192)
        finish()
    elif debug_stop == "lnraw":
        ln_seq(0)
        pl.barrier()
        dbg(RT[:], 128 * 2048)
        dbg(TINY[:], 128 * 8)
        dbg(ZB[:], 128 * 2048, cast=True)
        finish()
    elif debug_stop == "trig":
        dbg(TRIGC, S * 512)
        dbg(TRIGS, S * 512)
        finish()
    elif debug_stop == "ln0":
        ln_seq(0)
        pl.barrier()
        dbg(ZT[:], 128 * 16384, cast=True)
        dbg(RSTD[:, 0:16], 128 * 16, cast=False)
        dbg(NEGB[:, 0:16], 128 * 16, cast=False)
        finish()
    elif debug_stop == "qkv0":
        ln_seq(0)
        pl.barrier()
        l0_attention(0, dbg_pair=0)
        pl.barrier()
        dbg(QKP[:], 128 * 4096, cast=True)
        dbg(QR[:], 128 * 2048, cast=True)
        dbg(KR[:], 128 * 2048, cast=True)
        dbg(VT[:], 128 * 2080, cast=True)
        dbg(NEGBC[:], 128 * 2)
        finish()
    elif debug_stop == "ctx0":
        ln_seq(0)
        pl.barrier()
        l0_attention(0, dbg_pair=0)
        pl.barrier()
        dbg(CTXA[:, 0:2048], 128 * 2048, cast=True)
        dbg(RCPR[:], 128 * 2048)
        finish()
    else:
        ln_seq(0)
        pl.barrier()
        l0_attention(0)
        pl.barrier()
        ln_seq(1)
        pl.barrier()
        l0_attention(1)
        pl.barrier()
        if debug_stop != "l0att":
            l0_es.close()
            emit_l0_ffn()
        if debug_stop in ("r1h", "l1att", "l1proj", "l1ctx", "l1h") or debug_stop is None:
            l1_es = contextlib.ExitStack()
            p1 = l1_alloc(l1_es)
            emit_l1_weights(p1)
            ln_seq(0, pooled=True)
            emit_pooled_finish(0)
            pl.barrier()
            ln_seq(1, pooled=True)
            emit_pooled_finish(1)
            pl.barrier()
            emit_l1_masks(p1)
            emit_l1_pooled(0, p1)
            emit_l1_kern_ck(0, p1)
            emit_l1_pooled(1, p1)
            emit_l1_kern_ck(1, p1)
            pl.barrier()
            l1_es.close()
            for s_ in range(NSEQ):
                ln_seq(s_)
                pl.barrier()
                l1a_es = contextlib.ExitStack()
                qq = l1a_alloc(l1a_es)
                emit_l1_proj(s_, qq)
                pl.barrier()
                if debug_stop == "l1proj" and s_ == 0:
                    dbg(qq["QSW"][:], 128 * 4096, cast=True)
                    dbg(qq["AUGQ"][:], 128 * 8192, cast=True)
                    dbg(ZT[:], 128 * 16384, cast=True)
                    finish()
                    l1a_es.close()
                    return _mk_finish(nc, es, pl)
                emit_l1_attn(s_, qq)
                pl.barrier()
                if debug_stop == "l1ctx" and s_ == 0:
                    dbg(qq["CTX1"][:], 128 * 16384, cast=True)
                    dbg(qq["KMT"][:], 128 * 256)
                    dbg(qq["BND"][:, 1:2], 128 * 1)
                    finish()
                    l1a_es.close()
                    return _mk_finish(nc, es, pl)
                emit_l1_outproj(s_, qq)
                pl.barrier()
                l1a_es.close()
            if debug_stop == "l1ctx":
                pass
            if debug_stop == "r1h":
                pl.dma("sync", OUT, R1H, sout)
                finish()
                return _mk_finish(nc, es, pl)
            r_ = emit_l1_inner()
            if debug_stop == "l1h":
                return _mk_finish(nc, es, pl)
            if debug_stop == "l1att":
                pl.dma("sync", OUT, RESH, sout)
                finish()
                return _mk_finish(nc, es, pl)
            emit_l1_ffn()
            finish()
            return _mk_finish(nc, es, pl)
        if debug_stop in ("ck0",):
            l1_es = contextlib.ExitStack()
            p1 = l1_alloc(l1_es)
            emit_l1_weights(p1)
            emit_l1_masks(p1)
            pl.barrier()
            ln_seq(0)
            pl.barrier()
            emit_l1_pooled(0, p1)
            pl.barrier()
            emit_l1_kern_ck(0, p1)
            pl.barrier()
            dbg(p1["CK"][0:16, 0:4096], 16 * 4096, parts=16)
            dbg(p1["WTC"][:], 128 * 4, cast=True)
            dbg(p1["KMH"][:], 128 * 8192, cast=True)
            finish()
            l1_es.close()
            return _mk_finish(nc, es, pl)
        if debug_stop == "l0att":
            l0_es.close()
        pl.dma("sync", OUT, RESH, sout)
        finish()

    return _mk_finish(nc, es, pl)


def _mk_finish(nc, es, pl):
    with es:
        block = es.enter_context(nc.Block())
        for name in ENGINES:
            def mk(name):
                def body(e):
                    for f in pl.q[name]:
                        f(e)
                return body
            getattr(block, name)(mk(name))
    return nc


def kernel(**inputs):
    nc = build(debug_stop=DEBUG_STOP)
    x = np.ascontiguousarray(np.asarray(inputs["L0_attn_x"], np.float32).reshape(16, S, D))
    shards = np.split(x, 8, axis=0)
    in_maps = []
    for c in range(8):
        m = {k: np.ascontiguousarray(np.asarray(v)) for k, v in inputs.items() if k != "L0_attn_x"}
        m["L0_attn_x"] = np.ascontiguousarray(shards[c].reshape(T, D))
        in_maps.append(m)
    res = run_bass_kernel_spmd(nc, in_maps, core_ids=list(range(8)))
    return np.concatenate([r["out"].reshape(2, S, D) for r in res.results], axis=0)


# version 03
# speedup vs baseline: 1.2262x; 1.1056x over previous
import contextlib
import numpy as np
import ml_dtypes
import concourse.bass as bass
import concourse.mybir as mybir
from concourse.bass_utils import run_bass_kernel_spmd

F32 = mybir.dt.float32
BF16 = mybir.dt.bfloat16
I32 = mybir.dt.int32
I8 = mybir.dt.int8
AF = mybir.ActivationFunctionType
ALU = mybir.AluOpType
AX = mybir.AxisListType

S, D, NSEQ = 2048, 1024, 2
T = NSEQ * S
NT = T // 128
TPS = S // 128
NH, HD, NPAIR = 16, 64, 8
LNEPS = 1e-5

TWO_PI = 2.0 * np.pi
C1 = float(np.float32(TWO_PI))
C2 = float(np.float32(TWO_PI - C1))
INV2PI = float(1.0 / TWO_PI)
LN1E4 = float(np.log(10000.0))
E_HI = float(np.float32(LN1E4 / 512.0))
E_LO = float(np.float32(LN1E4 / 512.0 - E_HI))

ENGINES = ("sync", "tensor", "vector", "scalar", "gpsimd")
DEBUG_STOP = None


class Sem:
    __slots__ = ("h", "n")

    def __init__(self, nc, name):
        self.h = nc.alloc_semaphore(name=name)
        self.n = 0


class PL:
    def __init__(self, nc):
        self.nc = nc
        self.q = {e: [] for e in ENGINES}
        self._sems = {}

    def sem(self, name):
        if name not in self._sems:
            self._sems[name] = Sem(self.nc, name)
        return self._sems[name]

    def em(self, eng, f):
        self.q[eng].append(f)

    def wait(self, eng, sem, val=None):
        v = sem.n if val is None else val
        if v > 0:
            self.em(eng, lambda e, s=sem.h, v=v: e.wait_ge(s, v))

    def dma(self, eng, out, in_, sem):
        sem.n += 16
        self.em(eng, lambda e, o=out, i=in_, s=sem.h: e.dma_start(out=o, in_=i).then_inc(s, 16))

    def dma_nc(self, eng, out, in_, sem):
        sem.n += 16
        self.em(eng, lambda e, o=out, i=in_, s=sem.h: e.dma_start(
            out=o, in_=i, allow_slow_non_contiguous=True).then_inc(s, 16))

    def op(self, eng, f, sem=None):
        if sem is None:
            self.em(eng, f)
        else:
            sem.n += 1
            self.em(eng, lambda e, f=f, s=sem.h: f(e).then_inc(s, 1))

    def barrier(self):
        bs = self.sem("bar")
        bs.n += len(ENGINES)
        tgt = bs.n
        for e in ENGINES:
            self.em(e, lambda eng, s=bs.h: eng.drain().then_inc(s, 1))
        for e in ENGINES:
            self.em(e, lambda eng, s=bs.h, v=tgt: eng.wait_ge(s, v))


def roundneg(pl, eng, sem, X, R1I, DD, OUTKN, n=None):
    pl.op(eng, lambda e: e.tensor_copy(R1I, X))
    pl.op(eng, lambda e: e.scalar_tensor_tensor(DD, R1I, 1.0, X, ALU.mult, ALU.subtract))
    pl.op(eng, lambda e: e.scalar_tensor_tensor(OUTKN, DD, -0.5, R1I, ALU.is_lt, ALU.add))
    pl.op(eng, lambda e: e.scalar_tensor_tensor(OUTKN, DD, 0.5, OUTKN, ALU.is_gt, ALU.subtract), sem)


def sin_reduce(pl, sem_v, sem_s, X, SCR_I, SCR_D, SCR_K, OUTSIN, pl_eng="vector"):
    pl.op(pl_eng, lambda e: e.tensor_scalar(SCR_K, X, INV2PI, None, ALU.mult))
    roundneg(pl, pl_eng, None, SCR_K, SCR_I, SCR_D, SCR_K)
    pl.op(pl_eng, lambda e: e.scalar_tensor_tensor(SCR_D, SCR_K, -C1, X, ALU.mult, ALU.subtract))
    pl.op(pl_eng, lambda e: e.scalar_tensor_tensor(SCR_D, SCR_K, C2, SCR_D, ALU.mult, ALU.subtract), sem_v)
    pl.wait("scalar", sem_v)
    pl.op("scalar", lambda e: e.activation(OUTSIN, SCR_D, AF.Sin), sem_s)


def build(debug_stop=None):
    nc = bass.Bass("TRN2", target_bir_lowering=False)

    def din(name, shape, dt=F32):
        return nc.dram_tensor(name, shape, dt, kind="ExternalInput").ap()

    X0 = din("L0_attn_x", [T, D])
    WQ = din("L0_attn_Wq", [D, D]); BQ = din("L0_attn_bq", [D])
    WK = din("L0_attn_Wk", [D, D]); BK = din("L0_attn_bk", [D])
    WV = din("L0_attn_Wv", [D, D]); BV = din("L0_attn_bv", [D])
    W0 = din("L0_attn_W0", [D, D]); B0 = din("L0_attn_b0", [D])
    W8 = din("L0_ffn_weight_int8", [D, D], I8)
    WS8 = din("L0_ffn_weight_scale", [D, 1])
    BF8 = din("L0_ffn_bias", [D])
    VW = din("L1_attn_v_w", [D, D]); VB = din("L1_attn_v_b", [D])
    OW = din("L1_attn_out_w", [D, D]); OB = din("L1_attn_out_b", [D])
    QMW = din("L1_attn_qmod_w", [D, NH]); QMB = din("L1_attn_qmod_b", [NH])
    KMW = din("L1_attn_kmod_w", [D, NH]); KMB = din("L1_attn_kmod_b", [NH])
    GW = din("L1_attn_gate_w", [D, 256]); GB = din("L1_attn_gate_b", [256])
    MW = din("L1_attn_mod_w", [D, 128]); MB = din("L1_attn_mod_b", [128])
    MBAS = din("L1_attn_mod_basis", [8, NH, 16])
    FRQ = din("L1_attn_freqs", [NH, 16, 8, 2])
    AMP = din("L1_attn_amps", [NH, 16, 8])
    PHS = din("L1_attn_phases", [NH, 16, 8])
    CQW = din("L1_attn_cq_w", [D, 128])
    CKW = din("L1_attn_ck_w", [D, 128])
    FW1 = din("L1_attn_ffn_w1", [D, 4 * D]); FB1 = din("L1_attn_ffn_b1", [4 * D])
    FW2 = din("L1_attn_ffn_w2", [4 * D, D]); FB2 = din("L1_attn_ffn_b2", [D])
    LN1G = din("L1_attn_ln1_g", [D]); LN1B = din("L1_attn_ln1_b", [D])
    LN2G = din("L1_attn_ln2_g", [D]); LN2B = din("L1_attn_ln2_b", [D])
    QW1 = din("L1_ffn_W1", [4 * D, D]); QB1 = din("L1_ffn_b1", [4 * D])
    QW2 = din("L1_ffn_W2", [D, 4 * D]); QB2 = din("L1_ffn_b2", [D])

    OUT = nc.dram_tensor("out", [T, D], F32, kind="ExternalOutput").ap()

    RESH = nc.dram_tensor("resh", [T, D], F32, kind="Internal").ap()
    R1H = nc.dram_tensor("r1h", [T, D], F32, kind="Internal").ap()
    TRIGC = nc.dram_tensor("trigc", [S, 512], F32, kind="Internal").ap()
    TRIGS = nc.dram_tensor("trigs", [S, 512], F32, kind="Internal").ap()
    BOUNCE = nc.dram_tensor("bounce", [8192], F32, kind="Internal").ap()
    CCI = nc.dram_tensor("cci", [128, 128], F32, kind="Internal")
    CCO = nc.dram_tensor("cco", [128, 128], F32, kind="Internal")
    STSTRIDE = 128 * 4095
    STAIR = nc.dram_tensor("stair", [32 * STSTRIDE + 8], BF16, kind="Internal").ap()
    CKH = nc.dram_tensor("ckh", [32 * 4095 + 8], BF16, kind="Internal").ap()

    es = contextlib.ExitStack()

    def sbuf(name, shape, dt):
        return es.enter_context(nc.sbuf_tensor(name, shape, dt))

    IDN = sbuf("idn", [128, 128], BF16)
    EPSC = sbuf("epsc", [128, 1], F32)
    EPS6 = sbuf("eps6", [128, 1], F32)
    IDNI = sbuf("idni", [128, 128], F32)
    ONESB = sbuf("onesb", [128, 128], BF16)
    PS = es.enter_context(nc.psum_tensor("ps", [128, 4096], F32))

    def bank(i):
        return PS[:, 512 * i:512 * (i + 1)]

    def bankslab(i, nb):
        return PS[:, 512 * i:512 * (i + nb)]

    def bankb(i):
        return PS[:, 512 * i:512 * (i + 1)].bitcast(BF16)

    pl = PL(nc)
    qs = pl.sem("qsync")
    qg = pl.sem("qgp")
    ste = pl.sem("ste")
    sve = pl.sem("sve")
    sse = pl.sem("sse")
    sgx = pl.sem("sgx")
    sout = pl.sem("sout")

    pl.dma("sync", RESH, X0, qs)
    pl.op("gpsimd", lambda e: e.iota(IDNI[:], [[1, 128]], channel_multiplier=-1,
                                     allow_small_or_imprecise_dtypes=True))
    pl.op("gpsimd", lambda e: e.tensor_scalar(IDN[:], IDNI[:], 0.0, None, ALU.is_equal))
    pl.op("gpsimd", lambda e: e.memset(ONESB[:], 1.0))
    pl.op("gpsimd", lambda e: e.memset(EPSC[:], LNEPS))
    pl.op("gpsimd", lambda e: e.memset(EPS6[:], 1e-6), sgx)

    def p0_trig():
        tes = contextlib.ExitStack()

        def tsb(name, shape, dt):
            return tes.enter_context(nc.sbuf_tensor(name, shape, dt))
        IVF = tsb("t_ivf", [1, 512], F32)
        IOT = tsb("t_iot", [1, 512], F32)
        IVR = tsb("t_ivr", [128, 2048], F32)
        PCOL = tsb("t_pc", [128, 4], F32)
        ANG = tsb("t_ang", [128, 2048], F32)
        R1I = tsb("t_r1i", [128, 2048], I32)
        DD = tsb("t_dd", [128, 2048], F32)
        KN = tsb("t_kn", [128, 2048], F32)
        SN = tsb("t_sn", [128, 2048], F32)
        pl.op("gpsimd", lambda e: e.iota(IOT[:], [[1, 512]], channel_multiplier=0,
                                         allow_small_or_imprecise_dtypes=True))
        pl.op("gpsimd", lambda e: e.tensor_scalar(IVF[:], IOT[:], -E_HI, None, ALU.mult))
        pl.op("gpsimd", lambda e: e.tensor_scalar(IOT[:], IOT[:], -E_LO, None, ALU.mult))
        pl.op("gpsimd", lambda e: e.drain())
        pl.op("gpsimd", lambda e: e.tensor_tensor(IVF[:], IVF[:], IOT[:], ALU.add), sgx)
        pl.wait("scalar", sgx)
        pl.op("scalar", lambda e: e.activation(IVF[:], IVF[:], AF.Exp), sse)
        pl.wait("sync", sse)
        pl.dma("sync", bass.AP(BOUNCE.tensor, 0, [[512, 1], [1, 512]]), IVF[:], qs)
        pl.wait("sync", qs)
        pl.dma("sync", IVR[:], bass.AP(BOUNCE.tensor, 0, [[0, 128], [0, 4], [1, 512]]), qs)
        ivr_ready = qs.n
        pl.wait("vector", qs, ivr_ready)
        pl.wait("gpsimd", qs, ivr_ready)
        for g in range(4):
            pl.wait("gpsimd", sve)
            pl.op("gpsimd", lambda e, g=g: e.iota(PCOL[:], [[128, 4]], channel_multiplier=1,
                                                  base=512 * g,
                                                  allow_small_or_imprecise_dtypes=True), sgx)
            pl.wait("vector", sgx)
            for c in range(4):
                pl.op("vector", lambda e, c=c: e.tensor_scalar(
                    ANG[:, 512 * c:512 * (c + 1)], IVR[:, 512 * c:512 * (c + 1)],
                    PCOL[:, c:c + 1], None, ALU.mult))
            sin_reduce(pl, sve, sse, ANG[:], R1I[:], DD[:], KN[:], SN[:])
            pl.wait("sync", sse)
            pl.dma("sync", bass.AP(TRIGS.tensor, 512 * 512 * g,
                                   [[512, 128], [512 * 128, 4], [1, 512]]), SN[:], qs)
            sin_done = qs.n
            pl.wait("vector", qs, sin_done)
            pl.op("vector", lambda e: e.tensor_scalar(ANG[:], ANG[:], float(np.pi / 2), None, ALU.add))
            sin_reduce(pl, sve, sse, ANG[:], R1I[:], DD[:], KN[:], SN[:])
            pl.wait("sync", sse)
            pl.dma("sync", bass.AP(TRIGC.tensor, 512 * 512 * g,
                                   [[512, 128], [512 * 128, 4], [1, 512]]), SN[:], qs)
            pl.wait("vector", qs)

        return tes

    _tes = p0_trig()
    pl.barrier()
    _tes.close()

    RT = sbuf("ln_rt", [128, 2048], F32)
    ZB = sbuf("ln_z", [128, 2048], BF16)
    TINY = sbuf("ln_tiny", [128, 8], F32)
    LNSQ = sbuf("ln_sq", [128, 1024], F32)
    RSTD = sbuf("rstd", [128, NT], F32)
    NEGB = sbuf("negb", [128, NT], F32)
    ZT = sbuf("zt", [128, 8 * 2048], BF16)

    def resh_tile(src, tt):
        return bass.AP(src.tensor, tt * 128 * D, [[D, 128], [1, D]])

    def stats_tile(buf, rs_col, nb_col):
        c0, c1v, c2v, c3v = TINY[:, 0:1], TINY[:, 1:2], TINY[:, 2:3], TINY[:, 3:4]
        pl.op("scalar", lambda e, b=buf: e.activation(LNSQ[:], b, AF.Square), sse)
        pl.wait("vector", sse)
        pl.op("vector", lambda e, b=buf, c=c0: e.tensor_reduce(c, b, AX.X, ALU.add))
        pl.op("vector", lambda e, c=c1v: e.tensor_reduce(c, LNSQ[:], AX.X, ALU.add))
        pl.op("vector", lambda e, c=c0: e.tensor_scalar(c, c, 1.0 / D, None, ALU.mult))
        pl.op("vector", lambda e: e.drain())
        pl.op("vector", lambda e, c=c0, o=c2v: e.tensor_tensor(o, c, c, ALU.mult))
        pl.op("vector", lambda e: e.drain())
        pl.op("vector", lambda e, c=c1v, m=c2v, o=c3v: e.scalar_tensor_tensor(
            o, c, 1.0 / D, m, ALU.mult, ALU.subtract), sve)
        pl.wait("scalar", sve)
        pl.op("scalar", lambda e, v=c3v: e.activation(v, v, AF.Sqrt, bias=EPSC[:, 0:1]), sse)
        pl.wait("vector", sse)
        pl.op("vector", lambda e, v=c3v, o=rs_col: e.reciprocal(o, v))
        pl.op("vector", lambda e: e.drain())
        pl.op("vector", lambda e, m=c0, r=rs_col, o=c2v: e.tensor_tensor(o, m, r, ALU.mult))
        pl.op("vector", lambda e: e.drain())
        pl.op("vector", lambda e, i=c2v, o=nb_col: e.tensor_scalar(o, i, -1.0, None, ALU.mult), sve)
        pl.op("vector", lambda e: e.drain())

    def ln_pass(tiles, src=RESH, neg=False, make_zt=True, absmax_col=None, zt_dst=None,
                pooled=False):
        apply_done = {}
        tr_done = {}
        for li, tt in enumerate(tiles):
            buf = RT[:, 1024 * (li % 2):1024 * (li % 2) + 1024]
            zb = ZB[:, 1024 * (li % 2):1024 * (li % 2) + 1024]
            if li >= 2:
                pl.wait("sync", sve, apply_done[li - 2])
            pl.dma("sync", buf, resh_tile(src, tt), qs)
            pl.wait("scalar", qs)
            c0, c1v, c2v, c3v = TINY[:, 0:1], TINY[:, 1:2], TINY[:, 2:3], TINY[:, 3:4]
            pl.wait("scalar", sve)
            if li >= 2 and make_zt:
                pl.wait("scalar", ste, tr_done[li - 2])
            pl.op("scalar", lambda e, b=buf: e.activation(LNSQ[:], b, AF.Square), sse)
            pl.wait("vector", sse)
            pl.op("vector", lambda e, b=buf, c=c0: e.tensor_reduce(c, b, AX.X, ALU.add))
            pl.op("vector", lambda e, c=c1v: e.tensor_reduce(c, LNSQ[:], AX.X, ALU.add))
            pl.op("vector", lambda e, c=c0: e.tensor_scalar(c, c, 1.0 / D, None, ALU.mult))
            pl.op("vector", lambda e: e.drain())
            pl.op("vector", lambda e, c=c0, o=c2v: e.tensor_tensor(o, c, c, ALU.mult))
            pl.op("vector", lambda e: e.drain())
            pl.op("vector", lambda e, c=c1v, m=c2v, o=c3v: e.scalar_tensor_tensor(
                o, c, 1.0 / D, m, ALU.mult, ALU.subtract), sve)
            pl.wait("scalar", sve)
            pl.op("scalar", lambda e, v=c3v: e.activation(v, v, AF.Sqrt, bias=EPSC[:, 0:1]), sse)
            pl.wait("vector", sse)
            rs_col = RSTD[:, tt:tt + 1]
            nb_col = NEGB[:, tt:tt + 1]
            pl.op("vector", lambda e, v=c3v, o=rs_col: e.reciprocal(o, v))
            pl.op("vector", lambda e: e.drain())
            pl.op("vector", lambda e, m=c0, r=rs_col, o=c2v: e.tensor_tensor(o, m, r, ALU.mult))
            pl.op("vector", lambda e: e.drain())
            pl.op("vector", lambda e, i=c2v, o=nb_col: e.tensor_scalar(o, i, -1.0, None, ALU.mult), sve)
            pl.op("vector", lambda e: e.drain())
            if make_zt:
                if not neg:
                    pl.op("vector", lambda e, b=buf, z=zb, r=rs_col, nb=nb_col:
                          e.tensor_scalar(z, b, r, nb, ALU.mult, ALU.add), sve)
                else:
                    pl.op("vector", lambda e, b=buf, z=zb, r=rs_col, nb=nb_col:
                          e.tensor_scalar(z, b, r, nb, ALU.mult, ALU.add))
                    pl.op("vector", lambda e, z=zb: e.tensor_scalar(z, z, -1.0, None, ALU.mult), sve)
            if absmax_col is not None:
                pl.op("vector", lambda e, b=buf, r=rs_col, nb=nb_col:
                      e.tensor_scalar(LNSQ[:], b, r, nb, ALU.mult, ALU.add))
                pl.op("vector", lambda e: e.drain())
                pl.op("vector", lambda e, c=c2v: e.tensor_reduce(
                    c, LNSQ[:], AX.X, ALU.max, apply_absolute_value=True))
                pl.op("vector", lambda e: e.drain())
                pl.op("vector", lambda e, c=c2v, a=absmax_col: e.tensor_tensor(a, a, c, ALU.max), sve)
                pl.op("vector", lambda e: e.drain())
            apply_done[li] = sve.n
            if make_zt:
                ZTD = zt_dst
                pl.wait("tensor", sve)
                for fc in range(8):
                    pl.op("tensor", lambda e, fc=fc, z=zb: e.transpose(
                        bankb(7)[:, 128 * fc:128 * (fc + 1)], z[:, 128 * fc:128 * (fc + 1)], IDN[:]),
                        ste if (fc == 7 and not pooled) else None)
                if pooled:
                    for half in range(2):
                        pl.op("tensor", lambda e, z=zb, half=half, li=li: e.matmul(
                            bank(3 + half)[0:1, :], ONESB[:, 0:1], z[:, 512 * half:512 * (half + 1)],
                            start=(li == 0), stop=(li == len(tiles) - 1)),
                            ste if half == 1 else None)
                tr_done[li] = ste.n
                pl.wait("vector", ste)
                pbv = bankb(7)
                ps_in = pbv.rearrange("p (a b) -> p a b", a=8)
                zt_out = bass.AP(ZTD[:].tensor, 128 * li, [[8 * 2048, 128], [2048, 8], [1, 128]])
                pl.op("vector", lambda e, o=zt_out, i=ps_in: e.tensor_copy(o, i), sve)
                pl.wait("tensor", sve)
        return apply_done

    def ln_seq(s, src=RESH, neg=False, pooled=False):
        return ln_pass(list(range(s * TPS, (s + 1) * TPS)), src=src, neg=neg, zt_dst=ZT,
                       pooled=pooled)

    PLDR = sbuf("pldr", [1, 1024], BF16)
    PLDC = sbuf("pldc", [128, 16], BF16)
    ORT = sbuf("ort", [128, 2048], F32)
    ORT2 = sbuf("ort2", [128, 2048], F32)

    l0_es = contextlib.ExitStack()

    def sbuf0(name, shape, dt):
        return l0_es.enter_context(nc.sbuf_tensor(name, shape, dt))

    WQKV = sbuf0("wqkv", [128, 8 * 384], BF16)
    BQKVR = sbuf0("bqkv", [1, 384], BF16)
    QKP = sbuf0("qkp", [128, 16 * 256], BF16)
    VT = sbuf0("vt", [128, 16 * 130], BF16)
    COSA = sbuf0("cosa", [128, 16 * 64], F32)
    SINA = sbuf0("sina", [128, 16 * 64], F32)
    TMP1 = sbuf0("tmp1", [128, 1024], F32)
    TMP2 = sbuf0("tmp2", [128, 1024], F32)
    QR = sbuf0("qr", [128, 2048], BF16)
    KR = sbuf0("kr", [128, 2048], BF16)
    SQB = sbuf0("sqb", [128, 2048], BF16)
    NEGBC = sbuf0("negbc", [128, 2], F32)
    ATTN = sbuf0("attn", [128, 2048], BF16)
    CTXA = sbuf0("ctxa", [128, 8 * 2048], BF16)
    SROW = sbuf0("srow", [128, 4096], F32)
    CTXTMP = sbuf0("ctxtmp", [128, 512], BF16)
    RCPR = sbuf0("rcpr", [128, 2048], F32)
    W0S = sbuf0("w0s", [128, 8 * 1024], BF16)
    B0R = sbuf0("b0r", [1, 1024], BF16)

    def l0_attention(s, dbg_pair=None):
        pairs = range(NPAIR) if dbg_pair is None else [dbg_pair]
        for hp in pairs:
            hA, hB = 2 * hp, 2 * hp + 1
            last_mm_prev = ste.n
            pl.wait("gpsimd", ste, last_mm_prev)
            for wi, W in enumerate((WQ, WK, WV)):
                src = bass.AP(W.tensor, 128 * hp, [[D, 128], [D * 128, 8], [1, 128]])
                dst = bass.AP(WQKV[:].tensor, 128 * wi, [[8 * 384, 128], [384, 8], [1, 128]])
                pl.dma("gpsimd", dst, src, qg)
            for bi, B in enumerate((BQ, BK, BV)):
                src = bass.AP(B.tensor, 128 * hp, [[128, 1], [1, 128]])
                pl.dma("gpsimd", BQKVR[0:1, 128 * bi:128 * (bi + 1)], src, qg)
            pl.wait("sync", sve)
            pl.dma("sync", COSA[:], bass.AP(TRIGC.tensor, 64 * hp,
                                            [[512, 128], [512 * 128, 16], [1, 64]]), qs)
            pl.dma("sync", SINA[:], bass.AP(TRIGS.tensor, 64 * hp,
                                            [[512, 128], [512 * 128, 16], [1, 64]]), qs)
            trig_ready = qs.n
            pl.wait("vector", ste, last_mm_prev)
            vt_ones = bass.AP(VT[:].tensor, 64, [[16 * 130, 128], [130, 16], [65, 2], [1, 1]])
            pl.op("vector", lambda e, a=vt_ones: e.memset(a, 1.0), sve)
            vt_ones_done = sve.n

            w_ready = qg.n
            pl.wait("tensor", qg, w_ready)
            for g in range(4):
                for i in range(4):
                    lt = 4 * g + i
                    for fc in range(8):
                        pl.op("tensor", lambda e, fc=fc, lt=lt, i=i: e.matmul(
                            PS[:, 512 * i:512 * i + 384],
                            ZT[:, 2048 * fc + 128 * lt:2048 * fc + 128 * (lt + 1)],
                            WQKV[:, 384 * fc:384 * (fc + 1)],
                            start=(fc == 0), stop=False))
                    pl.op("tensor", lambda e, i=i: e.matmul(
                        PS[:, 512 * i:512 * i + 384],
                        ONESB[0:1, 0:128], BQKVR[0:1, :],
                        start=False, stop=True), ste if i == 3 else None)
                mm_done = ste.n
                pl.wait("vector", ste, mm_done)
                if g == 0:
                    pl.wait("vector", sve, vt_ones_done)
                qk_in = bass.AP(PS[:].tensor, 0, [[4096, 128], [512, 4], [1, 256]])
                qk_out = bass.AP(QKP[:].tensor, 1024 * g, [[16 * 256, 128], [256, 4], [1, 256]])
                pl.op("vector", lambda e, o=qk_out, i=qk_in: e.tensor_copy(o, i))
                v_in = bass.AP(PS[:].tensor, 256, [[4096, 128], [512, 4], [64, 2], [1, 64]])
                v_out = bass.AP(VT[:].tensor, 130 * 4 * g, [[16 * 130, 128], [130, 4], [65, 2], [1, 64]])
                pl.op("vector", lambda e, o=v_out, i=v_in: e.tensor_copy(o, i), sve)
                pl.wait("tensor", sve)

            pl.wait("vector", qs, trig_ready)

            def qkp_ap(base):
                return bass.AP(QKP[:].tensor, base, [[16 * 256, 128], [256, 16], [64, 2], [2, 32]])

            def trig_ap(TTT):
                return bass.AP(TTT[:].tensor, 0, [[16 * 64, 128], [64, 16], [32, 2], [1, 32]])

            def rot_out(R, base):
                return bass.AP(R[:].tensor, base, [[2048, 128], [128, 16], [64, 2], [2, 32]])

            for base_qk, R in ((0, QR), (128, KR)):
                X1 = qkp_ap(base_qk)
                X2 = qkp_ap(base_qk + 1)
                CO = trig_ap(COSA)
                SI = trig_ap(SINA)
                pl.op("vector", lambda e, a=X1, b=CO: e.tensor_tensor(TMP1[:], a, b, ALU.mult))
                pl.op("vector", lambda e, a=X2, b=SI: e.tensor_tensor(TMP2[:], a, b, ALU.mult))
                pl.op("vector", lambda e, o=rot_out(R, 0): e.tensor_tensor(o, TMP1[:], TMP2[:], ALU.subtract))
                pl.op("vector", lambda e, a=X1, b=SI: e.tensor_tensor(TMP1[:], a, b, ALU.mult))
                pl.op("vector", lambda e, a=X2, b=CO: e.tensor_tensor(TMP2[:], a, b, ALU.mult))
                pl.op("vector", lambda e, o=rot_out(R, 1): e.tensor_tensor(o, TMP1[:], TMP2[:], ALU.add), sve)
            rot_done = sve.n

            pl.wait("tensor", sve, rot_done)
            for R in (QR, KR):
                for half in range(2):
                    for i in range(8):
                        lt = 8 * half + i
                        pl.op("tensor", lambda e, R=R, lt=lt, i=i: e.transpose(
                            bankb(7)[:, 128 * i:128 * (i + 1)],
                            R[:, 128 * lt:128 * (lt + 1)], IDN[:]), ste)
                    tr_done = ste.n
                    pl.wait("vector", ste, tr_done)
                    pbv = bankb(7).rearrange("p (a b) -> p a b", a=8)
                    out = bass.AP(R[:].tensor, 1024 * half, [[2048, 128], [128, 8], [1, 128]])
                    pl.op("vector", lambda e, o=out, i=pbv: e.tensor_copy(o, i), sve)
                    pl.wait("tensor", sve)
            trans_done = sve.n

            for which, R in (("k", KR), ("q", QR)):
                pl.op("vector", lambda e, R=R: e.tensor_tensor(SQB[:], R[:], R[:], ALU.mult), sve)
                sq_done = sve.n
                pl.wait("tensor", sve, sq_done)
                for h in range(2):
                    for qcn in range(4):
                        pl.op("tensor", lambda e, h=h, qcn=qcn: e.matmul(
                            bank(qcn)[0:1, :],
                            ONESB[64 * h:64 * h + 64, 0:1],
                            SQB[64 * h:64 * h + 64, 512 * qcn:512 * (qcn + 1)],
                            start=True, stop=True), ste if qcn == 3 else None)
                    nm_done = ste.n
                    pl.wait("vector", ste, nm_done)
                    ci = 4 + 2 * h + (0 if which == "k" else 1)
                    col = TINY[0:1, ci:ci + 1]
                    pl.op("vector", lambda e, c=col: e.tensor_reduce(
                        c, PS[0:1, 0:2048], AX.X, ALU.max), sve)
                    pl.wait("tensor", sve)
            pl.wait("scalar", sve)
            for h in range(2):
                mk2 = TINY[0:1, 4 + 2 * h:4 + 2 * h + 1]
                mq2 = TINY[0:1, 5 + 2 * h:5 + 2 * h + 1]
                pl.op("scalar", lambda e, a=mk2: e.activation(a, a, AF.Sqrt))
                pl.op("scalar", lambda e, a=mq2: e.activation(a, a, AF.Sqrt), sse)
            pl.wait("vector", sse)
            for h in range(2):
                mk = TINY[0:1, 4 + 2 * h:4 + 2 * h + 1]
                mq = TINY[0:1, 5 + 2 * h:5 + 2 * h + 1]
                pl.op("vector", lambda e, a=mq, b=mk: e.tensor_tensor(a, a, b, ALU.mult))
                pl.op("vector", lambda e: e.drain())
                pl.op("vector", lambda e, a=mq: e.tensor_scalar(a, a, -0.125, None, ALU.mult), sve)
            pl.op("vector", lambda e: e.drain(), sve)
            pl.wait("sync", sve)
            bounce_negb = bass.AP(BOUNCE.tensor, 4096, [[2, 1], [1, 2]])
            tiny_negb = bass.AP(TINY[:].tensor, 5, [[8, 1], [2, 2]])
            pl.em("sync", lambda e, o=bounce_negb, i=tiny_negb, s=qs.h: e.dma_start(
                out=o, in_=i, allow_slow_non_contiguous=True).then_inc(s, 16))
            qs.n += 16
            pl.wait("sync", qs)
            pl.dma("sync", NEGBC[:], bass.AP(BOUNCE.tensor, 4096, [[0, 128], [1, 2]]), qs)
            negbc_ready = qs.n

            pl.wait("scalar", qs, negbc_ready)
            for h in range(2):
                hbase = 64 * h
                for qc in range(4):
                    pl.wait("tensor", sse)
                    for grp in range(8):
                        jt0 = 2 * grp
                        gb = 2 * (grp % 2)
                        pl.wait("tensor", sse)
                        for j in range(2):
                            jt = jt0 + j
                            pl.op("tensor", lambda e, jt=jt, gb=gb, j=j, hbase=hbase, qc=qc: e.matmul(
                                bank(gb + j),
                                KR[hbase:hbase + 64, 128 * jt:128 * (jt + 1)],
                                QR[hbase:hbase + 64, 512 * qc:512 * (qc + 1)],
                                start=True, stop=True), ste)
                        sc_done = ste.n
                        pl.wait("scalar", ste, sc_done)
                        abuf = ATTN[:, 1024 * (grp % 2):1024 * (grp % 2) + 1024]
                        pl.op("scalar", lambda e, gb=gb, a=abuf, h=h: e.activation(
                            a, bankslab(gb, 2), AF.Exp, bias=NEGBC[:, h:h + 1], scale=0.125), sse)
                        exp_done = sse.n
                        pl.wait("tensor", sse, exp_done)
                        for j in range(2):
                            jt = jt0 + j
                            pl.op("tensor", lambda e, jt=jt, j=j, h=h, a=abuf: e.matmul(
                                bank(4)[0:65, :],
                                VT[:, 130 * jt + 65 * h:130 * jt + 65 * h + 65],
                                a[:, 512 * j:512 * (j + 1)],
                                start=(jt == 0), stop=(jt == 15)), ste if j == 1 else None)
                    ctx_done = ste.n
                    pl.wait("scalar", ste, ctx_done)
                    if h == 0:
                        pl.op("scalar", lambda e, hp=hp, qc=qc: e.copy(
                            CTXA[0:64, 2048 * hp + 512 * qc:2048 * hp + 512 * (qc + 1)],
                            bank(4)[0:64, :]))
                    else:
                        pl.op("scalar", lambda e: e.copy(CTXTMP[0:64, :], bank(4)[0:64, :]), sse)
                        pl.wait("gpsimd", sse)
                        pl.dma("gpsimd", CTXA[64:128, 2048 * hp + 512 * qc:2048 * hp + 512 * (qc + 1)],
                               CTXTMP[0:64, :], qg)
                    pl.op("scalar", lambda e, h=h, qc=qc: e.copy(
                        SROW[64:65, 2048 * h + 512 * qc:2048 * h + 512 * (qc + 1)],
                        bank(4)[64:65, :]), sse)
            pl.wait("gpsimd", sse)
            pl.wait("scalar", qg)
            pl.op("scalar", lambda e: e.activation(SROW[64:65, :], SROW[64:65, :], AF.Ln))
            pl.op("scalar", lambda e: e.activation(SROW[64:65, :], SROW[64:65, :], AF.Exp, scale=-1.0), sse)
            pl.wait("sync", sse)
            pl.dma("sync", bass.AP(BOUNCE.tensor, 0, [[4096, 1], [1, 4096]]), SROW[64:65, :], qs)
            pl.wait("sync", qs)
            pl.dma("sync", RCPR[0:64, :], bass.AP(BOUNCE.tensor, 0, [[0, 64], [1, 2048]]), qs)
            pl.dma("sync", RCPR[64:128, :], bass.AP(BOUNCE.tensor, 2048, [[0, 64], [1, 2048]]), qs)
            pl.wait("vector", qs)
            pl.wait("vector", qg)
            pl.op("vector", lambda e, hp=hp: e.tensor_tensor(
                CTXA[:, 2048 * hp:2048 * (hp + 1)], CTXA[:, 2048 * hp:2048 * (hp + 1)],
                RCPR[:], ALU.mult), sve)

        if dbg_pair is not None:
            return
        pl.dma("gpsimd", W0S[:], bass.AP(W0.tensor, 0, [[D, 128], [D * 128, 8], [1, 1024]]), qg)
        pl.dma("gpsimd", B0R[:], bass.AP(B0.tensor, 0, [[1024, 1], [1, 1024]]), qg)
        w0_ready = qg.n
        pl.wait("tensor", qg, w0_ready)
        pl.wait("tensor", sve)
        store_sem = pl.sem("l0store")
        store_base = store_sem.n
        for lt in range(TPS):
            tt = s * TPS + lt
            rbuf = ORT[:, 1024 * (lt % 2):1024 * (lt % 2) + 1024]
            obuf = ORT2[:, 1024 * (lt % 2):1024 * (lt % 2) + 1024]
            if lt >= 2:
                pl.wait("sync", store_sem, store_base + 16 * (lt - 1))
            pl.dma("sync", rbuf, resh_tile(RESH, tt), qs)
            for oc in range(2):
                bk = 5 + oc
                pl.wait("tensor", sve)
                for hp2 in range(8):
                    pl.op("tensor", lambda e, hp2=hp2, lt=lt, oc=oc, bk=bk: e.matmul(
                        bank(bk),
                        CTXA[:, 2048 * hp2 + 128 * lt:2048 * hp2 + 128 * (lt + 1)],
                        W0S[:, 1024 * hp2 + 512 * oc:1024 * hp2 + 512 * (oc + 1)],
                        start=(hp2 == 0), stop=False))
                pl.op("tensor", lambda e, oc=oc, bk=bk: e.matmul(
                    bank(bk), ONESB[0:1, 0:128], B0R[0:1, 512 * oc:512 * (oc + 1)],
                    start=False, stop=True), ste)
                mm_done = ste.n
                pl.wait("vector", ste, mm_done)
                pl.wait("vector", qs)
                pl.op("vector", lambda e, bk=bk, oc=oc, r=rbuf, o=obuf: e.scalar_tensor_tensor(
                    o[:, 512 * oc:512 * (oc + 1)], bank(bk), 1.0,
                    r[:, 512 * oc:512 * (oc + 1)], ALU.mult, ALU.add), sve)
            pl.wait("sync", sve)
            pl.dma("sync", resh_tile(RESH, tt), obuf, store_sem)
        pl.wait("sync", store_sem)

    def emit_l0_ffn():
        fes = contextlib.ExitStack()

        def sbf(n, s, d):
            return fes.enter_context(nc.sbuf_tensor(n, s, d))
        W8T = sbf("w8t", [128, 8 * 1024], BF16)
        W8N = sbf("w8n", [128, 8 * 1024], BF16)
        WSC = sbf("wsc", [128, 8], F32)
        AMX = sbf("amx", [128, 1], F32)
        GXS = sbf("gxs", [1, 4], F32)
        RXN = sbf("rxn", [128, 2], F32)
        BN8 = sbf("bn8", [1, 1024], BF16)
        B8F = sbf("b8f", [1, 1024], F32)
        CCS = sbf("ccs", [128, 128], F32)
        XQN = sbf("xqn", [128, 1024], BF16)
        QI = sbf("qi", [128, 1024], I32)
        QD = sbf("qd", [128, 1024], F32)
        QG = sbf("qg2", [128, 1024], F32)
        XQT = sbf("xqt", [128, 1024], BF16)

        pl.dma("gpsimd", W8N[:], bass.AP(W8.tensor, 0, [[1024, 128], [128 * 1024, 8], [1, 1024]]), qg)
        wsc_src = bass.AP(WS8.tensor, 0, [[1, 128], [128, 8]])
        pl.em("gpsimd", lambda e, o=WSC[:], i=wsc_src, s=qg.h: e.dma_start(
            out=o, in_=i, allow_slow_non_contiguous=True).then_inc(s, 16))
        qg.n += 16
        pl.dma("sync", B8F[:], bass.AP(BF8.tensor, 0, [[1024, 1], [1, 1024]]), qs)
        pl.op("gpsimd", lambda e: e.memset(CCS[:], 0.0), sgx)
        w8_ready = qg.n
        pl.wait("vector", qg, w8_ready)
        for oc in range(8):
            pl.op("vector", lambda e, oc=oc: e.tensor_scalar(
                W8N[:, 1024 * oc:1024 * (oc + 1)], W8N[:, 1024 * oc:1024 * (oc + 1)],
                WSC[:, oc:oc + 1], None, ALU.mult), sve)
        w8s_done = sve.n
        pl.wait("tensor", sve, w8s_done)
        for oc in range(8):
            for fc in range(8):
                pl.op("tensor", lambda e, oc=oc, fc=fc: e.transpose(
                    bankb(7)[:, 128 * fc:128 * (fc + 1)],
                    W8N[:, 1024 * oc + 128 * fc:1024 * oc + 128 * (fc + 1)], IDN[:]),
                    ste if fc == 7 else None)
            pl.wait("vector", ste)
            pbv = bankb(7).rearrange("p (a b) -> p a b", a=8)
            w8t_out = bass.AP(W8T[:].tensor, 128 * oc, [[8 * 1024, 128], [1024, 8], [1, 128]])
            pl.op("vector", lambda e, o=w8t_out, i=pbv: e.tensor_copy(o, i), sve)
            pl.wait("tensor", sve)

        pl.op("vector", lambda e: e.memset(AMX[:], 0.0), sve)
        ln_pass(list(range(NT)), make_zt=False, absmax_col=AMX[:, 0:1])
        pl.wait("sync", sve)
        pl.dma("sync", bass.AP(BOUNCE.tensor, 0, [[1, 128], [1, 1]]), AMX[:], qs)
        pl.wait("sync", qs)
        pl.dma("sync", CCS[0:1, 0:128], bass.AP(BOUNCE.tensor, 0, [[128, 1], [1, 128]]), qs)
        pl.wait("vector", qs)
        pl.wait("vector", sgx)
        pl.op("vector", lambda e: e.tensor_reduce(GXS[0:1, 0:1], CCS[0:1, 0:128], AX.X, ALU.max))
        pl.op("vector", lambda e: e.drain())
        pl.op("vector", lambda e: e.tensor_scalar(
            GXS[0:1, 0:1], GXS[0:1, 0:1], 1.0 / 127.0, 1e-5, ALU.mult, ALU.max))
        pl.op("vector", lambda e: e.drain())
        pl.op("vector", lambda e: e.tensor_copy(CCS[0:1, 0:1], GXS[0:1, 0:1]), sve)
        pl.wait("gpsimd", sve)
        pl.dma("gpsimd", CCI.ap(), CCS[:], qg)
        pl.wait("gpsimd", qg)
        ccs_done = pl.sem("ccdone")
        ccs_done.n += 1
        pl.em("gpsimd", lambda e, s=ccs_done.h: e.collective_compute(
            "AllReduce", ALU.max, replica_groups=[list(range(8))],
            ins=[CCI.ap().opt()], outs=[CCO.ap().opt()]).then_inc(s, 1))
        pl.wait("gpsimd", ccs_done)
        pl.dma("gpsimd", GXS[0:1, 1:2], CCO.ap()[0:1, 0:1], qg)
        gxs_ready = qg.n
        pl.wait("vector", qg, gxs_ready)
        pl.op("vector", lambda e: e.reciprocal(GXS[0:1, 2:3], GXS[0:1, 1:2]))
        pl.op("vector", lambda e: e.drain())
        pl.op("vector", lambda e: e.tensor_scalar(GXS[0:1, 3:4], GXS[0:1, 1:2], -1.0, None, ALU.mult), sve)
        pl.wait("sync", sve)
        pl.dma("sync", bass.AP(BOUNCE.tensor, 0, [[2, 1], [1, 2]]),
               GXS[0:1, 2:4], qs)
        pl.wait("sync", qs)
        pl.dma("sync", RXN[:], bass.AP(BOUNCE.tensor, 0, [[0, 128], [1, 2]]), qs)
        rxn_ready = qs.n
        pl.wait("vector", qs, rxn_ready)
        pl.op("vector", lambda e: e.tensor_scalar(BN8[:], B8F[:], GXS[0:1, 2:3], -1.0, ALU.mult, ALU.mult), sve)
        bn8_ready = sve.n

        store_sem = pl.sem("ffnst")
        st_base = store_sem.n
        for tt in range(NT):
            rbuf = ORT[:, 1024 * (tt % 2):1024 * (tt % 2) + 1024]
            obuf = ORT2[:, 1024 * (tt % 2):1024 * (tt % 2) + 1024]
            if tt >= 2:
                pl.wait("sync", store_sem, st_base + 16 * (tt - 1))
            pl.dma("sync", rbuf, resh_tile(RESH, tt), qs)
            rq = qs.n
            pl.wait("vector", qs, rq)
            rs_col = RSTD[:, tt:tt + 1]
            nb_col = NEGB[:, tt:tt + 1]
            pl.wait("vector", ste)
            pl.op("vector", lambda e, b=rbuf, r=rs_col, nb=nb_col: e.tensor_scalar(
                QG[:], b, r, nb, ALU.mult, ALU.add))
            pl.op("vector", lambda e: e.tensor_scalar(QD[:], QG[:], RXN[:, 0:1], None, ALU.mult))
            pl.op("vector", lambda e: e.tensor_copy(QI[:], QD[:]))
            pl.op("vector", lambda e: e.scalar_tensor_tensor(QG[:], QI[:], 1.0, QD[:], ALU.mult, ALU.subtract))
            pl.op("vector", lambda e: e.scalar_tensor_tensor(QD[:], QG[:], -0.5, QI[:], ALU.is_lt, ALU.add))
            pl.op("vector", lambda e: e.scalar_tensor_tensor(XQN[:], QG[:], 0.5, QD[:], ALU.is_gt, ALU.subtract), sve)
            q_done = sve.n
            pl.wait("tensor", sve, q_done)
            for fc in range(8):
                pl.op("tensor", lambda e, fc=fc: e.transpose(
                    bankb(7)[:, 128 * fc:128 * (fc + 1)], XQN[:, 128 * fc:128 * (fc + 1)], IDN[:]),
                    ste if fc == 7 else None)
            pl.wait("vector", ste)
            pbv = bankb(7).rearrange("p (a b) -> p a b", a=8)
            pl.op("vector", lambda e, i=pbv: e.tensor_copy(XQT[:], i), sve)
            xqt_done = sve.n
            pl.wait("tensor", sve, xqt_done)
            for oc in range(2):
                for fc in range(8):
                    pl.op("tensor", lambda e, oc=oc, fc=fc: e.matmul(
                        bank(oc), XQT[:, 128 * fc:128 * (fc + 1)],
                        W8T[:, 1024 * fc + 512 * oc:1024 * fc + 512 * (oc + 1)],
                        start=(fc == 0), stop=False))
                pl.op("tensor", lambda e, oc=oc: e.matmul(
                    bank(oc), ONESB[0:1, 0:128], BN8[0:1, 512 * oc:512 * (oc + 1)],
                    start=False, stop=True), ste)
                mm_done = ste.n
                pl.wait("vector", ste, mm_done)
                pl.op("vector", lambda e, oc=oc, o=obuf, rb=rbuf: e.scalar_tensor_tensor(
                    o[:, 512 * oc:512 * (oc + 1)], bank(oc), RXN[:, 1:2],
                    rb[:, 512 * oc:512 * (oc + 1)], ALU.mult, ALU.add), sve)
            pl.wait("sync", sve)
            pl.dma("sync", resh_tile(RESH, tt), obuf, store_sem)
        pl.wait("sync", store_sem)
        pl.barrier()
        fes.close()

    ALPHA1, MIX1, STEPA1 = 3.0, 0.15, 6.0
    LRSCALE = float(MIX1 / np.sqrt(8.0))

    def emit_pooled_finish(s):
        pl.wait("vector", ste)
        for half in range(2):
            pl.op("vector", lambda e, half=half: e.tensor_scalar(
                PLDR[0:1, 512 * half:512 * (half + 1)], bank(3 + half)[0:1, :],
                1.0 / S, None, ALU.mult), sve if half == 1 else None)
        pr_done = sve.n
        pl.wait("tensor", sve, pr_done)
        for fc in range(8):
            pl.op("tensor", lambda e, fc=fc: e.transpose(
                bankb(2)[:, 2 * fc:2 * fc + 1], PLDR[0:1, 128 * fc:128 * (fc + 1)],
                IDN[0:1, 0:1]), ste if fc == 7 else None)
        pl.wait("vector", ste)
        pin = bass.AP(bankb(2).tensor, bankb(2).offset, [[8192, 128], [2, 8]])
        pl.op("vector", lambda e, s=s, i=pin: e.tensor_copy(PLDC[:, 8 * s:8 * s + 8], i), sve)
        pl.wait("tensor", sve)

    def l1_alloc(stk):
        def sb1(n, s, d):
            return stk.enter_context(nc.sbuf_tensor(n, s, d))
        p = {}
        p["PFM"] = sb1("pfm", [128, 8], F32)
        p["PFMB"] = sb1("pfmb", [128, 8], BF16)
        p["GWS"] = sb1("gws", [128, 8 * 256], BF16)
        p["GBR"] = sb1("gbr", [1, 256], BF16)
        p["MWS"] = sb1("mws", [128, 8 * 128], BF16)
        p["MBC"] = sb1("mbc", [1, 128], BF16)
        p["BASB"] = sb1("basb", [128, 256], BF16)
        p["MODR"] = sb1("modr", [1, 128], BF16)
        p["MODCC"] = sb1("modcc", [128, 1], BF16)
        p["GLROW"] = sb1("glrow", [1, 512], F32)
        p["WTSROW"] = sb1("wtsrow", [1, 256], BF16)
        p["WTC"] = sb1("wtc", [128, 4], F32)
        p["HSEL"] = sb1("hsel", [128, 64], F32)
        p["W2HS"] = sb1("w2hs", [128, 32], BF16)
        p["P1R"] = sb1("p1r", [128, 4095], F32)
        p["P2R"] = sb1("p2r", [128, 4095], F32)
        p["F1C"] = sb1("f1c", [128, 1], F32)
        p["F2C"] = sb1("f2c", [128, 1], F32)
        p["PHC"] = sb1("phc", [128, 1], F32)
        p["AMPC"] = sb1("ampc", [128, 1], F32)
        p["SELF"] = sb1("self", [128, 32], F32)
        p["AT2"] = sb1("at2", [128, 2048], F32)
        p["SELAB"] = sb1("selab", [128, 128], BF16)
        p["SINM"] = sb1("sinm", [128, 4095], BF16)
        p["ARG"] = sb1("arg", [128, 2048], F32)
        p["KI"] = sb1("l1ki", [128, 2048], I32)
        p["KD"] = sb1("l1kd", [128, 2048], F32)
        p["KK"] = sb1("l1kk", [128, 2048], F32)
        p["KMH"] = sb1("kmh", [128, 8192], BF16)
        p["CK"] = sb1("l1ck", [16, 4096], F32)
        p["PKT"] = sb1("pkt", [128, 5 * 32], F32)
        p["WREP"] = sb1("wrep", [16, 4095], F32)
        p["CKB"] = p["KD"][:].bitcast(BF16)
        return p

    def emit_l1_weights(p):
        pl.dma("gpsimd", p["GWS"][:], bass.AP(GW.tensor, 0, [[256, 128], [256 * 128, 8], [1, 256]]), qg)
        pl.dma("gpsimd", p["GBR"][:], bass.AP(GB.tensor, 0, [[256, 1], [1, 256]]), qg)
        pl.dma("gpsimd", p["MWS"][:], bass.AP(MW.tensor, 0, [[128, 128], [128 * 128, 8], [1, 128]]), qg)
        pl.dma("gpsimd", p["MBC"][:], bass.AP(MB.tensor, 0, [[128, 1], [1, 128]]), qg)
        pl.op("gpsimd", lambda e, t=p["BASB"]: e.memset(t[:], 0.0), sgx)
        pl.wait("gpsimd", sgx)
        for h in range(NH):
            pl.dma("gpsimd", p["BASB"][8 * h:8 * h + 8, 16 * h:16 * (h + 1)],
                   bass.AP(MBAS.tensor, 16 * h, [[256, 8], [1, 16]]), qg)

    def emit_l1_pooled(s, p):
        pl.wait("tensor", qg)
        for fc in range(8):
            pl.op("tensor", lambda e, fc=fc, s=s, w=p["GWS"]: e.matmul(
                bank(0)[0:1, 0:256], PLDC[:, 8 * s + fc:8 * s + fc + 1],
                w[:, 256 * fc:256 * (fc + 1)],
                start=(fc == 0), stop=False))
        pl.op("tensor", lambda e, r=p["GBR"]: e.matmul(
            bank(0)[0:1, 0:256], ONESB[0:1, 0:1], r[0:1, :], start=False, stop=False))
        for fc in range(8):
            pl.op("tensor", lambda e, fc=fc, s=s, w=p["MWS"]: e.matmul(
                bank(1)[0:1, 0:128], PLDC[:, 8 * s + fc:8 * s + fc + 1],
                w[:, 128 * fc:128 * (fc + 1)],
                start=(fc == 0), stop=False))
        pl.op("tensor", lambda e, r=p["MBC"]: e.matmul(
            bank(1)[0:1, 0:128], ONESB[0:1, 0:1], r[0:1, :], start=False, stop=True), ste)
        mod_done = ste.n
        pl.wait("scalar", ste, mod_done)
        pl.op("scalar", lambda e, m=p["MODR"]: e.copy(m[0:1, :], bank(1)[0:1, 0:128]), sse)
        pl.wait("tensor", sse)
        pl.op("tensor", lambda e, m=p["MODR"]: e.transpose(bankb(2)[:, 0:1], m[0:1, :], IDN[0:1, 0:1]), ste)
        pl.wait("vector", ste)
        pl.op("vector", lambda e, m=p["MODCC"]: e.tensor_copy(m[:, 0:1], bankb(2)[:, 0:1]), sve)
        modc_done = sve.n
        pl.wait("tensor", sve, modc_done)
        pl.op("tensor", lambda e, m=p["MODCC"], bb=p["BASB"]: e.matmul(
            bank(0)[0:1, 0:256], m[:, 0:1], bb[:], start=False, stop=True), ste)
        gl_done = ste.n
        G = p["GLROW"]
        pl.wait("scalar", ste, gl_done)
        pl.op("scalar", lambda e, G=G: e.copy(G[0:1, 0:256], bank(0)[0:1, 0:256]), sse)
        pl.wait("vector", sse)
        g3 = bass.AP(G[:].tensor, 0, [[512, 1], [16, 16], [1, 16]])
        mx = bass.AP(G[:].tensor, 256, [[512, 1], [1, 16], [1, 1]])
        mxb = bass.AP(G[:].tensor, 256, [[512, 1], [1, 16], [0, 16]])
        sm = bass.AP(G[:].tensor, 272, [[512, 1], [1, 16], [1, 1]])
        smb = bass.AP(G[:].tensor, 288, [[512, 1], [1, 16], [0, 16]])
        pl.op("vector", lambda e: e.tensor_reduce(mx, g3, AX.X, ALU.max))
        pl.op("vector", lambda e: e.drain())
        pl.op("vector", lambda e: e.tensor_tensor(g3, g3, mxb, ALU.subtract), sve)
        pl.wait("scalar", sve)
        pl.op("scalar", lambda e, G=G: e.activation(G[0:1, 0:256], G[0:1, 0:256], AF.Exp), sse)
        pl.wait("vector", sse)
        pl.op("vector", lambda e: e.tensor_reduce(sm, g3, AX.X, ALU.add))
        pl.op("vector", lambda e: e.drain())
        pl.op("vector", lambda e: e.reciprocal(
            bass.AP(G[:].tensor, 288, [[512, 1], [1, 16]]),
            bass.AP(G[:].tensor, 272, [[512, 1], [1, 16]])))
        pl.op("vector", lambda e: e.drain())
        pl.op("vector", lambda e: e.tensor_tensor(g3, g3, smb, ALU.mult))
        pl.op("vector", lambda e: e.drain())
        pl.op("vector", lambda e, w=p["WTSROW"], G=G: e.tensor_copy(w[0:1, :], G[0:1, 0:256]), sve)
        wts_done = sve.n
        pl.wait("tensor", sve, wts_done)
        for half in range(2):
            pl.op("tensor", lambda e, half=half, w=p["WTSROW"]: e.transpose(
                bankb(2)[:, 2 * half:2 * half + 1],
                w[0:1, 128 * half:128 * (half + 1)], IDN[0:1, 0:1]), ste if half == 1 else None)
        pl.wait("vector", ste)
        for half in range(2):
            pl.op("vector", lambda e, p=p, s=s, half=half: e.tensor_copy(
                p["WTC"][:, 2 * s + half:2 * s + half + 1], bankb(2)[:, 2 * half:2 * half + 1]),
                sve if half == 1 else None)
        pl.wait("tensor", sve)

    def emit_l1_masks(p):
        PKT = p["PKT"]
        REL, P1, P2, WIN, SCR = (PKT[:, 0:32], PKT[:, 32:64], PKT[:, 64:96],
                                 PKT[:, 96:128], PKT[:, 128:160])
        pl.op("gpsimd", lambda e: e.iota(REL, [[1, 32]], channel_multiplier=32, base=-2047,
                                         allow_small_or_imprecise_dtypes=True), sgx)
        pl.wait("vector", sgx)
        pl.op("vector", lambda e: e.tensor_scalar(P1, REL, 1.0 / S, None, ALU.mult), sve)
        pl.wait("scalar", sgx)
        pl.op("scalar", lambda e: e.activation(SCR, REL, AF.Abs))
        pl.op("scalar", lambda e: e.activation(SCR, SCR, AF.Sqrt, bias=EPS6[:, 0:1]))
        pl.op("scalar", lambda e: e.activation(P2, REL, AF.Sign), sse)
        pl.wait("vector", sse)
        pl.op("vector", lambda e: e.scalar_tensor_tensor(
            P2, SCR, float(1.0 / np.sqrt(S)), P2, ALU.mult, ALU.mult), sve)
        p2_done = sve.n
        pl.wait("scalar", sve, p2_done)
        pl.op("scalar", lambda e: e.activation(SCR, REL, AF.Square))
        pl.op("scalar", lambda e: e.activation(WIN, SCR, AF.Exp,
                                               scale=float(-1.0 / (2.0 * (S / 4.0) ** 2))), sse)
        pl.wait("vector", sse)
        pl.op("vector", lambda e: e.tensor_scalar(WIN, WIN, ALPHA1, None, ALU.mult), sve)
        win_done = sve.n
        pl.wait("sync", sve, win_done)
        pl.dma("sync", bass.AP(BOUNCE.tensor, 0, [[32, 128], [1, 32]]), P1, qs)
        pl.wait("sync", qs)
        pl.dma("sync", p["P1R"][:], bass.AP(BOUNCE.tensor, 0, [[0, 128], [1, 4095]]), qs)
        pl.wait("sync", qs)
        pl.dma("sync", bass.AP(BOUNCE.tensor, 0, [[32, 128], [1, 32]]), P2, qs)
        pl.wait("sync", qs)
        pl.dma("sync", p["P2R"][:], bass.AP(BOUNCE.tensor, 0, [[0, 128], [1, 4095]]), qs)
        pl.wait("sync", qs)
        pl.dma("sync", bass.AP(BOUNCE.tensor, 0, [[32, 128], [1, 32]]), WIN, qs)
        pl.wait("sync", qs)
        pl.dma("sync", p["WREP"][:], bass.AP(BOUNCE.tensor, 0, [[0, 16], [1, 4095]]), qs)
        prep_done = qs.n
        for half in range(2):
            hs = p["HSEL"][:, 16 * half:16 * (half + 1)]
            pl.op("gpsimd", lambda e, hs=hs, half=half: e.iota(
                hs, [[-16, 16]], channel_multiplier=1, base=128 * half,
                allow_small_or_imprecise_dtypes=True), sgx)
        pl.wait("vector", sgx)
        for half in range(2):
            hs = p["HSEL"][:, 16 * half:16 * (half + 1)]
            m1 = p["HSEL"][:, 32 + 16 * half:48 + 16 * half]
            pl.op("vector", lambda e, hs=hs, m1=m1: e.tensor_scalar(m1, hs, 0.0, None, ALU.is_ge))
            pl.op("vector", lambda e, hs=hs: e.tensor_scalar(hs, hs, 16.0, None, ALU.is_lt))
            pl.op("vector", lambda e: e.drain())
            pl.op("vector", lambda e, hs=hs, m1=m1: e.tensor_tensor(hs, hs, m1, ALU.mult), sve)
        pl.wait("gpsimd", qs, prep_done)
        for ci in range(16):
            pl.wait("gpsimd", ste)
            pl.dma_nc("gpsimd", p["F1C"][:], bass.AP(FRQ.tensor, 256 * ci, [[2, 128], [1, 1]]), qg)
            pl.dma_nc("gpsimd", p["F2C"][:], bass.AP(FRQ.tensor, 256 * ci + 1, [[2, 128], [1, 1]]), qg)
            pl.dma("gpsimd", p["PHC"][:], bass.AP(PHS.tensor, 128 * ci, [[1, 128], [1, 1]]), qg)
            pl.dma("gpsimd", p["AMPC"][:], bass.AP(AMP.tensor, 128 * ci, [[1, 128], [1, 1]]), qg)
            cready = qg.n
            pl.wait("gpsimd", qg, cready)
            for hw in range(2):
                w = 2048 if hw == 0 else 2047
                c0_ = 2048 * hw
                A = p["ARG"][:, 0:w]
                KK_, KI_, KD_ = p["KK"][:, 0:w], p["KI"][:, 0:w], p["KD"][:, 0:w]
                pl.wait("gpsimd", sve)
                pl.op("gpsimd", lambda e, A=A, p=p, c0_=c0_, w=w: e.tensor_scalar(
                    A, p["P1R"][:, c0_:c0_ + w], p["F1C"][:, 0:1], None, ALU.mult))
                pl.op("gpsimd", lambda e, p=p, c0_=c0_, w=w: e.tensor_scalar(
                    p["AT2"][:, 0:w], p["P2R"][:, c0_:c0_ + w], p["F2C"][:, 0:1], None, ALU.mult))
                pl.op("gpsimd", lambda e: e.drain())
                pl.op("gpsimd", lambda e, A=A, p=p, w=w: e.tensor_tensor(
                    A, A, p["AT2"][:, 0:w], ALU.add))
                pl.op("gpsimd", lambda e: e.drain())
                pl.op("gpsimd", lambda e, A=A, p=p: e.tensor_scalar(
                    A, A, p["PHC"][:, 0:1], None, ALU.add), sgx)
                arg_done = sgx.n
                pl.wait("vector", sgx, arg_done)
                pl.op("vector", lambda e, A=A, KK_=KK_: e.tensor_scalar(KK_, A, INV2PI, None, ALU.mult))
                pl.op("vector", lambda e, KK_=KK_, KI_=KI_: e.tensor_copy(KI_, KK_))
                pl.op("vector", lambda e, KI_=KI_, KK_=KK_, KD_=KD_: e.scalar_tensor_tensor(
                    KD_, KI_, 1.0, KK_, ALU.mult, ALU.subtract))
                pl.op("vector", lambda e, KD_=KD_, KI_=KI_, KK_=KK_: e.scalar_tensor_tensor(
                    KK_, KD_, -0.5, KI_, ALU.is_lt, ALU.add))
                pl.op("vector", lambda e: e.drain())
                pl.op("vector", lambda e, KD_=KD_, KK_=KK_: e.scalar_tensor_tensor(
                    KK_, KD_, 0.5, KK_, ALU.is_gt, ALU.subtract))
                pl.op("vector", lambda e: e.drain())
                pl.op("vector", lambda e, KK_=KK_, A=A, KD_=KD_: e.scalar_tensor_tensor(
                    KD_, KK_, -C1, A, ALU.mult, ALU.subtract))
                pl.op("vector", lambda e: e.drain())
                pl.op("vector", lambda e, KK_=KK_, KD_=KD_: e.scalar_tensor_tensor(
                    KD_, KK_, C2, KD_, ALU.mult, ALU.subtract), sve)
                rr_done = sve.n
                pl.wait("scalar", sve, rr_done)
                if hw == 0 and ci > 0:
                    pl.wait("scalar", ste)
                pl.op("scalar", lambda e, p=p, c0_=c0_, w=w, KD_=KD_: e.activation(
                    p["SINM"][:, c0_:c0_ + w], KD_, AF.Sin), sse)
            sin_done = sse.n
            pl.op("gpsimd", lambda e, p=p: e.memset(p["SELAB"][:], 0.0))
            sel = p["SELF"][:, 0:16]
            m1 = p["SELF"][:, 16:32]
            pl.op("gpsimd", lambda e, sel=sel: e.iota(sel, [[-8, 16]], channel_multiplier=1, base=0,
                                                      allow_small_or_imprecise_dtypes=True))
            pl.op("gpsimd", lambda e: e.drain())
            pl.op("gpsimd", lambda e, sel=sel, m1=m1: e.tensor_scalar(m1, sel, 0.0, None, ALU.is_ge))
            pl.op("gpsimd", lambda e, sel=sel: e.tensor_scalar(sel, sel, 8.0, None, ALU.is_lt))
            pl.op("gpsimd", lambda e: e.drain())
            pl.op("gpsimd", lambda e, sel=sel, m1=m1: e.tensor_tensor(sel, sel, m1, ALU.mult))
            pl.op("gpsimd", lambda e: e.drain())
            pl.op("gpsimd", lambda e, sel=sel, p=p, ci=ci: e.tensor_scalar(
                p["SELAB"][:, 16 * (ci % 8):16 * (ci % 8) + 16],
                sel, p["AMPC"][:, 0:1], None, ALU.mult), sgx)
            sela_done = sgx.n
            pl.wait("tensor", sse, sin_done)
            pl.wait("tensor", sgx, sela_done)
            for pc in range(8):
                n = 512 if pc < 7 else 511
                pl.op("tensor", lambda e, pc=pc, ci=ci, n=n, p=p: e.matmul(
                    bank(pc)[:, 0:n], p["SELAB"][:],
                    p["SINM"][:, 512 * pc:512 * pc + n],
                    start=(ci % 8 == 0), stop=(ci % 8 == 7)), ste if pc == 7 else None)
            if ci % 8 == 7:
                half = ci // 8
                mmk_done = ste.n
                pl.wait("scalar", ste, mmk_done)
                for pc in range(8):
                    n = 512 if pc < 7 else 511
                    pl.op("scalar", lambda e, pc=pc, half=half, n=n, p=p: e.copy(
                        p["KMH"][:, 4096 * half + 512 * pc:4096 * half + 512 * pc + n],
                        bank(pc)[:, 0:n]), sse if pc == 7 else None)
                pl.wait("tensor", sse)

    def emit_l1_kern_ck(s, p):
        pl.wait("vector", ste)
        for half in range(2):
            pl.op("vector", lambda e, half=half, p=p, s=s: e.tensor_scalar(
                p["W2HS"][:, 16 * half:16 * (half + 1)],
                p["HSEL"][:, 16 * half:16 * (half + 1)],
                p["WTC"][:, 2 * s + half:2 * s + half + 1], None, ALU.mult), sve)
        w2_done = sve.n
        pl.wait("tensor", sve, w2_done)
        for pc in range(8):
            n = 512 if pc < 7 else 511
            for half in range(2):
                pl.op("tensor", lambda e, pc=pc, half=half, n=n, p=p: e.matmul(
                    bank(pc)[0:16, 0:n],
                    p["W2HS"][:, 16 * half:16 * (half + 1)],
                    p["KMH"][:, 4096 * half + 512 * pc:4096 * half + 512 * pc + n],
                    start=(half == 0), stop=(half == 1)), ste if half == 1 else None)
        kern_done = ste.n
        pl.wait("scalar", ste, kern_done)
        for pc in range(8):
            n = 512 if pc < 7 else 511
            pl.op("scalar", lambda e, pc=pc, n=n, p=p: e.activation(
                p["CK"][0:16, 512 * pc:512 * pc + n], bank(pc)[0:16, 0:n],
                AF.Sigmoid, scale=STEPA1), sse if pc == 7 else None)
        sig_done = sse.n
        pl.wait("vector", sse, sig_done)
        pl.op("vector", lambda e, p=p: e.tensor_tensor(
            p["CK"][0:16, 0:4095], p["CK"][0:16, 0:4095], p["WREP"][:], ALU.mult))
        pl.op("vector", lambda e, p=p: e.memset(p["CK"][0:16, 0:2047], -4e9))
        pl.op("vector", lambda e: e.drain())
        pl.op("vector", lambda e, p=p: e.tensor_copy(
            p["CKB"][0:16, 0:4095], p["CK"][0:16, 0:4095]), sve)
        ck_done = sve.n
        pl.wait("sync", sve, ck_done)
        pl.dma("sync", bass.AP(CKH.tensor, 4095 * 16 * s, [[4095, 16], [1, 4095]]),
               p["CKB"][0:16, 0:4095], qs)
        pl.wait("sync", qs)
        stair_sem = pl.sem("stair")
        pl.wait("gpsimd", qs)
        for h in range(16):
            bh = 16 * s + h
            pl.dma("gpsimd",
                   bass.AP(STAIR.tensor, bh * STSTRIDE, [[4095, 128], [1, 4095]]),
                   bass.AP(CKH.tensor, 4095 * bh, [[0, 128], [1, 4095]]), stair_sem)
        pl.wait("gpsimd", stair_sem)

    _l1a_ctr = [0]

    def l1a_alloc(stk):
        _l1a_ctr[0] += 1
        sfx = str(_l1a_ctr[0])

        def sb2(n, s, d):
            return stk.enter_context(nc.sbuf_tensor(n + sfx, s, d))
        q = {}
        q["QSW"] = sb2("qsw", [128, 8 * 512], BF16)
        q["KSW"] = sb2("ksw", [128, 8 * 512], BF16)
        q["BRK"] = sb2("brk", [1, 512], BF16)
        q["AUGQ"] = sb2("augq", [128, 4 * 2048], BF16)
        q["AUGK"] = sb2("augk", [128, 4 * 2048], BF16)
        q["KMT"] = sb2("kmt", [128, 256], F32)
        q["KMB"] = sb2("kmb", [128, 256], F32)
        q["QTMP"] = sb2("qtmp", [128, 512], BF16)
        q["KTMP"] = sb2("ktmp", [128, 512], BF16)
        q["VT1"] = sb2("vt1", [128, 16 * 130], BF16)
        q["VWS"] = sb2("vws", [128, 8 * 128], BF16)
        q["VBR"] = sb2("vbr", [1, 128], BF16)
        q["ATT1"] = sb2("att1", [128, 2048], BF16)
        q["TOEPM"] = sb2("toepm", [128, 16 * 512], BF16)
        q["EXPIN"] = sb2("expin", [128, 2048], F32)
        q["SRO"] = sb2("sro", [128, 2048], F32)
        q["RCP1"] = sb2("rcp1", [128, 2048], F32)
        q["CTX1"] = sb2("ctx1", [128, 8 * 2048], BF16)
        q["CTXT"] = sb2("ctxt1", [128, 512], BF16)
        q["SQ2"] = sb2("sq2", [128, 512], F32)
        q["BND"] = sb2("bnd", [128, 2], F32)

        q["OBR"] = sb2("obr", [1, 1024], BF16)

        q["OWS"] = q["AUGQ"]
        q["R1Z"] = q["EXPIN"][:, 0:1024]
        q["_toep_free"] = [0]
        q["_bank_last"] = {0: 0, 1: 0, 2: 0, 3: 0}
        return q

    def emit_l1_proj(s, q):
        pl.op("gpsimd", lambda e, q=q: e.memset(q["QSW"][:], 0.0))
        pl.op("gpsimd", lambda e, q=q: e.memset(q["KSW"][:], 0.0))
        pl.op("gpsimd", lambda e, q=q: e.memset(q["BRK"][:], 0.0), sgx)
        pl.wait("gpsimd", sgx)
        for fc in range(8):
            pl.dma("gpsimd",
                   bass.AP(q["QSW"][:].tensor, 512 * fc, [[8 * 512, 128], [32, 16], [1, 8]]),
                   bass.AP(CQW.tensor, 128 * 128 * fc, [[128, 128], [8, 16], [1, 8]]), qg)
            pl.dma("gpsimd",
                   bass.AP(q["KSW"][:].tensor, 512 * fc, [[8 * 512, 128], [32, 16], [1, 8]]),
                   bass.AP(CKW.tensor, 128 * 128 * fc, [[128, 128], [8, 16], [1, 8]]), qg)
            pl.dma_nc("gpsimd",
                      bass.AP(q["KSW"][:].tensor, 512 * fc + 8, [[8 * 512, 128], [32, 16], [1, 1]]),
                      bass.AP(KMW.tensor, 16 * 128 * fc, [[16, 128], [1, 16]]), qg)
        pl.dma_nc("gpsimd", bass.AP(q["BRK"][:].tensor, 8, [[512, 1], [32, 16], [1, 1]]),
                  bass.AP(KMB.tensor, 0, [[16, 1], [1, 16]]), qg)
        w_ready = qg.n
        pl.wait("tensor", qg, w_ready)
        for lt in range(TPS):
            pl.wait("tensor", sve)
            for fc in range(8):
                pl.op("tensor", lambda e, fc=fc, lt=lt, q=q: e.matmul(
                    bank(0), ZT[:, 2048 * fc + 128 * lt:2048 * fc + 128 * (lt + 1)],
                    q["QSW"][:, 512 * fc:512 * (fc + 1)], start=(fc == 0), stop=(fc == 7)))
            for fc in range(8):
                pl.op("tensor", lambda e, fc=fc, lt=lt, q=q: e.matmul(
                    bank(1), ZT[:, 2048 * fc + 128 * lt:2048 * fc + 128 * (lt + 1)],
                    q["KSW"][:, 512 * fc:512 * (fc + 1)], start=(fc == 0), stop=False))
            pl.op("tensor", lambda e, q=q: e.matmul(
                bank(1), ONESB[0:1, 0:128], q["BRK"][0:1, :], start=False, stop=True), ste)
            mm_done = ste.n
            pl.wait("vector", ste, mm_done)
            pl.op("vector", lambda e, q=q: e.tensor_scalar(q["QTMP"][:], bank(0), LRSCALE, None, ALU.mult))
            pl.op("vector", lambda e, q=q: e.tensor_copy(q["KTMP"][:], bank(1)))
            kmt_in = bass.AP(PS[:].tensor, 512 * 1 + 8, [[4096, 128], [32, 16]])
            pl.op("vector", lambda e, q=q, lt=lt, i=kmt_in: e.tensor_copy(
                q["KMT"][:, 16 * lt:16 * (lt + 1)], i), sve)
            ev_done = sve.n
            pl.wait("tensor", sve, ev_done)
            for ti, SRC in ((0, q["QTMP"]), (1, q["KTMP"])):
                for c in range(4):
                    pl.op("tensor", lambda e, SRC=SRC, c=c, ti=ti: e.transpose(
                        bankb(7)[:, 512 * ti + 128 * c:512 * ti + 128 * (c + 1)],
                        SRC[:, 128 * c:128 * (c + 1)], IDN[:]),
                        ste if (ti, c) == (1, 3) else None)
            tr_done = ste.n
            pl.wait("vector", ste, tr_done)
            pbv = bankb(7).rearrange("p (a b) -> p a b", a=8)
            for ti, DST in ((0, q["AUGQ"]), (1, q["AUGK"])):
                out = bass.AP(DST[:].tensor, 128 * lt, [[4 * 2048, 128], [2048, 4], [1, 128]])
                inp = bass.AP(pbv.tensor, pbv.offset + 512 * ti, [[8192, 128], [128, 4], [1, 128]])
                pl.op("vector", lambda e, o=out, i=inp: e.tensor_copy(o, i),
                      sve if ti == 1 else None)
            pl.wait("tensor", sve)
        pl.op("vector", lambda e, q=q: e.tensor_reduce(
            q["BND"][:, 0:1], q["KMT"][:], AX.X, ALU.max, apply_absolute_value=True))
        pl.op("vector", lambda e: e.drain())
        pl.op("vector", lambda e, q=q: e.tensor_reduce(
            q["SQ2"][:, 0:1], q["AUGQ"][:], AX.X, ALU.max, apply_absolute_value=True))
        pl.op("vector", lambda e: e.drain())
        pl.op("vector", lambda e, q=q: e.tensor_reduce(
            q["SQ2"][:, 1:2], q["AUGK"][:], AX.X, ALU.max, apply_absolute_value=True))
        pl.op("vector", lambda e: e.drain())
        pl.op("vector", lambda e, q=q: e.tensor_copy(q["SQ2"][:, 2:3], q["BND"][:, 0:1]), sve)
        pl.wait("sync", sve)
        pl.dma("sync", bass.AP(BOUNCE.tensor, 0, [[3, 128], [1, 3]]),
               q["SQ2"][:, 0:3], qs)
        pl.wait("sync", qs)
        pl.dma_nc("sync", q["SQ2"][0:1, 4:388],
                  bass.AP(BOUNCE.tensor, 0, [[384, 1], [1, 384]]), qs)
        pl.wait("vector", qs)
        qm = q["SQ2"][0:1, 388:389]
        km_ = q["SQ2"][0:1, 389:390]
        mm_ = q["SQ2"][0:1, 390:391]
        pl.op("vector", lambda e, q=q, o=qm: e.tensor_reduce(
            o, bass.AP(q["SQ2"][:].tensor, 4, [[512, 1], [3, 128]]), AX.X, ALU.max))
        pl.op("vector", lambda e, q=q, o=km_: e.tensor_reduce(
            o, bass.AP(q["SQ2"][:].tensor, 5, [[512, 1], [3, 128]]), AX.X, ALU.max))
        pl.op("vector", lambda e, q=q, o=mm_: e.tensor_reduce(
            o, bass.AP(q["SQ2"][:].tensor, 6, [[512, 1], [3, 128]]), AX.X, ALU.max))
        pl.op("vector", lambda e: e.drain())
        pl.op("vector", lambda e, qm=qm, km_=km_: e.tensor_tensor(qm, qm, km_, ALU.mult))
        pl.op("vector", lambda e: e.drain())
        pl.op("vector", lambda e, qm=qm: e.tensor_scalar(qm, qm, float(8.0 * LRSCALE), 3.0, ALU.mult, ALU.add))
        pl.op("vector", lambda e: e.drain())
        pl.op("vector", lambda e, qm=qm, mm_=mm_: e.tensor_tensor(qm, qm, mm_, ALU.add), sve)
        pl.wait("sync", sve)
        pl.dma("sync", bass.AP(BOUNCE.tensor, 0, [[1, 1], [1, 1]]), qm, qs)
        pl.wait("sync", qs)
        pl.dma("sync", q["BND"][:, 1:2], bass.AP(BOUNCE.tensor, 0, [[0, 128], [1, 1]]), qs)
        b_ready = qs.n
        pl.wait("vector", qs, b_ready)
        pl.op("vector", lambda e, q=q: e.tensor_scalar(
            q["KMB"][:], q["KMT"][:], q["BND"][:, 1:2], None, ALU.subtract), sve)

    def emit_l1_attn(s, q):
        for hp in range(NPAIR):
            pl.wait("gpsimd", ste)
            pl.dma("gpsimd", q["VWS"][:],
                   bass.AP(VW.tensor, 128 * hp, [[D, 128], [D * 128, 8], [1, 128]]), qg)
            pl.dma("gpsimd", q["VBR"][:], bass.AP(VB.tensor, 128 * hp, [[128, 1], [1, 128]]), qg)
            vw_ready = qg.n
            pl.wait("vector", ste)
            vt_ones = bass.AP(q["VT1"][:].tensor, 64, [[16 * 130, 128], [130, 16], [65, 2], [1, 1]])
            pl.op("vector", lambda e, a=vt_ones: e.memset(a, 1.0), sve)
            pl.wait("tensor", qg, vw_ready)
            pl.wait("tensor", sve)
            for lt in range(TPS):
                for fc in range(8):
                    pl.op("tensor", lambda e, fc=fc, lt=lt, q=q: e.matmul(
                        bank(5)[:, 0:128], ZT[:, 2048 * fc + 128 * lt:2048 * fc + 128 * (lt + 1)],
                        q["VWS"][:, 128 * fc:128 * (fc + 1)], start=(fc == 0), stop=False))
                pl.op("tensor", lambda e, q=q: e.matmul(
                    bank(5)[:, 0:128], ONESB[0:1, 0:128], q["VBR"][0:1, :],
                    start=False, stop=True), ste)
                vmm = ste.n
                pl.wait("vector", ste, vmm)
                vo = bass.AP(q["VT1"][:].tensor, 130 * lt, [[16 * 130, 128], [65, 2], [1, 64]])
                vi = bass.AP(PS[:].tensor, 512 * 5, [[4096, 128], [64, 2], [1, 64]])
                pl.op("vector", lambda e, o=vo, i=vi: e.tensor_copy(o, i), sve)
                pl.wait("tensor", sve)
            for hh in range(2):
                h = 2 * hp + hh
                bh = 16 * s + h
                c = h // 4
                rowblk = 32 * (h % 4)
                toep_sem = pl.sem("toepq")
                if hh == 0:
                    pl.wait("scalar", qs)
                pl.wait("sync", sve, q["_toep_free"][0])
                half_cnt = []
                for half in range(2):
                    pl.dma("sync",
                           bass.AP(q["TOEPM"][:].tensor, 4096 * half,
                                   [[8192, 128], [512, 8], [1, 512]]),
                           bass.AP(STAIR.tensor, bh * STSTRIDE + 1663 + 1024 * half,
                                   [[4094, 128], [128, 8], [1, 512]]), toep_sem)
                    half_cnt.append(toep_sem.n)
                exp_cnt = []
                fj = 0
                pl.wait("vector", sse)
                for qc in range(4):
                    njt = 4 * (qc + 1)
                    pl.wait("tensor", sse)
                    if qc == 0:
                        pl.wait("vector", toep_sem, half_cnt[0])
                    elif qc == 2:
                        pl.wait("vector", toep_sem, half_cnt[1])
                    for b4 in range(njt // 4):
                        jts = list(range(4 * b4, 4 * b4 + 4))
                        sc_done = []
                        for k, jt in enumerate(jts):
                            gb = (fj + k) % 4
                            pl.wait("tensor", sve, q["_bank_last"][gb])
                            pl.op("tensor", lambda e, gb=gb, jt=jt, qc=qc, q=q, c=c,
                                  rowblk=rowblk: e.matmul(
                                bank(gb),
                                q["AUGK"][rowblk:rowblk + 8, 2048 * c + 128 * jt:2048 * c + 128 * (jt + 1)],
                                q["AUGQ"][rowblk:rowblk + 8, 2048 * c + 512 * qc:2048 * c + 512 * (qc + 1)],
                                start=True, stop=True, tile_position=(rowblk, 0)), ste)
                            sc_done.append(ste.n)
                        for k, jt in enumerate(jts):
                            fjj = fj + k
                            gb = fjj % 4
                            mi = 4 * qc - jt + 3
                            pl.wait("vector", ste, sc_done[k])
                            if fjj >= 4:
                                pl.wait("vector", sse, exp_cnt[fjj - 4])
                            pl.op("vector", lambda e, gb=gb, mi=mi, q=q: e.scalar_tensor_tensor(
                                q["EXPIN"][:, 512 * gb:512 * (gb + 1)], bank(gb), 1.0,
                                q["TOEPM"][:, 512 * mi:512 * (mi + 1)], ALU.mult, ALU.add), sve)
                            q["_bank_last"][gb] = sve.n
                            pl.wait("scalar", sve, sve.n)
                            ab = q["ATT1"][:, 512 * gb:512 * (gb + 1)]
                            pl.op("scalar", lambda e, gb=gb, ab=ab, q=q, jt=jt, h=h: e.activation(
                                ab, q["EXPIN"][:, 512 * gb:512 * (gb + 1)], AF.Exp,
                                bias=q["KMB"][:, 16 * jt + h:16 * jt + h + 1]), sse)
                            exp_cnt.append(sse.n)
                        for k, jt in enumerate(jts):
                            gb = (fj + k) % 4
                            ab = q["ATT1"][:, 512 * gb:512 * (gb + 1)]
                            pl.wait("tensor", sse, exp_cnt[fj + k])
                            pl.op("tensor", lambda e, jt=jt, hh=hh, ab=ab, q=q, njt=njt: e.matmul(
                                bank(4)[0:65, :],
                                q["VT1"][:, 130 * jt + 65 * hh:130 * jt + 65 * hh + 65],
                                ab, start=(jt == 0), stop=(jt == njt - 1)),
                                ste if jt == njt - 1 else None)
                        fj += 4
                    ctx_done = ste.n
                    pl.wait("scalar", ste, ctx_done)
                    if hh == 0:
                        pl.op("scalar", lambda e, hp=hp, qc=qc, q=q: e.copy(
                            q["CTX1"][0:64, 2048 * hp + 512 * qc:2048 * hp + 512 * (qc + 1)],
                            bank(4)[0:64, :]))
                    else:
                        pl.op("scalar", lambda e, q=q: e.copy(q["CTXT"][0:64, :], bank(4)[0:64, :]), sse)
                        pl.wait("gpsimd", sse)
                        pl.dma("gpsimd", q["CTX1"][64:128, 2048 * hp + 512 * qc:2048 * hp + 512 * (qc + 1)],
                               q["CTXT"][0:64, :], qg)
                    pl.op("scalar", lambda e, qc=qc, hh=hh, q=q: e.copy(
                        q["SRO"][64 + 32 * hh:65 + 32 * hh, 512 * qc:512 * (qc + 1)],
                        bank(4)[64:65, :]), sse)
                q["_toep_free"][0] = sve.n
            pl.wait("scalar", sse)
            for hh in range(2):
                r0 = 64 + 32 * hh
                pl.op("scalar", lambda e, q=q, r0=r0: e.activation(
                    q["SRO"][r0:r0 + 1, :], q["SRO"][r0:r0 + 1, :], AF.Ln))
                pl.op("scalar", lambda e, q=q, r0=r0: e.activation(
                    q["SRO"][r0:r0 + 1, :], q["SRO"][r0:r0 + 1, :], AF.Exp, scale=-1.0),
                    sse if hh == 1 else None)
            pl.wait("sync", sse)
            for hh in range(2):
                pl.dma("sync", bass.AP(BOUNCE.tensor, 2048 * hh, [[2048, 1], [1, 2048]]),
                       q["SRO"][64 + 32 * hh:65 + 32 * hh, :], qs)
            pl.wait("sync", qs)
            for hh in range(2):
                pl.dma("sync", q["RCP1"][64 * hh:64 * hh + 64, :],
                       bass.AP(BOUNCE.tensor, 2048 * hh, [[0, 64], [1, 2048]]), qs)
            pl.wait("vector", qs)
            pl.wait("vector", qg)
            pl.op("vector", lambda e, hp=hp, q=q: e.tensor_tensor(
                q["CTX1"][:, 2048 * hp:2048 * (hp + 1)], q["CTX1"][:, 2048 * hp:2048 * (hp + 1)],
                q["RCP1"][:], ALU.mult), sve)

    def emit_l1_outproj(s, q2):
        OWS, OBR, R1Z = q2["OWS"], q2["OBR"], q2["R1Z"]
        CTX1 = q2["CTX1"]
        pl.dma("gpsimd", OWS[:], bass.AP(OW.tensor, 0, [[D, 128], [D * 128, 8], [1, 1024]]), qg)
        pl.dma("gpsimd", OBR[:], bass.AP(OB.tensor, 0, [[1024, 1], [1, 1024]]), qg)
        ow_ready = qg.n
        pl.wait("tensor", qg, ow_ready)
        pl.wait("tensor", sve)
        store_sem = pl.sem("l1store")
        st_base = store_sem.n
        for lt in range(TPS):
            tt = s * TPS + lt
            rbuf = ORT[:, 1024 * (lt % 2):1024 * (lt % 2) + 1024]
            obuf = ORT2[:, 1024 * (lt % 2):1024 * (lt % 2) + 1024]
            if lt >= 2:
                pl.wait("sync", store_sem, st_base + 16 * (lt - 1))
            pl.dma("sync", rbuf, resh_tile(RESH, tt), qs)
            rq = qs.n
            pl.wait("vector", qs, rq)
            pl.wait("vector", ste)
            pl.op("vector", lambda e, b=rbuf, r=RSTD[:, tt:tt + 1], nb=NEGB[:, tt:tt + 1], R1Z=R1Z:
                  e.tensor_scalar(R1Z[:], b, r, nb, ALU.mult, ALU.add), sve)
            z3_done = sve.n
            for oc in range(2):
                bk = 5 + oc
                pl.wait("tensor", sve)
                for hp2 in range(8):
                    pl.op("tensor", lambda e, hp2=hp2, lt=lt, oc=oc, bk=bk, CTX1=CTX1: e.matmul(
                        bank(bk),
                        CTX1[:, 2048 * hp2 + 128 * lt:2048 * hp2 + 128 * (lt + 1)],
                        OWS[:, 1024 * hp2 + 512 * oc:1024 * hp2 + 512 * (oc + 1)],
                        start=(hp2 == 0), stop=False))
                pl.op("tensor", lambda e, oc=oc, bk=bk, OBR=OBR: e.matmul(
                    bank(bk), ONESB[0:1, 0:128], OBR[0:1, 512 * oc:512 * (oc + 1)],
                    start=False, stop=True), ste)
                mm_done = ste.n
                pl.wait("vector", ste, mm_done)
                pl.op("vector", lambda e, bk=bk, oc=oc, o=obuf, R1Z=R1Z: e.scalar_tensor_tensor(
                    o[:, 512 * oc:512 * (oc + 1)], bank(bk), 1.0,
                    R1Z[:, 512 * oc:512 * (oc + 1)], ALU.mult, ALU.add), sve)
            pl.wait("sync", sve)
            pl.dma("sync", resh_tile(R1H, tt), obuf, store_sem)
        pl.wait("sync", store_sem)

    def emit_l1_inner():
        ies = contextlib.ExitStack()

        def sbi(n, s, d):
            return ies.enter_context(nc.sbuf_tensor(n, s, d))
        W1S = sbi("iw1", [128, 8 * 2048], BF16)
        W2S = sbi("iw2", [128, 16 * 1024], BF16)
        G = sbi("ig", [128, 32 * 512], BF16)
        HBF = sbi("ihbf", [128, 4 * 1024], BF16)
        HT = sbi("iht", [128, 8 * 512], BF16)
        R2T = sbi("ir2", [128, 1024], F32)
        T2F = sbi("it2", [128, 1024], F32)
        G1R = sbi("ig1r", [128, 1024], BF16)
        B1R = sbi("ib1r", [128, 1024], BF16)
        G2R = sbi("ig2r", [128, 1024], BF16)
        B2R = sbi("ib2r", [128, 1024], BF16)
        FB1C = sbi("ifb1c", [128, 32], F32)
        FB2R = sbi("ifb2r", [1, 1024], BF16)
        for T_, SRCW in ((G1R, LN1G), (B1R, LN1B), (G2R, LN2G), (B2R, LN2B)):
            pl.dma("gpsimd", T_[:], bass.AP(SRCW.tensor, 0, [[0, 128], [1, 1024]]), qg)
        pl.dma_nc("gpsimd", FB1C[:], bass.AP(FB1.tensor, 0, [[1, 128], [128, 32]]), qg)
        pl.dma("gpsimd", FB2R[:], bass.AP(FB2.tensor, 0, [[1024, 1], [1, 1024]]), qg)
        const_ready = qg.n
        pl.wait("vector", qg, const_ready)
        pl.wait("tensor", qg, const_ready)
        w1_sem = pl.sem("iw1s")
        w2_sem = pl.sem("iw2s")
        store_sem = pl.sem("istore")
        for g in range(8):
            for ti in range(4):
                tt = 4 * g + ti
                rbuf = ORT[:, 1024 * (ti % 2):1024 * (ti % 2) + 1024]
                pl.wait("sync", sve)
                pl.wait("sync", ste)
                pl.dma("sync", rbuf, resh_tile(R1H, tt), qs)
                pl.wait("scalar", qs)
                pl.wait("scalar", sve)
                stats_tile(rbuf, TINY[:, 4:5], TINY[:, 5:6])
                pl.wait("vector", sse)
                hb = HBF[:, 1024 * ti:1024 * (ti + 1)]
                pl.op("vector", lambda e, b=rbuf: e.tensor_scalar(
                    T2F[:], b, TINY[:, 4:5], TINY[:, 5:6], ALU.mult, ALU.add))
                pl.op("vector", lambda e: e.drain())
                pl.op("vector", lambda e, hb=hb: e.tensor_tensor(hb, T2F[:], G1R[:], ALU.mult))
                pl.op("vector", lambda e: e.drain())
                pl.op("vector", lambda e, hb=hb: e.tensor_tensor(hb, hb, B1R[:], ALU.add), sve)
                h_done = sve.n
                pl.wait("tensor", sve, h_done)
                for fc in range(8):
                    pl.op("tensor", lambda e, fc=fc, ti=ti, hb=hb: e.transpose(
                        bankb(7)[:, 128 * fc:128 * (fc + 1)], hb[:, 128 * fc:128 * (fc + 1)],
                        IDN[:]), ste if fc == 7 else None)
                pl.wait("vector", ste)
                pbv = bankb(7).rearrange("p (a b) -> p a b", a=8)
                ht_out = bass.AP(HT[:].tensor, 128 * ti, [[8 * 512, 128], [512, 8], [1, 128]])
                pl.op("vector", lambda e, o=ht_out, i=pbv: e.tensor_copy(o, i), sve)
                pl.wait("tensor", sve)
            pl.wait("tensor", sve)
            for half in range(2):
                pl.wait("gpsimd", ste)
                pl.dma("gpsimd", W1S[:],
                       bass.AP(FW1.tensor, 2048 * half, [[4096, 128], [4096 * 128, 8], [1, 2048]]),
                       w1_sem)
                pl.wait("tensor", w1_sem)
                for ohh in range(16):
                    oc = 16 * half + ohh
                    bk = ohh % 2
                    pl.wait("tensor", sse)
                    for fc in range(8):
                        pl.op("tensor", lambda e, fc=fc, ohh=ohh, bk=bk: e.matmul(
                            bank(bk), W1S[:, 2048 * fc + 128 * ohh:2048 * fc + 128 * (ohh + 1)],
                            HT[:, 512 * fc:512 * (fc + 1)],
                            start=(fc == 0), stop=(fc == 7)), ste if fc == 7 else None)
                    mm_done = ste.n
                    pl.wait("scalar", ste, mm_done)
                    pl.op("scalar", lambda e, oc=oc, bk=bk: e.activation(
                        G[:, 512 * oc:512 * (oc + 1)], bank(bk), AF.Gelu,
                        bias=FB1C[:, oc:oc + 1]), sse)
            pl.wait("tensor", sse)
            for h2 in range(2):
                pl.wait("gpsimd", ste)
                pl.dma("gpsimd", W2S[:],
                       bass.AP(FW2.tensor, 16 * 128 * 1024 * h2,
                               [[1024, 128], [1024 * 128, 16], [1, 1024]]), w2_sem)
                pl.wait("tensor", w2_sem)
                for ti in range(4):
                    for dc in range(2):
                        bk = 2 * ti + dc
                        for occ in range(16):
                            last = (ti, dc, occ) == (3, 1, 15)
                            pl.op("tensor", lambda e, occ=occ, ti=ti, dc=dc, bk=bk, h2=h2: e.matmul(
                                bank(bk),
                                G[:, 512 * (16 * h2 + occ) + 128 * ti:512 * (16 * h2 + occ) + 128 * (ti + 1)],
                                W2S[:, 1024 * occ + 512 * dc:1024 * occ + 512 * (dc + 1)],
                                start=(h2 == 0 and occ == 0), stop=False),
                                ste if last else None)
                if h2 == 1:
                    for ti in range(4):
                        for dc in range(2):
                            bk = 2 * ti + dc
                            pl.op("tensor", lambda e, bk=bk, dc=dc: e.matmul(
                                bank(bk), ONESB[0:1, 0:128], FB2R[0:1, 512 * dc:512 * (dc + 1)],
                                start=False, stop=True), ste if bk == 7 else None)
            f2_done = ste.n
            dbg_g0 = (debug_stop == "l1h" and g == 0)
            pl.wait("vector", ste, f2_done)
            pl.wait("vector", store_sem)
            for ti in range(4):
                tt = 4 * g + ti
                rbuf = ORT[:, 1024 * (ti % 2):1024 * (ti % 2) + 1024]
                obuf = ORT2[:, 1024 * (ti % 2):1024 * (ti % 2) + 1024]
                if ti >= 2:
                    pl.wait("sync", store_sem, store_sem.n - 16)
                pl.dma("sync", rbuf, resh_tile(RESH, tt), qs)
                rq = qs.n
                for dc in range(2):
                    pl.op("vector", lambda e, ti=ti, dc=dc: e.scalar_tensor_tensor(
                        R2T[:, 512 * dc:512 * (dc + 1)], bank(2 * ti + dc), 1.0,
                        HBF[:, 1024 * ti + 512 * dc:1024 * ti + 512 * (dc + 1)],
                        ALU.mult, ALU.add), sve if dc == 1 else None)
                pl.wait("scalar", sve)
                stats_tile(R2T[:], TINY[:, 6:7], TINY[:, 7:8])
                pl.wait("vector", sse)
                pl.op("vector", lambda e: e.tensor_scalar(
                    T2F[:], R2T[:], TINY[:, 6:7], TINY[:, 7:8], ALU.mult, ALU.add))
                pl.op("vector", lambda e: e.drain())
                pl.op("vector", lambda e: e.tensor_tensor(T2F[:], T2F[:], G2R[:], ALU.mult))
                pl.op("vector", lambda e: e.drain())
                pl.op("vector", lambda e: e.tensor_tensor(T2F[:], T2F[:], B2R[:], ALU.add))
                pl.wait("vector", qs, rq)
                pl.op("vector", lambda e, o=obuf, rb=rbuf: e.tensor_tensor(o, T2F[:], rb, ALU.add), sve)
                pl.wait("sync", sve)
                pl.dma("sync", resh_tile(RESH, tt), obuf, store_sem)
                if dbg_g0 and ti == 3:
                    pl.wait("sync", store_sem)
                    pl.barrier()
                    for tx in range(4):
                        dbg(bass.AP(RESH.tensor, tx * 128 * 1024, [[1024, 128], [1, 1024]]), 128 * 1024)
                    pl.wait("sync", sout)
                    pl.wait("gpsimd", sout)
                    ies.close()
                    return "dbg"
        pl.wait("sync", store_sem)
        pl.barrier()
        ies.close()

    W1QH = nc.dram_tensor("w1qh", [4 * D * D + 8], BF16, kind="Internal").ap()
    W2QH = nc.dram_tensor("w2qh", [4 * D * D + 8], BF16, kind="Internal").ap()

    def emit_l1_ffn():
        fes = contextlib.ExitStack()

        def sbq(n, s, d):
            return fes.enter_context(nc.sbuf_tensor(n, s, d))
        WB = sbq("qwb", [128, 2048], F32)
        QI2 = sbq("qqi", [128, 1024], I32)
        QD2 = sbq("qqd", [128, 1024], F32)
        QG2 = sbq("qqg", [128, 1024], F32)
        XQ2 = sbq("qxq", [128, 1024], BF16)
        GC = sbq("qgc", [128, 2], F32)
        RG = sbq("qrg", [128, 2], F32)
        GSR = sbq("qgsr", [1, 272], F32)
        QB1C = sbq("qb1c", [128, 32], F32)
        NB2R = sbq("qnb2", [1, 1024], BF16)
        B2F = sbq("qb2f", [1, 1024], F32)
        pl.op("vector", lambda e: e.memset(GC[:], 0.0), sve)
        gsem = pl.sem("qgsem")
        for wi, W_ in enumerate((QW1, QW2)):
            for c in range(16):
                if c > 0 or wi > 0:
                    pl.wait("sync", sve)
                pl.dma("sync", WB[:],
                       bass.AP(W_.tensor, 128 * 2048 * c, [[2048, 128], [1, 2048]]), gsem)
                pl.wait("vector", gsem)
                pl.op("vector", lambda e: e.tensor_reduce(
                    TINY[:, 0:1], WB[:], AX.X, ALU.add, apply_absolute_value=True))
                pl.op("vector", lambda e: e.drain())
                pl.op("vector", lambda e, wi=wi: e.tensor_tensor(
                    GC[:, wi:wi + 1], GC[:, wi:wi + 1], TINY[:, 0:1], ALU.add), sve)
                pl.op("vector", lambda e: e.drain())
        pl.wait("sync", sve)
        pl.dma("sync", bass.AP(BOUNCE.tensor, 0, [[2, 128], [1, 2]]), GC[:], qs)
        pl.wait("sync", qs)
        pl.dma("sync", GSR[0:1, 0:256], bass.AP(BOUNCE.tensor, 0, [[256, 1], [1, 256]]), qs)
        pl.wait("vector", qs)
        for wi in range(2):
            pl.op("vector", lambda e, wi=wi: e.tensor_reduce(
                GSR[0:1, 256 + wi:257 + wi],
                bass.AP(GSR[:].tensor, wi, [[272, 1], [2, 128]]), AX.X, ALU.add))
            pl.op("vector", lambda e: e.drain())
        NW = float(4 * D * D)
        pl.op("vector", lambda e: e.tensor_scalar(
            GSR[0:1, 260:262], GSR[0:1, 256:258], 1.0 / NW, 1e-5, ALU.mult, ALU.add))
        pl.op("vector", lambda e: e.drain())
        pl.op("vector", lambda e: e.reciprocal(GSR[0:1, 264:266], GSR[0:1, 260:262]), sve)
        pl.wait("sync", sve)
        pl.dma("sync", bass.AP(BOUNCE.tensor, 0, [[2, 1], [1, 2]]), GSR[0:1, 264:266], qs)
        pl.wait("sync", qs)
        pl.dma("sync", RG[:], bass.AP(BOUNCE.tensor, 0, [[0, 128], [1, 2]]), qs)
        rg_ready = qs.n
        pl.wait("vector", qs, rg_ready)
        qs2 = pl.sem("qffw")
        for wi, W_ in enumerate((QW1, QW2)):
            for c in range(32):
                if wi == 0:
                    src_ap = bass.AP(W_.tensor, 128 * 1024 * c, [[1024, 128], [1, 1024]])
                else:
                    dr, ob = c // 4, c % 4
                    src_ap = bass.AP(W_.tensor, 128 * 4096 * dr + 1024 * ob, [[4096, 128], [1, 1024]])
                pl.wait("sync", sve)
                pl.dma("sync", WB[:, 0:1024], src_ap, gsem)
                pl.wait("vector", gsem)
                pl.op("vector", lambda e, wi=wi: e.tensor_scalar(
                    QD2[:], WB[:, 0:1024], RG[:, wi:wi + 1], None, ALU.mult))
                pl.op("vector", lambda e: e.tensor_copy(QI2[:], QD2[:]))
                pl.op("vector", lambda e: e.scalar_tensor_tensor(
                    QG2[:], QI2[:], 1.0, QD2[:], ALU.mult, ALU.subtract))
                pl.op("vector", lambda e: e.scalar_tensor_tensor(
                    QD2[:], QG2[:], -0.5, QI2[:], ALU.is_lt, ALU.add))
                pl.op("vector", lambda e: e.drain())
                pl.op("vector", lambda e: e.scalar_tensor_tensor(
                    QG2[:], QG2[:], 0.5, QD2[:], ALU.is_gt, ALU.subtract))
                pl.op("vector", lambda e: e.drain())
                pl.op("vector", lambda e: e.tensor_scalar(
                    XQ2[:], QG2[:], -1.0, 1.0, ALU.max, ALU.min), sve)
                xq_done = sve.n
                pl.wait("tensor", sve, xq_done)
                for fs in range(8):
                    pl.op("tensor", lambda e, fs=fs: e.transpose(
                        bankb(7)[:, 128 * fs:128 * (fs + 1)], XQ2[:, 128 * fs:128 * (fs + 1)],
                        IDN[:]), ste if fs == 7 else None)
                pl.wait("vector", ste)
                pbv = bankb(7)
                pl.op("vector", lambda e, i=pbv: e.tensor_copy(XQ2[:], i), sve)
                tq_done = sve.n
                pl.wait("tensor", sve)
                pl.wait("sync", sve, tq_done)
                if wi == 0:
                    out_ap = bass.AP(W1QH.tensor, 128 * c,
                                     [[4096, 128], [4096 * 128, 8], [1, 128]])
                else:
                    dr, ob = c // 4, c % 4
                    out_ap = bass.AP(W2QH.tensor, 1024 * 1024 * ob + 128 * dr,
                                     [[1024, 128], [1024 * 128, 8], [1, 128]])
                pl.dma("sync", out_ap, XQ2[:], qs2)
        pl.wait("sync", qs2)
        pl.barrier()
        fes.close()
        mes = contextlib.ExitStack()

        def sbm(n, s, d):
            return mes.enter_context(nc.sbuf_tensor(n, s, d))
        W1QS = sbm("fw1q", [128, 8 * 2048], BF16)
        W2QS = sbm("fw2q", [128, 16 * 1024], BF16)
        HFM = sbm("fhfm", [128, 32 * 512], BF16)
        QB1Cm = sbm("fqb1", [128, 32], F32)
        NB2Rm = sbm("fnb2", [1, 1024], BF16)
        B2Fm = sbm("fb2f", [1, 1024], F32)
        pl.dma_nc("gpsimd", QB1Cm[:], bass.AP(QB1.tensor, 0, [[1, 128], [128, 32]]), qg)
        pl.dma("gpsimd", B2Fm[:], bass.AP(QB2.tensor, 0, [[1024, 1], [1, 1024]]), qg)
        cready = qg.n
        pl.wait("vector", qg, cready)
        pl.op("vector", lambda e: e.tensor_scalar(NB2Rm[:], B2Fm[:], -1.0, None, ALU.mult), sve)
        w1_sem = pl.sem("fw1s")
        w2_sem = pl.sem("fw2s")
        for s_ in range(NSEQ):
            ln_seq(s_, neg=True)
            for gg in range(4):
                tt0 = 16 * s_ + 4 * gg
                pl.wait("tensor", sve)
                for half in range(2):
                    pl.wait("gpsimd", ste)
                    pl.dma("gpsimd", W1QS[:],
                           bass.AP(W1QH.tensor, 2048 * half, [[4096, 128], [4096 * 128, 8], [1, 2048]]),
                           w1_sem)
                    pl.wait("tensor", w1_sem)
                    for ohh in range(16):
                        oc = 16 * half + ohh
                        bk = ohh % 2
                        pl.wait("tensor", sse)
                        for fc in range(8):
                            pl.op("tensor", lambda e, fc=fc, ohh=ohh, bk=bk, gg=gg: e.matmul(
                                bank(bk), W1QS[:, 2048 * fc + 128 * ohh:2048 * fc + 128 * (ohh + 1)],
                                ZT[:, 2048 * fc + 512 * gg:2048 * fc + 512 * (gg + 1)],
                                start=(fc == 0), stop=(fc == 7)), ste if fc == 7 else None)
                        mm_done = ste.n
                        pl.wait("scalar", ste, mm_done)
                        pl.op("scalar", lambda e, oc=oc, bk=bk: e.activation(
                            HFM[:, 512 * oc:512 * (oc + 1)], bank(bk), AF.Relu,
                            bias=QB1Cm[:, oc:oc + 1]), sse)
                pl.wait("tensor", sse)
                for h2 in range(2):
                    pl.wait("gpsimd", ste)
                    pl.dma("gpsimd", W2QS[:],
                           bass.AP(W2QH.tensor, 1024 * 1024 * 2 * h2,
                                   [[1024, 128], [1024 * 128, 16], [1, 1024]]), w2_sem)
                    pl.wait("tensor", w2_sem)
                    for ti in range(4):
                        for dc in range(2):
                            bk = 2 * ti + dc
                            for occ in range(16):
                                last = (ti, dc, occ) == (3, 1, 15)
                                pl.op("tensor", lambda e, occ=occ, ti=ti, dc=dc, bk=bk, h2=h2: e.matmul(
                                    bank(bk),
                                    HFM[:, 512 * (16 * h2 + occ) + 128 * ti:512 * (16 * h2 + occ) + 128 * (ti + 1)],
                                    W2QS[:, 1024 * occ + 512 * dc:1024 * occ + 512 * (dc + 1)],
                                    start=(h2 == 0 and occ == 0), stop=False),
                                    ste if last else None)
                if True:
                    for ti in range(4):
                        for dc in range(2):
                            bk = 2 * ti + dc
                            pl.op("tensor", lambda e, bk=bk, dc=dc: e.matmul(
                                bank(bk), ONESB[0:1, 0:128], NB2Rm[0:1, 512 * dc:512 * (dc + 1)],
                                start=False, stop=True), ste if bk == 7 else None)
                f2_done = ste.n
                pl.wait("vector", ste, f2_done)
                pl.wait("vector", sout)
                for ti in range(4):
                    tt = tt0 + ti
                    rbuf = ORT[:, 1024 * (ti % 2):1024 * (ti % 2) + 1024]
                    obuf = ORT2[:, 1024 * (ti % 2):1024 * (ti % 2) + 1024]
                    pl.dma("sync", rbuf, resh_tile(RESH, tt), qs)
                    rq = qs.n
                    pl.wait("vector", qs, rq)
                    for dc in range(2):
                        pl.op("vector", lambda e, ti=ti, dc=dc, o=obuf, rb=rbuf: e.scalar_tensor_tensor(
                            o[:, 512 * dc:512 * (dc + 1)], bank(2 * ti + dc), -1.0,
                            rb[:, 512 * dc:512 * (dc + 1)], ALU.mult, ALU.add),
                            sve if dc == 1 else None)
                    pl.wait("sync", sve)
                    pl.dma("sync", resh_tile(OUT, tt), obuf, sout)
        pl.wait("sync", sout)
        pl.wait("gpsimd", sout)
        pl.barrier()
        mes.close()

    dbg_off = [0]

    def dbg(ap_in, nelem, cast=False, parts=128):
        off = dbg_off[0]
        per = nelem // parts
        o = bass.AP(OUT.tensor, off, [[per, parts], [1, per]])
        pl.dma("gpsimd" if cast else "sync", o, ap_in, sout)
        dbg_off[0] += nelem

    def finish():
        pl.wait("sync", sout)
        pl.wait("gpsimd", sout)

    if debug_stop == "resh":
        pl.barrier()
        dbg(bass.AP(RESH.tensor, 0, [[8192, 128], [1, 8192]]), 128 * 8192)
        finish()
    elif debug_stop == "lnraw":
        ln_seq(0)
        pl.barrier()
        dbg(RT[:], 128 * 2048)
        dbg(TINY[:], 128 * 8)
        dbg(ZB[:], 128 * 2048, cast=True)
        finish()
    elif debug_stop == "trig":
        dbg(TRIGC, S * 512)
        dbg(TRIGS, S * 512)
        finish()
    elif debug_stop == "ln0":
        ln_seq(0)
        pl.barrier()
        dbg(ZT[:], 128 * 16384, cast=True)
        dbg(RSTD[:, 0:16], 128 * 16, cast=False)
        dbg(NEGB[:, 0:16], 128 * 16, cast=False)
        finish()
    elif debug_stop == "qkv0":
        ln_seq(0)
        pl.barrier()
        l0_attention(0, dbg_pair=0)
        pl.barrier()
        dbg(QKP[:], 128 * 4096, cast=True)
        dbg(QR[:], 128 * 2048, cast=True)
        dbg(KR[:], 128 * 2048, cast=True)
        dbg(VT[:], 128 * 2080, cast=True)
        dbg(NEGBC[:], 128 * 2)
        finish()
    elif debug_stop == "ctx0":
        ln_seq(0)
        pl.barrier()
        l0_attention(0, dbg_pair=0)
        pl.barrier()
        dbg(CTXA[:, 0:2048], 128 * 2048, cast=True)
        dbg(RCPR[:], 128 * 2048)
        finish()
    else:
        ln_seq(0)
        pl.barrier()
        l0_attention(0)
        pl.barrier()
        ln_seq(1)
        pl.barrier()
        l0_attention(1)
        pl.barrier()
        if debug_stop != "l0att":
            l0_es.close()
            emit_l0_ffn()
        if debug_stop in ("r1h", "l1att", "l1proj", "l1ctx", "l1h") or debug_stop is None:
            l1_es = contextlib.ExitStack()
            p1 = l1_alloc(l1_es)
            emit_l1_weights(p1)
            ln_seq(0, pooled=True)
            emit_pooled_finish(0)
            pl.barrier()
            ln_seq(1, pooled=True)
            emit_pooled_finish(1)
            pl.barrier()
            emit_l1_masks(p1)
            emit_l1_pooled(0, p1)
            emit_l1_kern_ck(0, p1)
            emit_l1_pooled(1, p1)
            emit_l1_kern_ck(1, p1)
            pl.barrier()
            l1_es.close()
            for s_ in range(NSEQ):
                ln_seq(s_)
                pl.barrier()
                l1a_es = contextlib.ExitStack()
                qq = l1a_alloc(l1a_es)
                emit_l1_proj(s_, qq)
                pl.barrier()
                if debug_stop == "l1proj" and s_ == 0:
                    dbg(qq["QSW"][:], 128 * 4096, cast=True)
                    dbg(qq["AUGQ"][:], 128 * 8192, cast=True)
                    dbg(ZT[:], 128 * 16384, cast=True)
                    finish()
                    l1a_es.close()
                    return _mk_finish(nc, es, pl)
                emit_l1_attn(s_, qq)
                pl.barrier()
                if debug_stop == "l1ctx" and s_ == 0:
                    dbg(qq["CTX1"][:], 128 * 16384, cast=True)
                    dbg(qq["KMT"][:], 128 * 256)
                    dbg(qq["BND"][:, 1:2], 128 * 1)
                    finish()
                    l1a_es.close()
                    return _mk_finish(nc, es, pl)
                emit_l1_outproj(s_, qq)
                pl.barrier()
                l1a_es.close()
            if debug_stop == "l1ctx":
                pass
            if debug_stop == "r1h":
                pl.dma("sync", OUT, R1H, sout)
                finish()
                return _mk_finish(nc, es, pl)
            r_ = emit_l1_inner()
            if debug_stop == "l1h":
                return _mk_finish(nc, es, pl)
            if debug_stop == "l1att":
                pl.dma("sync", OUT, RESH, sout)
                finish()
                return _mk_finish(nc, es, pl)
            emit_l1_ffn()
            finish()
            return _mk_finish(nc, es, pl)
        if debug_stop in ("ck0",):
            l1_es = contextlib.ExitStack()
            p1 = l1_alloc(l1_es)
            emit_l1_weights(p1)
            emit_l1_masks(p1)
            pl.barrier()
            ln_seq(0)
            pl.barrier()
            emit_l1_pooled(0, p1)
            pl.barrier()
            emit_l1_kern_ck(0, p1)
            pl.barrier()
            dbg(p1["CK"][0:16, 0:4096], 16 * 4096, parts=16)
            dbg(p1["WTC"][:], 128 * 4, cast=True)
            dbg(p1["KMH"][:], 128 * 8192, cast=True)
            finish()
            l1_es.close()
            return _mk_finish(nc, es, pl)
        if debug_stop == "l0att":
            l0_es.close()
        pl.dma("sync", OUT, RESH, sout)
        finish()

    return _mk_finish(nc, es, pl)


def _mk_finish(nc, es, pl):
    with es:
        block = es.enter_context(nc.Block())
        for name in ENGINES:
            def mk(name):
                def body(e):
                    for f in pl.q[name]:
                        f(e)
                return body
            getattr(block, name)(mk(name))
    return nc


def kernel(**inputs):
    nc = build(debug_stop=DEBUG_STOP)
    x = np.ascontiguousarray(np.asarray(inputs["L0_attn_x"], np.float32).reshape(16, S, D))
    shards = np.split(x, 8, axis=0)
    in_maps = []
    for c in range(8):
        m = {k: np.ascontiguousarray(np.asarray(v)) for k, v in inputs.items() if k != "L0_attn_x"}
        m["L0_attn_x"] = np.ascontiguousarray(shards[c].reshape(T, D))
        in_maps.append(m)
    res = run_bass_kernel_spmd(nc, in_maps, core_ids=list(range(8)))
    return np.concatenate([r["out"].reshape(2, S, D) for r in res.results], axis=0)


# version 05
# speedup vs baseline: 1.2505x; 1.0198x over previous
import contextlib
import numpy as np
import ml_dtypes
import concourse.bass as bass
import concourse.mybir as mybir
from concourse.bass_utils import run_bass_kernel_spmd

F32 = mybir.dt.float32
BF16 = mybir.dt.bfloat16
I32 = mybir.dt.int32
I8 = mybir.dt.int8
AF = mybir.ActivationFunctionType
ALU = mybir.AluOpType
AX = mybir.AxisListType

S, D, NSEQ = 2048, 1024, 2
T = NSEQ * S
NT = T // 128
TPS = S // 128
NH, HD, NPAIR = 16, 64, 8
LNEPS = 1e-5

TWO_PI = 2.0 * np.pi
C1 = float(np.float32(TWO_PI))
C2 = float(np.float32(TWO_PI - C1))
INV2PI = float(1.0 / TWO_PI)
LN1E4 = float(np.log(10000.0))
E_HI = float(np.float32(LN1E4 / 512.0))
E_LO = float(np.float32(LN1E4 / 512.0 - E_HI))

ENGINES = ("sync", "tensor", "vector", "scalar", "gpsimd")
DEBUG_STOP = None


class Sem:
    __slots__ = ("h", "n")

    def __init__(self, nc, name):
        self.h = nc.alloc_semaphore(name=name)
        self.n = 0


class PL:
    def __init__(self, nc):
        self.nc = nc
        self.q = {e: [] for e in ENGINES}
        self._sems = {}

    def sem(self, name):
        if name not in self._sems:
            self._sems[name] = Sem(self.nc, name)
        return self._sems[name]

    def em(self, eng, f):
        self.q[eng].append(f)

    def wait(self, eng, sem, val=None):
        v = sem.n if val is None else val
        if v > 0:
            self.em(eng, lambda e, s=sem.h, v=v: e.wait_ge(s, v))

    def dma(self, eng, out, in_, sem):
        sem.n += 16
        self.em(eng, lambda e, o=out, i=in_, s=sem.h: e.dma_start(out=o, in_=i).then_inc(s, 16))

    def dma_nc(self, eng, out, in_, sem):
        sem.n += 16
        self.em(eng, lambda e, o=out, i=in_, s=sem.h: e.dma_start(
            out=o, in_=i, allow_slow_non_contiguous=True).then_inc(s, 16))

    def op(self, eng, f, sem=None):
        if sem is None:
            self.em(eng, f)
        else:
            sem.n += 1
            self.em(eng, lambda e, f=f, s=sem.h: f(e).then_inc(s, 1))

    def barrier(self):
        bs = self.sem("bar")
        bs.n += len(ENGINES)
        tgt = bs.n
        for e in ENGINES:
            self.em(e, lambda eng, s=bs.h: eng.drain().then_inc(s, 1))
        for e in ENGINES:
            self.em(e, lambda eng, s=bs.h, v=tgt: eng.wait_ge(s, v))


def roundneg(pl, eng, sem, X, R1I, DD, OUTKN, n=None):
    pl.op(eng, lambda e: e.tensor_copy(R1I, X))
    pl.op(eng, lambda e: e.scalar_tensor_tensor(DD, R1I, 1.0, X, ALU.mult, ALU.subtract))
    pl.op(eng, lambda e: e.scalar_tensor_tensor(OUTKN, DD, -0.5, R1I, ALU.is_lt, ALU.add))
    pl.op(eng, lambda e: e.scalar_tensor_tensor(OUTKN, DD, 0.5, OUTKN, ALU.is_gt, ALU.subtract), sem)


def sin_reduce(pl, sem_v, sem_s, X, SCR_I, SCR_D, SCR_K, OUTSIN, pl_eng="vector"):
    pl.op(pl_eng, lambda e: e.tensor_scalar(SCR_K, X, INV2PI, None, ALU.mult))
    roundneg(pl, pl_eng, None, SCR_K, SCR_I, SCR_D, SCR_K)
    pl.op(pl_eng, lambda e: e.scalar_tensor_tensor(SCR_D, SCR_K, -C1, X, ALU.mult, ALU.subtract))
    pl.op(pl_eng, lambda e: e.scalar_tensor_tensor(SCR_D, SCR_K, C2, SCR_D, ALU.mult, ALU.subtract), sem_v)
    pl.wait("scalar", sem_v)
    pl.op("scalar", lambda e: e.activation(OUTSIN, SCR_D, AF.Sin), sem_s)


def build(debug_stop=None):
    nc = bass.Bass("TRN2", target_bir_lowering=False)

    def din(name, shape, dt=F32):
        return nc.dram_tensor(name, shape, dt, kind="ExternalInput").ap()

    X0 = din("L0_attn_x", [T, D])
    WQ = din("L0_attn_Wq", [D, D]); BQ = din("L0_attn_bq", [D])
    WK = din("L0_attn_Wk", [D, D]); BK = din("L0_attn_bk", [D])
    WV = din("L0_attn_Wv", [D, D]); BV = din("L0_attn_bv", [D])
    W0 = din("L0_attn_W0", [D, D]); B0 = din("L0_attn_b0", [D])
    W8 = din("L0_ffn_weight_int8", [D, D], I8)
    WS8 = din("L0_ffn_weight_scale", [D, 1])
    BF8 = din("L0_ffn_bias", [D])
    VW = din("L1_attn_v_w", [D, D]); VB = din("L1_attn_v_b", [D])
    OW = din("L1_attn_out_w", [D, D]); OB = din("L1_attn_out_b", [D])
    QMW = din("L1_attn_qmod_w", [D, NH]); QMB = din("L1_attn_qmod_b", [NH])
    KMW = din("L1_attn_kmod_w", [D, NH]); KMB = din("L1_attn_kmod_b", [NH])
    GW = din("L1_attn_gate_w", [D, 256]); GB = din("L1_attn_gate_b", [256])
    MW = din("L1_attn_mod_w", [D, 128]); MB = din("L1_attn_mod_b", [128])
    MBAS = din("L1_attn_mod_basis", [8, NH, 16])
    FRQ = din("L1_attn_freqs", [NH, 16, 8, 2])
    AMP = din("L1_attn_amps", [NH, 16, 8])
    PHS = din("L1_attn_phases", [NH, 16, 8])
    CQW = din("L1_attn_cq_w", [D, 128])
    CKW = din("L1_attn_ck_w", [D, 128])
    FW1 = din("L1_attn_ffn_w1", [D, 4 * D]); FB1 = din("L1_attn_ffn_b1", [4 * D])
    FW2 = din("L1_attn_ffn_w2", [4 * D, D]); FB2 = din("L1_attn_ffn_b2", [D])
    LN1G = din("L1_attn_ln1_g", [D]); LN1B = din("L1_attn_ln1_b", [D])
    LN2G = din("L1_attn_ln2_g", [D]); LN2B = din("L1_attn_ln2_b", [D])
    QW1 = din("L1_ffn_W1", [4 * D, D]); QB1 = din("L1_ffn_b1", [4 * D])
    QW2 = din("L1_ffn_W2", [D, 4 * D]); QB2 = din("L1_ffn_b2", [D])

    OUT = nc.dram_tensor("out", [T, D], F32, kind="ExternalOutput").ap()

    RESH = nc.dram_tensor("resh", [T, D], F32, kind="Internal").ap()
    R1H = nc.dram_tensor("r1h", [T, D], F32, kind="Internal").ap()
    TRIGC = nc.dram_tensor("trigc", [S, 512], F32, kind="Internal").ap()
    TRIGS = nc.dram_tensor("trigs", [S, 512], F32, kind="Internal").ap()
    BOUNCE = nc.dram_tensor("bounce", [8192], F32, kind="Internal").ap()
    CCI = nc.dram_tensor("cci", [128, 128], F32, kind="Internal")
    CCO = nc.dram_tensor("cco", [128, 128], F32, kind="Internal")
    STSTRIDE = 128 * 4095
    STAIR = nc.dram_tensor("stair", [32 * STSTRIDE + 8], BF16, kind="Internal").ap()
    CKH = nc.dram_tensor("ckh", [32 * 4095 + 8], BF16, kind="Internal").ap()
    ZTH = nc.dram_tensor("zth", [2 * 128 * 16384 + 8], BF16, kind="Internal").ap()
    FW1H = nc.dram_tensor("fw1h", [4 * D * D + 8], BF16, kind="Internal").ap()
    FW2H = nc.dram_tensor("fw2h", [4 * D * D + 8], BF16, kind="Internal").ap()

    es = contextlib.ExitStack()

    def sbuf(name, shape, dt):
        return es.enter_context(nc.sbuf_tensor(name, shape, dt))

    IDN = sbuf("idn", [128, 128], BF16)
    EPSC = sbuf("epsc", [128, 1], F32)
    EPS6 = sbuf("eps6", [128, 1], F32)
    IDNI = sbuf("idni", [128, 128], F32)
    ONESB = sbuf("onesb", [128, 128], BF16)
    PS = es.enter_context(nc.psum_tensor("ps", [128, 4096], F32))

    def bank(i):
        return PS[:, 512 * i:512 * (i + 1)]

    def bankslab(i, nb):
        return PS[:, 512 * i:512 * (i + nb)]

    def bankb(i):
        return PS[:, 512 * i:512 * (i + 1)].bitcast(BF16)

    pl = PL(nc)
    qs = pl.sem("qsync")
    qg = pl.sem("qgp")
    ste = pl.sem("ste")
    sve = pl.sem("sve")
    sse = pl.sem("sse")
    sgx = pl.sem("sgx")
    sout = pl.sem("sout")

    pl.dma("sync", RESH, X0, qs)
    pl.op("gpsimd", lambda e: e.iota(IDNI[:], [[1, 128]], channel_multiplier=-1,
                                     allow_small_or_imprecise_dtypes=True))
    pl.op("gpsimd", lambda e: e.tensor_scalar(IDN[:], IDNI[:], 0.0, None, ALU.is_equal))
    pl.op("gpsimd", lambda e: e.memset(ONESB[:], 1.0))
    pl.op("gpsimd", lambda e: e.memset(EPSC[:], LNEPS))
    pl.op("gpsimd", lambda e: e.memset(EPS6[:], 1e-6), sgx)

    def p0_trig():
        tes = contextlib.ExitStack()

        def tsb(name, shape, dt):
            return tes.enter_context(nc.sbuf_tensor(name, shape, dt))
        IVF = tsb("t_ivf", [1, 512], F32)
        IOT = tsb("t_iot", [1, 512], F32)
        IVR = tsb("t_ivr", [128, 2048], F32)
        PCOL = tsb("t_pc", [128, 4], F32)
        ANG = tsb("t_ang", [128, 2048], F32)
        R1I = tsb("t_r1i", [128, 2048], I32)
        DD = tsb("t_dd", [128, 2048], F32)
        KN = tsb("t_kn", [128, 2048], F32)
        SN = tsb("t_sn", [128, 2048], F32)
        pl.op("gpsimd", lambda e: e.iota(IOT[:], [[1, 512]], channel_multiplier=0,
                                         allow_small_or_imprecise_dtypes=True))
        pl.op("gpsimd", lambda e: e.tensor_scalar(IVF[:], IOT[:], -E_HI, None, ALU.mult))
        pl.op("gpsimd", lambda e: e.tensor_scalar(IOT[:], IOT[:], -E_LO, None, ALU.mult))
        pl.op("gpsimd", lambda e: e.drain())
        pl.op("gpsimd", lambda e: e.tensor_tensor(IVF[:], IVF[:], IOT[:], ALU.add), sgx)
        pl.wait("scalar", sgx)
        pl.op("scalar", lambda e: e.activation(IVF[:], IVF[:], AF.Exp), sse)
        pl.wait("sync", sse)
        pl.dma("sync", bass.AP(BOUNCE.tensor, 0, [[512, 1], [1, 512]]), IVF[:], qs)
        pl.wait("sync", qs)
        pl.dma("sync", IVR[:], bass.AP(BOUNCE.tensor, 0, [[0, 128], [0, 4], [1, 512]]), qs)
        ivr_ready = qs.n
        pl.wait("vector", qs, ivr_ready)
        pl.wait("gpsimd", qs, ivr_ready)
        for g in range(4):
            pl.wait("gpsimd", sve)
            pl.op("gpsimd", lambda e, g=g: e.iota(PCOL[:], [[128, 4]], channel_multiplier=1,
                                                  base=512 * g,
                                                  allow_small_or_imprecise_dtypes=True), sgx)
            pl.wait("vector", sgx)
            for c in range(4):
                pl.op("vector", lambda e, c=c: e.tensor_scalar(
                    ANG[:, 512 * c:512 * (c + 1)], IVR[:, 512 * c:512 * (c + 1)],
                    PCOL[:, c:c + 1], None, ALU.mult))
            sin_reduce(pl, sve, sse, ANG[:], R1I[:], DD[:], KN[:], SN[:])
            pl.wait("sync", sse)
            pl.dma("sync", bass.AP(TRIGS.tensor, 512 * 512 * g,
                                   [[512, 128], [512 * 128, 4], [1, 512]]), SN[:], qs)
            sin_done = qs.n
            pl.wait("vector", qs, sin_done)
            pl.op("vector", lambda e: e.tensor_scalar(ANG[:], ANG[:], float(np.pi / 2), None, ALU.add))
            sin_reduce(pl, sve, sse, ANG[:], R1I[:], DD[:], KN[:], SN[:])
            pl.wait("sync", sse)
            pl.dma("sync", bass.AP(TRIGC.tensor, 512 * 512 * g,
                                   [[512, 128], [512 * 128, 4], [1, 512]]), SN[:], qs)
            pl.wait("vector", qs)

        return tes

    _tes = p0_trig()
    pl.barrier()
    _tes.close()

    RT = sbuf("ln_rt", [128, 2048], F32)
    ZB = sbuf("ln_z", [128, 2048], BF16)
    TINY = sbuf("ln_tiny", [128, 8], F32)
    LNSQ = sbuf("ln_sq", [128, 1024], F32)
    RSTD = sbuf("rstd", [128, NT], F32)
    NEGB = sbuf("negb", [128, NT], F32)
    ZT = sbuf("zt", [128, 8 * 2048], BF16)

    def resh_tile(src, tt):
        return bass.AP(src.tensor, tt * 128 * D, [[D, 128], [1, D]])

    def stats_tile(buf, rs_col, nb_col):
        c0, c1v, c2v, c3v = TINY[:, 0:1], TINY[:, 1:2], TINY[:, 2:3], TINY[:, 3:4]
        pl.op("scalar", lambda e, b=buf: e.activation(LNSQ[:], b, AF.Square), sse)
        pl.wait("vector", sse)
        pl.op("vector", lambda e, b=buf, c=c0: e.tensor_reduce(c, b, AX.X, ALU.add))
        pl.op("vector", lambda e, c=c1v: e.tensor_reduce(c, LNSQ[:], AX.X, ALU.add))
        pl.op("vector", lambda e, c=c0: e.tensor_scalar(c, c, 1.0 / D, None, ALU.mult))
        pl.op("vector", lambda e: e.drain())
        pl.op("vector", lambda e, c=c0, o=c2v: e.tensor_tensor(o, c, c, ALU.mult))
        pl.op("vector", lambda e: e.drain())
        pl.op("vector", lambda e, c=c1v, m=c2v, o=c3v: e.scalar_tensor_tensor(
            o, c, 1.0 / D, m, ALU.mult, ALU.subtract), sve)
        pl.wait("scalar", sve)
        pl.op("scalar", lambda e, v=c3v: e.activation(v, v, AF.Sqrt, bias=EPSC[:, 0:1]), sse)
        pl.wait("vector", sse)
        pl.op("vector", lambda e, v=c3v, o=rs_col: e.reciprocal(o, v))
        pl.op("vector", lambda e: e.drain())
        pl.op("vector", lambda e, m=c0, r=rs_col, o=c2v: e.tensor_tensor(o, m, r, ALU.mult))
        pl.op("vector", lambda e: e.drain())
        pl.op("vector", lambda e, i=c2v, o=nb_col: e.tensor_scalar(o, i, -1.0, None, ALU.mult), sve)
        pl.op("vector", lambda e: e.drain())

    def ln_pass(tiles, src=RESH, neg=False, make_zt=True, absmax_col=None, zt_dst=None,
                pooled=False):
        apply_done = {}
        tr_done = {}
        for li, tt in enumerate(tiles):
            buf = RT[:, 1024 * (li % 2):1024 * (li % 2) + 1024]
            zb = ZB[:, 1024 * (li % 2):1024 * (li % 2) + 1024]
            if li >= 2:
                pl.wait("sync", sve, apply_done[li - 2])
            pl.dma("sync", buf, resh_tile(src, tt), qs)
            pl.wait("scalar", qs)
            c0, c1v, c2v, c3v = TINY[:, 0:1], TINY[:, 1:2], TINY[:, 2:3], TINY[:, 3:4]
            pl.wait("scalar", sve)
            if li >= 2 and make_zt:
                pl.wait("scalar", ste, tr_done[li - 2])
            pl.op("scalar", lambda e, b=buf: e.activation(LNSQ[:], b, AF.Square), sse)
            pl.wait("vector", sse)
            pl.op("vector", lambda e, b=buf, c=c0: e.tensor_reduce(c, b, AX.X, ALU.add))
            pl.op("vector", lambda e, c=c1v: e.tensor_reduce(c, LNSQ[:], AX.X, ALU.add))
            pl.op("vector", lambda e, c=c0: e.tensor_scalar(c, c, 1.0 / D, None, ALU.mult))
            pl.op("vector", lambda e: e.drain())
            pl.op("vector", lambda e, c=c0, o=c2v: e.tensor_tensor(o, c, c, ALU.mult))
            pl.op("vector", lambda e: e.drain())
            pl.op("vector", lambda e, c=c1v, m=c2v, o=c3v: e.scalar_tensor_tensor(
                o, c, 1.0 / D, m, ALU.mult, ALU.subtract), sve)
            pl.wait("scalar", sve)
            pl.op("scalar", lambda e, v=c3v: e.activation(v, v, AF.Sqrt, bias=EPSC[:, 0:1]), sse)
            pl.wait("vector", sse)
            rs_col = RSTD[:, tt:tt + 1]
            nb_col = NEGB[:, tt:tt + 1]
            pl.op("vector", lambda e, v=c3v, o=rs_col: e.reciprocal(o, v))
            pl.op("vector", lambda e: e.drain())
            pl.op("vector", lambda e, m=c0, r=rs_col, o=c2v: e.tensor_tensor(o, m, r, ALU.mult))
            pl.op("vector", lambda e: e.drain())
            pl.op("vector", lambda e, i=c2v, o=nb_col: e.tensor_scalar(o, i, -1.0, None, ALU.mult), sve)
            pl.op("vector", lambda e: e.drain())
            if make_zt:
                if not neg:
                    pl.op("vector", lambda e, b=buf, z=zb, r=rs_col, nb=nb_col:
                          e.tensor_scalar(z, b, r, nb, ALU.mult, ALU.add), sve)
                else:
                    pl.op("vector", lambda e, b=buf, z=zb, r=rs_col, nb=nb_col:
                          e.tensor_scalar(z, b, r, nb, ALU.mult, ALU.add))
                    pl.op("vector", lambda e, z=zb: e.tensor_scalar(z, z, -1.0, None, ALU.mult), sve)
            if absmax_col is not None:
                pl.op("vector", lambda e, b=buf, r=rs_col, nb=nb_col:
                      e.tensor_scalar(LNSQ[:], b, r, nb, ALU.mult, ALU.add))
                pl.op("vector", lambda e: e.drain())
                pl.op("vector", lambda e, c=c2v: e.tensor_reduce(
                    c, LNSQ[:], AX.X, ALU.max, apply_absolute_value=True))
                pl.op("vector", lambda e: e.drain())
                pl.op("vector", lambda e, c=c2v, a=absmax_col: e.tensor_tensor(a, a, c, ALU.max), sve)
                pl.op("vector", lambda e: e.drain())
            apply_done[li] = sve.n
            if make_zt:
                ZTD = zt_dst
                pl.wait("tensor", sve)
                for fc in range(8):
                    pl.op("tensor", lambda e, fc=fc, z=zb: e.transpose(
                        bankb(7)[:, 128 * fc:128 * (fc + 1)], z[:, 128 * fc:128 * (fc + 1)], IDN[:]),
                        ste if (fc == 7 and not pooled) else None)
                if pooled:
                    for half in range(2):
                        pl.op("tensor", lambda e, z=zb, half=half, li=li: e.matmul(
                            bank(3 + half)[0:1, :], ONESB[:, 0:1], z[:, 512 * half:512 * (half + 1)],
                            start=(li == 0), stop=(li == len(tiles) - 1)),
                            ste if half == 1 else None)
                tr_done[li] = ste.n
                pl.wait("vector", ste)
                pbv = bankb(7)
                ps_in = pbv.rearrange("p (a b) -> p a b", a=8)
                zt_out = bass.AP(ZTD[:].tensor, 128 * li, [[8 * 2048, 128], [2048, 8], [1, 128]])
                pl.op("vector", lambda e, o=zt_out, i=ps_in: e.tensor_copy(o, i), sve)
                pl.wait("tensor", sve)
        return apply_done

    def ln_seq(s, src=RESH, neg=False, pooled=False):
        return ln_pass(list(range(s * TPS, (s + 1) * TPS)), src=src, neg=neg, zt_dst=ZT,
                       pooled=pooled)

    PLDR = sbuf("pldr", [1, 1024], BF16)
    PLDC = sbuf("pldc", [128, 16], BF16)
    ORT = sbuf("ort", [128, 2048], F32)
    ORT2 = sbuf("ort2", [128, 2048], F32)

    l0_es = contextlib.ExitStack()

    def sbuf0(name, shape, dt):
        return l0_es.enter_context(nc.sbuf_tensor(name, shape, dt))

    WQKV = sbuf0("wqkv", [128, 8 * 384], BF16)
    BQKVR = sbuf0("bqkv", [1, 384], BF16)
    QKP = sbuf0("qkp", [128, 16 * 256], BF16)
    VT = sbuf0("vt", [128, 16 * 130], BF16)
    COSA = sbuf0("cosa", [128, 16 * 64], F32)
    SINA = sbuf0("sina", [128, 16 * 64], F32)
    TMP1 = sbuf0("tmp1", [128, 1024], F32)
    TMP2 = sbuf0("tmp2", [128, 1024], F32)
    QR = sbuf0("qr", [128, 2048], BF16)
    KR = sbuf0("kr", [128, 2048], BF16)
    SQB = sbuf0("sqb", [128, 2048], BF16)
    NEGBC = sbuf0("negbc", [128, 2], F32)
    ATTN = sbuf0("attn", [128, 2048], BF16)
    CTXA = sbuf0("ctxa", [128, 8 * 2048], BF16)
    SROW = sbuf0("srow", [128, 4096], F32)
    CTXTMP = sbuf0("ctxtmp", [128, 512], BF16)
    RCPR = sbuf0("rcpr", [128, 2048], F32)
    W0S = sbuf0("w0s", [128, 8 * 1024], BF16)
    B0R = sbuf0("b0r", [1, 1024], BF16)

    def l0_attention(s, dbg_pair=None):
        pairs = range(NPAIR) if dbg_pair is None else [dbg_pair]
        for hp in pairs:
            hA, hB = 2 * hp, 2 * hp + 1
            last_mm_prev = ste.n
            pl.wait("gpsimd", ste, last_mm_prev)
            for wi, W in enumerate((WQ, WK, WV)):
                src = bass.AP(W.tensor, 128 * hp, [[D, 128], [D * 128, 8], [1, 128]])
                dst = bass.AP(WQKV[:].tensor, 128 * wi, [[8 * 384, 128], [384, 8], [1, 128]])
                pl.dma("gpsimd", dst, src, qg)
            for bi, B in enumerate((BQ, BK, BV)):
                src = bass.AP(B.tensor, 128 * hp, [[128, 1], [1, 128]])
                pl.dma("gpsimd", BQKVR[0:1, 128 * bi:128 * (bi + 1)], src, qg)
            pl.wait("sync", sve)
            pl.dma("sync", COSA[:], bass.AP(TRIGC.tensor, 64 * hp,
                                            [[512, 128], [512 * 128, 16], [1, 64]]), qs)
            pl.dma("sync", SINA[:], bass.AP(TRIGS.tensor, 64 * hp,
                                            [[512, 128], [512 * 128, 16], [1, 64]]), qs)
            trig_ready = qs.n
            pl.wait("vector", ste, last_mm_prev)
            vt_ones = bass.AP(VT[:].tensor, 64, [[16 * 130, 128], [130, 16], [65, 2], [1, 1]])
            pl.op("vector", lambda e, a=vt_ones: e.memset(a, 1.0), sve)
            vt_ones_done = sve.n

            w_ready = qg.n
            pl.wait("tensor", qg, w_ready)
            for g in range(4):
                for i in range(4):
                    lt = 4 * g + i
                    for fc in range(8):
                        pl.op("tensor", lambda e, fc=fc, lt=lt, i=i: e.matmul(
                            PS[:, 512 * i:512 * i + 384],
                            ZT[:, 2048 * fc + 128 * lt:2048 * fc + 128 * (lt + 1)],
                            WQKV[:, 384 * fc:384 * (fc + 1)],
                            start=(fc == 0), stop=False))
                    pl.op("tensor", lambda e, i=i: e.matmul(
                        PS[:, 512 * i:512 * i + 384],
                        ONESB[0:1, 0:128], BQKVR[0:1, :],
                        start=False, stop=True), ste if i == 3 else None)
                mm_done = ste.n
                pl.wait("vector", ste, mm_done)
                if g == 0:
                    pl.wait("vector", sve, vt_ones_done)
                qk_in = bass.AP(PS[:].tensor, 0, [[4096, 128], [512, 4], [1, 256]])
                qk_out = bass.AP(QKP[:].tensor, 1024 * g, [[16 * 256, 128], [256, 4], [1, 256]])
                pl.op("vector", lambda e, o=qk_out, i=qk_in: e.tensor_copy(o, i))
                v_in = bass.AP(PS[:].tensor, 256, [[4096, 128], [512, 4], [64, 2], [1, 64]])
                v_out = bass.AP(VT[:].tensor, 130 * 4 * g, [[16 * 130, 128], [130, 4], [65, 2], [1, 64]])
                pl.op("vector", lambda e, o=v_out, i=v_in: e.tensor_copy(o, i), sve)
                pl.wait("tensor", sve)

            pl.wait("vector", qs, trig_ready)

            def qkp_ap(base):
                return bass.AP(QKP[:].tensor, base, [[16 * 256, 128], [256, 16], [64, 2], [2, 32]])

            def trig_ap(TTT):
                return bass.AP(TTT[:].tensor, 0, [[16 * 64, 128], [64, 16], [32, 2], [1, 32]])

            def rot_out(R, base):
                return bass.AP(R[:].tensor, base, [[2048, 128], [128, 16], [64, 2], [2, 32]])

            for base_qk, R in ((0, QR), (128, KR)):
                X1 = qkp_ap(base_qk)
                X2 = qkp_ap(base_qk + 1)
                CO = trig_ap(COSA)
                SI = trig_ap(SINA)
                pl.op("vector", lambda e, a=X1, b=CO: e.tensor_tensor(TMP1[:], a, b, ALU.mult))
                pl.op("vector", lambda e, a=X2, b=SI: e.tensor_tensor(TMP2[:], a, b, ALU.mult))
                pl.op("vector", lambda e, o=rot_out(R, 0): e.tensor_tensor(o, TMP1[:], TMP2[:], ALU.subtract))
                pl.op("vector", lambda e, a=X1, b=SI: e.tensor_tensor(TMP1[:], a, b, ALU.mult))
                pl.op("vector", lambda e, a=X2, b=CO: e.tensor_tensor(TMP2[:], a, b, ALU.mult))
                pl.op("vector", lambda e, o=rot_out(R, 1): e.tensor_tensor(o, TMP1[:], TMP2[:], ALU.add), sve)
            rot_done = sve.n

            pl.wait("tensor", sve, rot_done)
            for R in (QR, KR):
                for half in range(2):
                    for i in range(8):
                        lt = 8 * half + i
                        pl.op("tensor", lambda e, R=R, lt=lt, i=i: e.transpose(
                            bankb(7)[:, 128 * i:128 * (i + 1)],
                            R[:, 128 * lt:128 * (lt + 1)], IDN[:]), ste)
                    tr_done = ste.n
                    pl.wait("vector", ste, tr_done)
                    pbv = bankb(7).rearrange("p (a b) -> p a b", a=8)
                    out = bass.AP(R[:].tensor, 1024 * half, [[2048, 128], [128, 8], [1, 128]])
                    pl.op("vector", lambda e, o=out, i=pbv: e.tensor_copy(o, i), sve)
                    pl.wait("tensor", sve)
            trans_done = sve.n

            for which, R in (("k", KR), ("q", QR)):
                pl.op("vector", lambda e, R=R: e.tensor_tensor(SQB[:], R[:], R[:], ALU.mult), sve)
                sq_done = sve.n
                pl.wait("tensor", sve, sq_done)
                for h in range(2):
                    for qcn in range(4):
                        pl.op("tensor", lambda e, h=h, qcn=qcn: e.matmul(
                            bank(qcn)[0:1, :],
                            ONESB[64 * h:64 * h + 64, 0:1],
                            SQB[64 * h:64 * h + 64, 512 * qcn:512 * (qcn + 1)],
                            start=True, stop=True), ste if qcn == 3 else None)
                    nm_done = ste.n
                    pl.wait("vector", ste, nm_done)
                    ci = 4 + 2 * h + (0 if which == "k" else 1)
                    col = TINY[0:1, ci:ci + 1]
                    pl.op("vector", lambda e, c=col: e.tensor_reduce(
                        c, PS[0:1, 0:2048], AX.X, ALU.max), sve)
                    pl.wait("tensor", sve)
            pl.wait("scalar", sve)
            for h in range(2):
                mk2 = TINY[0:1, 4 + 2 * h:4 + 2 * h + 1]
                mq2 = TINY[0:1, 5 + 2 * h:5 + 2 * h + 1]
                pl.op("scalar", lambda e, a=mk2: e.activation(a, a, AF.Sqrt))
                pl.op("scalar", lambda e, a=mq2: e.activation(a, a, AF.Sqrt), sse)
            pl.wait("vector", sse)
            for h in range(2):
                mk = TINY[0:1, 4 + 2 * h:4 + 2 * h + 1]
                mq = TINY[0:1, 5 + 2 * h:5 + 2 * h + 1]
                pl.op("vector", lambda e, a=mq, b=mk: e.tensor_tensor(a, a, b, ALU.mult))
                pl.op("vector", lambda e: e.drain())
                pl.op("vector", lambda e, a=mq: e.tensor_scalar(a, a, -0.125, None, ALU.mult), sve)
            pl.op("vector", lambda e: e.drain(), sve)
            pl.wait("sync", sve)
            bounce_negb = bass.AP(BOUNCE.tensor, 4096, [[2, 1], [1, 2]])
            tiny_negb = bass.AP(TINY[:].tensor, 5, [[8, 1], [2, 2]])
            pl.em("sync", lambda e, o=bounce_negb, i=tiny_negb, s=qs.h: e.dma_start(
                out=o, in_=i, allow_slow_non_contiguous=True).then_inc(s, 16))
            qs.n += 16
            pl.wait("sync", qs)
            pl.dma("sync", NEGBC[:], bass.AP(BOUNCE.tensor, 4096, [[0, 128], [1, 2]]), qs)
            negbc_ready = qs.n

            pl.wait("scalar", qs, negbc_ready)
            for h in range(2):
                hbase = 64 * h
                for qc in range(4):
                    pl.wait("tensor", sse)
                    for grp in range(8):
                        jt0 = 2 * grp
                        gb = 2 * (grp % 2)
                        pl.wait("tensor", sse)
                        for j in range(2):
                            jt = jt0 + j
                            pl.op("tensor", lambda e, jt=jt, gb=gb, j=j, hbase=hbase, qc=qc: e.matmul(
                                bank(gb + j),
                                KR[hbase:hbase + 64, 128 * jt:128 * (jt + 1)],
                                QR[hbase:hbase + 64, 512 * qc:512 * (qc + 1)],
                                start=True, stop=True), ste)
                        sc_done = ste.n
                        pl.wait("scalar", ste, sc_done)
                        abuf = ATTN[:, 1024 * (grp % 2):1024 * (grp % 2) + 1024]
                        pl.op("scalar", lambda e, gb=gb, a=abuf, h=h: e.activation(
                            a, bankslab(gb, 2), AF.Exp, bias=NEGBC[:, h:h + 1], scale=0.125), sse)
                        exp_done = sse.n
                        pl.wait("tensor", sse, exp_done)
                        for j in range(2):
                            jt = jt0 + j
                            pl.op("tensor", lambda e, jt=jt, j=j, h=h, a=abuf: e.matmul(
                                bank(4)[0:65, :],
                                VT[:, 130 * jt + 65 * h:130 * jt + 65 * h + 65],
                                a[:, 512 * j:512 * (j + 1)],
                                start=(jt == 0), stop=(jt == 15)), ste if j == 1 else None)
                    ctx_done = ste.n
                    pl.wait("scalar", ste, ctx_done)
                    if h == 0:
                        pl.op("scalar", lambda e, hp=hp, qc=qc: e.copy(
                            CTXA[0:64, 2048 * hp + 512 * qc:2048 * hp + 512 * (qc + 1)],
                            bank(4)[0:64, :]))
                    else:
                        pl.op("scalar", lambda e: e.copy(CTXTMP[0:64, :], bank(4)[0:64, :]), sse)
                        pl.wait("gpsimd", sse)
                        pl.dma("gpsimd", CTXA[64:128, 2048 * hp + 512 * qc:2048 * hp + 512 * (qc + 1)],
                               CTXTMP[0:64, :], qg)
                    pl.op("scalar", lambda e, h=h, qc=qc: e.copy(
                        SROW[64:65, 2048 * h + 512 * qc:2048 * h + 512 * (qc + 1)],
                        bank(4)[64:65, :]), sse)
            pl.wait("gpsimd", sse)
            pl.wait("scalar", qg)
            pl.op("scalar", lambda e: e.activation(SROW[64:65, :], SROW[64:65, :], AF.Ln))
            pl.op("scalar", lambda e: e.activation(SROW[64:65, :], SROW[64:65, :], AF.Exp, scale=-1.0), sse)
            pl.wait("sync", sse)
            pl.dma("sync", bass.AP(BOUNCE.tensor, 0, [[4096, 1], [1, 4096]]), SROW[64:65, :], qs)
            pl.wait("sync", qs)
            pl.dma("sync", RCPR[0:64, :], bass.AP(BOUNCE.tensor, 0, [[0, 64], [1, 2048]]), qs)
            pl.dma("sync", RCPR[64:128, :], bass.AP(BOUNCE.tensor, 2048, [[0, 64], [1, 2048]]), qs)
            pl.wait("vector", qs)
            pl.wait("vector", qg)
            pl.op("vector", lambda e, hp=hp: e.tensor_tensor(
                CTXA[:, 2048 * hp:2048 * (hp + 1)], CTXA[:, 2048 * hp:2048 * (hp + 1)],
                RCPR[:], ALU.mult), sve)

        if dbg_pair is not None:
            return
        pl.dma("gpsimd", W0S[:], bass.AP(W0.tensor, 0, [[D, 128], [D * 128, 8], [1, 1024]]), qg)
        pl.dma("gpsimd", B0R[:], bass.AP(B0.tensor, 0, [[1024, 1], [1, 1024]]), qg)
        w0_ready = qg.n
        pl.wait("tensor", qg, w0_ready)
        pl.wait("tensor", sve)
        store_sem = pl.sem("l0store")
        store_base = store_sem.n
        for lt in range(TPS):
            tt = s * TPS + lt
            rbuf = ORT[:, 1024 * (lt % 2):1024 * (lt % 2) + 1024]
            obuf = ORT2[:, 1024 * (lt % 2):1024 * (lt % 2) + 1024]
            if lt >= 2:
                pl.wait("sync", store_sem, store_base + 16 * (lt - 1))
            pl.dma("sync", rbuf, resh_tile(RESH, tt), qs)
            for oc in range(2):
                bk = 5 + oc
                pl.wait("tensor", sve)
                for hp2 in range(8):
                    pl.op("tensor", lambda e, hp2=hp2, lt=lt, oc=oc, bk=bk: e.matmul(
                        bank(bk),
                        CTXA[:, 2048 * hp2 + 128 * lt:2048 * hp2 + 128 * (lt + 1)],
                        W0S[:, 1024 * hp2 + 512 * oc:1024 * hp2 + 512 * (oc + 1)],
                        start=(hp2 == 0), stop=False))
                pl.op("tensor", lambda e, oc=oc, bk=bk: e.matmul(
                    bank(bk), ONESB[0:1, 0:128], B0R[0:1, 512 * oc:512 * (oc + 1)],
                    start=False, stop=True), ste)
                mm_done = ste.n
                pl.wait("vector", ste, mm_done)
                pl.wait("vector", qs)
                pl.op("vector", lambda e, bk=bk, oc=oc, r=rbuf, o=obuf: e.scalar_tensor_tensor(
                    o[:, 512 * oc:512 * (oc + 1)], bank(bk), 1.0,
                    r[:, 512 * oc:512 * (oc + 1)], ALU.mult, ALU.add), sve)
            pl.wait("sync", sve)
            pl.dma("sync", resh_tile(RESH, tt), obuf, store_sem)
        pl.wait("sync", store_sem)

    def emit_l0_ffn():
        fes = contextlib.ExitStack()

        def sbf(n, s, d):
            return fes.enter_context(nc.sbuf_tensor(n, s, d))
        W8T = sbf("w8t", [128, 8 * 1024], BF16)
        W8N = sbf("w8n", [128, 8 * 1024], BF16)
        WSC = sbf("wsc", [128, 8], F32)
        AMX = sbf("amx", [128, 1], F32)
        GXS = sbf("gxs", [1, 4], F32)
        RXN = sbf("rxn", [128, 2], F32)
        BN8 = sbf("bn8", [1, 1024], BF16)
        B8F = sbf("b8f", [1, 1024], F32)
        CCS = sbf("ccs", [128, 128], F32)
        XQN = sbf("xqn", [128, 1024], BF16)
        QI = sbf("qi", [128, 1024], I32)
        QD = sbf("qd", [128, 1024], F32)
        QG = sbf("qg2", [128, 1024], F32)
        XQT = sbf("xqt", [128, 1024], BF16)

        pl.dma("gpsimd", W8N[:], bass.AP(W8.tensor, 0, [[1024, 128], [128 * 1024, 8], [1, 1024]]), qg)
        wsc_src = bass.AP(WS8.tensor, 0, [[1, 128], [128, 8]])
        pl.em("gpsimd", lambda e, o=WSC[:], i=wsc_src, s=qg.h: e.dma_start(
            out=o, in_=i, allow_slow_non_contiguous=True).then_inc(s, 16))
        qg.n += 16
        pl.dma("sync", B8F[:], bass.AP(BF8.tensor, 0, [[1024, 1], [1, 1024]]), qs)
        pl.op("gpsimd", lambda e: e.memset(CCS[:], 0.0), sgx)
        w8_ready = qg.n
        pl.wait("vector", qg, w8_ready)
        for oc in range(8):
            pl.op("vector", lambda e, oc=oc: e.tensor_scalar(
                W8N[:, 1024 * oc:1024 * (oc + 1)], W8N[:, 1024 * oc:1024 * (oc + 1)],
                WSC[:, oc:oc + 1], None, ALU.mult), sve)
        w8s_done = sve.n
        pl.wait("tensor", sve, w8s_done)
        for oc in range(8):
            for fc in range(8):
                pl.op("tensor", lambda e, oc=oc, fc=fc: e.transpose(
                    bankb(7)[:, 128 * fc:128 * (fc + 1)],
                    W8N[:, 1024 * oc + 128 * fc:1024 * oc + 128 * (fc + 1)], IDN[:]),
                    ste if fc == 7 else None)
            pl.wait("vector", ste)
            pbv = bankb(7).rearrange("p (a b) -> p a b", a=8)
            w8t_out = bass.AP(W8T[:].tensor, 128 * oc, [[8 * 1024, 128], [1024, 8], [1, 128]])
            pl.op("vector", lambda e, o=w8t_out, i=pbv: e.tensor_copy(o, i), sve)
            pl.wait("tensor", sve)

        pl.op("vector", lambda e: e.memset(AMX[:], 0.0), sve)
        ln_pass(list(range(NT)), make_zt=False, absmax_col=AMX[:, 0:1])
        pl.wait("sync", sve)
        pl.dma("sync", bass.AP(BOUNCE.tensor, 0, [[1, 128], [1, 1]]), AMX[:], qs)
        pl.wait("sync", qs)
        pl.dma("sync", CCS[0:1, 0:128], bass.AP(BOUNCE.tensor, 0, [[128, 1], [1, 128]]), qs)
        pl.wait("vector", qs)
        pl.wait("vector", sgx)
        pl.op("vector", lambda e: e.tensor_reduce(GXS[0:1, 0:1], CCS[0:1, 0:128], AX.X, ALU.max))
        pl.op("vector", lambda e: e.drain())
        pl.op("vector", lambda e: e.tensor_scalar(
            GXS[0:1, 0:1], GXS[0:1, 0:1], 1.0 / 127.0, 1e-5, ALU.mult, ALU.max))
        pl.op("vector", lambda e: e.drain())
        pl.op("vector", lambda e: e.tensor_copy(CCS[0:1, 0:1], GXS[0:1, 0:1]), sve)
        pl.wait("gpsimd", sve)
        pl.dma("gpsimd", CCI.ap(), CCS[:], qg)
        pl.wait("gpsimd", qg)
        ccs_done = pl.sem("ccdone")
        ccs_done.n += 1
        pl.em("gpsimd", lambda e, s=ccs_done.h: e.collective_compute(
            "AllReduce", ALU.max, replica_groups=[list(range(8))],
            ins=[CCI.ap().opt()], outs=[CCO.ap().opt()]).then_inc(s, 1))
        pl.wait("gpsimd", ccs_done)
        pl.dma("gpsimd", GXS[0:1, 1:2], CCO.ap()[0:1, 0:1], qg)
        gxs_ready = qg.n
        pl.wait("vector", qg, gxs_ready)
        pl.op("vector", lambda e: e.reciprocal(GXS[0:1, 2:3], GXS[0:1, 1:2]))
        pl.op("vector", lambda e: e.drain())
        pl.op("vector", lambda e: e.tensor_scalar(GXS[0:1, 3:4], GXS[0:1, 1:2], -1.0, None, ALU.mult), sve)
        pl.wait("sync", sve)
        pl.dma("sync", bass.AP(BOUNCE.tensor, 0, [[2, 1], [1, 2]]),
               GXS[0:1, 2:4], qs)
        pl.wait("sync", qs)
        pl.dma("sync", RXN[:], bass.AP(BOUNCE.tensor, 0, [[0, 128], [1, 2]]), qs)
        rxn_ready = qs.n
        pl.wait("vector", qs, rxn_ready)
        pl.op("vector", lambda e: e.tensor_scalar(BN8[:], B8F[:], GXS[0:1, 2:3], -1.0, ALU.mult, ALU.mult), sve)
        bn8_ready = sve.n

        store_sem = pl.sem("ffnst")
        st_base = store_sem.n
        for tt in range(NT):
            rbuf = ORT[:, 1024 * (tt % 2):1024 * (tt % 2) + 1024]
            obuf = ORT2[:, 1024 * (tt % 2):1024 * (tt % 2) + 1024]
            if tt >= 2:
                pl.wait("sync", store_sem, st_base + 16 * (tt - 1))
            pl.dma("sync", rbuf, resh_tile(RESH, tt), qs)
            rq = qs.n
            pl.wait("vector", qs, rq)
            rs_col = RSTD[:, tt:tt + 1]
            nb_col = NEGB[:, tt:tt + 1]
            pl.wait("vector", ste)
            pl.op("vector", lambda e, b=rbuf, r=rs_col, nb=nb_col: e.tensor_scalar(
                QG[:], b, r, nb, ALU.mult, ALU.add))
            pl.op("vector", lambda e: e.tensor_scalar(QD[:], QG[:], RXN[:, 0:1], None, ALU.mult))
            pl.op("vector", lambda e: e.tensor_copy(QI[:], QD[:]))
            pl.op("vector", lambda e: e.scalar_tensor_tensor(QG[:], QI[:], 1.0, QD[:], ALU.mult, ALU.subtract))
            pl.op("vector", lambda e: e.scalar_tensor_tensor(QD[:], QG[:], -0.5, QI[:], ALU.is_lt, ALU.add))
            pl.op("vector", lambda e: e.scalar_tensor_tensor(XQN[:], QG[:], 0.5, QD[:], ALU.is_gt, ALU.subtract), sve)
            q_done = sve.n
            pl.wait("tensor", sve, q_done)
            for fc in range(8):
                pl.op("tensor", lambda e, fc=fc: e.transpose(
                    bankb(7)[:, 128 * fc:128 * (fc + 1)], XQN[:, 128 * fc:128 * (fc + 1)], IDN[:]),
                    ste if fc == 7 else None)
            pl.wait("vector", ste)
            pbv = bankb(7).rearrange("p (a b) -> p a b", a=8)
            pl.op("vector", lambda e, i=pbv: e.tensor_copy(XQT[:], i), sve)
            xqt_done = sve.n
            pl.wait("tensor", sve, xqt_done)
            for oc in range(2):
                for fc in range(8):
                    pl.op("tensor", lambda e, oc=oc, fc=fc: e.matmul(
                        bank(oc), XQT[:, 128 * fc:128 * (fc + 1)],
                        W8T[:, 1024 * fc + 512 * oc:1024 * fc + 512 * (oc + 1)],
                        start=(fc == 0), stop=False))
                pl.op("tensor", lambda e, oc=oc: e.matmul(
                    bank(oc), ONESB[0:1, 0:128], BN8[0:1, 512 * oc:512 * (oc + 1)],
                    start=False, stop=True), ste)
                mm_done = ste.n
                pl.wait("vector", ste, mm_done)
                pl.op("vector", lambda e, oc=oc, o=obuf, rb=rbuf: e.scalar_tensor_tensor(
                    o[:, 512 * oc:512 * (oc + 1)], bank(oc), RXN[:, 1:2],
                    rb[:, 512 * oc:512 * (oc + 1)], ALU.mult, ALU.add), sve)
            pl.wait("sync", sve)
            pl.dma("sync", resh_tile(RESH, tt), obuf, store_sem)
        pl.wait("sync", store_sem)
        pl.barrier()
        fes.close()

    ALPHA1, MIX1, STEPA1 = 3.0, 0.15, 6.0
    LRSCALE = float(MIX1 / np.sqrt(8.0))

    def emit_pooled_finish(s):
        pl.wait("vector", ste)
        for half in range(2):
            pl.op("vector", lambda e, half=half: e.tensor_scalar(
                PLDR[0:1, 512 * half:512 * (half + 1)], bank(3 + half)[0:1, :],
                1.0 / S, None, ALU.mult), sve if half == 1 else None)
        pr_done = sve.n
        pl.wait("tensor", sve, pr_done)
        for fc in range(8):
            pl.op("tensor", lambda e, fc=fc: e.transpose(
                bankb(2)[:, 2 * fc:2 * fc + 1], PLDR[0:1, 128 * fc:128 * (fc + 1)],
                IDN[0:1, 0:1]), ste if fc == 7 else None)
        pl.wait("vector", ste)
        pin = bass.AP(bankb(2).tensor, bankb(2).offset, [[8192, 128], [2, 8]])
        pl.op("vector", lambda e, s=s, i=pin: e.tensor_copy(PLDC[:, 8 * s:8 * s + 8], i), sve)
        pl.wait("tensor", sve)

    def l1_alloc(stk):
        def sb1(n, s, d):
            return stk.enter_context(nc.sbuf_tensor(n, s, d))
        p = {}
        p["PFM"] = sb1("pfm", [128, 8], F32)
        p["PFMB"] = sb1("pfmb", [128, 8], BF16)
        p["GWS"] = sb1("gws", [128, 8 * 256], BF16)
        p["GBR"] = sb1("gbr", [1, 256], BF16)
        p["MWS"] = sb1("mws", [128, 8 * 128], BF16)
        p["MBC"] = sb1("mbc", [1, 128], BF16)
        p["BASB"] = sb1("basb", [128, 256], BF16)
        p["MODR"] = sb1("modr", [1, 128], BF16)
        p["MODCC"] = sb1("modcc", [128, 1], BF16)
        p["GLROW"] = sb1("glrow", [1, 512], F32)
        p["WTSROW"] = sb1("wtsrow", [1, 256], BF16)
        p["WTC"] = sb1("wtc", [128, 4], F32)
        p["HSEL"] = sb1("hsel", [128, 64], F32)
        p["W2HS"] = sb1("w2hs", [128, 32], BF16)
        p["P1R"] = sb1("p1r", [128, 4095], F32)
        p["P2R"] = sb1("p2r", [128, 4095], F32)
        p["F1C"] = sb1("f1c", [128, 1], F32)
        p["F2C"] = sb1("f2c", [128, 1], F32)
        p["PHC"] = sb1("phc", [128, 1], F32)
        p["AMPC"] = sb1("ampc", [128, 1], F32)
        p["SELF"] = sb1("self", [128, 32], F32)
        p["AT2"] = sb1("at2", [128, 2048], F32)
        p["SELAB"] = sb1("selab", [128, 128], BF16)
        p["SINM"] = sb1("sinm", [128, 4095], BF16)
        p["ARG"] = sb1("arg", [128, 2048], F32)
        p["KI"] = sb1("l1ki", [128, 2048], I32)
        p["KD"] = sb1("l1kd", [128, 2048], F32)
        p["KK"] = sb1("l1kk", [128, 2048], F32)
        p["KMH"] = sb1("kmh", [128, 8192], BF16)
        p["CK"] = sb1("l1ck", [16, 4096], F32)
        p["PKT"] = sb1("pkt", [128, 5 * 32], F32)
        p["WREP"] = sb1("wrep", [16, 4095], F32)
        p["CKB"] = p["KD"][:].bitcast(BF16)
        return p

    def emit_l1_weights(p):
        pl.dma("gpsimd", p["GWS"][:], bass.AP(GW.tensor, 0, [[256, 128], [256 * 128, 8], [1, 256]]), qg)
        pl.dma("gpsimd", p["GBR"][:], bass.AP(GB.tensor, 0, [[256, 1], [1, 256]]), qg)
        pl.dma("gpsimd", p["MWS"][:], bass.AP(MW.tensor, 0, [[128, 128], [128 * 128, 8], [1, 128]]), qg)
        pl.dma("gpsimd", p["MBC"][:], bass.AP(MB.tensor, 0, [[128, 1], [1, 128]]), qg)
        pl.op("gpsimd", lambda e, t=p["BASB"]: e.memset(t[:], 0.0), sgx)
        pl.wait("gpsimd", sgx)
        for h in range(NH):
            pl.dma("gpsimd", p["BASB"][8 * h:8 * h + 8, 16 * h:16 * (h + 1)],
                   bass.AP(MBAS.tensor, 16 * h, [[256, 8], [1, 16]]), qg)

    def emit_l1_pooled(s, p):
        pl.wait("tensor", qg)
        for fc in range(8):
            pl.op("tensor", lambda e, fc=fc, s=s, w=p["GWS"]: e.matmul(
                bank(0)[0:1, 0:256], PLDC[:, 8 * s + fc:8 * s + fc + 1],
                w[:, 256 * fc:256 * (fc + 1)],
                start=(fc == 0), stop=False))
        pl.op("tensor", lambda e, r=p["GBR"]: e.matmul(
            bank(0)[0:1, 0:256], ONESB[0:1, 0:1], r[0:1, :], start=False, stop=False))
        for fc in range(8):
            pl.op("tensor", lambda e, fc=fc, s=s, w=p["MWS"]: e.matmul(
                bank(1)[0:1, 0:128], PLDC[:, 8 * s + fc:8 * s + fc + 1],
                w[:, 128 * fc:128 * (fc + 1)],
                start=(fc == 0), stop=False))
        pl.op("tensor", lambda e, r=p["MBC"]: e.matmul(
            bank(1)[0:1, 0:128], ONESB[0:1, 0:1], r[0:1, :], start=False, stop=True), ste)
        mod_done = ste.n
        pl.wait("scalar", ste, mod_done)
        pl.op("scalar", lambda e, m=p["MODR"]: e.copy(m[0:1, :], bank(1)[0:1, 0:128]), sse)
        pl.wait("tensor", sse)
        pl.op("tensor", lambda e, m=p["MODR"]: e.transpose(bankb(2)[:, 0:1], m[0:1, :], IDN[0:1, 0:1]), ste)
        pl.wait("vector", ste)
        pl.op("vector", lambda e, m=p["MODCC"]: e.tensor_copy(m[:, 0:1], bankb(2)[:, 0:1]), sve)
        modc_done = sve.n
        pl.wait("tensor", sve, modc_done)
        pl.op("tensor", lambda e, m=p["MODCC"], bb=p["BASB"]: e.matmul(
            bank(0)[0:1, 0:256], m[:, 0:1], bb[:], start=False, stop=True), ste)
        gl_done = ste.n
        G = p["GLROW"]
        pl.wait("scalar", ste, gl_done)
        pl.op("scalar", lambda e, G=G: e.copy(G[0:1, 0:256], bank(0)[0:1, 0:256]), sse)
        pl.wait("vector", sse)
        g3 = bass.AP(G[:].tensor, 0, [[512, 1], [16, 16], [1, 16]])
        mx = bass.AP(G[:].tensor, 256, [[512, 1], [1, 16], [1, 1]])
        mxb = bass.AP(G[:].tensor, 256, [[512, 1], [1, 16], [0, 16]])
        sm = bass.AP(G[:].tensor, 272, [[512, 1], [1, 16], [1, 1]])
        smb = bass.AP(G[:].tensor, 288, [[512, 1], [1, 16], [0, 16]])
        pl.op("vector", lambda e: e.tensor_reduce(mx, g3, AX.X, ALU.max))
        pl.op("vector", lambda e: e.drain())
        pl.op("vector", lambda e: e.tensor_tensor(g3, g3, mxb, ALU.subtract), sve)
        pl.wait("scalar", sve)
        pl.op("scalar", lambda e, G=G: e.activation(G[0:1, 0:256], G[0:1, 0:256], AF.Exp), sse)
        pl.wait("vector", sse)
        pl.op("vector", lambda e: e.tensor_reduce(sm, g3, AX.X, ALU.add))
        pl.op("vector", lambda e: e.drain())
        pl.op("vector", lambda e: e.reciprocal(
            bass.AP(G[:].tensor, 288, [[512, 1], [1, 16]]),
            bass.AP(G[:].tensor, 272, [[512, 1], [1, 16]])))
        pl.op("vector", lambda e: e.drain())
        pl.op("vector", lambda e: e.tensor_tensor(g3, g3, smb, ALU.mult))
        pl.op("vector", lambda e: e.drain())
        pl.op("vector", lambda e, w=p["WTSROW"], G=G: e.tensor_copy(w[0:1, :], G[0:1, 0:256]), sve)
        wts_done = sve.n
        pl.wait("tensor", sve, wts_done)
        for half in range(2):
            pl.op("tensor", lambda e, half=half, w=p["WTSROW"]: e.transpose(
                bankb(2)[:, 2 * half:2 * half + 1],
                w[0:1, 128 * half:128 * (half + 1)], IDN[0:1, 0:1]), ste if half == 1 else None)
        pl.wait("vector", ste)
        for half in range(2):
            pl.op("vector", lambda e, p=p, s=s, half=half: e.tensor_copy(
                p["WTC"][:, 2 * s + half:2 * s + half + 1], bankb(2)[:, 2 * half:2 * half + 1]),
                sve if half == 1 else None)
        pl.wait("tensor", sve)

    def emit_l1_masks(p):
        PKT = p["PKT"]
        REL, P1, P2, WIN, SCR = (PKT[:, 0:32], PKT[:, 32:64], PKT[:, 64:96],
                                 PKT[:, 96:128], PKT[:, 128:160])
        pl.op("gpsimd", lambda e: e.iota(REL, [[1, 32]], channel_multiplier=32, base=-2047,
                                         allow_small_or_imprecise_dtypes=True), sgx)
        pl.wait("vector", sgx)
        pl.op("vector", lambda e: e.tensor_scalar(P1, REL, 1.0 / S, None, ALU.mult), sve)
        pl.wait("scalar", sgx)
        pl.op("scalar", lambda e: e.activation(SCR, REL, AF.Abs))
        pl.op("scalar", lambda e: e.activation(SCR, SCR, AF.Sqrt, bias=EPS6[:, 0:1]))
        pl.op("scalar", lambda e: e.activation(P2, REL, AF.Sign), sse)
        pl.wait("vector", sse)
        pl.op("vector", lambda e: e.scalar_tensor_tensor(
            P2, SCR, float(1.0 / np.sqrt(S)), P2, ALU.mult, ALU.mult), sve)
        p2_done = sve.n
        pl.wait("scalar", sve, p2_done)
        pl.op("scalar", lambda e: e.activation(SCR, REL, AF.Square))
        pl.op("scalar", lambda e: e.activation(WIN, SCR, AF.Exp,
                                               scale=float(-1.0 / (2.0 * (S / 4.0) ** 2))), sse)
        pl.wait("vector", sse)
        pl.op("vector", lambda e: e.tensor_scalar(WIN, WIN, ALPHA1, None, ALU.mult), sve)
        win_done = sve.n
        pl.wait("sync", sve, win_done)
        pl.dma("sync", bass.AP(BOUNCE.tensor, 0, [[32, 128], [1, 32]]), P1, qs)
        pl.wait("sync", qs)
        pl.dma("sync", p["P1R"][:], bass.AP(BOUNCE.tensor, 0, [[0, 128], [1, 4095]]), qs)
        pl.wait("sync", qs)
        pl.dma("sync", bass.AP(BOUNCE.tensor, 0, [[32, 128], [1, 32]]), P2, qs)
        pl.wait("sync", qs)
        pl.dma("sync", p["P2R"][:], bass.AP(BOUNCE.tensor, 0, [[0, 128], [1, 4095]]), qs)
        pl.wait("sync", qs)
        pl.dma("sync", bass.AP(BOUNCE.tensor, 0, [[32, 128], [1, 32]]), WIN, qs)
        pl.wait("sync", qs)
        pl.dma("sync", p["WREP"][:], bass.AP(BOUNCE.tensor, 0, [[0, 16], [1, 4095]]), qs)
        prep_done = qs.n
        for half in range(2):
            hs = p["HSEL"][:, 16 * half:16 * (half + 1)]
            pl.op("gpsimd", lambda e, hs=hs, half=half: e.iota(
                hs, [[-16, 16]], channel_multiplier=1, base=128 * half,
                allow_small_or_imprecise_dtypes=True), sgx)
        pl.wait("vector", sgx)
        for half in range(2):
            hs = p["HSEL"][:, 16 * half:16 * (half + 1)]
            m1 = p["HSEL"][:, 32 + 16 * half:48 + 16 * half]
            pl.op("vector", lambda e, hs=hs, m1=m1: e.tensor_scalar(m1, hs, 0.0, None, ALU.is_ge))
            pl.op("vector", lambda e, hs=hs: e.tensor_scalar(hs, hs, 16.0, None, ALU.is_lt))
            pl.op("vector", lambda e: e.drain())
            pl.op("vector", lambda e, hs=hs, m1=m1: e.tensor_tensor(hs, hs, m1, ALU.mult), sve)
        pl.wait("gpsimd", qs, prep_done)
        for ci in range(16):
            pl.wait("gpsimd", ste)
            pl.dma_nc("gpsimd", p["F1C"][:], bass.AP(FRQ.tensor, 256 * ci, [[2, 128], [1, 1]]), qg)
            pl.dma_nc("gpsimd", p["F2C"][:], bass.AP(FRQ.tensor, 256 * ci + 1, [[2, 128], [1, 1]]), qg)
            pl.dma("gpsimd", p["PHC"][:], bass.AP(PHS.tensor, 128 * ci, [[1, 128], [1, 1]]), qg)
            pl.dma("gpsimd", p["AMPC"][:], bass.AP(AMP.tensor, 128 * ci, [[1, 128], [1, 1]]), qg)
            cready = qg.n
            pl.wait("gpsimd", qg, cready)
            for hw in range(2):
                w = 2048 if hw == 0 else 2047
                c0_ = 2048 * hw
                A = p["ARG"][:, 0:w]
                KK_, KI_, KD_ = p["KK"][:, 0:w], p["KI"][:, 0:w], p["KD"][:, 0:w]
                pl.wait("gpsimd", sve)
                pl.op("gpsimd", lambda e, A=A, p=p, c0_=c0_, w=w: e.tensor_scalar(
                    A, p["P1R"][:, c0_:c0_ + w], p["F1C"][:, 0:1], None, ALU.mult))
                pl.op("gpsimd", lambda e, p=p, c0_=c0_, w=w: e.tensor_scalar(
                    p["AT2"][:, 0:w], p["P2R"][:, c0_:c0_ + w], p["F2C"][:, 0:1], None, ALU.mult))
                pl.op("gpsimd", lambda e: e.drain())
                pl.op("gpsimd", lambda e, A=A, p=p, w=w: e.tensor_tensor(
                    A, A, p["AT2"][:, 0:w], ALU.add))
                pl.op("gpsimd", lambda e: e.drain())
                pl.op("gpsimd", lambda e, A=A, p=p: e.tensor_scalar(
                    A, A, p["PHC"][:, 0:1], None, ALU.add), sgx)
                arg_done = sgx.n
                pl.wait("vector", sgx, arg_done)
                pl.op("vector", lambda e, A=A, KK_=KK_: e.tensor_scalar(KK_, A, INV2PI, None, ALU.mult))
                pl.op("vector", lambda e, KK_=KK_, KI_=KI_: e.tensor_copy(KI_, KK_))
                pl.op("vector", lambda e, KI_=KI_, KK_=KK_, KD_=KD_: e.scalar_tensor_tensor(
                    KD_, KI_, 1.0, KK_, ALU.mult, ALU.subtract))
                pl.op("vector", lambda e, KD_=KD_, KI_=KI_, KK_=KK_: e.scalar_tensor_tensor(
                    KK_, KD_, -0.5, KI_, ALU.is_lt, ALU.add))
                pl.op("vector", lambda e: e.drain())
                pl.op("vector", lambda e, KD_=KD_, KK_=KK_: e.scalar_tensor_tensor(
                    KK_, KD_, 0.5, KK_, ALU.is_gt, ALU.subtract))
                pl.op("vector", lambda e: e.drain())
                pl.op("vector", lambda e, KK_=KK_, A=A, KD_=KD_: e.scalar_tensor_tensor(
                    KD_, KK_, -C1, A, ALU.mult, ALU.subtract))
                pl.op("vector", lambda e: e.drain())
                pl.op("vector", lambda e, KK_=KK_, KD_=KD_: e.scalar_tensor_tensor(
                    KD_, KK_, C2, KD_, ALU.mult, ALU.subtract), sve)
                rr_done = sve.n
                pl.wait("scalar", sve, rr_done)
                if hw == 0 and ci > 0:
                    pl.wait("scalar", ste)
                pl.op("scalar", lambda e, p=p, c0_=c0_, w=w, KD_=KD_: e.activation(
                    p["SINM"][:, c0_:c0_ + w], KD_, AF.Sin), sse)
            sin_done = sse.n
            pl.op("gpsimd", lambda e, p=p: e.memset(p["SELAB"][:], 0.0))
            sel = p["SELF"][:, 0:16]
            m1 = p["SELF"][:, 16:32]
            pl.op("gpsimd", lambda e, sel=sel: e.iota(sel, [[-8, 16]], channel_multiplier=1, base=0,
                                                      allow_small_or_imprecise_dtypes=True))
            pl.op("gpsimd", lambda e: e.drain())
            pl.op("gpsimd", lambda e, sel=sel, m1=m1: e.tensor_scalar(m1, sel, 0.0, None, ALU.is_ge))
            pl.op("gpsimd", lambda e, sel=sel: e.tensor_scalar(sel, sel, 8.0, None, ALU.is_lt))
            pl.op("gpsimd", lambda e: e.drain())
            pl.op("gpsimd", lambda e, sel=sel, m1=m1: e.tensor_tensor(sel, sel, m1, ALU.mult))
            pl.op("gpsimd", lambda e: e.drain())
            pl.op("gpsimd", lambda e, sel=sel, p=p, ci=ci: e.tensor_scalar(
                p["SELAB"][:, 16 * (ci % 8):16 * (ci % 8) + 16],
                sel, p["AMPC"][:, 0:1], None, ALU.mult), sgx)
            sela_done = sgx.n
            pl.wait("tensor", sse, sin_done)
            pl.wait("tensor", sgx, sela_done)
            for pc in range(8):
                n = 512 if pc < 7 else 511
                pl.op("tensor", lambda e, pc=pc, ci=ci, n=n, p=p: e.matmul(
                    bank(pc)[:, 0:n], p["SELAB"][:],
                    p["SINM"][:, 512 * pc:512 * pc + n],
                    start=(ci % 8 == 0), stop=(ci % 8 == 7)), ste if pc == 7 else None)
            if ci % 8 == 7:
                half = ci // 8
                mmk_done = ste.n
                pl.wait("scalar", ste, mmk_done)
                for pc in range(8):
                    n = 512 if pc < 7 else 511
                    pl.op("scalar", lambda e, pc=pc, half=half, n=n, p=p: e.copy(
                        p["KMH"][:, 4096 * half + 512 * pc:4096 * half + 512 * pc + n],
                        bank(pc)[:, 0:n]), sse if pc == 7 else None)
                pl.wait("tensor", sse)

    def emit_l1_kern_ck(s, p):
        pl.wait("vector", ste)
        for half in range(2):
            pl.op("vector", lambda e, half=half, p=p, s=s: e.tensor_scalar(
                p["W2HS"][:, 16 * half:16 * (half + 1)],
                p["HSEL"][:, 16 * half:16 * (half + 1)],
                p["WTC"][:, 2 * s + half:2 * s + half + 1], None, ALU.mult), sve)
        w2_done = sve.n
        pl.wait("tensor", sve, w2_done)
        for pc in range(8):
            n = 512 if pc < 7 else 511
            for half in range(2):
                pl.op("tensor", lambda e, pc=pc, half=half, n=n, p=p: e.matmul(
                    bank(pc)[0:16, 0:n],
                    p["W2HS"][:, 16 * half:16 * (half + 1)],
                    p["KMH"][:, 4096 * half + 512 * pc:4096 * half + 512 * pc + n],
                    start=(half == 0), stop=(half == 1)), ste if half == 1 else None)
        kern_done = ste.n
        pl.wait("scalar", ste, kern_done)
        for pc in range(8):
            n = 512 if pc < 7 else 511
            pl.op("scalar", lambda e, pc=pc, n=n, p=p: e.activation(
                p["CK"][0:16, 512 * pc:512 * pc + n], bank(pc)[0:16, 0:n],
                AF.Sigmoid, scale=STEPA1), sse if pc == 7 else None)
        sig_done = sse.n
        pl.wait("vector", sse, sig_done)
        pl.op("vector", lambda e, p=p: e.tensor_tensor(
            p["CK"][0:16, 0:4095], p["CK"][0:16, 0:4095], p["WREP"][:], ALU.mult))
        pl.op("vector", lambda e, p=p: e.memset(p["CK"][0:16, 0:2047], -4e9))
        pl.op("vector", lambda e: e.drain())
        pl.op("vector", lambda e, p=p: e.tensor_copy(
            p["CKB"][0:16, 0:4095], p["CK"][0:16, 0:4095]), sve)
        ck_done = sve.n
        pl.wait("sync", sve, ck_done)
        pl.dma("sync", bass.AP(CKH.tensor, 4095 * 16 * s, [[4095, 16], [1, 4095]]),
               p["CKB"][0:16, 0:4095], qs)
        pl.wait("sync", qs)
        stair_sem = pl.sem("stair")
        pl.wait("gpsimd", qs)
        for h in range(16):
            bh = 16 * s + h
            pl.dma("gpsimd",
                   bass.AP(STAIR.tensor, bh * STSTRIDE, [[4095, 128], [1, 4095]]),
                   bass.AP(CKH.tensor, 4095 * bh, [[0, 128], [1, 4095]]), stair_sem)
        pl.wait("gpsimd", stair_sem)

    _l1a_ctr = [0]

    def l1a_alloc(stk):
        _l1a_ctr[0] += 1
        sfx = str(_l1a_ctr[0])

        def sb2(n, s, d):
            return stk.enter_context(nc.sbuf_tensor(n + sfx, s, d))
        q = {}
        q["QSW"] = sb2("qsw", [128, 8 * 512], BF16)
        q["KSW"] = sb2("ksw", [128, 8 * 512], BF16)
        q["BRK"] = sb2("brk", [1, 512], BF16)
        q["AUGQ"] = sb2("augq", [128, 4 * 2048], BF16)
        q["AUGK"] = sb2("augk", [128, 4 * 2048], BF16)
        q["KMT"] = sb2("kmt", [128, 256], F32)
        q["KMB"] = sb2("kmb", [128, 256], F32)
        q["QTMP"] = sb2("qtmp", [128, 512], BF16)
        q["KTMP"] = sb2("ktmp", [128, 512], BF16)
        q["VT1"] = sb2("vt1", [128, 16 * 130], BF16)
        q["VWS"] = sb2("vws", [128, 8 * 128], BF16)
        q["VBR"] = sb2("vbr", [1, 128], BF16)
        q["ATT1"] = sb2("att1", [128, 2048], BF16)
        q["TOEPM"] = sb2("toepm", [128, 16 * 512], BF16)
        q["EXPIN"] = sb2("expin", [128, 2048], F32)
        q["SRO"] = sb2("sro", [128, 2048], F32)
        q["RCP1"] = sb2("rcp1", [128, 2048], F32)
        q["CTX1"] = sb2("ctx1", [128, 8 * 2048], BF16)
        q["CTXT"] = sb2("ctxt1", [128, 512], BF16)
        q["SQ2"] = sb2("sq2", [128, 512], F32)
        q["BND"] = sb2("bnd", [128, 2], F32)

        q["OBR"] = sb2("obr", [1, 1024], BF16)

        q["OWS"] = q["AUGQ"]
        q["R1Z"] = q["EXPIN"][:, 0:1024]
        q["_toep_free"] = [0]
        q["_bank_last"] = {0: 0, 1: 0, 2: 0, 3: 0}
        return q

    def emit_l1_proj(s, q):
        pl.op("gpsimd", lambda e, q=q: e.memset(q["QSW"][:], 0.0))
        pl.op("gpsimd", lambda e, q=q: e.memset(q["KSW"][:], 0.0))
        pl.op("gpsimd", lambda e, q=q: e.memset(q["BRK"][:], 0.0), sgx)
        pl.wait("gpsimd", sgx)
        for fc in range(8):
            pl.dma("gpsimd",
                   bass.AP(q["QSW"][:].tensor, 512 * fc, [[8 * 512, 128], [32, 16], [1, 8]]),
                   bass.AP(CQW.tensor, 128 * 128 * fc, [[128, 128], [8, 16], [1, 8]]), qg)
            pl.dma("gpsimd",
                   bass.AP(q["KSW"][:].tensor, 512 * fc, [[8 * 512, 128], [32, 16], [1, 8]]),
                   bass.AP(CKW.tensor, 128 * 128 * fc, [[128, 128], [8, 16], [1, 8]]), qg)
            pl.dma_nc("gpsimd",
                      bass.AP(q["KSW"][:].tensor, 512 * fc + 8, [[8 * 512, 128], [32, 16], [1, 1]]),
                      bass.AP(KMW.tensor, 16 * 128 * fc, [[16, 128], [1, 16]]), qg)
        pl.dma_nc("gpsimd", bass.AP(q["BRK"][:].tensor, 8, [[512, 1], [32, 16], [1, 1]]),
                  bass.AP(KMB.tensor, 0, [[16, 1], [1, 16]]), qg)
        w_ready = qg.n
        pl.wait("tensor", qg, w_ready)
        for lt in range(TPS):
            pl.wait("tensor", sve)
            for fc in range(8):
                pl.op("tensor", lambda e, fc=fc, lt=lt, q=q: e.matmul(
                    bank(0), ZT[:, 2048 * fc + 128 * lt:2048 * fc + 128 * (lt + 1)],
                    q["QSW"][:, 512 * fc:512 * (fc + 1)], start=(fc == 0), stop=(fc == 7)))
            for fc in range(8):
                pl.op("tensor", lambda e, fc=fc, lt=lt, q=q: e.matmul(
                    bank(1), ZT[:, 2048 * fc + 128 * lt:2048 * fc + 128 * (lt + 1)],
                    q["KSW"][:, 512 * fc:512 * (fc + 1)], start=(fc == 0), stop=False))
            pl.op("tensor", lambda e, q=q: e.matmul(
                bank(1), ONESB[0:1, 0:128], q["BRK"][0:1, :], start=False, stop=True), ste)
            mm_done = ste.n
            pl.wait("vector", ste, mm_done)
            pl.op("vector", lambda e, q=q: e.tensor_scalar(q["QTMP"][:], bank(0), LRSCALE, None, ALU.mult))
            pl.op("vector", lambda e, q=q: e.tensor_copy(q["KTMP"][:], bank(1)))
            kmt_in = bass.AP(PS[:].tensor, 512 * 1 + 8, [[4096, 128], [32, 16]])
            pl.op("vector", lambda e, q=q, lt=lt, i=kmt_in: e.tensor_copy(
                q["KMT"][:, 16 * lt:16 * (lt + 1)], i), sve)
            ev_done = sve.n
            pl.wait("tensor", sve, ev_done)
            for ti, SRC in ((0, q["QTMP"]), (1, q["KTMP"])):
                for c in range(4):
                    pl.op("tensor", lambda e, SRC=SRC, c=c, ti=ti: e.transpose(
                        bankb(7)[:, 512 * ti + 128 * c:512 * ti + 128 * (c + 1)],
                        SRC[:, 128 * c:128 * (c + 1)], IDN[:]),
                        ste if (ti, c) == (1, 3) else None)
            tr_done = ste.n
            pl.wait("vector", ste, tr_done)
            pbv = bankb(7).rearrange("p (a b) -> p a b", a=8)
            for ti, DST in ((0, q["AUGQ"]), (1, q["AUGK"])):
                out = bass.AP(DST[:].tensor, 128 * lt, [[4 * 2048, 128], [2048, 4], [1, 128]])
                inp = bass.AP(pbv.tensor, pbv.offset + 512 * ti, [[8192, 128], [128, 4], [1, 128]])
                pl.op("vector", lambda e, o=out, i=inp: e.tensor_copy(o, i),
                      sve if ti == 1 else None)
            pl.wait("tensor", sve)
        pl.op("vector", lambda e, q=q: e.tensor_reduce(
            q["BND"][:, 0:1], q["KMT"][:], AX.X, ALU.max, apply_absolute_value=True))
        pl.op("vector", lambda e: e.drain())
        pl.op("vector", lambda e, q=q: e.tensor_reduce(
            q["SQ2"][:, 0:1], q["AUGQ"][:], AX.X, ALU.max, apply_absolute_value=True))
        pl.op("vector", lambda e: e.drain())
        pl.op("vector", lambda e, q=q: e.tensor_reduce(
            q["SQ2"][:, 1:2], q["AUGK"][:], AX.X, ALU.max, apply_absolute_value=True))
        pl.op("vector", lambda e: e.drain())
        pl.op("vector", lambda e, q=q: e.tensor_copy(q["SQ2"][:, 2:3], q["BND"][:, 0:1]), sve)
        pl.wait("sync", sve)
        pl.dma("sync", bass.AP(BOUNCE.tensor, 0, [[3, 128], [1, 3]]),
               q["SQ2"][:, 0:3], qs)
        pl.wait("sync", qs)
        pl.dma_nc("sync", q["SQ2"][0:1, 4:388],
                  bass.AP(BOUNCE.tensor, 0, [[384, 1], [1, 384]]), qs)
        pl.wait("vector", qs)
        qm = q["SQ2"][0:1, 388:389]
        km_ = q["SQ2"][0:1, 389:390]
        mm_ = q["SQ2"][0:1, 390:391]
        pl.op("vector", lambda e, q=q, o=qm: e.tensor_reduce(
            o, bass.AP(q["SQ2"][:].tensor, 4, [[512, 1], [3, 128]]), AX.X, ALU.max))
        pl.op("vector", lambda e, q=q, o=km_: e.tensor_reduce(
            o, bass.AP(q["SQ2"][:].tensor, 5, [[512, 1], [3, 128]]), AX.X, ALU.max))
        pl.op("vector", lambda e, q=q, o=mm_: e.tensor_reduce(
            o, bass.AP(q["SQ2"][:].tensor, 6, [[512, 1], [3, 128]]), AX.X, ALU.max))
        pl.op("vector", lambda e: e.drain())
        pl.op("vector", lambda e, qm=qm, km_=km_: e.tensor_tensor(qm, qm, km_, ALU.mult))
        pl.op("vector", lambda e: e.drain())
        pl.op("vector", lambda e, qm=qm: e.tensor_scalar(qm, qm, float(8.0 * LRSCALE), 3.0, ALU.mult, ALU.add))
        pl.op("vector", lambda e: e.drain())
        pl.op("vector", lambda e, qm=qm, mm_=mm_: e.tensor_tensor(qm, qm, mm_, ALU.add), sve)
        pl.wait("sync", sve)
        pl.dma("sync", bass.AP(BOUNCE.tensor, 0, [[1, 1], [1, 1]]), qm, qs)
        pl.wait("sync", qs)
        pl.dma("sync", q["BND"][:, 1:2], bass.AP(BOUNCE.tensor, 0, [[0, 128], [1, 1]]), qs)
        b_ready = qs.n
        pl.wait("vector", qs, b_ready)
        pl.op("vector", lambda e, q=q: e.tensor_scalar(
            q["KMB"][:], q["KMT"][:], q["BND"][:, 1:2], None, ALU.subtract), sve)

    def emit_l1_attn(s, q):
        for hp in range(NPAIR):
            pl.wait("gpsimd", ste)
            pl.dma("gpsimd", q["VWS"][:],
                   bass.AP(VW.tensor, 128 * hp, [[D, 128], [D * 128, 8], [1, 128]]), qg)
            pl.dma("gpsimd", q["VBR"][:], bass.AP(VB.tensor, 128 * hp, [[128, 1], [1, 128]]), qg)
            vw_ready = qg.n
            pl.wait("vector", ste)
            vt_ones = bass.AP(q["VT1"][:].tensor, 64, [[16 * 130, 128], [130, 16], [65, 2], [1, 1]])
            pl.op("vector", lambda e, a=vt_ones: e.memset(a, 1.0), sve)
            pl.wait("tensor", qg, vw_ready)
            pl.wait("tensor", sve)
            for lt in range(TPS):
                for fc in range(8):
                    pl.op("tensor", lambda e, fc=fc, lt=lt, q=q: e.matmul(
                        bank(5)[:, 0:128], ZT[:, 2048 * fc + 128 * lt:2048 * fc + 128 * (lt + 1)],
                        q["VWS"][:, 128 * fc:128 * (fc + 1)], start=(fc == 0), stop=False))
                pl.op("tensor", lambda e, q=q: e.matmul(
                    bank(5)[:, 0:128], ONESB[0:1, 0:128], q["VBR"][0:1, :],
                    start=False, stop=True), ste)
                vmm = ste.n
                pl.wait("vector", ste, vmm)
                vo = bass.AP(q["VT1"][:].tensor, 130 * lt, [[16 * 130, 128], [65, 2], [1, 64]])
                vi = bass.AP(PS[:].tensor, 512 * 5, [[4096, 128], [64, 2], [1, 64]])
                pl.op("vector", lambda e, o=vo, i=vi: e.tensor_copy(o, i), sve)
                pl.wait("tensor", sve)
            for hh in range(2):
                h = 2 * hp + hh
                bh = 16 * s + h
                c = h // 4
                rowblk = 32 * (h % 4)
                toep_sem = pl.sem("toepq")
                if hh == 0:
                    pl.wait("scalar", qs)
                pl.wait("sync", sve, q["_toep_free"][0])
                half_cnt = []
                for half in range(2):
                    pl.dma("sync",
                           bass.AP(q["TOEPM"][:].tensor, 4096 * half,
                                   [[8192, 128], [512, 8], [1, 512]]),
                           bass.AP(STAIR.tensor, bh * STSTRIDE + 1663 + 1024 * half,
                                   [[4094, 128], [128, 8], [1, 512]]), toep_sem)
                    half_cnt.append(toep_sem.n)
                exp_cnt = []
                fj = 0
                pl.wait("vector", sse)
                for qc in range(4):
                    njt = 4 * (qc + 1)
                    pl.wait("tensor", sse)
                    if qc == 0:
                        pl.wait("vector", toep_sem, half_cnt[0])
                    elif qc == 2:
                        pl.wait("vector", toep_sem, half_cnt[1])
                    for b4 in range(njt // 4):
                        jts = list(range(4 * b4, 4 * b4 + 4))
                        sc_done = []
                        for k, jt in enumerate(jts):
                            gb = (fj + k) % 4
                            pl.wait("tensor", sve, q["_bank_last"][gb])
                            pl.op("tensor", lambda e, gb=gb, jt=jt, qc=qc, q=q, c=c,
                                  rowblk=rowblk: e.matmul(
                                bank(gb),
                                q["AUGK"][rowblk:rowblk + 8, 2048 * c + 128 * jt:2048 * c + 128 * (jt + 1)],
                                q["AUGQ"][rowblk:rowblk + 8, 2048 * c + 512 * qc:2048 * c + 512 * (qc + 1)],
                                start=True, stop=True, tile_position=(rowblk, 0)), ste)
                            sc_done.append(ste.n)
                        for k, jt in enumerate(jts):
                            fjj = fj + k
                            gb = fjj % 4
                            mi = 4 * qc - jt + 3
                            pl.wait("vector", ste, sc_done[k])
                            if fjj >= 4:
                                pl.wait("vector", sse, exp_cnt[fjj - 4])
                            pl.op("vector", lambda e, gb=gb, mi=mi, q=q: e.scalar_tensor_tensor(
                                q["EXPIN"][:, 512 * gb:512 * (gb + 1)], bank(gb), 1.0,
                                q["TOEPM"][:, 512 * mi:512 * (mi + 1)], ALU.mult, ALU.add), sve)
                            q["_bank_last"][gb] = sve.n
                            pl.wait("scalar", sve, sve.n)
                            ab = q["ATT1"][:, 512 * gb:512 * (gb + 1)]
                            pl.op("scalar", lambda e, gb=gb, ab=ab, q=q, jt=jt, h=h: e.activation(
                                ab, q["EXPIN"][:, 512 * gb:512 * (gb + 1)], AF.Exp,
                                bias=q["KMB"][:, 16 * jt + h:16 * jt + h + 1]), sse)
                            exp_cnt.append(sse.n)
                        for k, jt in enumerate(jts):
                            gb = (fj + k) % 4
                            ab = q["ATT1"][:, 512 * gb:512 * (gb + 1)]
                            pl.wait("tensor", sse, exp_cnt[fj + k])
                            pl.op("tensor", lambda e, jt=jt, hh=hh, ab=ab, q=q, njt=njt: e.matmul(
                                bank(4)[0:65, :],
                                q["VT1"][:, 130 * jt + 65 * hh:130 * jt + 65 * hh + 65],
                                ab, start=(jt == 0), stop=(jt == njt - 1)),
                                ste if jt == njt - 1 else None)
                        fj += 4
                    ctx_done = ste.n
                    pl.wait("scalar", ste, ctx_done)
                    if hh == 0:
                        pl.op("scalar", lambda e, hp=hp, qc=qc, q=q: e.copy(
                            q["CTX1"][0:64, 2048 * hp + 512 * qc:2048 * hp + 512 * (qc + 1)],
                            bank(4)[0:64, :]))
                    else:
                        pl.op("scalar", lambda e, q=q: e.copy(q["CTXT"][0:64, :], bank(4)[0:64, :]), sse)
                        pl.wait("gpsimd", sse)
                        pl.dma("gpsimd", q["CTX1"][64:128, 2048 * hp + 512 * qc:2048 * hp + 512 * (qc + 1)],
                               q["CTXT"][0:64, :], qg)
                    pl.op("scalar", lambda e, qc=qc, hh=hh, q=q: e.copy(
                        q["SRO"][64 + 32 * hh:65 + 32 * hh, 512 * qc:512 * (qc + 1)],
                        bank(4)[64:65, :]), sse)
                q["_toep_free"][0] = sve.n
            pl.wait("scalar", sse)
            for hh in range(2):
                r0 = 64 + 32 * hh
                pl.op("scalar", lambda e, q=q, r0=r0: e.activation(
                    q["SRO"][r0:r0 + 1, :], q["SRO"][r0:r0 + 1, :], AF.Ln))
                pl.op("scalar", lambda e, q=q, r0=r0: e.activation(
                    q["SRO"][r0:r0 + 1, :], q["SRO"][r0:r0 + 1, :], AF.Exp, scale=-1.0),
                    sse if hh == 1 else None)
            pl.wait("sync", sse)
            for hh in range(2):
                pl.dma("sync", bass.AP(BOUNCE.tensor, 2048 * hh, [[2048, 1], [1, 2048]]),
                       q["SRO"][64 + 32 * hh:65 + 32 * hh, :], qs)
            pl.wait("sync", qs)
            for hh in range(2):
                pl.dma("sync", q["RCP1"][64 * hh:64 * hh + 64, :],
                       bass.AP(BOUNCE.tensor, 2048 * hh, [[0, 64], [1, 2048]]), qs)
            pl.wait("vector", qs)
            pl.wait("vector", qg)
            pl.op("vector", lambda e, hp=hp, q=q: e.tensor_tensor(
                q["CTX1"][:, 2048 * hp:2048 * (hp + 1)], q["CTX1"][:, 2048 * hp:2048 * (hp + 1)],
                q["RCP1"][:], ALU.mult), sve)

    def emit_l1_outproj(s, q2):
        OWS, OBR, R1Z = q2["OWS"], q2["OBR"], q2["R1Z"]
        CTX1 = q2["CTX1"]
        pl.dma("gpsimd", OWS[:], bass.AP(OW.tensor, 0, [[D, 128], [D * 128, 8], [1, 1024]]), qg)
        pl.dma("gpsimd", OBR[:], bass.AP(OB.tensor, 0, [[1024, 1], [1, 1024]]), qg)
        ow_ready = qg.n
        pl.wait("tensor", qg, ow_ready)
        pl.wait("tensor", sve)
        store_sem = pl.sem("l1store")
        st_base = store_sem.n
        for lt in range(TPS):
            tt = s * TPS + lt
            rbuf = ORT[:, 1024 * (lt % 2):1024 * (lt % 2) + 1024]
            obuf = ORT2[:, 1024 * (lt % 2):1024 * (lt % 2) + 1024]
            if lt >= 2:
                pl.wait("sync", store_sem, st_base + 16 * (lt - 1))
            pl.dma("sync", rbuf, resh_tile(RESH, tt), qs)
            rq = qs.n
            pl.wait("vector", qs, rq)
            pl.wait("vector", ste)
            pl.op("vector", lambda e, b=rbuf, r=RSTD[:, tt:tt + 1], nb=NEGB[:, tt:tt + 1], R1Z=R1Z:
                  e.tensor_scalar(R1Z[:], b, r, nb, ALU.mult, ALU.add), sve)
            z3_done = sve.n
            for oc in range(2):
                bk = 5 + oc
                pl.wait("tensor", sve)
                for hp2 in range(8):
                    pl.op("tensor", lambda e, hp2=hp2, lt=lt, oc=oc, bk=bk, CTX1=CTX1: e.matmul(
                        bank(bk),
                        CTX1[:, 2048 * hp2 + 128 * lt:2048 * hp2 + 128 * (lt + 1)],
                        OWS[:, 1024 * hp2 + 512 * oc:1024 * hp2 + 512 * (oc + 1)],
                        start=(hp2 == 0), stop=False))
                pl.op("tensor", lambda e, oc=oc, bk=bk, OBR=OBR: e.matmul(
                    bank(bk), ONESB[0:1, 0:128], OBR[0:1, 512 * oc:512 * (oc + 1)],
                    start=False, stop=True), ste)
                mm_done = ste.n
                pl.wait("vector", ste, mm_done)
                pl.op("vector", lambda e, bk=bk, oc=oc, o=obuf, R1Z=R1Z: e.scalar_tensor_tensor(
                    o[:, 512 * oc:512 * (oc + 1)], bank(bk), 1.0,
                    R1Z[:, 512 * oc:512 * (oc + 1)], ALU.mult, ALU.add), sve)
            pl.wait("sync", sve)
            pl.dma("sync", resh_tile(R1H, tt), obuf, store_sem)
        pl.wait("sync", store_sem)

    def emit_l1_inner():
        ies = contextlib.ExitStack()

        def sbi(n, s, d):
            return ies.enter_context(nc.sbuf_tensor(n, s, d))
        W1S = sbi("iw1", [128, 8 * 2048], BF16)
        W2S = sbi("iw2", [128, 16 * 1024], BF16)
        G = sbi("ig", [128, 32 * 512], BF16)
        HBF = sbi("ihbf", [128, 4 * 1024], BF16)
        HT = sbi("iht", [128, 8 * 512], BF16)
        R2T = sbi("ir2", [128, 1024], F32)
        T2F = sbi("it2", [128, 1024], F32)
        G1R = sbi("ig1r", [128, 1024], BF16)
        B1R = sbi("ib1r", [128, 1024], BF16)
        G2R = sbi("ig2r", [128, 1024], BF16)
        B2R = sbi("ib2r", [128, 1024], BF16)
        FB1C = sbi("ifb1c", [128, 32], F32)
        FB2R = sbi("ifb2r", [1, 1024], BF16)
        for T_, SRCW in ((G1R, LN1G), (B1R, LN1B), (G2R, LN2G), (B2R, LN2B)):
            pl.dma("gpsimd", T_[:], bass.AP(SRCW.tensor, 0, [[0, 128], [1, 1024]]), qg)
        pl.dma_nc("gpsimd", FB1C[:], bass.AP(FB1.tensor, 0, [[1, 128], [128, 32]]), qg)
        pl.dma("gpsimd", FB2R[:], bass.AP(FB2.tensor, 0, [[1024, 1], [1, 1024]]), qg)
        const_ready = qg.n
        pl.wait("vector", qg, const_ready)
        pl.wait("tensor", qg, const_ready)
        w1_sem = pl.sem("iw1s")
        w2_sem = pl.sem("iw2s")
        store_sem = pl.sem("istore")
        pl.wait("gpsimd", fwq_sem)
        pl.wait("sync", fwq_sem)
        for g in range(8):
            for ti in range(4):
                tt = 4 * g + ti
                rbuf = ORT[:, 1024 * (ti % 2):1024 * (ti % 2) + 1024]
                pl.wait("sync", sve)
                pl.wait("sync", ste)
                pl.dma("sync", rbuf, resh_tile(R1H, tt), qs)
                pl.wait("scalar", qs)
                pl.wait("scalar", sve)
                stats_tile(rbuf, TINY[:, 4:5], TINY[:, 5:6])
                pl.wait("vector", sse)
                hb = HBF[:, 1024 * ti:1024 * (ti + 1)]
                pl.op("vector", lambda e, b=rbuf: e.tensor_scalar(
                    T2F[:], b, TINY[:, 4:5], TINY[:, 5:6], ALU.mult, ALU.add))
                pl.op("vector", lambda e: e.drain())
                pl.op("vector", lambda e, hb=hb: e.tensor_tensor(hb, T2F[:], G1R[:], ALU.mult))
                pl.op("vector", lambda e: e.drain())
                pl.op("vector", lambda e, hb=hb: e.tensor_tensor(hb, hb, B1R[:], ALU.add), sve)
                h_done = sve.n
                pl.wait("tensor", sve, h_done)
                for fc in range(8):
                    pl.op("tensor", lambda e, fc=fc, ti=ti, hb=hb: e.transpose(
                        bankb(7)[:, 128 * fc:128 * (fc + 1)], hb[:, 128 * fc:128 * (fc + 1)],
                        IDN[:]), ste if fc == 7 else None)
                pl.wait("vector", ste)
                pbv = bankb(7).rearrange("p (a b) -> p a b", a=8)
                ht_out = bass.AP(HT[:].tensor, 128 * ti, [[8 * 512, 128], [512, 8], [1, 128]])
                pl.op("vector", lambda e, o=ht_out, i=pbv: e.tensor_copy(o, i), sve)
                pl.wait("tensor", sve)
            pl.wait("tensor", sve)
            for half in range(2):
                pl.wait("gpsimd", ste)
                pl.wait("sync", ste)
                pl.dma("gpsimd", W1S[:, 0:8192],
                       bass.AP(FW1H.tensor, 2048 * half, [[4096, 128], [4096 * 128, 4], [1, 2048]]),
                       w1_sem)
                pl.dma("sync", W1S[:, 8192:16384],
                       bass.AP(FW1H.tensor, 2048 * half + 4096 * 128 * 4,
                               [[4096, 128], [4096 * 128, 4], [1, 2048]]), w1_sem)
                pl.wait("tensor", w1_sem)
                for ohh in range(16):
                    oc = 16 * half + ohh
                    bk = ohh % 2
                    pl.wait("tensor", sse)
                    for fc in range(8):
                        pl.op("tensor", lambda e, fc=fc, ohh=ohh, bk=bk: e.matmul(
                            bank(bk), W1S[:, 2048 * fc + 128 * ohh:2048 * fc + 128 * (ohh + 1)],
                            HT[:, 512 * fc:512 * (fc + 1)],
                            start=(fc == 0), stop=(fc == 7)), ste if fc == 7 else None)
                    mm_done = ste.n
                    pl.wait("scalar", ste, mm_done)
                    pl.op("scalar", lambda e, oc=oc, bk=bk: e.activation(
                        G[:, 512 * oc:512 * (oc + 1)], bank(bk), AF.Gelu,
                        bias=FB1C[:, oc:oc + 1]), sse)
            pl.wait("tensor", sse)
            for h2 in range(2):
                pl.wait("gpsimd", ste)
                pl.wait("sync", ste)
                pl.dma("gpsimd", W2S[:, 0:8192],
                       bass.AP(FW2H.tensor, 16 * 128 * 1024 * h2,
                               [[1024, 128], [1024 * 128, 8], [1, 1024]]), w2_sem)
                pl.dma("sync", W2S[:, 8192:16384],
                       bass.AP(FW2H.tensor, 16 * 128 * 1024 * h2 + 1024 * 128 * 8,
                               [[1024, 128], [1024 * 128, 8], [1, 1024]]), w2_sem)
                pl.wait("tensor", w2_sem)
                for ti in range(4):
                    for dc in range(2):
                        bk = 2 * ti + dc
                        for occ in range(16):
                            last = (ti, dc, occ) == (3, 1, 15)
                            pl.op("tensor", lambda e, occ=occ, ti=ti, dc=dc, bk=bk, h2=h2: e.matmul(
                                bank(bk),
                                G[:, 512 * (16 * h2 + occ) + 128 * ti:512 * (16 * h2 + occ) + 128 * (ti + 1)],
                                W2S[:, 1024 * occ + 512 * dc:1024 * occ + 512 * (dc + 1)],
                                start=(h2 == 0 and occ == 0), stop=False),
                                ste if last else None)
                if h2 == 1:
                    for ti in range(4):
                        for dc in range(2):
                            bk = 2 * ti + dc
                            pl.op("tensor", lambda e, bk=bk, dc=dc: e.matmul(
                                bank(bk), ONESB[0:1, 0:128], FB2R[0:1, 512 * dc:512 * (dc + 1)],
                                start=False, stop=True), ste if bk == 7 else None)
            f2_done = ste.n
            dbg_g0 = (debug_stop == "l1h" and g == 0)
            pl.wait("vector", ste, f2_done)
            pl.wait("vector", store_sem)
            for ti in range(4):
                tt = 4 * g + ti
                rbuf = ORT[:, 1024 * (ti % 2):1024 * (ti % 2) + 1024]
                obuf = ORT2[:, 1024 * (ti % 2):1024 * (ti % 2) + 1024]
                if ti >= 2:
                    pl.wait("sync", store_sem, store_sem.n - 16)
                pl.dma("sync", rbuf, resh_tile(RESH, tt), qs)
                rq = qs.n
                for dc in range(2):
                    pl.op("vector", lambda e, ti=ti, dc=dc: e.scalar_tensor_tensor(
                        R2T[:, 512 * dc:512 * (dc + 1)], bank(2 * ti + dc), 1.0,
                        HBF[:, 1024 * ti + 512 * dc:1024 * ti + 512 * (dc + 1)],
                        ALU.mult, ALU.add), sve if dc == 1 else None)
                pl.wait("scalar", sve)
                stats_tile(R2T[:], TINY[:, 6:7], TINY[:, 7:8])
                pl.wait("vector", sse)
                pl.op("vector", lambda e: e.tensor_scalar(
                    T2F[:], R2T[:], TINY[:, 6:7], TINY[:, 7:8], ALU.mult, ALU.add))
                pl.op("vector", lambda e: e.drain())
                pl.op("vector", lambda e: e.tensor_tensor(T2F[:], T2F[:], G2R[:], ALU.mult))
                pl.op("vector", lambda e: e.drain())
                pl.op("vector", lambda e: e.tensor_tensor(T2F[:], T2F[:], B2R[:], ALU.add))
                pl.wait("vector", qs, rq)
                pl.op("vector", lambda e, o=obuf, rb=rbuf: e.tensor_tensor(o, T2F[:], rb, ALU.add), sve)
                pl.wait("sync", sve)
                pl.dma("sync", resh_tile(RESH, tt), obuf, store_sem)
                if dbg_g0 and ti == 3:
                    pl.wait("sync", store_sem)
                    pl.barrier()
                    for tx in range(4):
                        dbg(bass.AP(RESH.tensor, tx * 128 * 1024, [[1024, 128], [1, 1024]]), 128 * 1024)
                    pl.wait("sync", sout)
                    pl.wait("gpsimd", sout)
                    ies.close()
                    return "dbg"
        pl.wait("sync", store_sem)
        pl.barrier()
        ies.close()

    W1QH = nc.dram_tensor("w1qh", [4 * D * D + 8], BF16, kind="Internal").ap()
    W2QH = nc.dram_tensor("w2qh", [4 * D * D + 8], BF16, kind="Internal").ap()

    def emit_l1_ffn():
        fes = contextlib.ExitStack()

        def sbq(n, s, d):
            return fes.enter_context(nc.sbuf_tensor(n, s, d))
        WB = sbq("qwb", [128, 2048], F32)
        QI2 = sbq("qqi", [128, 1024], I32)
        QD2 = sbq("qqd", [128, 1024], F32)
        QG2 = sbq("qqg", [128, 1024], F32)
        XQ2 = sbq("qxq", [128, 1024], BF16)
        GC = sbq("qgc", [128, 2], F32)
        RG = sbq("qrg", [128, 2], F32)
        GSR = sbq("qgsr", [1, 272], F32)
        QB1C = sbq("qb1c", [128, 32], F32)
        NB2R = sbq("qnb2", [1, 1024], BF16)
        B2F = sbq("qb2f", [1, 1024], F32)
        pl.op("vector", lambda e: e.memset(GC[:], 0.0), sve)
        gsem = pl.sem("qgsem")
        for wi, W_ in enumerate((QW1, QW2)):
            for c in range(16):
                if c > 0 or wi > 0:
                    pl.wait("sync", sve)
                pl.dma("sync", WB[:],
                       bass.AP(W_.tensor, 128 * 2048 * c, [[2048, 128], [1, 2048]]), gsem)
                pl.wait("vector", gsem)
                pl.op("vector", lambda e: e.tensor_reduce(
                    TINY[:, 0:1], WB[:], AX.X, ALU.add, apply_absolute_value=True))
                pl.op("vector", lambda e: e.drain())
                pl.op("vector", lambda e, wi=wi: e.tensor_tensor(
                    GC[:, wi:wi + 1], GC[:, wi:wi + 1], TINY[:, 0:1], ALU.add), sve)
                pl.op("vector", lambda e: e.drain())
        pl.wait("sync", sve)
        pl.dma("sync", bass.AP(BOUNCE.tensor, 0, [[2, 128], [1, 2]]), GC[:], qs)
        pl.wait("sync", qs)
        pl.dma("sync", GSR[0:1, 0:256], bass.AP(BOUNCE.tensor, 0, [[256, 1], [1, 256]]), qs)
        pl.wait("vector", qs)
        for wi in range(2):
            pl.op("vector", lambda e, wi=wi: e.tensor_reduce(
                GSR[0:1, 256 + wi:257 + wi],
                bass.AP(GSR[:].tensor, wi, [[272, 1], [2, 128]]), AX.X, ALU.add))
            pl.op("vector", lambda e: e.drain())
        NW = float(4 * D * D)
        pl.op("vector", lambda e: e.tensor_scalar(
            GSR[0:1, 260:262], GSR[0:1, 256:258], 1.0 / NW, 1e-5, ALU.mult, ALU.add))
        pl.op("vector", lambda e: e.drain())
        pl.op("vector", lambda e: e.reciprocal(GSR[0:1, 264:266], GSR[0:1, 260:262]), sve)
        pl.wait("sync", sve)
        pl.dma("sync", bass.AP(BOUNCE.tensor, 0, [[2, 1], [1, 2]]), GSR[0:1, 264:266], qs)
        pl.wait("sync", qs)
        pl.dma("sync", RG[:], bass.AP(BOUNCE.tensor, 0, [[0, 128], [1, 2]]), qs)
        rg_ready = qs.n
        pl.wait("vector", qs, rg_ready)
        qs2 = pl.sem("qffw")
        for wi, W_ in enumerate((QW1, QW2)):
            for c in range(32):
                if wi == 0:
                    src_ap = bass.AP(W_.tensor, 128 * 1024 * c, [[1024, 128], [1, 1024]])
                else:
                    dr, ob = c // 4, c % 4
                    src_ap = bass.AP(W_.tensor, 128 * 4096 * dr + 1024 * ob, [[4096, 128], [1, 1024]])
                pl.wait("sync", sve)
                pl.dma("sync", WB[:, 0:1024], src_ap, gsem)
                pl.wait("vector", gsem)
                pl.op("vector", lambda e, wi=wi: e.tensor_scalar(
                    QD2[:], WB[:, 0:1024], RG[:, wi:wi + 1], None, ALU.mult))
                pl.op("vector", lambda e: e.tensor_copy(QI2[:], QD2[:]))
                pl.op("vector", lambda e: e.scalar_tensor_tensor(
                    QG2[:], QI2[:], 1.0, QD2[:], ALU.mult, ALU.subtract))
                pl.op("vector", lambda e: e.scalar_tensor_tensor(
                    QD2[:], QG2[:], -0.5, QI2[:], ALU.is_lt, ALU.add))
                pl.op("vector", lambda e: e.drain())
                pl.op("vector", lambda e: e.scalar_tensor_tensor(
                    QG2[:], QG2[:], 0.5, QD2[:], ALU.is_gt, ALU.subtract))
                pl.op("vector", lambda e: e.drain())
                pl.op("vector", lambda e: e.tensor_scalar(
                    XQ2[:], QG2[:], -1.0, 1.0, ALU.max, ALU.min), sve)
                xq_done = sve.n
                pl.wait("tensor", sve, xq_done)
                for fs in range(8):
                    pl.op("tensor", lambda e, fs=fs: e.transpose(
                        bankb(7)[:, 128 * fs:128 * (fs + 1)], XQ2[:, 128 * fs:128 * (fs + 1)],
                        IDN[:]), ste if fs == 7 else None)
                pl.wait("vector", ste)
                pbv = bankb(7)
                pl.op("vector", lambda e, i=pbv: e.tensor_copy(XQ2[:], i), sve)
                tq_done = sve.n
                pl.wait("tensor", sve)
                pl.wait("sync", sve, tq_done)
                if wi == 0:
                    out_ap = bass.AP(W1QH.tensor, 128 * c,
                                     [[4096, 128], [4096 * 128, 8], [1, 128]])
                else:
                    dr, ob = c // 4, c % 4
                    out_ap = bass.AP(W2QH.tensor, 1024 * 1024 * ob + 128 * dr,
                                     [[1024, 128], [1024 * 128, 8], [1, 128]])
                pl.dma("sync", out_ap, XQ2[:], qs2)
        pl.wait("sync", qs2)
        pl.barrier()
        fes.close()
        mes = contextlib.ExitStack()

        def sbm(n, s, d):
            return mes.enter_context(nc.sbuf_tensor(n, s, d))
        W1QS = sbm("fw1q", [128, 8 * 2048], BF16)
        W2QS = sbm("fw2q", [128, 16 * 1024], BF16)
        HFM = sbm("fhfm", [128, 32 * 512], BF16)
        QB1Cm = sbm("fqb1", [128, 32], F32)
        NB2Rm = sbm("fnb2", [1, 1024], BF16)
        B2Fm = sbm("fb2f", [1, 1024], F32)
        pl.dma_nc("gpsimd", QB1Cm[:], bass.AP(QB1.tensor, 0, [[1, 128], [128, 32]]), qg)
        pl.dma("gpsimd", B2Fm[:], bass.AP(QB2.tensor, 0, [[1024, 1], [1, 1024]]), qg)
        cready = qg.n
        pl.wait("vector", qg, cready)
        pl.op("vector", lambda e: e.tensor_scalar(NB2Rm[:], B2Fm[:], -1.0, None, ALU.mult), sve)
        w1_sem = pl.sem("fw1s")
        w2_sem = pl.sem("fw2s")
        for s_ in range(NSEQ):
            ln_seq(s_, neg=True)
            for gg in range(4):
                tt0 = 16 * s_ + 4 * gg
                pl.wait("tensor", sve)
                for half in range(2):
                    pl.wait("gpsimd", ste)
                    pl.wait("sync", ste)
                    pl.dma("gpsimd", W1QS[:, 0:8192],
                           bass.AP(W1QH.tensor, 2048 * half, [[4096, 128], [4096 * 128, 4], [1, 2048]]),
                           w1_sem)
                    pl.dma("sync", W1QS[:, 8192:16384],
                           bass.AP(W1QH.tensor, 2048 * half + 4096 * 128 * 4,
                                   [[4096, 128], [4096 * 128, 4], [1, 2048]]), w1_sem)
                    pl.wait("tensor", w1_sem)
                    for ohh in range(16):
                        oc = 16 * half + ohh
                        bk = ohh % 2
                        pl.wait("tensor", sse)
                        for fc in range(8):
                            pl.op("tensor", lambda e, fc=fc, ohh=ohh, bk=bk, gg=gg: e.matmul(
                                bank(bk), W1QS[:, 2048 * fc + 128 * ohh:2048 * fc + 128 * (ohh + 1)],
                                ZT[:, 2048 * fc + 512 * gg:2048 * fc + 512 * (gg + 1)],
                                start=(fc == 0), stop=(fc == 7)), ste if fc == 7 else None)
                        mm_done = ste.n
                        pl.wait("scalar", ste, mm_done)
                        pl.op("scalar", lambda e, oc=oc, bk=bk: e.activation(
                            HFM[:, 512 * oc:512 * (oc + 1)], bank(bk), AF.Relu,
                            bias=QB1Cm[:, oc:oc + 1]), sse)
                pl.wait("tensor", sse)
                for h2 in range(2):
                    pl.wait("gpsimd", ste)
                    pl.wait("sync", ste)
                    pl.dma("gpsimd", W2QS[:, 0:8192],
                           bass.AP(W2QH.tensor, 1024 * 1024 * 2 * h2,
                                   [[1024, 128], [1024 * 128, 8], [1, 1024]]), w2_sem)
                    pl.dma("sync", W2QS[:, 8192:16384],
                           bass.AP(W2QH.tensor, 1024 * 1024 * 2 * h2 + 1024 * 128 * 8,
                                   [[1024, 128], [1024 * 128, 8], [1, 1024]]), w2_sem)
                    pl.wait("tensor", w2_sem)
                    for ti in range(4):
                        for dc in range(2):
                            bk = 2 * ti + dc
                            for occ in range(16):
                                last = (ti, dc, occ) == (3, 1, 15)
                                pl.op("tensor", lambda e, occ=occ, ti=ti, dc=dc, bk=bk, h2=h2: e.matmul(
                                    bank(bk),
                                    HFM[:, 512 * (16 * h2 + occ) + 128 * ti:512 * (16 * h2 + occ) + 128 * (ti + 1)],
                                    W2QS[:, 1024 * occ + 512 * dc:1024 * occ + 512 * (dc + 1)],
                                    start=(h2 == 0 and occ == 0), stop=False),
                                    ste if last else None)
                if True:
                    for ti in range(4):
                        for dc in range(2):
                            bk = 2 * ti + dc
                            pl.op("tensor", lambda e, bk=bk, dc=dc: e.matmul(
                                bank(bk), ONESB[0:1, 0:128], NB2Rm[0:1, 512 * dc:512 * (dc + 1)],
                                start=False, stop=True), ste if bk == 7 else None)
                f2_done = ste.n
                pl.wait("vector", ste, f2_done)
                pl.wait("vector", sout)
                for ti in range(4):
                    tt = tt0 + ti
                    rbuf = ORT[:, 1024 * (ti % 2):1024 * (ti % 2) + 1024]
                    obuf = ORT2[:, 1024 * (ti % 2):1024 * (ti % 2) + 1024]
                    pl.dma("sync", rbuf, resh_tile(RESH, tt), qs)
                    rq = qs.n
                    pl.wait("vector", qs, rq)
                    for dc in range(2):
                        pl.op("vector", lambda e, ti=ti, dc=dc, o=obuf, rb=rbuf: e.scalar_tensor_tensor(
                            o[:, 512 * dc:512 * (dc + 1)], bank(2 * ti + dc), -1.0,
                            rb[:, 512 * dc:512 * (dc + 1)], ALU.mult, ALU.add),
                            sve if dc == 1 else None)
                    pl.wait("sync", sve)
                    pl.dma("sync", resh_tile(OUT, tt), obuf, sout)
        pl.wait("sync", sout)
        pl.wait("gpsimd", sout)
        pl.barrier()
        mes.close()

    dbg_off = [0]

    def dbg(ap_in, nelem, cast=False, parts=128):
        off = dbg_off[0]
        per = nelem // parts
        o = bass.AP(OUT.tensor, off, [[per, parts], [1, per]])
        pl.dma("gpsimd" if cast else "sync", o, ap_in, sout)
        dbg_off[0] += nelem

    def finish():
        pl.wait("sync", sout)
        pl.wait("gpsimd", sout)

    if debug_stop == "resh":
        pl.barrier()
        dbg(bass.AP(RESH.tensor, 0, [[8192, 128], [1, 8192]]), 128 * 8192)
        finish()
    elif debug_stop == "lnraw":
        ln_seq(0)
        pl.barrier()
        dbg(RT[:], 128 * 2048)
        dbg(TINY[:], 128 * 8)
        dbg(ZB[:], 128 * 2048, cast=True)
        finish()
    elif debug_stop == "trig":
        dbg(TRIGC, S * 512)
        dbg(TRIGS, S * 512)
        finish()
    elif debug_stop == "ln0":
        ln_seq(0)
        pl.barrier()
        dbg(ZT[:], 128 * 16384, cast=True)
        dbg(RSTD[:, 0:16], 128 * 16, cast=False)
        dbg(NEGB[:, 0:16], 128 * 16, cast=False)
        finish()
    elif debug_stop == "qkv0":
        ln_seq(0)
        pl.barrier()
        l0_attention(0, dbg_pair=0)
        pl.barrier()
        dbg(QKP[:], 128 * 4096, cast=True)
        dbg(QR[:], 128 * 2048, cast=True)
        dbg(KR[:], 128 * 2048, cast=True)
        dbg(VT[:], 128 * 2080, cast=True)
        dbg(NEGBC[:], 128 * 2)
        finish()
    elif debug_stop == "ctx0":
        ln_seq(0)
        pl.barrier()
        l0_attention(0, dbg_pair=0)
        pl.barrier()
        dbg(CTXA[:, 0:2048], 128 * 2048, cast=True)
        dbg(RCPR[:], 128 * 2048)
        finish()
    else:
        ln_seq(0)
        pl.barrier()
        l0_attention(0)
        pl.barrier()
        ln_seq(1)
        pl.barrier()
        l0_attention(1)
        pl.barrier()
        if debug_stop != "l0att":
            l0_es.close()
            emit_l0_ffn()
        if debug_stop in ("r1h", "l1att", "l1proj", "l1ctx", "l1h", "l1wts") or debug_stop is None:
            l1_es = contextlib.ExitStack()
            p1 = l1_alloc(l1_es)
            emit_l1_weights(p1)
            zt_sp = pl.sem("ztspill")
            fwq_sem = pl.sem("fwqcast")
            ln_seq(0, pooled=True)
            emit_pooled_finish(0)
            pl.barrier()
            pl.dma("sync", bass.AP(ZTH.tensor, 0, [[16384, 128], [1, 16384]]), ZT[:], zt_sp)
            pl.wait("vector", zt_sp)
            ln_seq(1, pooled=True)
            emit_pooled_finish(1)
            pl.barrier()
            pl.dma("sync", bass.AP(ZTH.tensor, 128 * 16384, [[16384, 128], [1, 16384]]),
                   ZT[:], zt_sp)
            pl.wait("sync", zt_sp)
            emit_l1_masks(p1)
            emit_l1_pooled(0, p1)
            emit_l1_kern_ck(0, p1)
            emit_l1_pooled(1, p1)
            emit_l1_kern_ck(1, p1)
            pl.barrier()
            l1_es.close()
            if debug_stop == "l1wts":
                pl.dma("sync", OUT, RESH, sout)
                finish()
                return _mk_finish(nc, es, pl)
            for s_ in range(NSEQ):
                pl.dma("sync", ZT[:],
                       bass.AP(ZTH.tensor, s_ * 128 * 16384, [[16384, 128], [1, 16384]]), zt_sp)
                pl.wait("tensor", zt_sp)
                pl.wait("vector", zt_sp)
                pl.barrier()
                l1a_es = contextlib.ExitStack()
                qq = l1a_alloc(l1a_es)
                emit_l1_proj(s_, qq)
                if s_ == 0:
                    pl.dma("gpsimd", bass.AP(FW1H.tensor, 0, [[4096, 1024], [1, 4096]]),
                           bass.AP(FW1.tensor, 0, [[4096, 1024], [1, 4096]]), fwq_sem)
                    pl.dma("gpsimd", bass.AP(FW2H.tensor, 0, [[1024, 4096], [1, 1024]]),
                           bass.AP(FW2.tensor, 0, [[1024, 4096], [1, 1024]]), fwq_sem)
                pl.barrier()
                if debug_stop == "l1proj" and s_ == 0:
                    dbg(qq["QSW"][:], 128 * 4096, cast=True)
                    dbg(qq["AUGQ"][:], 128 * 8192, cast=True)
                    dbg(ZT[:], 128 * 16384, cast=True)
                    finish()
                    l1a_es.close()
                    return _mk_finish(nc, es, pl)
                emit_l1_attn(s_, qq)
                pl.barrier()
                if debug_stop == "l1ctx" and s_ == 0:
                    dbg(qq["CTX1"][:], 128 * 16384, cast=True)
                    dbg(qq["KMT"][:], 128 * 256)
                    dbg(qq["BND"][:, 1:2], 128 * 1)
                    finish()
                    l1a_es.close()
                    return _mk_finish(nc, es, pl)
                emit_l1_outproj(s_, qq)
                pl.barrier()
                l1a_es.close()
            if debug_stop == "l1ctx":
                pass
            if debug_stop == "r1h":
                pl.dma("sync", OUT, R1H, sout)
                finish()
                return _mk_finish(nc, es, pl)
            r_ = emit_l1_inner()
            if debug_stop == "l1h":
                return _mk_finish(nc, es, pl)
            if debug_stop == "l1att":
                pl.dma("sync", OUT, RESH, sout)
                finish()
                return _mk_finish(nc, es, pl)
            emit_l1_ffn()
            finish()
            return _mk_finish(nc, es, pl)
        if debug_stop in ("ck0",):
            l1_es = contextlib.ExitStack()
            p1 = l1_alloc(l1_es)
            emit_l1_weights(p1)
            emit_l1_masks(p1)
            pl.barrier()
            ln_seq(0)
            pl.barrier()
            emit_l1_pooled(0, p1)
            pl.barrier()
            emit_l1_kern_ck(0, p1)
            pl.barrier()
            dbg(p1["CK"][0:16, 0:4096], 16 * 4096, parts=16)
            dbg(p1["WTC"][:], 128 * 4, cast=True)
            dbg(p1["KMH"][:], 128 * 8192, cast=True)
            finish()
            l1_es.close()
            return _mk_finish(nc, es, pl)
        if debug_stop == "l0att":
            l0_es.close()
        pl.dma("sync", OUT, RESH, sout)
        finish()

    return _mk_finish(nc, es, pl)


def _mk_finish(nc, es, pl):
    with es:
        block = es.enter_context(nc.Block())
        for name in ENGINES:
            def mk(name):
                def body(e):
                    for f in pl.q[name]:
                        f(e)
                return body
            getattr(block, name)(mk(name))
    return nc


def kernel(**inputs):
    nc = build(debug_stop=DEBUG_STOP)
    x = np.ascontiguousarray(np.asarray(inputs["L0_attn_x"], np.float32).reshape(16, S, D))
    shards = np.split(x, 8, axis=0)
    in_maps = []
    for c in range(8):
        m = {k: np.ascontiguousarray(np.asarray(v)) for k, v in inputs.items() if k != "L0_attn_x"}
        m["L0_attn_x"] = np.ascontiguousarray(shards[c].reshape(T, D))
        in_maps.append(m)
    res = run_bass_kernel_spmd(nc, in_maps, core_ids=list(range(8)))
    return np.concatenate([r["out"].reshape(2, S, D) for r in res.results], axis=0)
